# version 17
# baseline (speedup 1.0000x reference)
import math
from contextlib import ExitStack

import numpy as np
import ml_dtypes
import sys

sys.path.insert(0, "/opt/trn_rl_repo")
import concourse.bass as bass
import concourse.mybir as mybir
from concourse.bass_utils import run_bass_kernel_spmd

F32 = mybir.dt.float32
BF16 = mybir.dt.bfloat16
AF = mybir.ActivationFunctionType
ALU = mybir.AluOpType
AX = mybir.AxisListType
BF = ml_dtypes.bfloat16
I32 = mybir.dt.int32

NCORES = 8
T = 2048
D = 1024
ND = D // 128
DI = 2048
NDI = DI // 128
FF = 4096
NF = FF // 128
NEXP = 8
RLORA = 16
NHEAD = 16
HD = 64
NSTATE = 16
EPS_LN = 1e-5

MAGIC = float(np.float32(1.5 * 2 ** 23))


ENGS = ("pe", "dve", "act", "pool", "sp")


def _key(x):
    if isinstance(x, tuple):
        return x
    if isinstance(x, str):
        return (x, "*")
    return (x.name, "*")


class Op:
    __slots__ = ("i", "eng", "fn", "deps", "dma", "tag", "sem", "val", "waits", "dkey")

    def __init__(self, i, eng, fn, deps, dma, tag):
        self.i, self.eng, self.fn, self.deps, self.dma, self.tag = i, eng, fn, deps, dma, tag
        self.sem = None
        self.val = None


class Prog:
    def __init__(self, name="prog"):
        self.name = name
        self.nc = bass.Bass("TRN2", target_bir_lowering=False)
        self.stack = ExitStack()
        self.ops = []
        self.writers = {}
        self.readers = {}
        self.last_dma = {}
        self.bar = []
        self.ext_inputs = []
        self.ext_outputs = []
        self._uid = 0
        self._consts = {}

    def const_col(self, val):
        val = float(val)
        t = self._consts.get(val)
        if t is None:
            t = self.stack.enter_context(self.nc.sbuf_tensor(f"constc_{len(self._consts)}", [128, 1], F32, side="right"))
            self.op("dve", lambda h: h.memset(t[:], val), writes=[t[:]], tag=f"const_{val}")
            self._consts[val] = t
        return t

    def dram(self, name, shape, dtype, kind="Internal"):
        if not hasattr(self, "_dram_cache"):
            self._dram_cache = {}
        if name in self._dram_cache:
            return self._dram_cache[name]
        h = self.nc.dram_tensor(name, list(shape), dtype, kind=kind)
        if kind == "ExternalInput":
            self.ext_inputs.append(name)
        elif kind == "ExternalOutput":
            self.ext_outputs.append(name)
        self._dram_cache[name] = h
        return h

    def sb(self, name, shape, dtype):
        return self.stack.enter_context(self.nc.sbuf_tensor(name, list(shape), dtype))

    def ps(self, name, shape, dtype=F32):
        return self.stack.enter_context(self.nc.psum_tensor(name, list(shape), dtype))

    def uid(self):
        self._uid += 1
        return self._uid

    def op(self, eng, fn, reads=(), writes=(), dma=False, tag=""):
        deps = set(self.bar)
        for r in reads:
            n, t = _key(r)
            for wt, wi in self.writers.get(n, {}).items():
                if t == "*" or wt == "*" or wt == t:
                    deps.add(wi)
        wkeys = [_key(w) for w in writes]
        for n, t in wkeys:
            for wt, wi in self.writers.get(n, {}).items():
                if t == "*" or wt == "*" or wt == t:
                    deps.add(wi)
            for rt, ris in self.readers.get(n, {}).items():
                if t == "*" or rt == "*" or rt == t:
                    deps.update(ris)
        i = len(self.ops)
        o = Op(i, eng, fn, deps, dma, tag)
        o.dkey = None
        self.ops.append(o)
        for r in reads:
            n, t = _key(r)
            self.readers.setdefault(n, {}).setdefault(t, []).append(i)
        for n, t in wkeys:
            w = self.writers.setdefault(n, {})
            if t == "*":
                w.clear()
            w[t] = i
            rd = self.readers.get(n)
            if rd is not None:
                if t == "*":
                    rd.clear()
                else:
                    rd.pop(t, None)
        return i

    def barrier(self):
        last = {}
        for o in self.ops:
            if not o.dma:
                last[o.eng] = o.i
        keep = {}
        for i in self.bar:
            if self.ops[i].dma:
                keep[self.ops[i].dkey] = i
        keep.update(self.last_dma)
        self.bar = list(last.values()) + list(keep.values())
        self.last_dma = {}

    def finalize(self):
        ops = self.ops
        for o in ops:
            if o.eng == "pe":
                o.deps = {d for d in o.deps if ops[d].eng != "pe"}
        needed = set()
        for o in ops:
            needed.update(o.deps)
        for o in ops:
            if o.dma:
                needed.add(o.i)
        dnames = []
        for o in ops:
            if o.dma:
                if o.dkey is None:
                    o.dkey = f"anon{o.i}"
                if o.dkey not in dnames:
                    dnames.append(o.dkey)
        ngroups = min(len(dnames), 90) if dnames else 0
        dmap = {nm: f"dma{idx % ngroups}" for idx, nm in enumerate(dnames)}
        counters = {}
        for o in ops:
            if o.i in needed:
                kind = dmap[o.dkey] if o.dma else o.eng
                step = 16 if o.dma else 1
                counters[kind] = counters.get(kind, 0) + step
                o.sem, o.val = kind, counters[kind]
        self.final_counts = dict(counters)
        for o in ops:
            w = {}
            for d in o.deps:
                od = ops[d]
                if od.sem is None:
                    continue
                if w.get(od.sem, 0) < od.val:
                    w[od.sem] = od.val
            o.waits = sorted(w.items())
        qs = {e: [o for o in ops if o.eng == e] for e in ENGS}
        pos = {e: 0 for e in ENGS}
        sem = {k: 0 for k in counters}
        total = len(ops)
        done = 0
        progress = True
        while done < total and progress:
            progress = False
            for e in ENGS:
                while pos[e] < len(qs[e]):
                    o = qs[e][pos[e]]
                    if all(sem.get(k, 0) >= v for k, v in o.waits):
                        if o.sem:
                            sem[o.sem] = o.val
                        pos[e] += 1
                        done += 1
                        progress = True
                    else:
                        break
        if done < total:
            stuck = [(e, qs[e][pos[e]].tag, qs[e][pos[e]].waits) for e in ENGS if pos[e] < len(qs[e])]
            raise RuntimeError(f"deadlock in schedule: {stuck[:5]} sem={sem}")

    def emit(self):
        self.finalize()
        nc = self.nc
        sems = {}
        for k in sorted(set(list(self.final_counts.keys()))):
            sems[k] = self.stack.enter_context(nc.semaphore("sem_" + k))
        block = self.stack.enter_context(nc.Block())
        handles = {"pe": None, "dve": None, "act": None, "pool": None, "sp": None}

        def emit_engine(eng_name, h):
            seen = {}
            for o in self.ops:
                if o.eng != eng_name:
                    continue
                for k, v in o.waits:
                    if seen.get(k, 0) < v:
                        h.wait_ge(sems[k], v)
                        seen[k] = v
                inst = o.fn(h)
                if o.sem is not None:
                    inst.then_inc(sems[o.sem], 16 if o.dma else 1)

        engs_with_ops = {o.eng for o in self.ops}

        if "pe" in engs_with_ops:
            @block.tensor
            def _(h):
                emit_engine("pe", h)
        if "dve" in engs_with_ops:
            @block.vector
            def _(h):
                emit_engine("dve", h)
        if "act" in engs_with_ops:
            @block.scalar
            def _(h):
                emit_engine("act", h)
        if "pool" in engs_with_ops:
            @block.gpsimd
            def _(h):
                emit_engine("pool", h)

        @block.sync
        def _(h):
            emit_engine("sp", h)
            for k, v in sorted(self.final_counts.items()):
                h.wait_ge(sems[k], v)

    def run(self, host_arrays, n_cores=NCORES, **kw):
        self.emit()
        in_maps = []
        for c in range(n_cores):
            in_maps.append({n: host_arrays[c][n] for n in self.ext_inputs})
        res = run_bass_kernel_spmd(self.nc, in_maps, core_ids=list(range(n_cores)), **kw)
        self.stack.close()
        return res



def dma(S, q, out_ap, in_ap, reads=(), writes=(), tag="dma"):
    i = S.op(q, lambda h: h.dma_start(out=out_ap, in_=in_ap),
             reads=list(reads) + [in_ap], writes=list(writes) + [out_ap], dma=True, tag=tag)
    S.ops[i].dkey = out_ap.name
    S.last_dma[out_ap.name] = i
    return i


def mm(S, out, lhsT, rhs, start, stop, reads=(), writes=(), tag="mm"):
    return S.op("pe", lambda h: h.matmul(out, lhsT, rhs, start=start, stop=stop),
                reads=list(reads) + [lhsT, rhs], writes=list(writes) + [out], tag=tag)


def act(S, out, in_, func, bias=0.0, scale=1.0, accum_out=None, reads=(), writes=(), tag="act"):
    np_ = in_.shape[0]
    if isinstance(bias, (int, float)) and float(bias) != 0.0:
        bias = S.const_col(bias)[0:np_, 0:1]
    if isinstance(scale, (int, float)) and float(scale) != 1.0:
        scale = S.const_col(scale)[0:np_, 0:1]
    ex_r = [x for x in (bias, scale) if isinstance(x, bass.AP)]
    ex_w = [accum_out] if accum_out is not None else []
    return S.op("act", lambda h: h.activation(out, in_, func, bias=bias, scale=scale, accum_out=accum_out),
                reads=list(reads) + [in_] + ex_r, writes=list(writes) + [out] + ex_w, tag=tag)


def v_op(S, fname, out, *ins, reads=(), writes=(), tag=None, **kw):
    aps = [x for x in ins if isinstance(x, bass.AP)]
    return S.op("dve", lambda h: getattr(h, fname)(out, *ins, **kw),
                reads=list(reads) + aps, writes=list(writes) + [out], tag=tag or fname)



def col_layout(v):
    n = v.shape[0] // 128
    return np.ascontiguousarray(v.reshape(n, 128).T).astype(np.float32)


def prep_host(inputs):
    g = {}
    x = np.asarray(inputs["L0_attn_a_x"])

    g["c_ident_bf"] = np.eye(128, dtype=BF)
    g["c_ident_f32"] = np.eye(128, dtype=np.float32)
    g["c_ones_r_bf"] = np.ones((1, 512), dtype=BF)
    g["c_ones_c_f32"] = np.ones((1, 128), dtype=np.float32)
    g["c_ones_col_bf"] = np.ones((128, 1), dtype=BF)
    g["c_ones_col_f32"] = np.ones((128, 1), dtype=np.float32)

    g["m_w_in"] = np.ascontiguousarray(inputs["L0_attn_a_w_in"])
    g["m_norm_scale"] = col_layout(inputs["L0_attn_a_norm_scale"])
    g["m_conv_w"] = np.ascontiguousarray(
        inputs["L0_attn_a_conv_w"][:, 0, :].T.reshape(NDI, 128, 4).transpose(1, 0, 2)).astype(np.float32)
    g["m_conv_b"] = col_layout(inputs["L0_attn_a_conv_b"])
    g["m_A_log"] = np.ascontiguousarray(
        inputs["L0_attn_a_A_log"].reshape(NDI, 128, NSTATE).transpose(1, 0, 2)).astype(np.float32)
    g["m_D"] = col_layout(inputs["L0_attn_a_D"])
    wx = np.asarray(inputs["L0_attn_a_w_xproj"])
    g["m_wx_delta"] = np.ascontiguousarray(wx[:, :64])
    g["m_wx_B"] = np.ascontiguousarray(wx[:, 64:80])
    g["m_wx_C"] = np.ascontiguousarray(wx[:, 80:96])
    g["m_w_dt"] = np.ascontiguousarray(inputs["L0_attn_a_w_dt"])
    g["m_b_dt"] = col_layout(inputs["L0_attn_a_b_dt"])
    g["m_w_out"] = np.ascontiguousarray(inputs["L0_attn_a_w_out"])

    g["f_norm_w"] = col_layout(inputs["L0_ffn_norm_w"])
    g["f_w1T"] = np.ascontiguousarray(inputs["L0_ffn_w1"].T)
    g["f_w3T"] = np.ascontiguousarray(inputs["L0_ffn_w3"].T)
    g["f_w2T"] = np.ascontiguousarray(inputs["L0_ffn_w2"].T)
    g["f_gateT"] = np.ascontiguousarray(inputs["L0_ffn_gate_w"].T)
    a1 = np.asarray(inputs["L0_ffn_a1"])
    a3 = np.asarray(inputs["L0_ffn_a3"])
    g["f_a1p"] = np.stack([np.concatenate([a1[4 * gi + j].T for j in range(4)], axis=1) for gi in range(2)])
    g["f_a3p"] = np.stack([np.concatenate([a3[4 * gi + j].T for j in range(4)], axis=1) for gi in range(2)])
    b1 = np.asarray(inputs["L0_ffn_b1"])
    b3 = np.asarray(inputs["L0_ffn_b3"])

    def padrows(bm, e):
        z = np.zeros((64, FF), dtype=np.float32)
        z[16 * (e % 4):16 * (e % 4) + 16, :] = bm[e].T
        return z
    g["f_b1p"] = np.ascontiguousarray(np.stack([padrows(b1, e) for e in range(NEXP)]))
    g["f_b3p"] = np.ascontiguousarray(np.stack([padrows(b3, e) for e in range(NEXP)]))
    g["f_b2T3"] = np.ascontiguousarray(np.asarray(inputs["L0_ffn_b2"]).transpose(2, 0, 1).reshape(16, NEXP * D))
    g["f_a2Tr"] = np.ascontiguousarray(np.asarray(inputs["L0_ffn_a2"]).transpose(2, 0, 1))
    g["f_b2T"] = np.ascontiguousarray(np.asarray(inputs["L0_ffn_b2"]).transpose(0, 2, 1))

    g["nb_Wq"] = np.ascontiguousarray(inputs["L0_attn_b_Wq"])
    g["nb_Wk"] = np.ascontiguousarray(inputs["L0_attn_b_Wk"])
    g["nb_Wv"] = np.ascontiguousarray(inputs["L0_attn_b_Wv"])
    g["nb_bq"] = np.asarray(inputs["L0_attn_b_bq"]).reshape(1, D).astype(np.float32)
    g["nb_bk"] = np.asarray(inputs["L0_attn_b_bk"]).reshape(1, D).astype(np.float32)
    g["nb_bv"] = np.asarray(inputs["L0_attn_b_bv"]).reshape(1, D).astype(np.float32)
    cw = np.asarray(inputs["L0_attn_b_conv_w"])[:, 0, :, 0]
    pat = np.zeros((NHEAD, 3, 128, 128), dtype=np.float32)
    for h in range(NHEAD):
        for j in range(3):
            rel = (j - 1) * 128
            for r in range(128):
                for c in range(128):
                    idx = rel + r - c + 32
                    if 0 <= idx <= 64:
                        pat[h, j, r, c] = cw[h, idx]
    g["nb_pat"] = pat

    g["ta_projT"] = np.ascontiguousarray(inputs["L1_attn_a_proj_w"].T)
    g["ta_fc1T"] = np.ascontiguousarray(inputs["L1_attn_a_fc1_w"].T)
    g["ta_fc2T"] = np.ascontiguousarray(inputs["L1_attn_a_fc2_w"].T)
    g["ta_g1"] = col_layout(inputs["L1_attn_a_ln1_g"])
    g["ta_b1c"] = col_layout(inputs["L1_attn_a_ln1_b"])
    g["ta_projb"] = np.asarray(inputs["L1_attn_a_proj_b"]).reshape(1, D).astype(np.float32)
    g["ta_g2"] = col_layout(inputs["L1_attn_a_ln2_g"])
    g["ta_b2c"] = col_layout(inputs["L1_attn_a_ln2_b"])
    g["ta_fc1b"] = np.asarray(inputs["L1_attn_a_fc1_b"]).reshape(1, FF).astype(np.float32)
    g["ta_lnhg"] = col_layout(inputs["L1_attn_a_lnh_g"])
    g["ta_lnhbc"] = col_layout(inputs["L1_attn_a_lnh_b"])
    g["ta_fc2b"] = np.asarray(inputs["L1_attn_a_fc2_b"]).reshape(1, D).astype(np.float32)
    g["ta_gam1"] = col_layout(inputs["L1_attn_a_gamma1"])
    g["ta_gam2"] = col_layout(inputs["L1_attn_a_gamma2"])

    for wn, k in (("tb_wqT", "L1_attn_b_wq"), ("tb_wkT", "L1_attn_b_wk"),
                  ("tb_wvT", "L1_attn_b_wv"), ("tb_woT", "L1_attn_b_wo")):
        g[wn] = np.ascontiguousarray(inputs[k].T)
    g["tb_w1T"] = np.ascontiguousarray(inputs["L1_attn_b_w1"].T)
    g["tb_w2T"] = np.ascontiguousarray(inputs["L1_attn_b_w2"].T)
    g["tb_bq"] = np.asarray(inputs["L1_attn_b_bq"]).reshape(1, D).astype(np.float32)
    g["tb_bk"] = np.asarray(inputs["L1_attn_b_bk"]).reshape(1, D).astype(np.float32)
    g["tb_bv"] = np.asarray(inputs["L1_attn_b_bv"]).reshape(1, D).astype(np.float32)
    g["tb_bo"] = np.asarray(inputs["L1_attn_b_bo"]).reshape(1, D).astype(np.float32)
    g["tb_g1"] = col_layout(inputs["L1_attn_b_g1"])
    g["tb_b1c"] = col_layout(inputs["L1_attn_b_b1"])
    g["tb_g2"] = col_layout(inputs["L1_attn_b_g2"])
    g["tb_b2c"] = col_layout(inputs["L1_attn_b_b2"])
    g["tb_bf1"] = np.asarray(inputs["L1_attn_b_bf1"]).reshape(1, FF).astype(np.float32)
    g["tb_bf2"] = np.asarray(inputs["L1_attn_b_bf2"]).reshape(1, D).astype(np.float32)
    hm = np.zeros((ND, 128, NHEAD), dtype=BF)
    for j in range(ND):
        for p in range(128):
            hm[j, p, 2 * j + p // 64] = 1
    g["c_headmask"] = hm
    sel = np.zeros((NHEAD, NHEAD, 64), dtype=BF)
    for h in range(NHEAD):
        sel[h, h, :] = 1
    g["c_sel64"] = sel
    sel2 = np.zeros((NHEAD, NHEAD, 128), dtype=BF)
    for h in range(NHEAD):
        sel2[h, h, :] = 1
    g["c_sel128"] = sel2
    g["c_ones_c_bf"] = np.ones((1, 128), dtype=BF)

    g["q_w"] = np.ascontiguousarray(inputs["L1_ffn_weight"])
    g["q_bias"] = np.asarray(inputs["L1_ffn_bias"]).reshape(1, D).astype(np.float32)

    per_core = []
    for c in range(NCORES):
        d = dict(g)
        d["x_in"] = np.ascontiguousarray(x[c].T).astype(np.float32)
        per_core.append(d)
    return per_core



class Common:

    def __init__(self, S):
        self.ident_bf = S.sb("c_ident", [128, 128], BF16)
        self.ident_f32 = S.sb("c_identf", [128, 128], F32)
        self.ones_r_bf = S.sb("c_ones_r", [1, 512], BF16)
        self.ones_c_f32 = S.sb("c_ones_cf", [1, 128], F32)
        self.ones_col_bf = S.sb("c_ones_col", [128, 1], BF16)
        self.ones_col_f32 = S.sb("c_ones_colf", [128, 1], F32)
        d_id = S.dram("c_ident_bf", [128, 128], BF16, kind="ExternalInput")
        d_idf = S.dram("c_ident_f32", [128, 128], F32, kind="ExternalInput")
        d_or = S.dram("c_ones_r_bf", [1, 512], BF16, kind="ExternalInput")
        d_oc = S.dram("c_ones_c_f32", [1, 128], F32, kind="ExternalInput")
        d_ocol = S.dram("c_ones_col_bf", [128, 1], BF16, kind="ExternalInput")
        d_ocolf = S.dram("c_ones_col_f32", [128, 1], F32, kind="ExternalInput")
        dma(S, "sp", self.ident_bf[:], d_id.ap(), tag="ld_ident")
        dma(S, "sp", self.ident_f32[:], d_idf.ap(), tag="ld_identf")
        dma(S, "sp", self.ones_r_bf[:], d_or.ap(), tag="ld_ones_r")
        dma(S, "sp", self.ones_c_f32[:], d_oc.ap(), tag="ld_ones_c")
        dma(S, "sp", self.ones_col_bf[:], d_ocol.ap(), tag="ld_ones_col")
        dma(S, "sp", self.ones_col_f32[:], d_ocolf.ap(), tag="ld_ones_colf")


def fm_dram(h, t0, t1, nd=ND):
    return h.ap()[:, t0:t1].rearrange("(a p) t -> p a t", p=128)


def ln_stats(S, C, xb_bf, xsq_bf, nfeat_tiles, Tc, ps_s, ps_ss, name):
    for j in range(nfeat_tiles):
        mm(S, ps_s[:], C.ones_col_bf[:], xb_bf[:, j, :], start=(j == 0), stop=(j == nfeat_tiles - 1), tag=f"{name}_s{j}")
        mm(S, ps_ss[:], C.ones_col_bf[:], xsq_bf[:, j, :], start=(j == 0), stop=(j == nfeat_tiles - 1), tag=f"{name}_ss{j}")


def ln_rows(S, ps_s, ps_ss, rows, nfeat, eps, name, eps2=None):
    inv = 1.0 / nfeat
    v_op(S, "tensor_scalar", rows["m"][:], ps_s[:], inv, None, ALU.mult, tag=f"{name}_m")
    v_op(S, "tensor_mul", rows["t1"][:], rows["m"][:], rows["m"][:], tag=f"{name}_msq")
    v_op(S, "scalar_tensor_tensor", rows["var"][:], ps_ss[:], inv, rows["t1"][:], ALU.mult, ALU.subtract, tag=f"{name}_var")
    act(S, rows["t1"][:], rows["var"][:], AF.Ln, bias=float(eps), tag=f"{name}_lnv")
    act(S, rows["rs"][:], rows["t1"][:], AF.Exp, scale=-0.5, tag=f"{name}_rs")
    if eps2 is not None:
        v_op(S, "tensor_mul", rows["t1"][:], rows["rs"][:], rows["rs"][:], tag=f"{name}_u")
        v_op(S, "tensor_mul", rows["t2"][:], rows["t1"][:], rows["var"][:], tag=f"{name}_u2")
        act(S, rows["t1"][:], rows["t2"][:], AF.Ln, bias=float(eps2), tag=f"{name}_lnv2")
        act(S, rows["t2"][:], rows["t1"][:], AF.Exp, scale=-0.5, tag=f"{name}_r2")
        v_op(S, "tensor_mul", rows["rs2"][:], rows["rs"][:], rows["t2"][:], tag=f"{name}_rs2")


def bcast_row(S, C, ps_out, row_ap, tag):
    mm(S, ps_out, C.ones_c_f32[:], row_ap, start=True, stop=True, tag=tag)


def mk_ln_rows(S, Tc, name, eps2=False, alloc=None):
    keys = ["m", "var", "rs", "t1", "t2"] + (["rs2"] if eps2 else [])
    al = alloc if alloc is not None else (lambda n, s, d=F32: S.sb(n, s, d))
    return {k: al(f"{name}_{k}_{S.uid()}", [1, Tc]) for k in keys}


def ln_normalize(S, C, x_sb, nd, Tc, mb_ps, sb_ps, out_bf, gain=None, xc_scratch=None, tag="ln"):
    for j in range(nd):
        xc = xc_scratch[:, j, :] if xc_scratch is not None else None
        v_op(S, "tensor_sub", xc, x_sb[:, j, :], mb_ps, tag=f"{tag}_sub{j}")
        if gain is not None:
            v_op(S, "scalar_tensor_tensor", out_bf[:, j, :], xc, gain[:, j:j + 1], sb_ps, ALU.mult, ALU.mult, tag=f"{tag}_sc{j}")
        else:
            v_op(S, "tensor_mul", out_bf[:, j, :], xc, sb_ps, tag=f"{tag}_mul{j}")


def load_cast_w(S, dram_h, sb_t, rows, cols, tag, rowsl=None, colsl=None):
    q = "pool"
    bf = getattr(S, "wbf", {}).get(dram_h.name)
    if bf is not None:
        dram_h = bf
    src = dram_h.ap()
    if rowsl is not None:
        src = src[rowsl[0]:rowsl[1], :]
        rows = rowsl[1] - rowsl[0]
    if colsl is not None:
        src = src[:, colsl[0]:colsl[1]]
        cols = colsl[1] - colsl[0]
    if rows > 128:
        src = src.rearrange("(a p) n -> p a n", p=128)
    dma(S, q, sb_t[:], src, tag=tag)


def bias_row_mm(S, C, ps, bias_col_bf, w_sb, nk, n0, n1, tag):
    for j in range(nk):
        mm(S, ps[:, 0:n1 - n0], bias_col_bf[:, j:j + 1], w_sb[:, j, n0:n1], start=(j == 0), stop=(j == nk - 1), tag=f"{tag}{j}")



from contextlib import contextmanager


@contextmanager
def pscope(S):
    st = ExitStack()
    try:
        yield st
    finally:
        S.barrier()
        st.close()


def emit_l1_attn_a(S, C, x_in_h, x_out_h):
    nc = S.nc
    d_projT = S.dram("ta_projT", [D, D], F32, kind="ExternalInput")
    d_fc1T = S.dram("ta_fc1T", [D, FF], F32, kind="ExternalInput")
    d_fc2T = S.dram("ta_fc2T", [FF, D], F32, kind="ExternalInput")
    d_g1 = S.dram("ta_g1", [128, ND], F32, kind="ExternalInput")
    d_b1c = S.dram("ta_b1c", [128, ND], F32, kind="ExternalInput")
    d_projb = S.dram("ta_projb", [1, D], F32, kind="ExternalInput")
    d_g2 = S.dram("ta_g2", [128, ND], F32, kind="ExternalInput")
    d_b2c = S.dram("ta_b2c", [128, ND], F32, kind="ExternalInput")
    d_fc1b = S.dram("ta_fc1b", [1, FF], F32, kind="ExternalInput")
    d_lnhg = S.dram("ta_lnhg", [128, NF], F32, kind="ExternalInput")
    d_lnhbc = S.dram("ta_lnhbc", [128, NF], F32, kind="ExternalInput")
    d_fc2b = S.dram("ta_fc2b", [1, D], F32, kind="ExternalInput")
    d_gam1 = S.dram("ta_gam1", [128, ND], F32, kind="ExternalInput")
    d_gam2 = S.dram("ta_gam2", [128, ND], F32, kind="ExternalInput")
    u0 = S.uid()
    kind_dbg = "ExternalOutput" if getattr(S, "debug_spill", False) else "Internal"
    z2_d = S.dram(f"z2_d{u0}", [D, T], BF16, kind=kind_dbg)
    h_d = S.dram(f"h_d{u0}", [FF, T], BF16, kind=kind_dbg)
    x_mid = S.dram(f"xmid_d{u0}", [D, T], F32, kind=kind_dbg)

    with pscope(S) as ph:
        sbl = lambda n, s, d=F32: ph.enter_context(nc.sbuf_tensor(f"{n}_{S.uid()}", list(s), d))
        g1 = sbl("g1", [128, ND]); dma(S, "sp", g1[:], d_g1.ap(), tag="ld_g1")
        g2 = sbl("g2", [128, ND]); dma(S, "sp", g2[:], d_g2.ap(), tag="ld_g2")
        lnhg = sbl("lnhg", [128, NF]); dma(S, "sp", lnhg[:], d_lnhg.ap(), tag="ld_lnhg")
        gam1 = sbl("gam1", [128, ND]); dma(S, "sp", gam1[:], d_gam1.ap(), tag="ld_gam1")
        gam2 = sbl("gam2", [128, ND]); dma(S, "sp", gam2[:], d_gam2.ap(), tag="ld_gam2")
        bias1 = sbl("bias1", [1, D], BF16)
        bias2 = sbl("bias2", [1, FF], BF16)
        bias3 = sbl("bias3", [1, D], BF16)
        lnh_s = sbl("lnh_s", [1, T])
        lnh_ss = sbl("lnh_ss", [1, T])

        with pscope(S) as pp:
            sb2 = lambda n, s, d=F32: pp.enter_context(nc.sbuf_tensor(f"{n}_{S.uid()}", list(s), d))
            ps2 = lambda n, s: pp.enter_context(nc.psum_tensor(f"{n}_{S.uid()}", list(s), F32))
            projb = sb2("projb", [1, D]); dma(S, "sp", projb[:], d_projb.ap(), tag="ld_projb")
            fc1b = sb2("fc1b", [1, FF]); dma(S, "sp", fc1b[:], d_fc1b.ap(), tag="ld_fc1b")
            fc2b = sb2("fc2b", [1, D]); dma(S, "sp", fc2b[:], d_fc2b.ap(), tag="ld_fc2b")
            b1c = sb2("b1c", [128, ND], BF16); dma(S, "pool", b1c[:], d_b1c.ap(), tag="ld_b1c")
            b2c = sb2("b2c", [128, ND], BF16); dma(S, "pool", b2c[:], d_b2c.ap(), tag="ld_b2c")
            lnhbc = sb2("lnhbc", [128, NF], BF16); dma(S, "pool", lnhbc[:], d_lnhbc.ap(), tag="ld_lnhbc")
            ps_b = ps2("ps_b", [1, 512])
            w512 = sb2("w512", [128, ND, 512], BF16)
            for n0 in range(0, D, 512):
                load_cast_w(S, d_projT, w512, D, 512, f"ld_pjs{n0}", colsl=(n0, n0 + 512))
                bias_row_mm(S, C, ps_b, b1c, w512, ND, 0, 512, f"b1r{n0}")
                v_op(S, "tensor_tensor", bias1[:, n0:n0 + 512], ps_b[:, :], projb[:, n0:n0 + 512], ALU.add, tag=f"b1e{n0}")
            for n0 in range(0, FF, 512):
                load_cast_w(S, d_fc1T, w512, D, 512, f"ld_fc1s{n0}", colsl=(n0, n0 + 512))
                bias_row_mm(S, C, ps_b, b2c, w512, ND, 0, 512, f"b2r{n0}")
                v_op(S, "tensor_tensor", bias2[:, n0:n0 + 512], ps_b[:, :], fc1b[:, n0:n0 + 512], ALU.add, tag=f"b2e{n0}")
            wf512 = sb2("wf512", [128, NF, 512], BF16)
            for n0 in range(0, D, 512):
                load_cast_w(S, d_fc2T, wf512, FF, 512, f"ld_fc2s{n0}", colsl=(n0, n0 + 512))
                bias_row_mm(S, C, ps_b, lnhbc, wf512, NF, 0, 512, f"b3r{n0}")
                v_op(S, "tensor_tensor", bias3[:, n0:n0 + 512], ps_b[:, :], fc2b[:, n0:n0 + 512], ALU.add, tag=f"b3e{n0}")

        Tc = 512
        nch = T // Tc
        with pscope(S) as pp:
            sb2 = lambda n, s, d=F32: pp.enter_context(nc.sbuf_tensor(f"{n}_{S.uid()}", list(s), d))
            ps2 = lambda n, s: pp.enter_context(nc.psum_tensor(f"{n}_{S.uid()}", list(s), F32))
            projT = sb2("projT", [128, ND, D], BF16)
            load_cast_w(S, d_projT, projT, D, D, "ld_projT")
            x_sb = sb2("xA", [128, ND, Tc])
            xb = sb2("xAb", [128, ND, Tc], BF16)
            xq = sb2("xAq", [128, ND, Tc], BF16)
            xc = sb2("xAc", [128, ND, Tc])
            y_bf = sb2("yA", [128, ND, Tc], BF16)
            z2buf = sb2("z2buf", [128, ND, Tc], BF16)
            ps_s, ps_ss = ps2("psA_s", [1, Tc]), ps2("psA_ss", [1, Tc])
            ps_bc = ps2("psA_bc", [128, 2 * Tc])
            ps_m, ps_r = ps_bc[:, 0:Tc], ps_bc[:, Tc:2 * Tc]
            ps_o = [ps2("psA_o0", [128, Tc]), ps2("psA_o1", [128, Tc])]
            rowsA = mk_ln_rows(S, Tc, "rA", eps2=True, alloc=sb2)
            rows2 = mk_ln_rows(S, Tc, "r2A", alloc=sb2)
            for ci in range(nch):
                t0 = ci * Tc
                dma(S, "sp", x_sb[:], fm_dram(x_in_h, t0, t0 + Tc), tag=f"A_ldx{ci}")
                for j in range(ND):
                    v_op(S, "tensor_copy", xb[:, j, :], x_sb[:, j, :], tag=f"A_xb{ci}_{j}")
                    act(S, xq[:, j, :], x_sb[:, j, :], AF.Square, tag=f"A_xq{ci}_{j}")
                ln_stats(S, C, xb, xq, ND, Tc, ps_s, ps_ss, f"A_st{ci}")
                ln_rows(S, ps_s, ps_ss, rowsA, D, EPS_LN, f"A_r{ci}", eps2=EPS_LN)
                bcast_row(S, C, ps_m, rowsA["m"][:], f"A_bm{ci}")
                bcast_row(S, C, ps_r, rowsA["rs2"][:], f"A_br{ci}")
                ln_normalize(S, C, x_sb, ND, Tc, ps_m, ps_r, y_bf, gain=g1, xc_scratch=xc, tag=f"A_ln{ci}")
                bcast_row(S, C, ps_m, rowsA["rs"][:], f"A_bro{ci}")
                for j in range(ND):
                    v_op(S, "tensor_mul", x_sb[:, j, :], xc[:, j, :], ps_m, tag=f"A_z{ci}_{j}")
                for mi in range(ND):
                    po = ps_o[mi % 2]
                    for j in range(ND):
                        mm(S, po[:], projT[:, j, 128 * mi:128 * (mi + 1)], y_bf[:, j, :], start=(j == 0), stop=False, tag=f"A_pj{ci}_{mi}_{j}")
                    mm(S, po[:], bias1[:, 128 * mi:128 * (mi + 1)], C.ones_r_bf[:, :Tc], start=False, stop=True, tag=f"A_pb{ci}_{mi}")
                    v_op(S, "scalar_tensor_tensor", x_sb[:, mi, :], po[:], gam1[:, mi:mi + 1], x_sb[:, mi, :], ALU.mult, ALU.add, tag=f"A_res{ci}_{mi}")
                dma(S, "sp", fm_dram(x_mid, t0, t0 + Tc), x_sb[:], tag=f"A_stx{ci}")
                for j in range(ND):
                    v_op(S, "tensor_copy", xb[:, j, :], x_sb[:, j, :], tag=f"A_x1b{ci}_{j}")
                    act(S, xq[:, j, :], x_sb[:, j, :], AF.Square, tag=f"A_x1q{ci}_{j}")
                ln_stats(S, C, xb, xq, ND, Tc, ps_s, ps_ss, f"A2_st{ci}")
                ln_rows(S, ps_s, ps_ss, rows2, D, EPS_LN, f"A2_r{ci}")
                bcast_row(S, C, ps_m, rows2["m"][:], f"A2_bm{ci}")
                bcast_row(S, C, ps_r, rows2["rs"][:], f"A2_br{ci}")
                for j in range(ND):
                    v_op(S, "tensor_sub", xc[:, j, :], x_sb[:, j, :], ps_m, tag=f"A2_sub{ci}_{j}")
                    v_op(S, "scalar_tensor_tensor", z2buf[:, j, :], xc[:, j, :], g2[:, j:j + 1], ps_r, ALU.mult, ALU.mult, tag=f"A2_sc{ci}_{j}")
                dma(S, "sp", fm_dram(z2_d, t0, t0 + Tc), z2buf[:], tag=f"A_stz{ci}")

        with pscope(S) as pp:
            sb2 = lambda n, s, d=F32: pp.enter_context(nc.sbuf_tensor(f"{n}_{S.uid()}", list(s), d))
            ps2 = lambda n, s: pp.enter_context(nc.psum_tensor(f"{n}_{S.uid()}", list(s), F32))
            z2c = sb2("z2c", [128, ND, Tc], BF16)
            hq = sb2("hq", [128, Tc], BF16)
            hst = [sb2("hst0", [128, Tc], BF16), sb2("hst1", [128, Tc], BF16)]
            wB = [sb2("wB0", [128, ND, 128], BF16), sb2("wB1", [128, ND, 128], BF16)]
            ps_h = [ps2("psB_h0", [128, Tc]), ps2("psB_h1", [128, Tc])]
            ps_s, ps_ss = ps2("psB_s", [1, Tc]), ps2("psB_ss", [1, Tc])
            for ci in range(nch):
                t0 = ci * Tc
                dma(S, "sp", z2c[:], fm_dram(z2_d, t0, t0 + Tc), tag=f"B_ldz{ci}")
                for mi in range(NF):
                    wb = wB[mi % 2]
                    load_cast_w(S, d_fc1T, wb, D, 128, f"B_w{ci}_{mi}", colsl=(128 * mi, 128 * (mi + 1)))
                    ph_ = ps_h[mi % 2]
                    hs = hst[mi % 2]
                    for j in range(ND):
                        mm(S, ph_[:], wb[:, j, :], z2c[:, j, :], start=(j == 0), stop=False, tag=f"B_mm{ci}_{mi}_{j}")
                    mm(S, ph_[:], bias2[:, 128 * mi:128 * (mi + 1)], C.ones_r_bf[:, :Tc], start=False, stop=True, tag=f"B_bb{ci}_{mi}")
                    act(S, hs[:], ph_[:], AF.Gelu, tag=f"B_gelu{ci}_{mi}")
                    dma(S, "sp", h_d.ap()[128 * mi:128 * (mi + 1), t0:t0 + Tc], hs[:], tag=f"B_sth{ci}_{mi}")
                    act(S, hq[:], hs[:], AF.Square, tag=f"B_hq{ci}_{mi}")
                    mm(S, ps_s[:], C.ones_col_bf[:], hs[:], start=(mi == 0), stop=(mi == NF - 1), tag=f"B_s{ci}_{mi}")
                    mm(S, ps_ss[:], C.ones_col_bf[:], hq[:], start=(mi == 0), stop=(mi == NF - 1), tag=f"B_ss{ci}_{mi}")
                v_op(S, "tensor_copy", lnh_s[:, t0:t0 + Tc], ps_s[:], tag=f"B_cs{ci}")
                v_op(S, "tensor_copy", lnh_ss[:, t0:t0 + Tc], ps_ss[:], tag=f"B_css{ci}")

        with pscope(S) as pp:
            sb2 = lambda n, s, d=F32: pp.enter_context(nc.sbuf_tensor(f"{n}_{S.uid()}", list(s), d))
            ps2 = lambda n, s: pp.enter_context(nc.psum_tensor(f"{n}_{S.uid()}", list(s), F32))
            xC = sb2("xC", [128, ND, Tc])
            x0 = sb2("x0", [128, ND, Tc])
            hl = [sb2("hl0", [128, Tc], BF16), sb2("hl1", [128, Tc], BF16)]
            ht_all = sb2("ht_all", [128, NF, Tc], BF16)
            htf = sb2("htf", [128, Tc])
            mb_sb = sb2("mb_sbC", [128, Tc])
            rb_sb = sb2("rb_sbC", [128, Tc])
            wC = [sb2("wC0", [128, 512], BF16), sb2("wC1", [128, 512], BF16)]
            ps_big = ps2("psC_big", [128, 4, 512])
            ps_bc = ps2("psC_bc", [128, 512])
            rowsC = mk_ln_rows(S, Tc, "rC", alloc=sb2)
            for ci in range(nch):
                t0 = ci * Tc
                ln_rows(S, lnh_s[:, t0:t0 + Tc], lnh_ss[:, t0:t0 + Tc], rowsC, FF, EPS_LN, f"C_r{ci}")
                bcast_row(S, C, ps_bc[:], rowsC["m"][:], f"C_bm{ci}")
                v_op(S, "tensor_copy", mb_sb[:], ps_bc[:], tag=f"C_cm{ci}")
                bcast_row(S, C, ps_bc[:], rowsC["rs"][:], f"C_br{ci}")
                v_op(S, "tensor_copy", rb_sb[:], ps_bc[:], tag=f"C_cr{ci}")
                dma(S, "sp", xC[:], fm_dram(x_mid, t0, t0 + Tc), tag=f"C_ldx{ci}")
                dma(S, "sp", x0[:], fm_dram(x_in_h, t0, t0 + Tc), tag=f"C_ldx0{ci}")
                for mi in range(NF):
                    hli = hl[mi % 2]
                    dma(S, "sp", hli[:], h_d.ap()[128 * mi:128 * (mi + 1), t0:t0 + Tc], tag=f"C_ldh{ci}_{mi}")
                    v_op(S, "tensor_sub", htf[:], hli[:], mb_sb[:], tag=f"C_sub{ci}_{mi}")
                    v_op(S, "scalar_tensor_tensor", ht_all[:, mi, :], htf[:], lnhg[:, mi:mi + 1], rb_sb[:], ALU.mult, ALU.mult, tag=f"C_sc{ci}_{mi}")
                for half in range(2):
                    for mi in range(NF):
                        wc = wC[mi % 2]
                        load_cast_w(S, d_fc2T, wc, 128, 512, f"C_w{ci}_{half}_{mi}", rowsl=(128 * mi, 128 * (mi + 1)), colsl=(512 * half, 512 * (half + 1)))
                        for dq in range(4):
                            mm(S, ps_big[:, dq, :], wc[:, 128 * dq:128 * (dq + 1)], ht_all[:, mi, :], start=(mi == 0), stop=False, tag=f"C_mm{ci}_{half}_{mi}_{dq}")
                    for dq in range(4):
                        di = 4 * half + dq
                        mm(S, ps_big[:, dq, :], bias3[:, 128 * di:128 * (di + 1)], C.ones_r_bf[:, :Tc], start=False, stop=True, tag=f"C_bb{ci}_{half}_{dq}")
                        v_op(S, "scalar_tensor_tensor", xC[:, di, :], ps_big[:, dq, :], gam2[:, di:di + 1], xC[:, di, :], ALU.mult, ALU.add, tag=f"C_res{ci}_{half}_{dq}")
                        v_op(S, "tensor_add", xC[:, di, :], xC[:, di, :], x0[:, di, :], tag=f"C_addx{ci}_{half}_{dq}")
                dma(S, "sp", fm_dram(x_out_h, t0, t0 + Tc), xC[:], tag=f"C_stx{ci}")


def run_l1_attn_a(per_core, n_cores=NCORES, debug_spill=False, **kw):
    S = Prog("l1a")
    S.debug_spill = debug_spill
    x_in = S.dram("x_in", [D, T], F32, kind="ExternalInput")
    x_out = S.dram("x_out", [D, T], F32, kind="ExternalOutput")
    C = Common(S)
    emit_l1_attn_a(S, C, x_in, x_out)
    res = S.run(per_core, n_cores, **kw)
    return [r["x_out"] for r in res.results], res


def tr_op(S, out_ps, in_ap, ident, tag):
    return S.op("pe", lambda h: h.transpose(out_ps, in_ap, ident),
                reads=[in_ap, ident], writes=[out_ps], tag=tag)


def quantize_rows(S, sc, src_ap, out_bf_ap, Tn, tag):
    nb = Tn // 32
    bview = lambda ap: ap.rearrange("p (a b) -> p a b", b=32)
    S.op("dve", lambda h: h.tensor_reduce(sc["amax"][:], bview(src_ap), AX.X, ALU.max),
         reads=[src_ap], writes=[sc["amax"][:]], tag=f"{tag}_amax")
    S.op("dve", lambda h: h.tensor_reduce(sc["amin"][:], bview(src_ap), AX.X, ALU.min),
         reads=[src_ap], writes=[sc["amin"][:]], tag=f"{tag}_amin")
    v_op(S, "scalar_tensor_tensor", sc["aab"][:], sc["amin"][:], -1.0, sc["amax"][:], ALU.mult, ALU.max, tag=f"{tag}_aabs")
    v_op(S, "tensor_scalar", sc["s"][:], sc["aab"][:], 1.0 / 7.5, 1e-30, ALU.mult, ALU.max, tag=f"{tag}_s")
    v_op(S, "reciprocal", sc["si"][:], sc["s"][:], tag=f"{tag}_si")
    sx = bview(sc["sexp"][:])
    v_op(S, "tensor_copy", sx[:, :, 0:1], sc["si"][:].rearrange("p (a o) -> p a o", o=1), tag=f"{tag}_e0")
    w = 1
    while w < 32:
        v_op(S, "tensor_copy", sx[:, :, w:2 * w], sx[:, :, 0:w], tag=f"{tag}_e{w}")
        w *= 2
    v_op(S, "tensor_mul", sc["y"][:], src_ap, sc["sexp"][:], tag=f"{tag}_y")
    v_op(S, "tensor_mul", sc["ay"][:], sc["y"][:], sc["y"][:], tag=f"{tag}_ay")
    v_op(S, "tensor_single_scalar", sc["m2"][:], sc["ay"][:], 4.0, ALU.is_lt, tag=f"{tag}_m2")
    v_op(S, "tensor_single_scalar", sc["m4"][:], sc["ay"][:], 16.0, ALU.is_lt, tag=f"{tag}_m4")
    for nm, mul in (("r8", 8.0), ("r4", 4.0), ("r2", 2.0)):
        v_op(S, "tensor_scalar", sc[nm][:], sc["y"][:], mul, MAGIC, ALU.mult, ALU.add, tag=f"{tag}_{nm}a")
        v_op(S, "tensor_single_scalar", sc[nm][:], sc[nm][:], MAGIC, ALU.subtract, tag=f"{tag}_{nm}b")
    sx2 = bview(sc["sexp2"][:])
    v_op(S, "tensor_copy", sx2[:, :, 0:1], sc["s"][:].rearrange("p (a o) -> p a o", o=1), tag=f"{tag}_f0")
    w = 1
    while w < 32:
        v_op(S, "tensor_copy", sx2[:, :, w:2 * w], sx2[:, :, 0:w], tag=f"{tag}_f{w}")
        w *= 2
    for nm, inv in (("r8", 0.125), ("r4", 0.25), ("r2", 0.5)):
        v_op(S, "scalar_tensor_tensor", sc[nm][:], sc[nm][:], inv, sc["sexp2"][:], ALU.mult, ALU.mult, tag=f"{tag}_t{nm}")
    v_op(S, "select", sc["ay"][:], sc["m4"][:], sc["r4"][:], sc["r2"][:], tag=f"{tag}_sel1")
    S.op("dve", lambda h: h.select(out_bf_ap, sc["m2"][:], sc["r8"][:], sc["ay"][:]),
         reads=[sc["m2"][:], sc["r8"][:], sc["ay"][:]], writes=[out_bf_ap], tag=f"{tag}_sel2")


def mk_qscratch(alloc, Tn, name):
    d = {}
    for k in ("y", "ay", "r8", "r4", "r2", "sexp", "sexp2"):
        d[k] = alloc(f"{name}_{k}", [128, Tn])
    for k in ("m2", "m4"):
        d[k] = alloc(f"{name}_{k}", [128, Tn], I32)
    for k in ("amax", "amin", "aab", "s", "si"):
        d[k] = alloc(f"{name}_{k}", [128, Tn // 32])
    return d


def emit_l1_ffn(S, C, x_in_h, x_out_h, idx=None):
    nc = S.nc
    idx = 1 if hasattr(S, "d_qw") else 0
    if idx == 0:
        S.d_qw = S.dram("q_w", [D, D], F32, kind="ExternalInput")
        S.d_qb = S.dram("q_bias", [1, D], F32, kind="ExternalInput")
        S.qwT_d = S.dram(f"qwT_d", [D, D], BF16, kind="ExternalOutput" if getattr(S, "debug_spill", False) else "Internal")
    d_qw, d_qb, qwT_d = S.d_qw, S.d_qb, S.qwT_d

    with pscope(S) as ph:
        sbl = lambda n, s, d=F32: ph.enter_context(nc.sbuf_tensor(f"{n}_{S.uid()}", list(s), d))
        qbias = sbl("qbias", [1, D], BF16)
        qbf = sbl("qbf", [1, D])
        dma(S, "sp", qbf[:], d_qb.ap(), tag="ld_qb")
        v_op(S, "tensor_copy", qbias[:], qbf[:], tag="qb_bf")

        if idx == 0:
            with pscope(S) as pp:
                sb2 = lambda n, s, d=F32: pp.enter_context(nc.sbuf_tensor(f"{n}_{S.uid()}", list(s), d))
                ps2 = lambda n, s, d=F32: pp.enter_context(nc.psum_tensor(f"{n}_{S.uid()}", list(s), d))
                wq = sb2("wq", [128, ND, D])
                dma(S, "sp", wq[:], d_qw.ap().rearrange("(a p) n -> p a n", p=128), tag="ld_qw")
                wdf = sb2("wdf", [128, ND, D])
                sc = mk_qscratch(sb2, D, "qw")
                for j in range(ND):
                    quantize_rows(S, sc, wq[:, j, :], wdf[:, j, :], D, f"qw{j}")
                if getattr(S, "debug_spill", False):
                    wdf_dbg = S.dram(f"wdf_dbg{S.uid()}", [D, D], F32, kind="ExternalOutput")
                    dma(S, "sp", wdf_dbg.ap().rearrange("(a p) n -> p a n", p=128), wdf[:], tag="dbg_wdf")
                    y_dbg = S.dram(f"y_dbg{S.uid()}", [128, D], F32, kind="ExternalOutput")
                    s_dbg = S.dram(f"s_dbg{S.uid()}", [128, D // 32], F32, kind="ExternalOutput")
                    dma(S, "sp", y_dbg.ap(), sc["y"][:], tag="dbg_y")
                    dma(S, "sp", s_dbg.ap(), sc["s"][:], tag="dbg_s")
                ps_t = ps2("ps_tw", [128, 128])
                wrow = sb2("wrow", [128, D], BF16)
                for kj in range(ND):
                    for dj in range(ND):
                        tr_op(S, ps_t[:], wdf[:, dj, 128 * kj:128 * (kj + 1)], C.ident_f32[:], f"qwt{kj}_{dj}")
                        v_op(S, "tensor_copy", wrow[:, 128 * dj:128 * (dj + 1)], ps_t[:], tag=f"qwte{kj}_{dj}")
                    dma(S, "sp", qwT_d.ap()[128 * kj:128 * (kj + 1), :], wrow[:], tag=f"st_qwT{kj}")

        Tc = 512
        nch = T // Tc
        dbg = getattr(S, "debug_spill", False)
        if dbg:
            xn_dbg = S.dram(f"xn_dbg{S.uid()}", [D, T], F32, kind="ExternalOutput")
            xd_dbg = S.dram(f"xd_dbg{S.uid()}", [D, T], BF16, kind="ExternalOutput")
        with pscope(S) as pp:
            sb2 = lambda n, s, d=F32: pp.enter_context(nc.sbuf_tensor(f"{n}_{S.uid()}", list(s), d))
            ps2 = lambda n, s, d=F32: pp.enter_context(nc.psum_tensor(f"{n}_{S.uid()}", list(s), d))
            x_sb = sb2("xQ", [128, ND, Tc])
            xb = sb2("xQb", [128, ND, Tc], BF16)
            xq = sb2("xQq", [128, ND, Tc], BF16)
            xc = sb2("xQc", [128, ND, Tc])
            xn = sb2("xnQ", [128, ND, Tc])
            xtok = sb2("xtok", [128, D])
            xdt = sb2("xdt", [128, D], BF16)
            xd = sb2("xdQ", [128, ND, Tc], BF16)
            wT = sb2("wTQ", [128, ND, D], BF16)
            sc = mk_qscratch(sb2, D, "qx")
            rows = mk_ln_rows(S, Tc, "rQ", alloc=sb2)
            mb_sb = sb2("mbQ", [128, Tc])
            rb_sb = sb2("rbQ", [128, Tc])
            ps_s, ps_ss = ps2("psQ_s", [1, Tc]), ps2("psQ_ss", [1, Tc])
            ps_bc = ps2("psQ_bc", [128, Tc])
            ps_t = ps2("psQ_t", [128, 128])
            ps_tb = ps2("psQ_tb", [128, 128], BF16)
            ps_big = ps2("psQ_big", [128, 2, 512])
            dma(S, "sp", wT[:], qwT_d.ap().rearrange("(a p) n -> p a n", p=128), tag="ld_wT")
            for ci in range(nch):
                t0 = ci * Tc
                dma(S, "sp", x_sb[:], fm_dram(x_in_h, t0, t0 + Tc), tag=f"Q_ldx{ci}")
                for j in range(ND):
                    v_op(S, "tensor_copy", xb[:, j, :], x_sb[:, j, :], tag=f"Q_xb{ci}_{j}")
                    act(S, xq[:, j, :], x_sb[:, j, :], AF.Square, tag=f"Q_xq{ci}_{j}")
                ln_stats(S, C, xb, xq, ND, Tc, ps_s, ps_ss, f"Q_st{ci}")
                ln_rows(S, ps_s, ps_ss, rows, D, EPS_LN, f"Q_r{ci}")
                bcast_row(S, C, ps_bc[:], rows["m"][:], f"Q_bm{ci}")
                v_op(S, "tensor_copy", mb_sb[:], ps_bc[:], tag=f"Q_cm{ci}")
                bcast_row(S, C, ps_bc[:], rows["rs"][:], f"Q_br{ci}")
                v_op(S, "tensor_copy", rb_sb[:], ps_bc[:], tag=f"Q_cr{ci}")
                for j in range(ND):
                    v_op(S, "tensor_sub", xc[:, j, :], x_sb[:, j, :], mb_sb[:], tag=f"Q_sub{ci}_{j}")
                    v_op(S, "tensor_mul", xn[:, j, :], xc[:, j, :], rb_sb[:], tag=f"Q_mul{ci}_{j}")
                for st in range(Tc // 128):
                    s0 = st * 128
                    for j in range(ND):
                        tr_op(S, ps_t[:], xn[:, j, s0:s0 + 128], C.ident_f32[:], f"Q_t{ci}_{st}_{j}")
                        v_op(S, "tensor_copy", xtok[:, 128 * j:128 * (j + 1)], ps_t[:], tag=f"Q_te{ci}_{st}_{j}")
                    quantize_rows(S, sc, xtok[:], xdt[:], D, f"Q_q{ci}_{st}")
                    for j in range(ND):
                        tr_op(S, ps_tb[:], xdt[:, 128 * j:128 * (j + 1)], C.ident_bf[:], f"Q_tb{ci}_{st}_{j}")
                        v_op(S, "tensor_copy", xd[:, j, s0:s0 + 128], ps_tb[:], tag=f"Q_tbe{ci}_{st}_{j}")
                if dbg:
                    dma(S, "sp", fm_dram(xn_dbg, t0, t0 + Tc), xn[:], tag=f"dbg_xn{ci}")
                    dma(S, "sp", fm_dram(xd_dbg, t0, t0 + Tc), xd[:], tag=f"dbg_xd{ci}")
                for half in range(4):
                    for dq in range(2):
                        di = 2 * half + dq
                        for kj in range(ND):
                            mm(S, ps_big[:, dq, :], wT[:, kj, 128 * di:128 * (di + 1)], xd[:, kj, :], start=(kj == 0), stop=False, tag=f"Q_mm{ci}_{di}_{kj}")
                        mm(S, ps_big[:, dq, :], qbias[:, 128 * di:128 * (di + 1)], C.ones_r_bf[:, :Tc], start=False, stop=True, tag=f"Q_bb{ci}_{di}")
                        v_op(S, "tensor_add", x_sb[:, di, :], x_sb[:, di, :], ps_big[:, dq, :], tag=f"Q_res{ci}_{di}")
                dma(S, "sp", fm_dram(x_out_h, t0, t0 + Tc), x_sb[:], tag=f"Q_stx{ci}")


def run_l1_ffn(per_core, n_cores=NCORES, debug_spill=False, **kw):
    S = Prog("l1f")
    S.debug_spill = debug_spill
    x_in = S.dram("x_in", [D, T], F32, kind="ExternalInput")
    x_out = S.dram("x_out", [D, T], F32, kind="ExternalOutput")
    C = Common(S)
    emit_l1_ffn(S, C, x_in, x_out, idx=0)
    res = S.run(per_core, n_cores, **kw)
    return [r["x_out"] for r in res.results], res


def conv_w_to_bf16(S, src_h, dst_h, rows, cols, tmp_sb, tag, qload="pool", qstore="sp"):
    nt = rows // 128
    for i in range(nt):
        t = tmp_sb[i % 2]
        dma(S, qload, t[:, 0:cols], src_h.ap()[128 * i:128 * (i + 1), :], tag=f"{tag}_l{i}")
        dma(S, qstore, dst_h.ap()[128 * i:128 * (i + 1), :], t[:, 0:cols], tag=f"{tag}_s{i}")


WCONV = (("tb_wqT", [D, D]), ("tb_wkT", [D, D]), ("tb_wvT", [D, D]), ("tb_woT", [D, D]),
         ("tb_w1T", [D, FF]), ("tb_w2T", [FF, D]),
         ("nb_Wq", [D, D]), ("nb_Wk", [D, D]), ("nb_Wv", [D, D]))


def wconv_all(S, alloc):
    if hasattr(S, "wbf"):
        return
    S.wbf = {}
    tmp = getattr(S, "_wcv_tmp", None) or [alloc(f"wcv{i}_{S.uid()}", [128, FF], BF16) for i in range(2)]
    for nm, sh in WCONV:
        src = S.dram(nm, sh, F32, kind="ExternalInput")
        dst = S.dram(nm + "_bf", sh, BF16, kind="Internal")
        S.wbf[nm] = dst
        conv_w_to_bf16(S, src, dst, sh[0], sh[1], tmp, f"wc_{nm}", qload="pool", qstore="act")


def l0f_wprep(S, alloc=None):
    if hasattr(S, "f_d"):
        return
    S.f_d = {}
    for nm, sh in (("f_norm_w", [128, ND]), ("f_gateT", [D, NEXP]),
                   ("f_a1p", [2, D, 64]), ("f_a3p", [2, D, 64]),
                   ("f_b1p", [NEXP, 64, FF]), ("f_b3p", [NEXP, 64, FF]),
                   ("f_a2Tr", [FF, NEXP, RLORA]), ("f_b2T3", [16, NEXP * D]),
                   ("f_w1T", [D, FF]), ("f_w3T", [D, FF]), ("f_w2T", [FF, D])):
        S.f_d[nm] = S.dram(nm, sh, F32, kind="ExternalInput")
    S.f_scr = {"f_w1T": S.dram("f_w1T_bf", [D, FF], BF16, kind="Internal"),
               "f_w3T": S.dram("f_w3T_bf", [D, FF], BF16, kind="Internal"),
               "f_w2T": S.dram("f_w2T_bf", [FF, D], BF16, kind="Internal")}

    def _emit(al, qa, qb):
        tmp = [al(f"cvt{i}_{S.uid()}", [128, FF], BF16) for i in range(2)]
        S._wcv_tmp = tmp
        tmp2 = [al(f"cvu{i}_{S.uid()}", [128, D], BF16) for i in range(2)]
        conv_w_to_bf16(S, S.f_d["f_w1T"], S.f_scr["f_w1T"], D, FF, tmp, "cv_w1", qload=qa, qstore=qb)
        conv_w_to_bf16(S, S.f_d["f_w3T"], S.f_scr["f_w3T"], D, FF, tmp, "cv_w3", qload=qa, qstore=qb)
        conv_w_to_bf16(S, S.f_d["f_w2T"], S.f_scr["f_w2T"], FF, D, tmp2, "cv_w2", qload=qa, qstore=qb)

    if alloc is not None:
        _emit(alloc, "pool", "act")
    else:
        with pscope(S) as pp:
            al = lambda n, s, d=BF16: pp.enter_context(S.nc.sbuf_tensor(n, list(s), d))
            _emit(al, "pool", "sp")


def emit_l0_ffn(S, C, x_in_h, x_out_h, idx=None):
    nc = S.nc
    idx = 1 if getattr(S, "f_used", False) else 0
    S.f_used = True
    l0f_wprep(S)
    FD = S.f_d
    Tc = 512
    nch = T // Tc
    nst = Tc // 128
    u0 = S.uid()
    wrows_d = S.dram(f"wrows_d{u0}", [NEXP, Tc], F32, kind="Internal")

    with pscope(S) as ph:
        sbl = lambda n, s, d=F32: ph.enter_context(nc.sbuf_tensor(f"{n}_{S.uid()}", list(s), d))
        norm_w = sbl("fnw", [128, ND]); dma(S, "sp", norm_w[:], FD["f_norm_w"].ap(), tag="ld_fnw")
        gateT = sbl("gateT", [128, ND, NEXP]); dma(S, "sp", gateT[:], FD["f_gateT"].ap().rearrange("(a p) n -> p a n", p=128), tag="ld_gateT")
        for j in range(ND):
            v_op(S, "tensor_scalar", gateT[:, j, :], gateT[:, j, :], norm_w[:, j:j + 1], None, ALU.mult, tag=f"gw_fold{j}")
        a1p = sbl("a1p", [128, 2, ND, 64], BF16)
        a3p = sbl("a3p", [128, 2, ND, 64], BF16)
        for gi in range(2):
            dma(S, "pool", a1p[:, gi, :, :], FD["f_a1p"].ap()[gi].rearrange("(a p) n -> p a n", p=128), tag=f"ld_a1p{gi}")
            dma(S, "pool", a3p[:, gi, :, :], FD["f_a3p"].ap()[gi].rearrange("(a p) n -> p a n", p=128), tag=f"ld_a3p{gi}")
        a2T = sbl("a2T", [128, NF, NEXP, RLORA], BF16)
        dma(S, "pool", a2T[:], FD["f_a2Tr"].ap().rearrange("(a p) e r -> p a e r", p=128), tag="ld_a2T")
        b2T = sbl("b2T", [16, NEXP, D], BF16)
        dma(S, "pool", b2T[:], FD["f_b2T3"].ap().rearrange("p (e n) -> p e n", e=NEXP), tag="ld_b2T")

        x_sb = sbl("xF", [128, ND, Tc])
        xn = sbl("xnF", [128, ND, Tc], BF16)
        w_bc = sbl("w_bc", [128, NEXP, Tc], BF16)
        etr1 = sbl("etr1", [64, 2, Tc], BF16)
        etr3 = sbl("etr3", [64, 2, Tc], BF16)
        actmix = sbl("actmix", [128, NF, Tc], BF16)
        etr2w = [sbl(f"etr2w{e}", [16, Tc], BF16) for e in range(NEXP)]

        for ci in range(nch):
            t0 = ci * Tc
            with pscope(S) as pq:
                sb3 = lambda n, s, d=F32: pq.enter_context(nc.sbuf_tensor(f"{n}_{S.uid()}", list(s), d))
                ps3 = lambda n, s, d=F32: pq.enter_context(nc.psum_tensor(f"{n}_{S.uid()}", list(s), d))
                xb = sb3("xFb", [128, ND, Tc], BF16)
                xq = sb3("xFq", [128, ND, Tc], BF16)
                xc = sb3("xFc", [128, ND, Tc])
                rows = mk_ln_rows(S, Tc, "rF", eps2=True, alloc=sb3)
                glog = sb3("glog", [NEXP, Tc])
                wbe = sb3("wbe", [NEXP, Tc])
                wrows = sb3("wrows", [1, NEXP, Tc])
                u_tok = sb3("u_tok", [128, NEXP])
                l_tok = sb3("l_tok", [128, NEXP])
                msk1 = sb3("msk1", [128, NEXP])
                msk2 = sb3("msk2", [128, NEXP])
                l2m = sb3("l2m", [128, NEXP])
                wtok = sb3("wtok", [128, NEXP])
                rs2c = sb3("rs2c", [128, nst])
                lm = {k: sb3(f"lm_{k}", [128, 1]) for k in ("negm1", "m1", "m2", "e2", "den", "winv", "w2c")}
                ps_s, ps_ss = ps3("psF_s", [1, Tc]), ps3("psF_ss", [1, Tc])
                ps_bc = ps3("psF_bc", [128, Tc])
                ps_g = ps3("psF_g", [NEXP, Tc])
                ps_t = ps3("psF_t", [128, 128])
                ps_e = ps3("psF_e", [64, Tc])
                dma(S, "sp", x_sb[:], fm_dram(x_in_h, t0, t0 + Tc), tag=f"F_ldx{ci}")
                for j in range(ND):
                    v_op(S, "tensor_copy", xb[:, j, :], x_sb[:, j, :], tag=f"F_xb{ci}_{j}")
                    act(S, xq[:, j, :], x_sb[:, j, :], AF.Square, tag=f"F_xq{ci}_{j}")
                ln_stats(S, C, xb, xq, ND, Tc, ps_s, ps_ss, f"F_st{ci}")
                ln_rows(S, ps_s, ps_ss, rows, D, EPS_LN, f"F_r{ci}", eps2=1e-6)
                bcast_row(S, C, ps_bc[:], rows["m"][:], f"F_bm{ci}")
                for j in range(ND):
                    v_op(S, "tensor_sub", xc[:, j, :], x_sb[:, j, :], ps_bc[:], tag=f"F_sub{ci}_{j}")
                bcast_row(S, C, ps_bc[:], rows["rs2"][:], f"F_br{ci}")
                for j in range(ND):
                    v_op(S, "scalar_tensor_tensor", xn[:, j, :], xc[:, j, :], norm_w[:, j:j + 1], ps_bc[:], ALU.mult, ALU.mult, tag=f"F_xn{ci}_{j}")
                for j in range(ND):
                    mm(S, ps_g[:], gateT[:, j, :], xc[:, j, :], start=(j == 0), stop=(j == ND - 1), tag=f"F_g{ci}_{j}")
                v_op(S, "tensor_copy", glog[:], ps_g[:], tag=f"F_gc{ci}")
                for st in range(nst):
                    tr_op(S, ps_t[0:128, 0:1], rows["rs2"][:, 128 * st:128 * (st + 1)], C.ident_f32[0:1, 0:1], f"F_rt{ci}_{st}")
                    v_op(S, "tensor_copy", rs2c[:, st:st + 1], ps_t[0:128, 0:1], tag=f"F_rtc{ci}_{st}")
                for st in range(nst):
                    s0 = 128 * st
                    tr_op(S, ps_t[0:128, 0:NEXP], glog[:, s0:s0 + 128], C.ident_f32[0:NEXP, 0:NEXP], f"F_gt{ci}_{st}")
                    v_op(S, "tensor_copy", u_tok[:], ps_t[0:128, 0:NEXP], tag=f"F_gte{ci}_{st}")
                    v_op(S, "tensor_scalar", l_tok[:], u_tok[:], rs2c[:, st:st + 1], None, ALU.mult, tag=f"F_lt{ci}_{st}")
                    v_op(S, "reduce_max", lm["negm1"][:], l_tok[:], AX.X, negate=True, tag=f"F_m1{ci}_{st}")
                    v_op(S, "tensor_scalar", lm["m1"][:], lm["negm1"][:], -1.0, None, ALU.mult, tag=f"F_m1p{ci}_{st}")
                    v_op(S, "tensor_scalar", msk1[:], l_tok[:], lm["m1"][:], None, ALU.is_ge, tag=f"F_k1{ci}_{st}")
                    v_op(S, "scalar_tensor_tensor", l2m[:], msk1[:], -1e30, l_tok[:], ALU.mult, ALU.add, tag=f"F_l2{ci}_{st}")
                    v_op(S, "reduce_max", lm["m2"][:], l2m[:], AX.X, tag=f"F_m2{ci}_{st}")
                    v_op(S, "tensor_scalar", msk2[:], l2m[:], lm["m2"][:], None, ALU.is_ge, tag=f"F_k2{ci}_{st}")
                    act(S, lm["e2"][:], lm["m2"][:], AF.Exp, bias=lm["negm1"][:], tag=f"F_e2{ci}_{st}")
                    v_op(S, "tensor_scalar", lm["den"][:], lm["e2"][:], 1.0, None, ALU.add, tag=f"F_dn{ci}_{st}")
                    v_op(S, "reciprocal", lm["winv"][:], lm["den"][:], tag=f"F_wi{ci}_{st}")
                    v_op(S, "tensor_mul", lm["w2c"][:], lm["e2"][:], lm["winv"][:], tag=f"F_w2{ci}_{st}")
                    v_op(S, "tensor_scalar", wtok[:], msk1[:], lm["winv"][:], None, ALU.mult, tag=f"F_wt{ci}_{st}")
                    v_op(S, "scalar_tensor_tensor", wtok[:], msk2[:], lm["w2c"][:], wtok[:], ALU.mult, ALU.add, tag=f"F_wt2{ci}_{st}")
                    tr_op(S, ps_t[0:NEXP, 0:128], wtok[:], C.ident_f32[:], f"F_wtt{ci}_{st}")
                    v_op(S, "tensor_copy", wbe[:, s0:s0 + 128], ps_t[0:NEXP, 0:128], tag=f"F_wte{ci}_{st}")
                if getattr(S, "debug_spill", False):
                    if not hasattr(S, "wbe_dbg"):
                        S.wbe_dbg = S.dram(f"wbe_dbg{S.uid()}", [NEXP, T], F32, kind="ExternalOutput")
                    dma(S, "sp", S.wbe_dbg.ap()[:, t0:t0 + Tc], wbe[:], tag=f"dbg_wbe{ci}")
                dma(S, "sp", wrows_d.ap(), wbe[:], tag=f"F_wd1{ci}")
                dma(S, "sp", wrows[:], wrows_d.ap().rearrange("(o e) n -> o e n", o=1), tag=f"F_wd2{ci}")
                for e in range(NEXP):
                    mm(S, ps_bc[:], C.ones_c_f32[:], wrows[0:1, e, :], start=True, stop=True, tag=f"F_wb{ci}_{e}")
                    v_op(S, "tensor_copy", w_bc[:, e, :], ps_bc[:], tag=f"F_wbc{ci}_{e}")
                for gi in range(2):
                    for fam, ap_, dst in (("1", a1p, etr1), ("3", a3p, etr3)):
                        for j in range(ND):
                            mm(S, ps_e[:], ap_[:, gi, j, :], xn[:, j, :], start=(j == 0), stop=(j == ND - 1), tag=f"F_a{fam}_{ci}_{gi}_{j}")
                        v_op(S, "tensor_scalar", dst[:, gi, :], ps_e[:], 2.0, None, ALU.mult, tag=f"F_ae{fam}_{ci}_{gi}")

            with pscope(S) as pq:
                sb3 = lambda n, s, d=F32: pq.enter_context(nc.sbuf_tensor(f"{n}_{S.uid()}", list(s), d))
                ps3 = lambda n, s, d=F32: pq.enter_context(nc.psum_tensor(f"{n}_{S.uid()}", list(s), d))
                sigs = [sb3(f"sig{i}", [128, Tc], BF16) for i in range(2)]
                qqs = [sb3(f"qq{i}", [128, Tc], BF16) for i in range(2)]
                p_bfs = [sb3(f"p_bf{i}", [128, Tc], BF16) for i in range(2)]
                h1s = [sb3(f"h1s{i}", [128, Tc], BF16) for i in range(2)]
                h3s = [sb3(f"h3s{i}", [128, Tc], BF16) for i in range(2)]
                b1sb = [sb3(f"b1sb{i}", [128, Tc], BF16) for i in range(2)]
                b3sb = [sb3(f"b3sb{i}", [128, Tc], BF16) for i in range(2)]
                w1t = [sb3(f"w1t{i}", [128, ND, 512], BF16) for i in range(2)]
                w3t = [sb3(f"w3t{i}", [128, ND, 512], BF16) for i in range(2)]
                b1t = [sb3(f"b1t{i}", [64, NEXP, 512], BF16) for i in range(2)]
                b3t = [sb3(f"b3t{i}", [64, NEXP, 512], BF16) for i in range(2)]
                ps_bs = ps3("psF_bs", [128, Tc])
                ps_l1 = ps3("psF_l1", [128, 2, Tc])
                ps_l3 = ps3("psF_l3", [128, 2, Tc])
                ps_e2 = [ps3(f"psF_e2_{i}", [128, Tc]) for i in range(3)]

                def grp_load(fg, par):
                    c0 = 512 * fg
                    dma(S, "sp", w1t[par][:], S.f_scr["f_w1T"].ap()[:, c0:c0 + 512].rearrange("(a p) n -> p a n", p=128), tag=f"F_w1{ci}_{fg}")
                    dma(S, "act", w3t[par][:], S.f_scr["f_w3T"].ap()[:, c0:c0 + 512].rearrange("(a p) n -> p a n", p=128), tag=f"F_w3{ci}_{fg}")
                    dma(S, "pool", b1t[par][:], FD["f_b1p"].ap()[:, :, c0:c0 + 512].rearrange("e p n -> p e n"), tag=f"F_b1l{ci}_{fg}")
                    dma(S, "pool", b3t[par][:], FD["f_b3p"].ap()[:, :, c0:c0 + 512].rearrange("e p n -> p e n"), tag=f"F_b3l{ci}_{fg}")

                def base_gen(ft):
                    bpar = ft % 2
                    if ft % 4 == 1 and ft // 4 + 1 < NF // 4:
                        grp_load(ft // 4 + 1, (ft // 4 + 1) % 2)
                    w1 = w1t[(ft // 4) % 2]; w3 = w3t[(ft // 4) % 2]
                    fo = 128 * (ft % 4)
                    yield
                    for j in range(ND):
                        mm(S, ps_bs[:], w1[:, j, fo:fo + 128], xn[:, j, :], start=(j == 0), stop=(j == ND - 1), tag=f"F_b1{ci}_{ft}_{j}")
                        yield
                    yield
                    act(S, b1sb[bpar][:], ps_bs[:], AF.Identity, tag=f"F_be1{ci}_{ft}")
                    for j in range(ND):
                        mm(S, ps_bs[:], w3[:, j, fo:fo + 128], xn[:, j, :], start=(j == 0), stop=(j == ND - 1), tag=f"F_b3{ci}_{ft}_{j}")
                        yield
                    act(S, b3sb[bpar][:], ps_bs[:], AF.Identity, tag=f"F_be3{ci}_{ft}")
                    while True:
                        yield
                grp_load(0, 0)
                g = base_gen(0)
                for _ in range(19):
                    next(g)

                pend_a2 = []

                def emit_a2(ft, e):
                    bk, off = e // 3, 32 * (e % 3)
                    S.op("pe", lambda h, _o=ps_e2[bk][off:off + 16, :], _l=a2T[:, ft, e, :], _r=p_bfs[e % 2][:], _s=(ft == 0), _t=(ft == NF - 1), _p=(0, off):
                         h.matmul(_o, _l, _r, start=_s, stop=_t, tile_position=_p),
                         reads=[a2T[:, ft, e, :], p_bfs[e % 2][:]], writes=[ps_e2[bk][off:off + 16, :]], tag=f"F_e2_{ci}_{ft}_{e}")

                for ft in range(NF):
                    bpar = ft % 2
                    gpar = (ft // 4) % 2
                    fo = 128 * (ft % 4)
                    b1s = b1t[gpar]; b3s = b3t[gpar]
                    g = base_gen(ft + 1) if ft + 1 < NF else None
                    for e in range(NEXP):
                        gi = e // 4
                        ep = e % 2
                        sig = sigs[ep]; qq = qqs[ep]; p_bf = p_bfs[ep]
                        mm(S, ps_l1[:, ep, :], b1s[:, e, fo:fo + 128], etr1[:, gi, :], start=True, stop=True, tag=f"F_l1{ci}_{ft}_{e}")
                        v_op(S, "tensor_add", h1s[ep][:], ps_l1[:, ep, :], b1sb[bpar][:], tag=f"F_h1{ci}_{ft}_{e}")
                        mm(S, ps_l3[:, ep, :], b3s[:, e, fo:fo + 128], etr3[:, gi, :], start=True, stop=True, tag=f"F_l3{ci}_{ft}_{e}")
                        act(S, sig[:], h1s[ep][:], AF.Silu, tag=f"F_sg{ci}_{ft}_{e}")
                        v_op(S, "tensor_add", h3s[ep][:], ps_l3[:, ep, :], b3sb[bpar][:], tag=f"F_h3{ci}_{ft}_{e}")
                        v_op(S, "tensor_mul", qq[:], sig[:], h3s[ep][:], tag=f"F_q{ci}_{ft}_{e}")
                        v_op(S, "tensor_mul", p_bf[:], qq[:], w_bc[:, e, :], tag=f"F_p{ci}_{ft}_{e}")
                        if e == 0:
                            v_op(S, "tensor_copy", actmix[:, ft, :], p_bf[:], tag=f"F_ac{ci}_{ft}_{e}")
                        else:
                            v_op(S, "tensor_add", actmix[:, ft, :], actmix[:, ft, :], p_bf[:], tag=f"F_ac{ci}_{ft}_{e}")
                        if pend_a2:
                            emit_a2(*pend_a2.pop())
                        pend_a2.append((ft, e))
                        if g is not None:
                            next(g)
                            next(g)
                    if g is not None:
                        for _ in range(4):
                            next(g)
                if pend_a2:
                    emit_a2(*pend_a2.pop())
                if getattr(S, "debug_spill", False):
                    if not hasattr(S, "am_dbg"):
                        S.am_dbg = S.dram(f"am_dbg{S.uid()}", [FF, T], BF16, kind="ExternalOutput")
                    for ftd in range(NF):
                        dma(S, "sp", S.am_dbg.ap()[128 * ftd:128 * (ftd + 1), t0:t0 + Tc], actmix[:, ftd, :], tag=f"dbg_am{ci}_{ftd}")
                for e in range(NEXP):
                    bk, off = e // 3, 32 * (e % 3)
                    v_op(S, "tensor_scalar", etr2w[e][:], ps_e2[bk][off:off + 16, :], 2.0, None, ALU.mult, tag=f"F_e2w{ci}_{e}")
                if getattr(S, "debug_spill", False):
                    if not hasattr(S, "e2_dbg"):
                        S.e2_dbg = S.dram(f"e2_dbg{S.uid()}", [NEXP, 16, T], BF16, kind="ExternalOutput")
                    for e in range(NEXP):
                        dma(S, "sp", S.e2_dbg.ap()[e, :, t0:t0 + Tc], etr2w[e][:], tag=f"dbg_e2{ci}_{e}")

            with pscope(S) as pq:
                sb3 = lambda n, s, d=F32: pq.enter_context(nc.sbuf_tensor(f"{n}_{S.uid()}", list(s), d))
                ps3 = lambda n, s, d=F32: pq.enter_context(nc.psum_tensor(f"{n}_{S.uid()}", list(s), d))
                w2t = [sb3(f"w2t{i}", [128, D], BF16) for i in range(2)]
                ps_o = ps3("psF_o", [128, 8, 512])
                for ft in range(NF):
                    w2 = w2t[ft % 2]
                    dma(S, ("sp", "act")[ft % 2], w2[:], S.f_scr["f_w2T"].ap()[128 * ft:128 * (ft + 1), :], tag=f"F_w2{ci}_{ft}")
                    for di in range(8):
                        mm(S, ps_o[:, di, :], w2[:, 128 * di:128 * (di + 1)], actmix[:, ft, :], start=(ft == 0), stop=False, tag=f"F_o{ci}_{ft}_{di}")
                for di in range(8):
                    for e in range(NEXP):
                        mm(S, ps_o[:, di, :], b2T[0:16, e, 128 * di:128 * (di + 1)], etr2w[e][:], start=False, stop=(e == NEXP - 1), tag=f"F_ob{ci}_{di}_{e}")
                    v_op(S, "tensor_add", x_sb[:, di, :], x_sb[:, di, :], ps_o[:, di, :], tag=f"F_ox{ci}_{di}")
                dma(S, "sp", fm_dram(x_out_h, t0, t0 + Tc), x_sb[:], tag=f"F_stx{ci}")


def run_l0_ffn(per_core, n_cores=NCORES, debug_spill=False, **kw):
    S = Prog("l0f")
    S.debug_spill = debug_spill
    x_in = S.dram("x_in", [D, T], F32, kind="ExternalInput")
    x_out = S.dram("x_out", [D, T], F32, kind="ExternalOutput")
    C = Common(S)
    emit_l0_ffn(S, C, x_in, x_out, idx=0)
    res = S.run(per_core, n_cores, **kw)
    return [r["x_out"] for r in res.results], res


def emit_l1_attn_b(S, C, x_in_h, x_out_h):
    nc = S.nc
    dd = {}
    for nm, sh in (("tb_wqT", [D, D]), ("tb_wkT", [D, D]), ("tb_wvT", [D, D]), ("tb_woT", [D, D]),
                   ("tb_w1T", [D, FF]), ("tb_w2T", [FF, D]),
                   ("tb_bq", [1, D]), ("tb_bk", [1, D]), ("tb_bv", [1, D]), ("tb_bo", [1, D]),
                   ("tb_g1", [128, ND]), ("tb_b1c", [128, ND]), ("tb_g2", [128, ND]), ("tb_b2c", [128, ND]),
                   ("tb_bf1", [1, FF]), ("tb_bf2", [1, D]),
                   ("c_headmask", [ND, 128, NHEAD]), ("c_sel64", [NHEAD, NHEAD, 64]), ("c_ones_c_bf", [1, 128])):
        dd[nm] = S.dram(nm, sh, F32 if not nm.startswith("c_") else BF16, kind="ExternalInput")
    u0 = S.uid()
    y_d = S.dram(f"tb_y{u0}", [D, T], F32, kind="Internal")
    q_d = S.dram(f"tb_q{u0}", [D, T], BF16, kind="Internal")
    bias1_d = S.dram(f"tb_bias1{u0}", [1, FF], BF16, kind="Internal")
    MN = 1.0 - 1e-5
    ITEMP = float(np.float32(2.0 / math.sqrt(HD)))

    with pscope(S) as ph:
        sbl = lambda n, s, d=F32: ph.enter_context(nc.sbuf_tensor(f"{n}_{S.uid()}", list(s), d))
        g1 = sbl("g1", [128, ND]); dma(S, "sp", g1[:], dd["tb_g1"].ap(), tag="ld_g1")
        g2 = sbl("g2", [128, ND]); dma(S, "sp", g2[:], dd["tb_g2"].ap(), tag="ld_g2")
        hmask = sbl("hmask", [128, ND, NHEAD], BF16)
        dma(S, "sp", hmask[:], dd["c_headmask"].ap().rearrange("a p n -> p a n"), tag="ld_hm")
        sel64 = sbl("sel64", [NHEAD, NHEAD, 64], BF16)
        dma(S, "sp", sel64[:], dd["c_sel64"].ap().rearrange("a p n -> p a n"), tag="ld_sel")
        ones_cb = sbl("ones_cb", [1, 128], BF16); dma(S, "sp", ones_cb[:], dd["c_ones_c_bf"].ap(), tag="ld_ocb")
        bias_2 = sbl("bias_2", [1, D], BF16)

        Tc = 512
        nch = T // Tc
        pa = ph.enter_context(pscope(S))
        sba = lambda n, s, d=F32: pa.enter_context(nc.sbuf_tensor(f"{n}_{S.uid()}", list(s), d))
        bias_q = sba("bias_q", [1, D], BF16)
        bias_k = sba("bias_k", [1, D], BF16)
        bias_v = sba("bias_v", [1, D], BF16)
        bias_o = sba("bias_o", [1, D], BF16)
        kaug = sba("kaug", [128, NHEAD, T], BF16)
        vtok = sba("vtok", [128, T // 128, NHEAD * 65], BF16)
        fq_all = sba("fq_all", [16, T], BF16)
        fk_all = sba("fk_all", [16, T], BF16)
        with pscope(S) as pp:
            sb2 = lambda n, s, d=F32: pp.enter_context(nc.sbuf_tensor(f"{n}_{S.uid()}", list(s), d))
            ps2 = lambda n, s, d=F32: pp.enter_context(nc.psum_tensor(f"{n}_{S.uid()}", list(s), d))
            b1c = sb2("b1c", [128, ND], BF16); dma(S, "pool", b1c[:], dd["tb_b1c"].ap(), tag="ld_b1c")
            b2c = sb2("b2c", [128, ND], BF16); dma(S, "pool", b2c[:], dd["tb_b2c"].ap(), tag="ld_b2c")
            w512 = sb2("w512", [128, ND, 512], BF16)
            ps_b = ps2("ps_b", [1, 512])
            rowbuf = sb2("rowbuf", [1, FF])
            bias_1 = sb2("bias_1", [1, FF], BF16)
            for wn, bn, brow in (("tb_wqT", "tb_bq", bias_q), ("tb_wkT", "tb_bk", bias_k), ("tb_wvT", "tb_bv", bias_v)):
                dma(S, "sp", rowbuf[:, 0:D], dd[bn].ap(), tag=f"ld_{bn}")
                for n0 in range(0, D, 512):
                    load_cast_w(S, dd[wn], w512, D, 512, f"ldb_{wn}{n0}", colsl=(n0, n0 + 512))
                    bias_row_mm(S, C, ps_b, b1c, w512, ND, 0, 512, f"br_{wn}{n0}")
                    v_op(S, "tensor_tensor", brow[:, n0:n0 + 512], ps_b[:, :], rowbuf[:, n0:n0 + 512], ALU.add, tag=f"be_{wn}{n0}")
            dma(S, "sp", rowbuf[:, 0:D], dd["tb_bo"].ap(), tag="ld_bo")
            v_op(S, "tensor_copy", bias_o[:], rowbuf[:, 0:D], tag="bo_bf")
            dma(S, "sp", rowbuf[:, 0:D], dd["tb_bf2"].ap(), tag="ld_bf2")
            v_op(S, "tensor_copy", bias_2[:], rowbuf[:, 0:D], tag="bf2_bf")
            dma(S, "sp", rowbuf[:], dd["tb_bf1"].ap(), tag="ld_bf1")
            for n0 in range(0, FF, 512):
                load_cast_w(S, dd["tb_w1T"], w512, D, 512, f"ldb_w1{n0}", colsl=(n0, n0 + 512))
                bias_row_mm(S, C, ps_b, b2c, w512, ND, 0, 512, f"br_w1{n0}")
                v_op(S, "tensor_tensor", bias_1[:, n0:n0 + 512], ps_b[:, :], rowbuf[:, n0:n0 + 512], ALU.add, tag=f"be_w1{n0}")
            dma(S, "sp", bias1_d.ap(), bias_1[:], tag="st_b1d")

        with pscope(S) as pp:
            sb2 = lambda n, s, d=F32: pp.enter_context(nc.sbuf_tensor(f"{n}_{S.uid()}", list(s), d))
            ps2 = lambda n, s, d=F32: pp.enter_context(nc.psum_tensor(f"{n}_{S.uid()}", list(s), d))
            x_sb = sb2("xT", [128, ND, Tc])
            q_fm = sb2("q_fmP", [128, ND, Tc], BF16)
            xc = sb2("xTc", [128, ND, Tc])
            xn = sb2("xnT", [128, ND, Tc], BF16)
            k_fm = sb2("k_fmT", [128, ND, Tc], BF16)
            wsl = [sb2(f"wsl{i}", [128, ND, 128], BF16) for i in range(2)]
            wvl = [sb2(f"wvl{i}", [128, ND, 512], BF16) for i in range(1)]
            qsq = sb2("qsqT", [128, Tc], BF16)
            h2row = sb2("h2row", [16, Tc])
            k2c_c = sb2("k2c_c", [16, Tc], BF16)
            rows = mk_ln_rows(S, Tc, "rT", eps2=True, alloc=sb2)
            ps_s, ps_ss = ps2("psT_s", [1, Tc]), ps2("psT_ss", [1, Tc])
            ps_bc = ps2("psT_bc", [128, Tc])
            ps_o = [ps2("psT_o0", [128, Tc]), ps2("psT_o1", [128, Tc])]
            ps_h2 = ps2("psT_h2", [16, Tc])
            ps_f = ps2("psT_f", [64, Tc])
            for ci in range(nch):
                t0 = ci * Tc
                dma(S, "sp", x_sb[:], fm_dram(x_in_h, t0, t0 + Tc), tag=f"P_ldx{ci}")
                for j in range(ND):
                    v_op(S, "tensor_copy", q_fm[:, j, :], x_sb[:, j, :], tag=f"P_xb{ci}_{j}")
                    mm(S, ps_s[:], C.ones_col_bf[:], q_fm[:, j, :], start=(j == 0), stop=(j == ND - 1), tag=f"P_s{ci}_{j}")
                    act(S, qsq[:], x_sb[:, j, :], AF.Square, tag=f"P_xq{ci}_{j}")
                    mm(S, ps_ss[:], C.ones_col_bf[:], qsq[:], start=(j == 0), stop=(j == ND - 1), tag=f"P_ss{ci}_{j}")
                ln_rows(S, ps_s, ps_ss, rows, D, EPS_LN, f"P_r{ci}", eps2=EPS_LN)
                bcast_row(S, C, ps_bc[:], rows["m"][:], f"P_bm{ci}")
                for j in range(ND):
                    v_op(S, "tensor_sub", xc[:, j, :], x_sb[:, j, :], ps_bc[:], tag=f"P_sub{ci}_{j}")
                bcast_row(S, C, ps_bc[:], rows["rs"][:], f"P_bro{ci}")
                for j in range(ND):
                    v_op(S, "tensor_mul", x_sb[:, j, :], xc[:, j, :], ps_bc[:], tag=f"P_y{ci}_{j}")
                dma(S, "sp", fm_dram(y_d, t0, t0 + Tc), x_sb[:], tag=f"P_sty{ci}")
                bcast_row(S, C, ps_bc[:], rows["rs2"][:], f"P_br{ci}")
                for j in range(ND):
                    v_op(S, "scalar_tensor_tensor", xn[:, j, :], xc[:, j, :], g1[:, j:j + 1], ps_bc[:], ALU.mult, ALU.mult, tag=f"P_xn{ci}_{j}")
                for dst, wn, bias in ((q_fm, "tb_wqT", bias_q), (k_fm, "tb_wkT", bias_k)):
                    for nt in range(ND):
                        w = wsl[nt % 2]
                        load_cast_w(S, dd[wn], w, D, 128, f"P_{wn}{ci}_{nt}", colsl=(128 * nt, 128 * (nt + 1)))
                        po = ps_o[nt % 2]
                        for j in range(ND):
                            mm(S, po[:], w[:, j, :], xn[:, j, :], start=(j == 0), stop=False, tag=f"P_{wn}m{ci}_{nt}_{j}")
                        mm(S, po[:], bias[:, 128 * nt:128 * (nt + 1)], C.ones_r_bf[:, :Tc], start=False, stop=True, tag=f"P_{wn}b{ci}_{nt}")
                        act(S, dst[:, nt, :], po[:], AF.Identity, tag=f"P_{wn}e{ci}_{nt}")
                for tt in range(Tc // 128):
                    tile = (t0 + 128 * tt) // 128
                    v_op(S, "memset", vtok[:, tile, :].rearrange("p (h c) -> p h c", c=65)[:, :, 64:65], 1.0, tag=f"P_vone{ci}_{tt}")
                for half in range(2):
                    wv = wvl[0]
                    load_cast_w(S, dd["tb_wvT"], wv, D, 512, f"P_wv{ci}_{half}", colsl=(512 * half, 512 * (half + 1)))
                    for tt in range(Tc // 128):
                        tile = (t0 + 128 * tt) // 128
                        po = ps_o[tt % 2]
                        for j in range(ND):
                            mm(S, po[:], xn[:, j, 128 * tt:128 * (tt + 1)], wv[:, j, :], start=(j == 0), stop=False, tag=f"P_v{ci}_{half}_{tt}_{j}")
                        mm(S, po[:], ones_cb[:], bias_v[:, 512 * half:512 * (half + 1)], start=False, stop=True, tag=f"P_vb{ci}_{half}_{tt}")
                        for hh in range(8):
                            h = 8 * half + hh
                            v_op(S, "tensor_copy", vtok[:, tile, 65 * h:65 * h + 64], po[:, 64 * hh:64 * (hh + 1)], tag=f"P_ve{ci}_{half}_{tt}_{hh}")
                dma(S, "sp", fm_dram(q_d, t0, t0 + Tc), q_fm[:], tag=f"P_stq{ci}")
                for nm, src, frow in (("q", q_fm, fq_all), ("k", k_fm, fk_all)):
                    for j in range(ND):
                        act(S, qsq[:], src[:, j, :], AF.Square, tag=f"P_{nm}sq{ci}_{j}")
                        mm(S, ps_h2[:], hmask[:, j, :], qsq[:], start=(j == 0), stop=(j == ND - 1), tag=f"P_{nm}2{ci}_{j}")
                    v_op(S, "tensor_copy", h2row[:], ps_h2[:], tag=f"P_{nm}2c{ci}")
                    act(S, h2row[:], h2row[:], AF.Sqrt, tag=f"P_{nm}n{ci}")
                    v_op(S, "reciprocal", h2row[:], h2row[:], tag=f"P_{nm}ri{ci}")
                    v_op(S, "tensor_scalar", frow[:, t0:t0 + Tc], h2row[:], MN, 1.0, ALU.mult, ALU.min, tag=f"P_{nm}f{ci}")
                    if nm == "k":
                        v_op(S, "tensor_copy", k2c_c[:], ps_h2[:], tag=f"P_k2r{ci}")
                        v_op(S, "tensor_mul", h2row[:], frow[:, t0:t0 + Tc], frow[:, t0:t0 + Tc], tag=f"P_kf2{ci}")
                        v_op(S, "tensor_mul", k2c_c[:], k2c_c[:], h2row[:], tag=f"P_k2c{ci}")
                for h in range(NHEAD):
                    mm(S, ps_f[:], sel64[:, h, :], fk_all[:, t0:t0 + Tc], start=True, stop=True, tag=f"P_fkb{ci}_{h}")
                    v_op(S, "tensor_mul", kaug[0:64, h, t0:t0 + Tc], k_fm[64 * (h % 2):64 * (h % 2) + 64, h // 2, :], ps_f[:], tag=f"P_ka{ci}_{h}")
                    mm(S, ps_f[:], sel64[:, h, :], k2c_c[:], start=True, stop=True, tag=f"P_k2b{ci}_{h}")
                    v_op(S, "tensor_copy", kaug[64:65, h, t0:t0 + Tc], ps_f[0:1, :], tag=f"P_kr{ci}_{h}")

        with pscope(S) as pp:
            sb2 = lambda n, s, d=F32: pp.enter_context(nc.sbuf_tensor(f"{n}_{S.uid()}", list(s), d))
            ps2 = lambda n, s, d=F32: pp.enter_context(nc.psum_tensor(f"{n}_{S.uid()}", list(s), d))
            qaug = sb2("qaug", [128, NHEAD, Tc], BF16)
            q_fm = sb2("q_fmA", [128, ND, Tc], BF16)
            expS = sb2("expS", [128, T // 128, Tc], BF16)
            att_o = sb2("att_o", [128, ND, Tc], BF16)
            den = sb2("den", [1, Tc])
            otmp = sb2("otmp", [64, Tc], BF16)
            x1 = sb2("x1T", [128, ND, Tc])
            wsl = [sb2(f"wslA{i}", [128, ND, 128], BF16) for i in range(2)]
            rowsA = mk_ln_rows(S, Tc, "rTA", alloc=sb2)
            ps_S = [ps2("psA_S0", [128, Tc]), ps2("psA_S1", [128, Tc])]
            ps_av = ps2("psA_av", [128, Tc])
            ps_dn = ps2("psA_dn", [64, Tc])
            ps_f = ps2("psA_f", [64, Tc])
            ps_s, ps_ss = ps2("psA_s", [1, Tc]), ps2("psA_ss", [1, Tc])
            ps_bc = ps2("psA_bc", [128, Tc])
            for ci in range(nch):
                t0 = ci * Tc
                dma(S, "sp", q_fm[:], fm_dram(q_d, t0, t0 + Tc), tag=f"A_ldq{ci}")
                v_op(S, "memset", qaug[64:128, :, :], -1.0 / 8.0, tag=f"A_qm{ci}")
                for h in range(NHEAD):
                    mm(S, ps_f[:], sel64[:, h, :], fq_all[:, t0:t0 + Tc], start=True, stop=True, tag=f"A_fqb{ci}_{h}")
                    v_op(S, "scalar_tensor_tensor", qaug[0:64, h, :], ps_f[:], ITEMP, q_fm[64 * (h % 2):64 * (h % 2) + 64, h // 2, :], ALU.mult, ALU.mult, tag=f"A_qa{ci}_{h}")
                for h in range(NHEAD):
                    for kt in range(T // 128):
                        pS = ps_S[kt % 2]
                        mm(S, pS[:], kaug[0:65, h, 128 * kt:128 * (kt + 1)], qaug[0:65, h, :], start=True, stop=True, tag=f"A_S{ci}_{h}_{kt}")
                        act(S, expS[:, kt, :], pS[:], AF.Exp, tag=f"A_e{ci}_{h}_{kt}")
                    for kt in range(T // 128):
                        mm(S, ps_av[0:65, :], vtok[:, kt, 65 * h:65 * (h + 1)], expS[:, kt, :], start=(kt == 0), stop=(kt == T // 128 - 1), tag=f"A_av{ci}_{h}_{kt}")
                    v_op(S, "tensor_copy", den[:], ps_av[64:65, :], tag=f"A_dc{ci}_{h}")
                    v_op(S, "reciprocal", den[:], den[:], tag=f"A_dr{ci}_{h}")
                    act(S, otmp[:], ps_av[0:64, :], AF.Identity, tag=f"A_oc{ci}_{h}")
                    mm(S, ps_dn[:], C.ones_c_f32[0:1, 0:64], den[:], start=True, stop=True, tag=f"A_db{ci}_{h}")
                    v_op(S, "tensor_mul", att_o[64 * (h % 2):64 * (h % 2) + 64, h // 2, :], otmp[:], ps_dn[:], tag=f"A_om{ci}_{h}")
                dma(S, "sp", x1[:], fm_dram(y_d, t0, t0 + Tc), tag=f"A_ldy{ci}")
                for nt in range(ND):
                    w = wsl[nt % 2]
                    load_cast_w(S, dd["tb_woT"], w, D, 128, f"A_wo{ci}_{nt}", colsl=(128 * nt, 128 * (nt + 1)))
                    po = ps_S[nt % 2]
                    for j in range(ND):
                        mm(S, po[:], w[:, j, :], att_o[:, j, :], start=(j == 0), stop=False, tag=f"A_wom{ci}_{nt}_{j}")
                    mm(S, po[:], bias_o[:, 128 * nt:128 * (nt + 1)], C.ones_r_bf[:, :Tc], start=False, stop=True, tag=f"A_wob{ci}_{nt}")
                    v_op(S, "tensor_add", x1[:, nt, :], x1[:, nt, :], po[:], tag=f"A_res{ci}_{nt}")
                for j in range(ND):
                    act(S, qaug[:, 8 + j, :], x1[:, j, :], AF.Square, tag=f"A_sq{ci}_{j}")
                    mm(S, ps_s[:], C.ones_col_bf[:], qaug[:, 8 + j, :], start=(j == 0), stop=(j == ND - 1), tag=f"A_n2{ci}_{j}")
                act(S, rowsA["var"][:], ps_s[:], AF.Sqrt, tag=f"A_nn{ci}")
                v_op(S, "reciprocal", rowsA["rs"][:], rowsA["var"][:], tag=f"A_nr{ci}")
                v_op(S, "tensor_scalar", rowsA["m"][:], rowsA["rs"][:], MN, 1.0, ALU.mult, ALU.min, tag=f"A_nf{ci}")
                bcast_row(S, C, ps_bc[:], rowsA["m"][:], f"A_nb{ci}")
                for j in range(ND):
                    v_op(S, "tensor_mul", x1[:, j, :], x1[:, j, :], ps_bc[:], tag=f"A_px{ci}_{j}")
                dma(S, "sp", fm_dram(y_d, t0, t0 + Tc), x1[:], tag=f"A_stx1{ci}")

        pa.close()
        S.barrier()
        with pscope(S) as pp:
            sb2 = lambda n, s, d=F32: pp.enter_context(nc.sbuf_tensor(f"{n}_{S.uid()}", list(s), d))
            ps2 = lambda n, s, d=F32: pp.enter_context(nc.psum_tensor(f"{n}_{S.uid()}", list(s), d))
            x1 = sb2("x1B", [128, ND, Tc])
            xb = sb2("xBb", [128, ND, Tc], BF16)
            xq = sb2("xBq", [128, ND, Tc], BF16)
            xc = sb2("xBc", [128, ND, Tc])
            zn = sb2("znB", [128, ND, Tc], BF16)
            h_bf = sb2("h_bfB", [128, NF, Tc], BF16)
            x0 = sb2("x0B", [128, ND, Tc])
            wB = [sb2(f"wB{i}", [128, ND, 128], BF16) for i in range(2)]
            w2B = [sb2(f"w2B{i}", [128, 512], BF16) for i in range(2)]
            rowsB = mk_ln_rows(S, Tc, "rTB", alloc=sb2)
            bias_1 = sb2("bias_1B", [1, FF], BF16)
            dma(S, "sp", bias_1[:], bias1_d.ap(), tag="ld_b1d")
            ps_s, ps_ss = ps2("psB_s", [1, Tc]), ps2("psB_ss", [1, Tc])
            ps_bc = ps2("psB_bc", [128, Tc])
            ps_h = [ps2("psB_h0", [128, Tc]), ps2("psB_h1", [128, Tc])]
            ps_o = ps2("psB_o", [128, 2, Tc])
            for ci in range(nch):
                t0 = ci * Tc
                dma(S, "sp", x1[:], fm_dram(y_d, t0, t0 + Tc), tag=f"B_ldx{ci}")
                for j in range(ND):
                    v_op(S, "tensor_copy", xb[:, j, :], x1[:, j, :], tag=f"B_xb{ci}_{j}")
                    act(S, xq[:, j, :], x1[:, j, :], AF.Square, tag=f"B_xq{ci}_{j}")
                ln_stats(S, C, xb, xq, ND, Tc, ps_s, ps_ss, f"B_st{ci}")
                ln_rows(S, ps_s, ps_ss, rowsB, D, EPS_LN, f"B_r{ci}")
                bcast_row(S, C, ps_bc[:], rowsB["m"][:], f"B_bm{ci}")
                for j in range(ND):
                    v_op(S, "tensor_sub", xc[:, j, :], x1[:, j, :], ps_bc[:], tag=f"B_sub{ci}_{j}")
                bcast_row(S, C, ps_bc[:], rowsB["rs"][:], f"B_br{ci}")
                for j in range(ND):
                    v_op(S, "scalar_tensor_tensor", zn[:, j, :], xc[:, j, :], g2[:, j:j + 1], ps_bc[:], ALU.mult, ALU.mult, tag=f"B_zn{ci}_{j}")
                for mi in range(NF):
                    w = wB[mi % 2]
                    load_cast_w(S, dd["tb_w1T"], w, D, 128, f"B_w1{ci}_{mi}", colsl=(128 * mi, 128 * (mi + 1)))
                    ph_ = ps_h[mi % 2]
                    for j in range(ND):
                        mm(S, ph_[:], w[:, j, :], zn[:, j, :], start=(j == 0), stop=False, tag=f"B_mm{ci}_{mi}_{j}")
                    mm(S, ph_[:], bias_1[:, 128 * mi:128 * (mi + 1)], C.ones_r_bf[:, :Tc], start=False, stop=True, tag=f"B_bb{ci}_{mi}")
                    act(S, h_bf[:, mi, :], ph_[:], AF.Gelu, tag=f"B_g{ci}_{mi}")
                for half in range(4):
                    for mi in range(NF):
                        w2 = w2B[mi % 2]
                        load_cast_w(S, dd["tb_w2T"], w2, 128, 512, f"B_w2{ci}_{half}_{mi}", rowsl=(128 * mi, 128 * (mi + 1)), colsl=(256 * half, 256 * (half + 1)) if False else (512 * (half // 2), 512 * (half // 2 + 1)))
                        for dq in range(2):
                            di = 2 * half + dq
                            mm(S, ps_o[:, dq, :], w2[:, 128 * (di % 4):128 * (di % 4) + 128], h_bf[:, mi, :], start=(mi == 0), stop=False, tag=f"B_o{ci}_{half}_{mi}_{dq}")
                    for dq in range(2):
                        di = 2 * half + dq
                        mm(S, ps_o[:, dq, :], bias_2[:, 128 * di:128 * (di + 1)], C.ones_r_bf[:, :Tc], start=False, stop=True, tag=f"B_ob{ci}_{di}")
                        v_op(S, "tensor_add", x1[:, di, :], x1[:, di, :], ps_o[:, dq, :], tag=f"B_oa{ci}_{di}")
                for j in range(ND):
                    act(S, xq[:, j, :], x1[:, j, :], AF.Square, tag=f"B_sq2{ci}_{j}")
                    mm(S, ps_s[:], C.ones_col_bf[:], xq[:, j, :], start=(j == 0), stop=(j == ND - 1), tag=f"B_n2{ci}_{j}")
                act(S, rowsB["var"][:], ps_s[:], AF.Sqrt, tag=f"B_nn{ci}")
                v_op(S, "reciprocal", rowsB["rs"][:], rowsB["var"][:], tag=f"B_nr{ci}")
                v_op(S, "tensor_scalar", rowsB["m"][:], rowsB["rs"][:], MN, 1.0, ALU.mult, ALU.min, tag=f"B_nf{ci}")
                bcast_row(S, C, ps_bc[:], rowsB["m"][:], f"B_nb{ci}")
                dma(S, "sp", x0[:], fm_dram(x_in_h, t0, t0 + Tc), tag=f"B_ldx0{ci}")
                for j in range(ND):
                    v_op(S, "tensor_mul", x1[:, j, :], x1[:, j, :], ps_bc[:], tag=f"B_px{ci}_{j}")
                    v_op(S, "tensor_add", x1[:, j, :], x1[:, j, :], x0[:, j, :], tag=f"B_ax{ci}_{j}")
                dma(S, "sp", fm_dram(x_out_h, t0, t0 + Tc), x1[:], tag=f"B_stx{ci}")


def run_l1_attn_b(per_core, n_cores=NCORES, **kw):
    S = Prog("l1b")
    x_in = S.dram("x_in", [D, T], F32, kind="ExternalInput")
    x_out = S.dram("x_out", [D, T], F32, kind="ExternalOutput")
    C = Common(S)
    emit_l1_attn_b(S, C, x_in, x_out)
    res = S.run(per_core, n_cores, **kw)
    return [r["x_out"] for r in res.results], res


def emit_l0_attn_b(S, C, x_in_h, x_out_h):
    nc = S.nc
    dd = {}
    for nm, sh, dt in (("nb_Wq", [D, D], F32), ("nb_Wk", [D, D], F32), ("nb_Wv", [D, D], F32),
                       ("nb_bq", [1, D], F32), ("nb_bk", [1, D], F32), ("nb_bv", [1, D], F32),
                       ("nb_pat", [NHEAD, 3, 128, 128], F32)):
        dd[nm] = S.dram(nm, sh, dt, kind="ExternalInput")
    u0 = S.uid()
    att_d = S.dram(f"nb_att{u0}", [D, T], BF16, kind="Internal")
    cc_in = S.dram(f"nb_ccin{u0}", [1, 8], F32, kind="Internal")
    cc_out = S.dram(f"nb_ccout{u0}", [1, 8], F32, kind="Internal")
    SC = float(np.float32(HD ** -0.25))
    L64 = 64

    with pscope(S) as ph:
        sbl = lambda n, s, d=F32: ph.enter_context(nc.sbuf_tensor(f"{n}_{S.uid()}", list(s), d))
        q_fm = sbl("q_fmN", [128, ND, T], BF16)
        k_fm = sbl("k_fmN", [128, ND, T], BF16)
        vtok = sbl("vtokN", [128, T // 128, D], BF16)
        pat = sbl("patN", [128, NHEAD, 3, 128], BF16)
        dma(S, "pool", pat[:], dd["nb_pat"].ap().rearrange("h j p n -> p h j n"), tag="ld_pat")
        q_l = sbl("q_lN", [128, ND, L64], BF16)
        k_l = sbl("k_lN", [128, ND, L64], BF16)
        biasq = sbl("biasqN", [1, D], BF16)
        biask = sbl("biaskN", [1, D], BF16)
        biasv = sbl("biasvN", [1, D], BF16)
        rowf = sbl("rowfN", [1, D])
        for bn, dst, sc_ in (("nb_bq", biasq, SC), ("nb_bk", biask, SC), ("nb_bv", biasv, 1.0)):
            dma(S, "sp", rowf[:], dd[bn].ap(), tag=f"ld_{bn}")
            v_op(S, "tensor_scalar", dst[:], rowf[:], sc_, None, ALU.mult, tag=f"sc_{bn}")

        Tc = 512
        nch = T // Tc
        with pscope(S) as pp:
            sb2 = lambda n, s, d=F32: pp.enter_context(nc.sbuf_tensor(f"{n}_{S.uid()}", list(s), d))
            ps2 = lambda n, s, d=F32: pp.enter_context(nc.psum_tensor(f"{n}_{S.uid()}", list(s), d))
            x_sb = sb2("xN", [128, ND, Tc])
            xb = sb2("xNb", [128, ND, Tc], BF16)
            xc = sb2("xNc", [128, ND, Tc])
            xn = sb2("xnN", [128, ND, Tc], BF16)
            xnq = sb2("xnqN", [128, ND, Tc], BF16)
            sq = sb2("sqN", [128, Tc], BF16)
            poolf = sb2("poolfN", [128, ND, 16])
            wsl = [sb2(f"wslN{i}", [128, ND, 128], BF16) for i in range(2)]
            wvl = [sb2(f"wvlN{i}", [128, ND, 512], BF16) for i in range(1)]
            rows = mk_ln_rows(S, Tc, "rN", alloc=sb2)
            ps_s, ps_ss = ps2("psN_s", [1, Tc]), ps2("psN_ss", [1, Tc])
            ps_bc = ps2("psN_bc", [128, Tc])
            ps_o = [ps2("psN_o0", [128, Tc]), ps2("psN_o1", [128, Tc])]
            for ci in range(nch):
                t0 = ci * Tc
                dma(S, "sp", x_sb[:], fm_dram(x_in_h, t0, t0 + Tc), tag=f"N_ldx{ci}")
                for j in range(ND):
                    v_op(S, "tensor_copy", xb[:, j, :], x_sb[:, j, :], tag=f"N_xb{ci}_{j}")
                    mm(S, ps_s[:], C.ones_col_bf[:], xb[:, j, :], start=(j == 0), stop=(j == ND - 1), tag=f"N_s{ci}_{j}")
                    act(S, sq[:], x_sb[:, j, :], AF.Square, tag=f"N_xq{ci}_{j}")
                    mm(S, ps_ss[:], C.ones_col_bf[:], sq[:], start=(j == 0), stop=(j == ND - 1), tag=f"N_ss{ci}_{j}")
                ln_rows(S, ps_s, ps_ss, rows, D, EPS_LN, f"N_r{ci}")
                bcast_row(S, C, ps_bc[:], rows["m"][:], f"N_bm{ci}")
                for j in range(ND):
                    v_op(S, "tensor_sub", xc[:, j, :], x_sb[:, j, :], ps_bc[:], tag=f"N_sub{ci}_{j}")
                bcast_row(S, C, ps_bc[:], rows["rs"][:], f"N_br{ci}")
                for j in range(ND):
                    v_op(S, "tensor_mul", xn[:, j, :], xc[:, j, :], ps_bc[:], tag=f"N_xn{ci}_{j}")
                    v_op(S, "tensor_scalar", xnq[:, j, :], xn[:, j, :], SC, None, ALU.mult, tag=f"N_xnq{ci}_{j}")
                for dst, wn, bias in ((q_fm, "nb_Wq", biasq), (k_fm, "nb_Wk", biask)):
                    for nt in range(ND):
                        w = wsl[nt % 2]
                        load_cast_w(S, dd[wn], w, D, 128, f"N_{wn}{ci}_{nt}", colsl=(128 * nt, 128 * (nt + 1)))
                        po = ps_o[nt % 2]
                        for j in range(ND):
                            mm(S, po[:], w[:, j, :], xnq[:, j, :], start=(j == 0), stop=False, tag=f"N_{wn}m{ci}_{nt}_{j}")
                        mm(S, po[:], bias[:, 128 * nt:128 * (nt + 1)], C.ones_r_bf[:, :Tc], start=False, stop=True, tag=f"N_{wn}b{ci}_{nt}")
                        act(S, dst[:, nt, t0:t0 + Tc], po[:], AF.Identity, tag=f"N_{wn}e{ci}_{nt}")
                for half in range(2):
                    wv = wvl[0]
                    load_cast_w(S, dd["nb_Wv"], wv, D, 512, f"N_wv{ci}_{half}", colsl=(512 * half, 512 * (half + 1)))
                    for tt in range(Tc // 128):
                        tile = (t0 + 128 * tt) // 128
                        po = ps_o[tt % 2]
                        for j in range(ND):
                            mm(S, po[:], xn[:, j, 128 * tt:128 * (tt + 1)], wv[:, j, :], start=(j == 0), stop=False, tag=f"N_v{ci}_{half}_{tt}_{j}")
                        mm(S, po[:], C.ones_r_bf[0:1, 0:128] if False else C.ones_col_bf[:, 0:1].rearrange("p o -> o p") if False else C.ones_r_bf[0:1, 0:128], biasv[:, 512 * half:512 * (half + 1)], start=False, stop=True, tag=f"N_vb{ci}_{half}_{tt}")
                        act(S, vtok[:, tile, 512 * half:512 * (half + 1)], po[:], AF.Identity, tag=f"N_ve{ci}_{half}_{tt}")
                for src, dst in ((q_fm, q_l), (k_fm, k_l)):
                    for j in range(ND):
                        S.op("dve", lambda h, _o=poolf[:, j, :], _i=src[:, j, t0:t0 + Tc].rearrange("p (a b) -> p a b", b=32):
                             h.tensor_reduce(_o, _i, AX.X, ALU.add),
                             reads=[src[:, j, t0:t0 + Tc]], writes=[poolf[:, j, :]], tag=f"N_pool{ci}_{j}")
                        v_op(S, "tensor_scalar", dst[:, j, 16 * ci:16 * (ci + 1)], poolf[:, j, :], 1.0 / 32.0, None, ALU.mult, tag=f"N_poolsc{ci}_{j}")

        inv_bf = sbl("inv_bf", [64, NHEAD, L64], BF16)
        with pscope(S) as pp:
            sb2 = lambda n, s, d=F32: pp.enter_context(nc.sbuf_tensor(f"{n}_{S.uid()}", list(s), d))
            ps2 = lambda n, s, d=F32: pp.enter_context(nc.psum_tensor(f"{n}_{S.uid()}", list(s), d))
            K2 = sb2("K2", [64, NHEAD, L64])
            K2t = sb2("K2t", [64, NHEAD, L64])
            Vt_ = sb2("Vt_", [64, NHEAD, L64])
            V_ = sb2("V_", [64, NHEAD, L64])
            colr = sb2("colr", [1, NHEAD * L64])
            lmax = sb2("lmax", [1, 8])
            i7 = sb2("i7", [64, L64])
            i15 = sb2("i15", [64, L64])
            i13 = sb2("i13", [64, L64])
            tA = sb2("tA", [64, L64])
            tAt = sb2("tAt", [64, L64])
            tB = sb2("tB", [64, L64])
            tBt = sb2("tBt", [64, L64])
            nmax = sb2("nmax", [64, 1])
            nsum = sb2("nsum", [64, 1])
            ps_k = [ps2(f"psK{i}", [64, 512]) for i in range(2)]
            ps_c = ps2("psKc", [1, 512])
            v_op(S, "tensor_scalar", i7[:], C.ident_f32[0:64, 0:64], 7.0, None, ALU.mult, tag="mk_i7")
            v_op(S, "tensor_scalar", i15[:], C.ident_f32[0:64, 0:64], 15.0, None, ALU.mult, tag="mk_i15")
            v_op(S, "tensor_scalar", i13[:], C.ident_f32[0:64, 0:64], 13.0, None, ALU.mult, tag="mk_i13")
            for h in range(NHEAD):
                hp, hj = 64 * (h % 2), h // 2
                mm(S, ps_k[0][:, 0:L64], q_l[hp:hp + 64, hj, :], k_l[hp:hp + 64, hj, :], start=True, stop=True, tag=f"K2_{h}")
                v_op(S, "reduce_max", nmax[:], ps_k[0][:, 0:L64], AX.X, negate=True, tag=f"K2m_{h}")
                act(S, K2[:, h, :], ps_k[0][:, 0:L64], AF.Exp, bias=nmax[:], accum_out=nsum[:], tag=f"K2e_{h}")
                v_op(S, "reciprocal", nsum[:], nsum[:], tag=f"K2r_{h}")
                v_op(S, "tensor_scalar", K2[:, h, :], K2[:, h, :], nsum[:], None, ALU.mult, tag=f"K2n_{h}")
                mm(S, ps_c[:, 0:L64], C.ones_col_f32[0:64, :], K2[:, h, :], start=True, stop=True, tag=f"K2c_{h}")
                v_op(S, "tensor_copy", colr[:, L64 * h:L64 * (h + 1)], ps_c[:, 0:L64], tag=f"K2cc_{h}")
                tr_op(S, ps_k[1][:, 0:L64], K2[:, h, :], C.ident_f32[0:64, 0:64], f"K2t_{h}")
                v_op(S, "tensor_copy", K2t[:, h, :], ps_k[1][:, 0:L64], tag=f"K2te_{h}")
            v_op(S, "reduce_max", lmax[0:1, 0:1], colr[:], AX.X, tag="gmax_l")
            v_op(S, "tensor_copy", lmax[0:1, 1:8], lmax[0:1, 0:1].to_broadcast((1, 7)) if False else lmax[0:1, 0:1], tag="gmax_pad") if False else None
            v_op(S, "memset", lmax[0:1, 1:8], 0.0, tag="gmax_z")
            dma(S, "sp", cc_in.ap(), lmax[:], tag="cc_st")
            S.op("pool", lambda h: h.collective_compute(
                "AllReduce", ALU.max, replica_groups=[list(range(NCORES))],
                ins=[cc_in[:]], outs=[cc_out[:]]),
                reads=[cc_in.ap()], writes=[cc_out.ap()], tag="cc_allreduce")
            dma(S, "sp", lmax[:], cc_out.ap(), tag="cc_ld")
            v_op(S, "reciprocal", lmax[0:1, 0:1], lmax[0:1, 0:1], tag="gmax_inv")
            mm(S, ps_c[:, 0:L64] if False else ps_k[0][:, 0:1], C.ones_c_f32[0:1, 0:64], lmax[0:1, 0:1], start=True, stop=True, tag="ginv_b")
            ginv = sb2("ginv", [64, 1])
            v_op(S, "tensor_copy", ginv[:], ps_k[0][0:64, 0:1], tag="ginv_c")
            for h in range(NHEAD):
                v_op(S, "tensor_scalar", V_[:, h, :], K2t[:, h, :], ginv[:], None, ALU.mult, tag=f"V0_{h}")
                v_op(S, "tensor_scalar", Vt_[:, h, :], K2[:, h, :], ginv[:], None, ALU.mult, tag=f"V0t_{h}")
                for it in range(6):
                    mm(S, ps_k[0][:, 0:L64], K2t[:, h, :], V_[:, h, :], start=True, stop=True, tag=f"NwM_{h}_{it}")
                    mm(S, ps_k[1][:, 0:L64], V_[:, h, :], K2t[:, h, :], start=True, stop=True, tag=f"NwMt_{h}_{it}")
                    v_op(S, "tensor_copy", tA[:], ps_k[0][:, 0:L64], tag=f"NwMc_{h}_{it}")
                    v_op(S, "tensor_copy", tAt[:], ps_k[1][:, 0:L64], tag=f"NwMtc_{h}_{it}")
                    v_op(S, "scalar_tensor_tensor", tB[:], tA[:], -1.0, i7[:], ALU.mult, ALU.add, tag=f"NwB_{h}_{it}")
                    v_op(S, "scalar_tensor_tensor", tBt[:], tAt[:], -1.0, i7[:], ALU.mult, ALU.add, tag=f"NwBt_{h}_{it}")
                    mm(S, ps_k[0][:, 0:L64], tAt[:], tB[:], start=True, stop=True, tag=f"NwMB_{h}_{it}")
                    mm(S, ps_k[1][:, 0:L64], tB[:], tAt[:], start=True, stop=True, tag=f"NwMBt_{h}_{it}")
                    v_op(S, "scalar_tensor_tensor", tB[:], ps_k[0][:, 0:L64], -1.0, i15[:], ALU.mult, ALU.add, tag=f"NwC_{h}_{it}")
                    v_op(S, "scalar_tensor_tensor", tBt[:], ps_k[1][:, 0:L64], -1.0, i15[:], ALU.mult, ALU.add, tag=f"NwCt_{h}_{it}")
                    mm(S, ps_k[0][:, 0:L64], tAt[:], tB[:], start=True, stop=True, tag=f"NwMC_{h}_{it}")
                    mm(S, ps_k[1][:, 0:L64], tBt[:] if False else tB[:], tAt[:], start=True, stop=True, tag=f"NwMCt_{h}_{it}")
                    v_op(S, "scalar_tensor_tensor", tB[:], ps_k[0][:, 0:L64], -1.0, i13[:], ALU.mult, ALU.add, tag=f"NwD_{h}_{it}")
                    v_op(S, "scalar_tensor_tensor", tBt[:], ps_k[1][:, 0:L64], -1.0, i13[:], ALU.mult, ALU.add, tag=f"NwDt_{h}_{it}")
                    mm(S, ps_k[0][:, 0:L64], Vt_[:, h, :], tB[:], start=True, stop=True, tag=f"NwV_{h}_{it}")
                    mm(S, ps_k[1][:, 0:L64], tB[:], Vt_[:, h, :], start=True, stop=True, tag=f"NwVt_{h}_{it}")
                    v_op(S, "tensor_scalar", V_[:, h, :], ps_k[0][:, 0:L64], 0.25, None, ALU.mult, tag=f"NwVc_{h}_{it}")
                    v_op(S, "tensor_scalar", Vt_[:, h, :], ps_k[1][:, 0:L64], 0.25, None, ALU.mult, tag=f"NwVtc_{h}_{it}")
                v_op(S, "tensor_copy", inv_bf[:, h, :], V_[:, h, :], tag=f"NwF_{h}")

        with pscope(S) as pp:
            sb2 = lambda n, s, d=F32: pp.enter_context(nc.sbuf_tensor(f"{n}_{S.uid()}", list(s), d))
            ps2 = lambda n, s, d=F32: pp.enter_context(nc.psum_tensor(f"{n}_{S.uid()}", list(s), d))
            K1n = [sb2(f"K1n{i}", [128, L64], BF16) for i in range(2)]
            K1t = [sb2(f"K1t{i}", [64, T], BF16) for i in range(2)]
            K3raw = [sb2(f"K3raw{i}", [64, T]) for i in range(2)]
            K3bf = [sb2(f"K3bf{i}", [64, T], BF16) for i in range(2)]
            K3t = [sb2(f"K3t{i}", [128, T // 128, L64], BF16) for i in range(2)]
            B3 = [sb2(f"B3{i}", [64, L64], BF16) for i in range(2)]
            at_t = [sb2(f"at_t{i}", [64, T], BF16) for i in range(2)]
            ctx_bf = [sb2(f"ctx_bf{i}", [64, Tc], BF16) for i in range(2)]
            nmax = [sb2(f"nmaxH{i}", [128, 1]) for i in range(2)]
            nsum = [sb2(f"nsumH{i}", [128, 1]) for i in range(2)]
            rinv3 = [sb2(f"rinv3{i}", [64, 1]) for i in range(2)]
            ps_a = [ps2(f"psH{i}", [128, 512]) for i in range(4)]
            ps_b3 = [ps2(f"psHb3{i}", [64, L64]) for i in range(2)]
            ps_t = [ps2(f"psHt{i}", [128, 128], BF16) for i in range(2)]

            def head_stages(h):
                par = h % 2
                K1n_, K1t_, K3raw_, K3bf_, K3t_, B3_, at_t_, ctx_bf_ = (
                    K1n[par], K1t[par], K3raw[par], K3bf[par], K3t[par], B3[par], at_t[par], ctx_bf[par])
                nmax_, nsum_, rinv3_ = nmax[par], nsum[par], rinv3[par]
                psa = [ps_a[2 * par], ps_a[2 * par + 1]]
                psb3 = ps_b3[par]
                pst = ps_t[par]
                hp, hj = 64 * (h % 2), h // 2
                for tt in range(T // 128):
                    pk = psa[tt % 2]
                    mm(S, pk[:, 0:L64], q_fm[hp:hp + 64, hj, 128 * tt:128 * (tt + 1)], k_l[hp:hp + 64, hj, :], start=True, stop=True, tag=f"H_K1_{h}_{tt}")
                    v_op(S, "reduce_max", nmax_[:], pk[:, 0:L64], AX.X, negate=True, tag=f"H_K1m_{h}_{tt}")
                    act(S, K1n_[:], pk[:, 0:L64], AF.Exp, bias=nmax_[:], accum_out=nsum_[:], tag=f"H_K1e_{h}_{tt}")
                    v_op(S, "reciprocal", nsum_[:], nsum_[:], tag=f"H_K1r_{h}_{tt}")
                    v_op(S, "tensor_scalar", K1n_[:], K1n_[:], nsum_[:], None, ALU.mult, tag=f"H_K1n_{h}_{tt}")
                    tr_op(S, pst[0:64, :], K1n_[:], C.ident_bf[:], f"H_K1t_{h}_{tt}")
                    v_op(S, "tensor_copy", K1t_[:, 128 * tt:128 * (tt + 1)], pst[0:64, :], tag=f"H_K1te_{h}_{tt}")
                    yield
                for cj in range(nch):
                    mm(S, psa[cj % 2][0:64, :], q_l[hp:hp + 64, hj, :], k_fm[hp:hp + 64, hj, 512 * cj:512 * (cj + 1)], start=True, stop=True, tag=f"H_K3_{h}_{cj}")
                    v_op(S, "tensor_copy", K3raw_[:, 512 * cj:512 * (cj + 1)], psa[cj % 2][0:64, :], tag=f"H_K3c_{h}_{cj}")
                    yield
                v_op(S, "reduce_max", rinv3_[:], K3raw_[:], AX.X, negate=True, tag=f"H_K3m_{h}")
                act(S, K3bf_[:], K3raw_[:], AF.Exp, bias=rinv3_[:], accum_out=rinv3_[:], tag=f"H_K3e_{h}")
                v_op(S, "reciprocal", rinv3_[:], rinv3_[:], tag=f"H_K3r_{h}")
                yield
                for tt in range(T // 128):
                    tr_op(S, pst[:, 0:64], K3bf_[:, 128 * tt:128 * (tt + 1)], C.ident_bf[0:64, 0:64], f"H_K3t_{h}_{tt}")
                    v_op(S, "tensor_copy", K3t_[:, tt, :], pst[:, 0:64], tag=f"H_K3te_{h}_{tt}")
                    if tt % 4 == 3:
                        yield
                for tt in range(T // 128):
                    mm(S, psb3[:], K3t_[:, tt, :], vtok[:, tt, 64 * h:64 * (h + 1)], start=(tt == 0), stop=(tt == T // 128 - 1), tag=f"H_B3_{h}_{tt}")
                v_op(S, "tensor_scalar", B3_[:], psb3[:], rinv3_[:], None, ALU.mult, tag=f"H_B3e_{h}")
                yield
                for cj in range(nch):
                    mm(S, psa[cj % 2][0:64, :], inv_bf[:, h, :], K1t_[:, 512 * cj:512 * (cj + 1)], start=True, stop=True, tag=f"H_at_{h}_{cj}")
                    v_op(S, "tensor_copy", at_t_[:, 512 * cj:512 * (cj + 1)], psa[cj % 2][0:64, :], tag=f"H_ate_{h}_{cj}")
                    yield
                for cj in range(nch):
                    pc = psa[cj % 2]
                    mm(S, pc[0:64, :], B3_[:], at_t_[:, 512 * cj:512 * (cj + 1)], start=True, stop=True, tag=f"H_ctx_{h}_{cj}")
                    for sub in range(4):
                        st0 = 512 * cj + 128 * sub
                        for rel in (-1, 0, 1):
                            tl = st0 // 128 + rel
                            if 0 <= tl < T // 128:
                                S.op("pe", lambda hh, _o=pc[0:64, 128 * sub:128 * (sub + 1)], _l=vtok[:, tl, 64 * h:64 * (h + 1)], _r=pat[:, h, rel + 1, :]:
                                     hh.matmul(_o, _l, _r, start=False, stop=True, skip_group_check=True),
                                     reads=[vtok[:, tl, 64 * h:64 * (h + 1)], pat[:, h, rel + 1, :]], writes=[pc[0:64, 128 * sub:128 * (sub + 1)]], tag=f"H_cv_{h}_{cj}_{sub}_{rel}")
                    v_op(S, "tensor_copy", ctx_bf_[:], pc[0:64, :], tag=f"H_ce_{h}_{cj}")
                    dma(S, "sp", att_d.ap()[64 * h:64 * (h + 1), 512 * cj:512 * (cj + 1)], ctx_bf_[:], tag=f"H_cst_{h}_{cj}")
                    yield

            for hp0 in range(0, NHEAD, 2):
                ga, gb = head_stages(hp0), head_stages(hp0 + 1)
                act_a = act_b = True
                while act_a or act_b:
                    if act_a:
                        try:
                            next(ga)
                        except StopIteration:
                            act_a = False
                    if act_b:
                        try:
                            next(gb)
                        except StopIteration:
                            act_b = False

        with pscope(S) as pp:
            sb2 = lambda n, s, d=F32: pp.enter_context(nc.sbuf_tensor(f"{n}_{S.uid()}", list(s), d))
            x_sb = sb2("xR", [128, ND, Tc])
            a_sb = sb2("aR", [128, ND, Tc], BF16)
            for ci in range(nch):
                t0 = ci * Tc
                dma(S, "sp", x_sb[:], fm_dram(x_in_h, t0, t0 + Tc), tag=f"R_ldx{ci}")
                dma(S, "sp", a_sb[:], fm_dram(att_d, t0, t0 + Tc), tag=f"R_lda{ci}")
                for j in range(ND):
                    v_op(S, "tensor_add", x_sb[:, j, :], x_sb[:, j, :], a_sb[:, j, :], tag=f"R_add{ci}_{j}")
                dma(S, "sp", fm_dram(x_out_h, t0, t0 + Tc), x_sb[:], tag=f"R_stx{ci}")


def run_l0_attn_b(per_core, n_cores=NCORES, **kw):
    S = Prog("l0b")
    x_in = S.dram("x_in", [D, T], F32, kind="ExternalInput")
    x_out = S.dram("x_out", [D, T], F32, kind="ExternalOutput")
    C = Common(S)
    emit_l0_attn_b(S, C, x_in, x_out)
    res = S.run(per_core, n_cores, **kw)
    return [r["x_out"] for r in res.results], res


def emit_l0_attn_a(S, C, x_in_h, x_out_h):
    nc = S.nc
    dd = {}
    for nm, sh in (("m_w_in", [D, 2 * DI]), ("m_norm_scale", [128, ND]), ("m_conv_w", [128, NDI, 4]),
                   ("m_conv_b", [128, NDI]), ("m_A_log", [128, NDI, NSTATE]), ("m_D", [128, NDI]),
                   ("m_wx_delta", [DI, 64]), ("m_wx_B", [DI, NSTATE]), ("m_wx_C", [DI, NSTATE]),
                   ("m_w_dt", [64, DI]), ("m_b_dt", [128, NDI]), ("m_w_out", [DI, D])):
        dd[nm] = S.dram(nm, sh, F32, kind="ExternalInput")
    u0 = S.uid()
    y_d = S.dram(f"m_y{u0}", [D, T], F32, kind="Internal")
    delta_d = S.dram(f"m_delta{u0}", [DI, T], F32, kind="Internal")
    xsc_d = S.dram(f"m_xsc{u0}", [DI, T], F32, kind="Internal")
    res_d = S.dram(f"m_res{u0}", [DI, T], F32, kind="Internal")
    oact_d = S.dram(f"m_oact{u0}", [DI, T], BF16, kind="Internal")
    Tc = 512
    nch = T // Tc

    with pscope(S) as ph:
        sbl = lambda n, s, d=F32: ph.enter_context(nc.sbuf_tensor(f"{n}_{S.uid()}", list(s), d))
        norm_scale = sbl("m_ns", [128, ND]); dma(S, "sp", norm_scale[:], dd["m_norm_scale"].ap(), tag="ld_mns")
        conv_w = sbl("m_cw", [128, NDI, 4]); dma(S, "sp", conv_w[:], dd["m_conv_w"].ap(), tag="ld_mcw")
        conv_b = sbl("m_cb", [128, NDI]); dma(S, "sp", conv_b[:], dd["m_conv_b"].ap(), tag="ld_mcb")
        D_sb = sbl("m_Dv", [128, NDI]); dma(S, "sp", D_sb[:], dd["m_D"].ap(), tag="ld_mD")
        b_dt = sbl("m_bdt", [128, NDI]); dma(S, "sp", b_dt[:], dd["m_b_dt"].ap(), tag="ld_mbdt")
        A_sb = sbl("m_A", [128, NDI, NSTATE])
        dma(S, "sp", A_sb[:], dd["m_A_log"].ap(), tag="ld_mA")
        act(S, A_sb[:], A_sb[:], AF.Exp, tag="mA_exp")
        v_op(S, "tensor_scalar", A_sb[:], A_sb[:], -1.0, None, ALU.mult, tag="mA_neg")
        wxd = sbl("m_wxd", [128, NDI, 64], BF16)
        dma(S, "pool", wxd[:], dd["m_wx_delta"].ap().rearrange("(a p) n -> p a n", p=128), tag="ld_wxd")
        wxB = sbl("m_wxB", [128, NDI, NSTATE], BF16)
        dma(S, "pool", wxB[:], dd["m_wx_B"].ap().rearrange("(a p) n -> p a n", p=128), tag="ld_wxB")
        wxC = sbl("m_wxC", [128, NDI, NSTATE], BF16)
        dma(S, "pool", wxC[:], dd["m_wx_C"].ap().rearrange("(a p) n -> p a n", p=128), tag="ld_wxC")
        w_dt = sbl("m_wdt", [64, DI], BF16)
        dma(S, "pool", w_dt[:], dd["m_w_dt"].ap(), tag="ld_wdt")
        B_all = sbl("m_Ball", [NSTATE, T], BF16)
        C_all = sbl("m_Call", [NSTATE, T], BF16)
        d_sel = S.dram("c_sel128", [NHEAD, NHEAD, 128], BF16, kind="ExternalInput")
        sel128 = sbl("m_sel", [NHEAD, NHEAD, 128], BF16)
        dma(S, "sp", sel128[:], d_sel.ap().rearrange("a p n -> p a n"), tag="ld_msel")

        with pscope(S) as pp:
            sb2 = lambda n, s, d=F32: pp.enter_context(nc.sbuf_tensor(f"{n}_{S.uid()}", list(s), d))
            ps2 = lambda n, s, d=F32: pp.enter_context(nc.psum_tensor(f"{n}_{S.uid()}", list(s), d))
            x_sb = sb2("xM", [128, ND, Tc])
            xh = sb2("xhM", [128, ND, Tc], BF16)
            xbq = sb2("xbqM", [128, Tc], BF16)
            rows = mk_ln_rows(S, Tc, "rM", eps2=True, alloc=sb2)
            xs_sb = sb2("xsM", [128, NDI, Tc])
            xs_ext = sb2("xseM", [128, NDI, Tc + 4])
            prev3 = sb2("p3M", [128, NDI, 4])
            xsb_bf = sb2("xsbM", [128, NDI, Tc], BF16)
            resb = [sb2(f"resM{i}", [128, Tc]) for i in range(2)]
            din_bf = sb2("dinM", [64, Tc], BF16)
            dbuf = [sb2(f"dbM{i}", [128, Tc]) for i in range(2)]
            obuf = [sb2(f"obM{i}", [128, Tc]) for i in range(2)]
            wsl = [sb2(f"wslM{i}", [128, ND, 128], BF16) for i in range(2)]
            ps_s, ps_ss = ps2("psM_s", [1, Tc]), ps2("psM_ss", [1, Tc])
            ps_bc = ps2("psM_bc", [128, Tc])
            ps_o = [ps2("psM_o0", [128, Tc]), ps2("psM_o1", [128, Tc])]
            ps_p = ps2("psM_p", [64, Tc])
            ps_n = ps2("psM_n", [NSTATE, 2 * Tc])
            v_op(S, "memset", prev3[:], 0.0, tag="m_p3z")
            for ci in range(nch):
                t0 = ci * Tc
                dma(S, "sp", x_sb[:], fm_dram(x_in_h, t0, t0 + Tc), tag=f"M_ldx{ci}")
                for j in range(ND):
                    v_op(S, "tensor_copy", xh[:, j, :], x_sb[:, j, :], tag=f"M_xb{ci}_{j}")
                    mm(S, ps_s[:], C.ones_col_bf[:], xh[:, j, :], start=(j == 0), stop=(j == ND - 1), tag=f"M_s{ci}_{j}")
                    act(S, xbq[:], x_sb[:, j, :], AF.Square, tag=f"M_xq{ci}_{j}")
                    mm(S, ps_ss[:], C.ones_col_bf[:], xbq[:], start=(j == 0), stop=(j == ND - 1), tag=f"M_ss{ci}_{j}")
                ln_rows(S, ps_s, ps_ss, rows, D, EPS_LN, f"M_r{ci}", eps2=EPS_LN)
                bcast_row(S, C, ps_bc[:], rows["m"][:], f"M_bm{ci}")
                for j in range(ND):
                    v_op(S, "tensor_sub", x_sb[:, j, :], x_sb[:, j, :], ps_bc[:], tag=f"M_sub{ci}_{j}")
                bcast_row(S, C, ps_bc[:], rows["rs2"][:], f"M_br{ci}")
                for j in range(ND):
                    v_op(S, "scalar_tensor_tensor", xh[:, j, :], x_sb[:, j, :], norm_scale[:, j:j + 1], ps_bc[:], ALU.mult, ALU.mult, tag=f"M_xh{ci}_{j}")
                bcast_row(S, C, ps_bc[:], rows["rs"][:], f"M_bro{ci}")
                for j in range(ND):
                    v_op(S, "tensor_mul", x_sb[:, j, :], x_sb[:, j, :], ps_bc[:], tag=f"M_y{ci}_{j}")
                dma(S, "sp", fm_dram(y_d, t0, t0 + Tc), x_sb[:], tag=f"M_sty{ci}")
                for mt in range(2 * NDI):
                    w = wsl[mt % 2]
                    load_cast_w(S, dd["m_w_in"], w, D, 128, f"M_wi{ci}_{mt}", colsl=(128 * mt, 128 * (mt + 1)))
                    po = ps_o[mt % 2]
                    for j in range(ND):
                        mm(S, po[:], w[:, j, :], xh[:, j, :], start=(j == 0), stop=(j == ND - 1), tag=f"M_h4{ci}_{mt}_{j}")
                    if mt < NDI:
                        act(S, xs_sb[:, mt, :], po[:], AF.Identity, tag=f"M_xse{ci}_{mt}")
                    else:
                        rb = resb[mt % 2]
                        act(S, rb[:], po[:], AF.Silu, tag=f"M_rese{ci}_{mt}")
                        dma(S, "sp", res_d.ap()[128 * (mt - NDI):128 * (mt - NDI + 1), t0:t0 + Tc], rb[:], tag=f"M_resst{ci}_{mt}")
                for c in range(NDI):
                    v_op(S, "tensor_copy", xs_ext[:, c, 0:4], prev3[:, c, :], tag=f"M_cp{ci}_{c}")
                    v_op(S, "tensor_copy", xs_ext[:, c, 4:4 + Tc], xs_sb[:, c, :], tag=f"M_ce{ci}_{c}")
                    v_op(S, "tensor_copy", prev3[:, c, :], xs_ext[:, c, Tc:Tc + 4], tag=f"M_cu{ci}_{c}")
                    v_op(S, "tensor_scalar", xs_sb[:, c, :], xs_ext[:, c, 1:1 + Tc], conv_w[:, c, 0:1], None, ALU.mult, tag=f"M_cv0{ci}_{c}")
                    for tap in range(1, 4):
                        v_op(S, "scalar_tensor_tensor", xs_sb[:, c, :], xs_ext[:, c, 1 + tap:1 + tap + Tc], conv_w[:, c, tap:tap + 1], xs_sb[:, c, :], ALU.mult, ALU.add, tag=f"M_cv{tap}{ci}_{c}")
                    act(S, xs_sb[:, c, :], xs_sb[:, c, :], AF.Silu, bias=conv_b[:, c:c + 1], tag=f"M_cs{ci}_{c}")
                    v_op(S, "tensor_copy", xsb_bf[:, c, :], xs_sb[:, c, :], tag=f"M_cb{ci}_{c}")
                dma(S, "sp", xsc_d.ap()[:, t0:t0 + Tc].rearrange("(a p) t -> p a t", p=128), xs_sb[:], tag=f"M_xsst{ci}")
                for j in range(NDI):
                    mm(S, ps_p[:], wxd[:, j, :], xsb_bf[:, j, :], start=(j == 0), stop=(j == NDI - 1), tag=f"M_pd{ci}_{j}")
                for j in range(NDI):
                    mm(S, ps_n[:, 0:Tc], wxB[:, j, :], xsb_bf[:, j, :], start=(j == 0), stop=(j == NDI - 1), tag=f"M_pB{ci}_{j}")
                for j in range(NDI):
                    mm(S, ps_n[:, Tc:2 * Tc], wxC[:, j, :], xsb_bf[:, j, :], start=(j == 0), stop=(j == NDI - 1), tag=f"M_pC{ci}_{j}")
                v_op(S, "tensor_copy", din_bf[:], ps_p[:], tag=f"M_dine{ci}")
                v_op(S, "tensor_copy", B_all[:, t0:t0 + Tc], ps_n[:, 0:Tc], tag=f"M_Be{ci}")
                v_op(S, "tensor_copy", C_all[:, t0:t0 + Tc], ps_n[:, Tc:2 * Tc], tag=f"M_Ce{ci}")
                for c in range(NDI):
                    po = ps_o[c % 2]
                    mm(S, po[:], w_dt[:, 128 * c:128 * (c + 1)], din_bf[:], start=True, stop=True, tag=f"M_dt{ci}_{c}")
                    db = dbuf[c % 2]
                    act(S, db[:], po[:], AF.Exp, bias=b_dt[:, c:c + 1], tag=f"M_de{ci}_{c}")
                    act(S, db[:], db[:], AF.Ln, bias=1.0, tag=f"M_dl{ci}_{c}")
                    dma(S, "sp", delta_d.ap()[128 * c:128 * (c + 1), t0:t0 + Tc], db[:], tag=f"M_dst{ci}_{c}")

        if getattr(S, "l0a_mid_hook", None) is not None:
            S.l0a_mid_hook(lambda n, s, d=BF16: ph.enter_context(nc.sbuf_tensor(n, list(s), d)))
        wo_sb = sbl("wo_sb", [128, NDI, D], BF16)
        for kt in range(NDI):
            dma(S, "pool", wo_sb[:, kt, :], dd["m_w_out"].ap()[128 * kt:128 * (kt + 1), :], tag=f"ld_wo{kt}")

        TH = T // 2
        with pscope(S) as pp:
            sb2 = lambda n, s, d=F32: pp.enter_context(nc.sbuf_tensor(f"{n}_{S.uid()}", list(s), d))
            ps2 = lambda n, s, d=F32: pp.enter_context(nc.psum_tensor(f"{n}_{S.uid()}", list(s), d))
            B_bc = sb2("B_bc", [128, NSTATE, TH], BF16)
            C_bc = sb2("C_bc", [128, NSTATE, TH], BF16)
            carries = sb2("carr", [128, NDI, NSTATE])
            ps_w = ps2("psB_w", [128, 2048])
            delta_c = [sb2(f"delta_c{i}", [128, TH]) for i in range(2)]
            xs_c = [sb2(f"xs_cB{i}", [128, TH]) for i in range(2)]
            res_c = [sb2(f"res_cB{i}", [128, TH]) for i in range(2)]
            u_c = [sb2(f"u_cB{i}", [128, TH], BF16) for i in range(2)]
            dA = [sb2(f"dA_B{i}", [128, TH]) for i in range(2)]
            dBx = [sb2(f"dBx_B{i}", [128, TH], BF16) for i in range(2)]
            s_f = sb2("s_fB", [128, TH], BF16)
            y_c = sb2("y_cB", [128, TH])
            p_f = [sb2(f"p_fB{i}", [128, TH], BF16) for i in range(2)]
            ob = sb2("obB", [128, TH], BF16)
            ps_y = pp.enter_context(nc.psum_tensor(f"psB_y_{S.uid()}", [128, TH], F32))
            v_op(S, "memset", carries[:], 0.0, tag="MB_cz")
            for half in range(2):
                h0 = half * TH
                for n in range(NSTATE):
                    for cj in range(TH // 512):
                        mm(S, ps_w[:, 512 * cj:512 * (cj + 1)], sel128[:, n, :], B_all[:, h0 + 512 * cj:h0 + 512 * (cj + 1)], start=True, stop=True, tag=f"MB_b{half}_{n}_{cj}")
                    v_op(S, "tensor_copy", B_bc[:, n, :], ps_w[:, 0:TH], tag=f"MB_be{half}_{n}")
                    for cj in range(TH // 512):
                        mm(S, ps_w[:, 512 * cj:512 * (cj + 1)], sel128[:, n, :], C_all[:, h0 + 512 * cj:h0 + 512 * (cj + 1)], start=True, stop=True, tag=f"MC_b{half}_{n}_{cj}")
                    v_op(S, "tensor_copy", C_bc[:, n, :], ps_w[:, 0:TH], tag=f"MC_be{half}_{n}")
                for c in range(NDI):
                    cb = c % 2
                    dma(S, "sp", delta_c[cb][:], delta_d.ap()[128 * c:128 * (c + 1), h0:h0 + TH], tag=f"MB_ldd{half}_{c}")
                    dma(S, "sp", xs_c[cb][:], xsc_d.ap()[128 * c:128 * (c + 1), h0:h0 + TH], tag=f"MB_ldxs{half}_{c}")
                    dma(S, "sp", res_c[cb][:], res_d.ap()[128 * c:128 * (c + 1), h0:h0 + TH], tag=f"MB_ldr{half}_{c}")
                    v_op(S, "tensor_mul", u_c[cb][:], delta_c[cb][:], xs_c[cb][:], tag=f"MB_u{half}_{c}")
                    for n in range(NSTATE):
                        da = dA[n % 2]
                        act(S, da[:], delta_c[cb][:], AF.Exp, scale=A_sb[:, c, n:n + 1], tag=f"MB_dA{half}_{c}_{n}")
                        db = dBx[n % 2]
                        v_op(S, "tensor_mul", db[:], u_c[cb][:], B_bc[:, n, :], tag=f"MB_dBx{half}_{c}_{n}")
                        S.op("dve", lambda h, _s=s_f[:], _a=da[:], _b=db[:], _i=(0.0 if half == 0 else carries[:, c, n:n + 1]):
                             h.tensor_tensor_scan(_s, _a, _b, _i, ALU.mult, ALU.add),
                             reads=[da[:], db[:]] + ([carries[:, c, n:n + 1]] if half else []),
                             writes=[s_f[:]], tag=f"MB_scan{half}_{c}_{n}")
                        if half == 0:
                            v_op(S, "tensor_copy", carries[:, c, n:n + 1], s_f[:, TH - 1:TH], tag=f"MB_cs{c}_{n}")
                        pf = p_f[n % 2]
                        v_op(S, "tensor_mul", pf[:], s_f[:], C_bc[:, n, :], tag=f"MB_p{half}_{c}_{n}")
                        for cj in range(TH // 512):
                            mm(S, ps_y[:, 512 * cj:512 * (cj + 1)], C.ident_bf[:], pf[:, 512 * cj:512 * (cj + 1)], start=(n == 0), stop=(n == NSTATE - 1), tag=f"MB_ya{half}_{c}_{n}_{cj}")
                    v_op(S, "scalar_tensor_tensor", y_c[:], xs_c[cb][:], D_sb[:, c:c + 1], ps_y[:], ALU.mult, ALU.add, tag=f"MB_yd{half}_{c}")
                    v_op(S, "tensor_mul", ob[:], y_c[:], res_c[cb][:], tag=f"MB_or{half}_{c}")
                    dma(S, "sp", oact_d.ap()[128 * c:128 * (c + 1), h0:h0 + TH], ob[:], tag=f"MB_ost{half}_{c}")

        with pscope(S) as pp:
            sb2 = lambda n, s, d=F32: pp.enter_context(nc.sbuf_tensor(f"{n}_{S.uid()}", list(s), d))
            ps2 = lambda n, s, d=F32: pp.enter_context(nc.psum_tensor(f"{n}_{S.uid()}", list(s), d))
            x_sb = sb2("xMC", [128, ND, Tc])
            y_sb = sb2("yMC", [128, ND, Tc])
            oa = [sb2(f"oaC{i}", [128, Tc], BF16) for i in range(2)]
            ps_o = ps2("psC_o", [128, 8, 512])
            for ci in range(nch):
                t0 = ci * Tc
                dma(S, "sp", x_sb[:], fm_dram(x_in_h, t0, t0 + Tc), tag=f"C_ldx{ci}")
                dma(S, "sp", y_sb[:], fm_dram(y_d, t0, t0 + Tc), tag=f"C_ldy{ci}")
                for j in range(ND):
                    v_op(S, "tensor_add", x_sb[:, j, :], x_sb[:, j, :], y_sb[:, j, :], tag=f"C_xy{ci}_{j}")
                for kt in range(NDI):
                    o = oa[kt % 2]
                    dma(S, "sp", o[:], oact_d.ap()[128 * kt:128 * (kt + 1), t0:t0 + Tc], tag=f"C_lo{ci}_{kt}")
                    for di in range(ND):
                        mm(S, ps_o[:, di, :], wo_sb[:, kt, 128 * di:128 * (di + 1)], o[:], start=(kt == 0), stop=(kt == NDI - 1), tag=f"C_mm{ci}_{kt}_{di}")
                for di in range(ND):
                    v_op(S, "tensor_add", x_sb[:, di, :], x_sb[:, di, :], ps_o[:, di, :], tag=f"C_res{ci}_{di}")
                dma(S, "sp", fm_dram(x_out_h, t0, t0 + Tc), x_sb[:], tag=f"C_stx{ci}")


def run_l0_attn_a(per_core, n_cores=NCORES, **kw):
    S = Prog("l0a")
    x_in = S.dram("x_in", [D, T], F32, kind="ExternalInput")
    x_out = S.dram("x_out", [D, T], F32, kind="ExternalOutput")
    C = Common(S)
    emit_l0_attn_a(S, C, x_in, x_out)
    res = S.run(per_core, n_cores, **kw)
    return [r["x_out"] for r in res.results], res


SUBLAYER_SEQ = (
    ("l0a", emit_l0_attn_a, {}),
    ("l0f1", emit_l0_ffn, {"idx": 0}),
    ("l0b", emit_l0_attn_b, {}),
    ("l0f2", emit_l0_ffn, {"idx": 0}),
    ("l1a", emit_l1_attn_a, {}),
    ("l1f1", emit_l1_ffn, {"idx": 0}),
    ("l1b", emit_l1_attn_b, {}),
    ("l1f2", emit_l1_ffn, {"idx": 0}),
)


def kernel_unfused(**inputs):
    per_core = prep_host(inputs)
    for name, emitfn, kw in SUBLAYER_SEQ:
        S = Prog(name)
        x_in = S.dram("x_in", [D, T], F32, kind="ExternalInput")
        x_out = S.dram("x_out", [D, T], F32, kind="ExternalOutput")
        C = Common(S)
        emitfn(S, C, x_in, x_out, **kw)
        res = S.run(per_core, NCORES)
        for c in range(NCORES):
            per_core[c]["x_in"] = np.ascontiguousarray(np.asarray(res.results[c]["x_out"]))
    out = np.stack([per_core[c]["x_in"].T for c in range(NCORES)]).astype(np.float32)
    return out


def build_mega():
    S = Prog("mega")
    def _hook(alloc):
        l0f_wprep(S, alloc)
        wconv_all(S, alloc)
    S.l0a_mid_hook = _hook
    x_ext_in = S.dram("x_in", [D, T], F32, kind="ExternalInput")
    x_ext_out = S.dram("x_out", [D, T], F32, kind="ExternalOutput")
    xs = [x_ext_in] + [S.dram(f"x_mid_{k}", [D, T], F32, kind="Internal") for k in range(len(SUBLAYER_SEQ) - 1)] + [x_ext_out]
    C = Common(S)
    S.phase_marks = []
    for k, (name, emitfn, kw) in enumerate(SUBLAYER_SEQ):
        with S.nc.named_scope(name):
            emitfn(S, C, xs[k], xs[k + 1], **kw)
        S.phase_marks.append((name, len(S.ops)))
    return S


def kernel(**inputs):
    per_core = prep_host(inputs)
    S = build_mega()
    res = S.run(per_core, NCORES)
    out = np.stack([np.asarray(res.results[c]["x_out"]).T for c in range(NCORES)]).astype(np.float32)
    return out


# version 18
# speedup vs baseline: 1.1562x; 1.1562x over previous
import math
from contextlib import ExitStack

import numpy as np
import ml_dtypes
import sys

sys.path.insert(0, "/opt/trn_rl_repo")
import concourse.bass as bass
import concourse.mybir as mybir
from concourse.bass_utils import run_bass_kernel_spmd

F32 = mybir.dt.float32
BF16 = mybir.dt.bfloat16
AF = mybir.ActivationFunctionType
ALU = mybir.AluOpType
AX = mybir.AxisListType
BF = ml_dtypes.bfloat16
I32 = mybir.dt.int32

NCORES = 8
T = 2048
D = 1024
ND = D // 128
DI = 2048
NDI = DI // 128
FF = 4096
NF = FF // 128
NEXP = 8
RLORA = 16
NHEAD = 16
HD = 64
NSTATE = 16
EPS_LN = 1e-5

MAGIC = float(np.float32(1.5 * 2 ** 23))


ENGS = ("pe", "dve", "act", "pool", "sp")


def _key(x):
    if isinstance(x, tuple):
        return x
    if isinstance(x, str):
        return (x, "*")
    return (x.name, "*")


class Op:
    __slots__ = ("i", "eng", "fn", "deps", "dma", "tag", "sem", "val", "waits", "dkey")

    def __init__(self, i, eng, fn, deps, dma, tag):
        self.i, self.eng, self.fn, self.deps, self.dma, self.tag = i, eng, fn, deps, dma, tag
        self.sem = None
        self.val = None


class Prog:
    def __init__(self, name="prog"):
        self.name = name
        self.nc = bass.Bass("TRN2", target_bir_lowering=False)
        self.stack = ExitStack()
        self.ops = []
        self.writers = {}
        self.readers = {}
        self.last_dma = {}
        self.bar = []
        self.ext_inputs = []
        self.ext_outputs = []
        self._uid = 0
        self._consts = {}

    def const_col(self, val):
        val = float(val)
        t = self._consts.get(val)
        if t is None:
            t = self.stack.enter_context(self.nc.sbuf_tensor(f"constc_{len(self._consts)}", [128, 1], F32, side="right"))
            self.op("dve", lambda h: h.memset(t[:], val), writes=[t[:]], tag=f"const_{val}")
            self._consts[val] = t
        return t

    def dram(self, name, shape, dtype, kind="Internal"):
        if not hasattr(self, "_dram_cache"):
            self._dram_cache = {}
        if name in self._dram_cache:
            return self._dram_cache[name]
        h = self.nc.dram_tensor(name, list(shape), dtype, kind=kind)
        if kind == "ExternalInput":
            self.ext_inputs.append(name)
        elif kind == "ExternalOutput":
            self.ext_outputs.append(name)
        self._dram_cache[name] = h
        return h

    def sb(self, name, shape, dtype):
        return self.stack.enter_context(self.nc.sbuf_tensor(name, list(shape), dtype))

    def ps(self, name, shape, dtype=F32):
        return self.stack.enter_context(self.nc.psum_tensor(name, list(shape), dtype))

    def uid(self):
        self._uid += 1
        return self._uid

    def op(self, eng, fn, reads=(), writes=(), dma=False, tag=""):
        deps = set(self.bar)
        for r in reads:
            n, t = _key(r)
            for wt, wi in self.writers.get(n, {}).items():
                if t == "*" or wt == "*" or wt == t:
                    deps.add(wi)
        wkeys = [_key(w) for w in writes]
        for n, t in wkeys:
            for wt, wi in self.writers.get(n, {}).items():
                if t == "*" or wt == "*" or wt == t:
                    deps.add(wi)
            for rt, ris in self.readers.get(n, {}).items():
                if t == "*" or rt == "*" or rt == t:
                    deps.update(ris)
        i = len(self.ops)
        o = Op(i, eng, fn, deps, dma, tag)
        o.dkey = None
        self.ops.append(o)
        for r in reads:
            n, t = _key(r)
            self.readers.setdefault(n, {}).setdefault(t, []).append(i)
        for n, t in wkeys:
            w = self.writers.setdefault(n, {})
            if t == "*":
                w.clear()
            w[t] = i
            rd = self.readers.get(n)
            if rd is not None:
                if t == "*":
                    rd.clear()
                else:
                    rd.pop(t, None)
        return i

    def barrier(self):
        last = {}
        for o in self.ops:
            if not o.dma:
                last[o.eng] = o.i
        keep = {}
        for i in self.bar:
            if self.ops[i].dma:
                keep[self.ops[i].dkey] = i
        keep.update(self.last_dma)
        self.bar = list(last.values()) + list(keep.values())
        self.last_dma = {}

    def finalize(self):
        ops = self.ops
        for o in ops:
            if o.eng == "pe":
                o.deps = {d for d in o.deps if ops[d].eng != "pe"}
        needed = set()
        for o in ops:
            needed.update(o.deps)
        for o in ops:
            if o.dma:
                needed.add(o.i)
        dnames = []
        for o in ops:
            if o.dma:
                if o.dkey is None:
                    o.dkey = f"anon{o.i}"
                if o.dkey not in dnames:
                    dnames.append(o.dkey)
        ngroups = min(len(dnames), 90) if dnames else 0
        dmap = {nm: f"dma{idx % ngroups}" for idx, nm in enumerate(dnames)}
        counters = {}
        for o in ops:
            if o.i in needed:
                kind = dmap[o.dkey] if o.dma else o.eng
                step = 16 if o.dma else 1
                counters[kind] = counters.get(kind, 0) + step
                o.sem, o.val = kind, counters[kind]
        self.final_counts = dict(counters)
        for o in ops:
            w = {}
            for d in o.deps:
                od = ops[d]
                if od.sem is None:
                    continue
                if w.get(od.sem, 0) < od.val:
                    w[od.sem] = od.val
            o.waits = sorted(w.items())
        qs = {e: [o for o in ops if o.eng == e] for e in ENGS}
        pos = {e: 0 for e in ENGS}
        sem = {k: 0 for k in counters}
        total = len(ops)
        done = 0
        progress = True
        while done < total and progress:
            progress = False
            for e in ENGS:
                while pos[e] < len(qs[e]):
                    o = qs[e][pos[e]]
                    if all(sem.get(k, 0) >= v for k, v in o.waits):
                        if o.sem:
                            sem[o.sem] = o.val
                        pos[e] += 1
                        done += 1
                        progress = True
                    else:
                        break
        if done < total:
            stuck = [(e, qs[e][pos[e]].tag, qs[e][pos[e]].waits) for e in ENGS if pos[e] < len(qs[e])]
            raise RuntimeError(f"deadlock in schedule: {stuck[:5]} sem={sem}")

    def emit(self):
        self.finalize()
        nc = self.nc
        sems = {}
        for k in sorted(set(list(self.final_counts.keys()))):
            sems[k] = self.stack.enter_context(nc.semaphore("sem_" + k))
        block = self.stack.enter_context(nc.Block())
        handles = {"pe": None, "dve": None, "act": None, "pool": None, "sp": None}

        def emit_engine(eng_name, h):
            seen = {}
            for o in self.ops:
                if o.eng != eng_name:
                    continue
                for k, v in o.waits:
                    if seen.get(k, 0) < v:
                        h.wait_ge(sems[k], v)
                        seen[k] = v
                inst = o.fn(h)
                if o.sem is not None:
                    inst.then_inc(sems[o.sem], 16 if o.dma else 1)

        engs_with_ops = {o.eng for o in self.ops}

        if "pe" in engs_with_ops:
            @block.tensor
            def _(h):
                emit_engine("pe", h)
        if "dve" in engs_with_ops:
            @block.vector
            def _(h):
                emit_engine("dve", h)
        if "act" in engs_with_ops:
            @block.scalar
            def _(h):
                emit_engine("act", h)
        if "pool" in engs_with_ops:
            @block.gpsimd
            def _(h):
                emit_engine("pool", h)

        @block.sync
        def _(h):
            emit_engine("sp", h)
            for k, v in sorted(self.final_counts.items()):
                h.wait_ge(sems[k], v)

    def run(self, host_arrays, n_cores=NCORES, **kw):
        self.emit()
        in_maps = []
        for c in range(n_cores):
            in_maps.append({n: host_arrays[c][n] for n in self.ext_inputs})
        res = run_bass_kernel_spmd(self.nc, in_maps, core_ids=list(range(n_cores)), **kw)
        self.stack.close()
        return res



def dma(S, q, out_ap, in_ap, reads=(), writes=(), tag="dma"):
    i = S.op(q, lambda h: h.dma_start(out=out_ap, in_=in_ap),
             reads=list(reads) + [in_ap], writes=list(writes) + [out_ap], dma=True, tag=tag)
    S.ops[i].dkey = out_ap.name
    S.last_dma[out_ap.name] = i
    return i


def mm(S, out, lhsT, rhs, start, stop, reads=(), writes=(), tag="mm"):
    return S.op("pe", lambda h: h.matmul(out, lhsT, rhs, start=start, stop=stop),
                reads=list(reads) + [lhsT, rhs], writes=list(writes) + [out], tag=tag)


def act(S, out, in_, func, bias=0.0, scale=1.0, accum_out=None, reads=(), writes=(), tag="act"):
    np_ = in_.shape[0]
    if isinstance(bias, (int, float)) and float(bias) != 0.0:
        bias = S.const_col(bias)[0:np_, 0:1]
    if isinstance(scale, (int, float)) and float(scale) != 1.0:
        scale = S.const_col(scale)[0:np_, 0:1]
    ex_r = [x for x in (bias, scale) if isinstance(x, bass.AP)]
    ex_w = [accum_out] if accum_out is not None else []
    return S.op("act", lambda h: h.activation(out, in_, func, bias=bias, scale=scale, accum_out=accum_out),
                reads=list(reads) + [in_] + ex_r, writes=list(writes) + [out] + ex_w, tag=tag)


def v_op(S, fname, out, *ins, reads=(), writes=(), tag=None, **kw):
    aps = [x for x in ins if isinstance(x, bass.AP)]
    return S.op("dve", lambda h: getattr(h, fname)(out, *ins, **kw),
                reads=list(reads) + aps, writes=list(writes) + [out], tag=tag or fname)



def col_layout(v):
    n = v.shape[0] // 128
    return np.ascontiguousarray(v.reshape(n, 128).T).astype(np.float32)


def prep_host(inputs):
    g = {}
    x = np.asarray(inputs["L0_attn_a_x"])

    g["c_ident_bf"] = np.eye(128, dtype=BF)
    g["c_ident_f32"] = np.eye(128, dtype=np.float32)
    g["c_ones_r_bf"] = np.ones((1, 512), dtype=BF)
    g["c_ones_c_f32"] = np.ones((1, 128), dtype=np.float32)
    g["c_ones_col_bf"] = np.ones((128, 1), dtype=BF)
    g["c_ones_col_f32"] = np.ones((128, 1), dtype=np.float32)

    g["m_w_in"] = np.ascontiguousarray(inputs["L0_attn_a_w_in"])
    g["m_norm_scale"] = col_layout(inputs["L0_attn_a_norm_scale"])
    g["m_conv_w"] = np.ascontiguousarray(
        inputs["L0_attn_a_conv_w"][:, 0, :].T.reshape(NDI, 128, 4).transpose(1, 0, 2)).astype(np.float32)
    g["m_conv_b"] = col_layout(inputs["L0_attn_a_conv_b"])
    g["m_A_log"] = np.ascontiguousarray(
        inputs["L0_attn_a_A_log"].reshape(NDI, 128, NSTATE).transpose(1, 0, 2)).astype(np.float32)
    g["m_D"] = col_layout(inputs["L0_attn_a_D"])
    wx = np.asarray(inputs["L0_attn_a_w_xproj"])
    g["m_wx_delta"] = np.ascontiguousarray(wx[:, :64])
    g["m_wx_B"] = np.ascontiguousarray(wx[:, 64:80])
    g["m_wx_C"] = np.ascontiguousarray(wx[:, 80:96])
    g["m_w_dt"] = np.ascontiguousarray(inputs["L0_attn_a_w_dt"])
    g["m_b_dt"] = col_layout(inputs["L0_attn_a_b_dt"])
    g["m_w_out"] = np.ascontiguousarray(inputs["L0_attn_a_w_out"])

    g["f_norm_w"] = col_layout(inputs["L0_ffn_norm_w"])
    g["f_w1T"] = np.ascontiguousarray(inputs["L0_ffn_w1"].T)
    g["f_w3T"] = np.ascontiguousarray(inputs["L0_ffn_w3"].T)
    g["f_w2T"] = np.ascontiguousarray(inputs["L0_ffn_w2"].T)
    g["f_gateT"] = np.ascontiguousarray(inputs["L0_ffn_gate_w"].T)
    a1 = np.asarray(inputs["L0_ffn_a1"])
    a3 = np.asarray(inputs["L0_ffn_a3"])
    g["f_a1p"] = np.stack([np.concatenate([a1[4 * gi + j].T for j in range(4)], axis=1) for gi in range(2)])
    g["f_a3p"] = np.stack([np.concatenate([a3[4 * gi + j].T for j in range(4)], axis=1) for gi in range(2)])
    b1 = np.asarray(inputs["L0_ffn_b1"])
    b3 = np.asarray(inputs["L0_ffn_b3"])

    def padrows(bm, e):
        z = np.zeros((64, FF), dtype=np.float32)
        z[16 * (e % 4):16 * (e % 4) + 16, :] = bm[e].T
        return z
    g["f_b1p"] = np.ascontiguousarray(np.stack([padrows(b1, e) for e in range(NEXP)]))
    g["f_b3p"] = np.ascontiguousarray(np.stack([padrows(b3, e) for e in range(NEXP)]))
    g["f_b2T3"] = np.ascontiguousarray(np.asarray(inputs["L0_ffn_b2"]).transpose(2, 0, 1).reshape(16, NEXP * D))
    g["f_a2Tr"] = np.ascontiguousarray(np.asarray(inputs["L0_ffn_a2"]).transpose(2, 0, 1))
    g["f_b2T"] = np.ascontiguousarray(np.asarray(inputs["L0_ffn_b2"]).transpose(0, 2, 1))

    g["nb_Wq"] = np.ascontiguousarray(inputs["L0_attn_b_Wq"])
    g["nb_Wk"] = np.ascontiguousarray(inputs["L0_attn_b_Wk"])
    g["nb_Wv"] = np.ascontiguousarray(inputs["L0_attn_b_Wv"])
    g["nb_bq"] = np.asarray(inputs["L0_attn_b_bq"]).reshape(1, D).astype(np.float32)
    g["nb_bk"] = np.asarray(inputs["L0_attn_b_bk"]).reshape(1, D).astype(np.float32)
    g["nb_bv"] = np.asarray(inputs["L0_attn_b_bv"]).reshape(1, D).astype(np.float32)
    cw = np.asarray(inputs["L0_attn_b_conv_w"])[:, 0, :, 0]
    pat = np.zeros((NHEAD, 3, 128, 128), dtype=np.float32)
    for h in range(NHEAD):
        for j in range(3):
            rel = (j - 1) * 128
            for r in range(128):
                for c in range(128):
                    idx = rel + r - c + 32
                    if 0 <= idx <= 64:
                        pat[h, j, r, c] = cw[h, idx]
    g["nb_pat"] = pat

    g["ta_projT"] = np.ascontiguousarray(inputs["L1_attn_a_proj_w"].T)
    g["ta_fc1T"] = np.ascontiguousarray(inputs["L1_attn_a_fc1_w"].T)
    g["ta_fc2T"] = np.ascontiguousarray(inputs["L1_attn_a_fc2_w"].T)
    g["ta_g1"] = col_layout(inputs["L1_attn_a_ln1_g"])
    g["ta_b1c"] = col_layout(inputs["L1_attn_a_ln1_b"])
    g["ta_projb"] = np.asarray(inputs["L1_attn_a_proj_b"]).reshape(1, D).astype(np.float32)
    g["ta_g2"] = col_layout(inputs["L1_attn_a_ln2_g"])
    g["ta_b2c"] = col_layout(inputs["L1_attn_a_ln2_b"])
    g["ta_fc1b"] = np.asarray(inputs["L1_attn_a_fc1_b"]).reshape(1, FF).astype(np.float32)
    g["ta_lnhg"] = col_layout(inputs["L1_attn_a_lnh_g"])
    g["ta_lnhbc"] = col_layout(inputs["L1_attn_a_lnh_b"])
    g["ta_fc2b"] = np.asarray(inputs["L1_attn_a_fc2_b"]).reshape(1, D).astype(np.float32)
    g["ta_gam1"] = col_layout(inputs["L1_attn_a_gamma1"])
    g["ta_gam2"] = col_layout(inputs["L1_attn_a_gamma2"])

    for wn, k in (("tb_wqT", "L1_attn_b_wq"), ("tb_wkT", "L1_attn_b_wk"),
                  ("tb_wvT", "L1_attn_b_wv"), ("tb_woT", "L1_attn_b_wo")):
        g[wn] = np.ascontiguousarray(inputs[k].T)
    g["tb_w1T"] = np.ascontiguousarray(inputs["L1_attn_b_w1"].T)
    g["tb_w2T"] = np.ascontiguousarray(inputs["L1_attn_b_w2"].T)
    g["tb_bq"] = np.asarray(inputs["L1_attn_b_bq"]).reshape(1, D).astype(np.float32)
    g["tb_bk"] = np.asarray(inputs["L1_attn_b_bk"]).reshape(1, D).astype(np.float32)
    g["tb_bv"] = np.asarray(inputs["L1_attn_b_bv"]).reshape(1, D).astype(np.float32)
    g["tb_bo"] = np.asarray(inputs["L1_attn_b_bo"]).reshape(1, D).astype(np.float32)
    g["tb_g1"] = col_layout(inputs["L1_attn_b_g1"])
    g["tb_b1c"] = col_layout(inputs["L1_attn_b_b1"])
    g["tb_g2"] = col_layout(inputs["L1_attn_b_g2"])
    g["tb_b2c"] = col_layout(inputs["L1_attn_b_b2"])
    g["tb_bf1"] = np.asarray(inputs["L1_attn_b_bf1"]).reshape(1, FF).astype(np.float32)
    g["tb_bf2"] = np.asarray(inputs["L1_attn_b_bf2"]).reshape(1, D).astype(np.float32)
    hm = np.zeros((ND, 128, NHEAD), dtype=BF)
    for j in range(ND):
        for p in range(128):
            hm[j, p, 2 * j + p // 64] = 1
    g["c_headmask"] = hm
    sel = np.zeros((NHEAD, NHEAD, 64), dtype=BF)
    for h in range(NHEAD):
        sel[h, h, :] = 1
    g["c_sel64"] = sel
    sel2 = np.zeros((NHEAD, NHEAD, 128), dtype=BF)
    for h in range(NHEAD):
        sel2[h, h, :] = 1
    g["c_sel128"] = sel2
    g["c_ones_c_bf"] = np.ones((1, 128), dtype=BF)

    g["q_w"] = np.ascontiguousarray(inputs["L1_ffn_weight"])
    g["q_bias"] = np.asarray(inputs["L1_ffn_bias"]).reshape(1, D).astype(np.float32)

    per_core = []
    for c in range(NCORES):
        d = dict(g)
        d["x_in"] = np.ascontiguousarray(x[c].T).astype(np.float32)
        per_core.append(d)
    return per_core



class Common:

    def __init__(self, S):
        self.ident_bf = S.sb("c_ident", [128, 128], BF16)
        self.ident_f32 = S.sb("c_identf", [128, 128], F32)
        self.ones_r_bf = S.sb("c_ones_r", [1, 512], BF16)
        self.ones_c_f32 = S.sb("c_ones_cf", [1, 128], F32)
        self.ones_col_bf = S.sb("c_ones_col", [128, 1], BF16)
        self.ones_col_f32 = S.sb("c_ones_colf", [128, 1], F32)
        d_id = S.dram("c_ident_bf", [128, 128], BF16, kind="ExternalInput")
        d_idf = S.dram("c_ident_f32", [128, 128], F32, kind="ExternalInput")
        d_or = S.dram("c_ones_r_bf", [1, 512], BF16, kind="ExternalInput")
        d_oc = S.dram("c_ones_c_f32", [1, 128], F32, kind="ExternalInput")
        d_ocol = S.dram("c_ones_col_bf", [128, 1], BF16, kind="ExternalInput")
        d_ocolf = S.dram("c_ones_col_f32", [128, 1], F32, kind="ExternalInput")
        dma(S, "sp", self.ident_bf[:], d_id.ap(), tag="ld_ident")
        dma(S, "sp", self.ident_f32[:], d_idf.ap(), tag="ld_identf")
        dma(S, "sp", self.ones_r_bf[:], d_or.ap(), tag="ld_ones_r")
        dma(S, "sp", self.ones_c_f32[:], d_oc.ap(), tag="ld_ones_c")
        dma(S, "sp", self.ones_col_bf[:], d_ocol.ap(), tag="ld_ones_col")
        dma(S, "sp", self.ones_col_f32[:], d_ocolf.ap(), tag="ld_ones_colf")


def fm_dram(h, t0, t1, nd=ND):
    return h.ap()[:, t0:t1].rearrange("(a p) t -> p a t", p=128)


def ln_stats(S, C, xb_bf, xsq_bf, nfeat_tiles, Tc, ps_s, ps_ss, name):
    for j in range(nfeat_tiles):
        mm(S, ps_s[:], C.ones_col_bf[:], xb_bf[:, j, :], start=(j == 0), stop=(j == nfeat_tiles - 1), tag=f"{name}_s{j}")
        mm(S, ps_ss[:], C.ones_col_bf[:], xsq_bf[:, j, :], start=(j == 0), stop=(j == nfeat_tiles - 1), tag=f"{name}_ss{j}")


def ln_rows(S, ps_s, ps_ss, rows, nfeat, eps, name, eps2=None):
    inv = 1.0 / nfeat
    v_op(S, "tensor_scalar", rows["m"][:], ps_s[:], inv, None, ALU.mult, tag=f"{name}_m")
    v_op(S, "tensor_mul", rows["t1"][:], rows["m"][:], rows["m"][:], tag=f"{name}_msq")
    v_op(S, "scalar_tensor_tensor", rows["var"][:], ps_ss[:], inv, rows["t1"][:], ALU.mult, ALU.subtract, tag=f"{name}_var")
    act(S, rows["t1"][:], rows["var"][:], AF.Ln, bias=float(eps), tag=f"{name}_lnv")
    act(S, rows["rs"][:], rows["t1"][:], AF.Exp, scale=-0.5, tag=f"{name}_rs")
    if eps2 is not None:
        v_op(S, "tensor_mul", rows["t1"][:], rows["rs"][:], rows["rs"][:], tag=f"{name}_u")
        v_op(S, "tensor_mul", rows["t2"][:], rows["t1"][:], rows["var"][:], tag=f"{name}_u2")
        act(S, rows["t1"][:], rows["t2"][:], AF.Ln, bias=float(eps2), tag=f"{name}_lnv2")
        act(S, rows["t2"][:], rows["t1"][:], AF.Exp, scale=-0.5, tag=f"{name}_r2")
        v_op(S, "tensor_mul", rows["rs2"][:], rows["rs"][:], rows["t2"][:], tag=f"{name}_rs2")


def bcast_row(S, C, ps_out, row_ap, tag):
    mm(S, ps_out, C.ones_c_f32[:], row_ap, start=True, stop=True, tag=tag)


def mk_ln_rows(S, Tc, name, eps2=False, alloc=None):
    keys = ["m", "var", "rs", "t1", "t2"] + (["rs2"] if eps2 else [])
    al = alloc if alloc is not None else (lambda n, s, d=F32: S.sb(n, s, d))
    return {k: al(f"{name}_{k}_{S.uid()}", [1, Tc]) for k in keys}


def ln_normalize(S, C, x_sb, nd, Tc, mb_ps, sb_ps, out_bf, gain=None, xc_scratch=None, tag="ln"):
    for j in range(nd):
        xc = xc_scratch[:, j, :] if xc_scratch is not None else None
        v_op(S, "tensor_sub", xc, x_sb[:, j, :], mb_ps, tag=f"{tag}_sub{j}")
        if gain is not None:
            v_op(S, "scalar_tensor_tensor", out_bf[:, j, :], xc, gain[:, j:j + 1], sb_ps, ALU.mult, ALU.mult, tag=f"{tag}_sc{j}")
        else:
            v_op(S, "tensor_mul", out_bf[:, j, :], xc, sb_ps, tag=f"{tag}_mul{j}")


def load_cast_w(S, dram_h, sb_t, rows, cols, tag, rowsl=None, colsl=None):
    q = "pool"
    bf = getattr(S, "wbf", {}).get(dram_h.name)
    if bf is not None:
        dram_h = bf
    src = dram_h.ap()
    if rowsl is not None:
        src = src[rowsl[0]:rowsl[1], :]
        rows = rowsl[1] - rowsl[0]
    if colsl is not None:
        src = src[:, colsl[0]:colsl[1]]
        cols = colsl[1] - colsl[0]
    if rows > 128:
        src = src.rearrange("(a p) n -> p a n", p=128)
    dma(S, q, sb_t[:], src, tag=tag)


def bias_row_mm(S, C, ps, bias_col_bf, w_sb, nk, n0, n1, tag):
    for j in range(nk):
        mm(S, ps[:, 0:n1 - n0], bias_col_bf[:, j:j + 1], w_sb[:, j, n0:n1], start=(j == 0), stop=(j == nk - 1), tag=f"{tag}{j}")



from contextlib import contextmanager


@contextmanager
def pscope(S):
    st = ExitStack()
    try:
        yield st
    finally:
        S.barrier()
        st.close()


def emit_l1_attn_a(S, C, x_in_h, x_out_h):
    nc = S.nc
    d_projT = S.dram("ta_projT", [D, D], F32, kind="ExternalInput")
    d_fc1T = S.dram("ta_fc1T", [D, FF], F32, kind="ExternalInput")
    d_fc2T = S.dram("ta_fc2T", [FF, D], F32, kind="ExternalInput")
    d_g1 = S.dram("ta_g1", [128, ND], F32, kind="ExternalInput")
    d_b1c = S.dram("ta_b1c", [128, ND], F32, kind="ExternalInput")
    d_projb = S.dram("ta_projb", [1, D], F32, kind="ExternalInput")
    d_g2 = S.dram("ta_g2", [128, ND], F32, kind="ExternalInput")
    d_b2c = S.dram("ta_b2c", [128, ND], F32, kind="ExternalInput")
    d_fc1b = S.dram("ta_fc1b", [1, FF], F32, kind="ExternalInput")
    d_lnhg = S.dram("ta_lnhg", [128, NF], F32, kind="ExternalInput")
    d_lnhbc = S.dram("ta_lnhbc", [128, NF], F32, kind="ExternalInput")
    d_fc2b = S.dram("ta_fc2b", [1, D], F32, kind="ExternalInput")
    d_gam1 = S.dram("ta_gam1", [128, ND], F32, kind="ExternalInput")
    d_gam2 = S.dram("ta_gam2", [128, ND], F32, kind="ExternalInput")
    u0 = S.uid()
    kind_dbg = "ExternalOutput" if getattr(S, "debug_spill", False) else "Internal"
    z2_d = S.dram(f"z2_d{u0}", [D, T], BF16, kind=kind_dbg)
    h_d = S.dram(f"h_d{u0}", [FF, T], BF16, kind=kind_dbg)
    x_mid = S.dram(f"xmid_d{u0}", [D, T], F32, kind=kind_dbg)

    with pscope(S) as ph:
        sbl = lambda n, s, d=F32: ph.enter_context(nc.sbuf_tensor(f"{n}_{S.uid()}", list(s), d))
        g1 = sbl("g1", [128, ND]); dma(S, "sp", g1[:], d_g1.ap(), tag="ld_g1")
        g2 = sbl("g2", [128, ND]); dma(S, "sp", g2[:], d_g2.ap(), tag="ld_g2")
        lnhg = sbl("lnhg", [128, NF]); dma(S, "sp", lnhg[:], d_lnhg.ap(), tag="ld_lnhg")
        gam1 = sbl("gam1", [128, ND]); dma(S, "sp", gam1[:], d_gam1.ap(), tag="ld_gam1")
        gam2 = sbl("gam2", [128, ND]); dma(S, "sp", gam2[:], d_gam2.ap(), tag="ld_gam2")
        bias1 = sbl("bias1", [1, D], BF16)
        bias2 = sbl("bias2", [1, FF], BF16)
        bias3 = sbl("bias3", [1, D], BF16)
        lnh_s = sbl("lnh_s", [1, T])
        lnh_ss = sbl("lnh_ss", [1, T])

        with pscope(S) as pp:
            sb2 = lambda n, s, d=F32: pp.enter_context(nc.sbuf_tensor(f"{n}_{S.uid()}", list(s), d))
            ps2 = lambda n, s: pp.enter_context(nc.psum_tensor(f"{n}_{S.uid()}", list(s), F32))
            projb = sb2("projb", [1, D]); dma(S, "sp", projb[:], d_projb.ap(), tag="ld_projb")
            fc1b = sb2("fc1b", [1, FF]); dma(S, "sp", fc1b[:], d_fc1b.ap(), tag="ld_fc1b")
            fc2b = sb2("fc2b", [1, D]); dma(S, "sp", fc2b[:], d_fc2b.ap(), tag="ld_fc2b")
            b1c = sb2("b1c", [128, ND], BF16); dma(S, "pool", b1c[:], d_b1c.ap(), tag="ld_b1c")
            b2c = sb2("b2c", [128, ND], BF16); dma(S, "pool", b2c[:], d_b2c.ap(), tag="ld_b2c")
            lnhbc = sb2("lnhbc", [128, NF], BF16); dma(S, "pool", lnhbc[:], d_lnhbc.ap(), tag="ld_lnhbc")
            ps_b = ps2("ps_b", [1, 512])
            w512 = sb2("w512", [128, ND, 512], BF16)
            for n0 in range(0, D, 512):
                load_cast_w(S, d_projT, w512, D, 512, f"ld_pjs{n0}", colsl=(n0, n0 + 512))
                bias_row_mm(S, C, ps_b, b1c, w512, ND, 0, 512, f"b1r{n0}")
                v_op(S, "tensor_tensor", bias1[:, n0:n0 + 512], ps_b[:, :], projb[:, n0:n0 + 512], ALU.add, tag=f"b1e{n0}")
            for n0 in range(0, FF, 512):
                load_cast_w(S, d_fc1T, w512, D, 512, f"ld_fc1s{n0}", colsl=(n0, n0 + 512))
                bias_row_mm(S, C, ps_b, b2c, w512, ND, 0, 512, f"b2r{n0}")
                v_op(S, "tensor_tensor", bias2[:, n0:n0 + 512], ps_b[:, :], fc1b[:, n0:n0 + 512], ALU.add, tag=f"b2e{n0}")
            wf512 = sb2("wf512", [128, NF, 512], BF16)
            for n0 in range(0, D, 512):
                load_cast_w(S, d_fc2T, wf512, FF, 512, f"ld_fc2s{n0}", colsl=(n0, n0 + 512))
                bias_row_mm(S, C, ps_b, lnhbc, wf512, NF, 0, 512, f"b3r{n0}")
                v_op(S, "tensor_tensor", bias3[:, n0:n0 + 512], ps_b[:, :], fc2b[:, n0:n0 + 512], ALU.add, tag=f"b3e{n0}")

        Tc = 512
        nch = T // Tc
        with pscope(S) as pp:
            sb2 = lambda n, s, d=F32: pp.enter_context(nc.sbuf_tensor(f"{n}_{S.uid()}", list(s), d))
            ps2 = lambda n, s: pp.enter_context(nc.psum_tensor(f"{n}_{S.uid()}", list(s), F32))
            projT = sb2("projT", [128, ND, D], BF16)
            load_cast_w(S, d_projT, projT, D, D, "ld_projT")
            x_sb = sb2("xA", [128, ND, Tc])
            xb = sb2("xAb", [128, ND, Tc], BF16)
            xq = sb2("xAq", [128, ND, Tc], BF16)
            xc = sb2("xAc", [128, ND, Tc])
            y_bf = sb2("yA", [128, ND, Tc], BF16)
            z2buf = sb2("z2buf", [128, ND, Tc], BF16)
            ps_s, ps_ss = ps2("psA_s", [1, Tc]), ps2("psA_ss", [1, Tc])
            ps_bc = ps2("psA_bc", [128, 2 * Tc])
            ps_m, ps_r = ps_bc[:, 0:Tc], ps_bc[:, Tc:2 * Tc]
            ps_o = [ps2("psA_o0", [128, Tc]), ps2("psA_o1", [128, Tc])]
            rowsA = mk_ln_rows(S, Tc, "rA", eps2=True, alloc=sb2)
            rows2 = mk_ln_rows(S, Tc, "r2A", alloc=sb2)
            for ci in range(nch):
                t0 = ci * Tc
                dma(S, "sp", x_sb[:], fm_dram(x_in_h, t0, t0 + Tc), tag=f"A_ldx{ci}")
                for j in range(ND):
                    v_op(S, "tensor_copy", xb[:, j, :], x_sb[:, j, :], tag=f"A_xb{ci}_{j}")
                    act(S, xq[:, j, :], x_sb[:, j, :], AF.Square, tag=f"A_xq{ci}_{j}")
                ln_stats(S, C, xb, xq, ND, Tc, ps_s, ps_ss, f"A_st{ci}")
                ln_rows(S, ps_s, ps_ss, rowsA, D, EPS_LN, f"A_r{ci}", eps2=EPS_LN)
                bcast_row(S, C, ps_m, rowsA["m"][:], f"A_bm{ci}")
                bcast_row(S, C, ps_r, rowsA["rs2"][:], f"A_br{ci}")
                ln_normalize(S, C, x_sb, ND, Tc, ps_m, ps_r, y_bf, gain=g1, xc_scratch=xc, tag=f"A_ln{ci}")
                bcast_row(S, C, ps_m, rowsA["rs"][:], f"A_bro{ci}")
                for j in range(ND):
                    v_op(S, "tensor_mul", x_sb[:, j, :], xc[:, j, :], ps_m, tag=f"A_z{ci}_{j}")
                for mi in range(ND):
                    po = ps_o[mi % 2]
                    for j in range(ND):
                        mm(S, po[:], projT[:, j, 128 * mi:128 * (mi + 1)], y_bf[:, j, :], start=(j == 0), stop=False, tag=f"A_pj{ci}_{mi}_{j}")
                    mm(S, po[:], bias1[:, 128 * mi:128 * (mi + 1)], C.ones_r_bf[:, :Tc], start=False, stop=True, tag=f"A_pb{ci}_{mi}")
                    v_op(S, "scalar_tensor_tensor", x_sb[:, mi, :], po[:], gam1[:, mi:mi + 1], x_sb[:, mi, :], ALU.mult, ALU.add, tag=f"A_res{ci}_{mi}")
                dma(S, "sp", fm_dram(x_mid, t0, t0 + Tc), x_sb[:], tag=f"A_stx{ci}")
                for j in range(ND):
                    v_op(S, "tensor_copy", xb[:, j, :], x_sb[:, j, :], tag=f"A_x1b{ci}_{j}")
                    act(S, xq[:, j, :], x_sb[:, j, :], AF.Square, tag=f"A_x1q{ci}_{j}")
                ln_stats(S, C, xb, xq, ND, Tc, ps_s, ps_ss, f"A2_st{ci}")
                ln_rows(S, ps_s, ps_ss, rows2, D, EPS_LN, f"A2_r{ci}")
                bcast_row(S, C, ps_m, rows2["m"][:], f"A2_bm{ci}")
                bcast_row(S, C, ps_r, rows2["rs"][:], f"A2_br{ci}")
                for j in range(ND):
                    v_op(S, "tensor_sub", xc[:, j, :], x_sb[:, j, :], ps_m, tag=f"A2_sub{ci}_{j}")
                    v_op(S, "scalar_tensor_tensor", z2buf[:, j, :], xc[:, j, :], g2[:, j:j + 1], ps_r, ALU.mult, ALU.mult, tag=f"A2_sc{ci}_{j}")
                dma(S, "sp", fm_dram(z2_d, t0, t0 + Tc), z2buf[:], tag=f"A_stz{ci}")

        with pscope(S) as pp:
            sb2 = lambda n, s, d=F32: pp.enter_context(nc.sbuf_tensor(f"{n}_{S.uid()}", list(s), d))
            ps2 = lambda n, s: pp.enter_context(nc.psum_tensor(f"{n}_{S.uid()}", list(s), F32))
            z2c = sb2("z2c", [128, ND, Tc], BF16)
            hq = sb2("hq", [128, Tc], BF16)
            hst = [sb2("hst0", [128, Tc], BF16), sb2("hst1", [128, Tc], BF16)]
            wB = [sb2("wB0", [128, ND, 128], BF16), sb2("wB1", [128, ND, 128], BF16)]
            ps_h = [ps2("psB_h0", [128, Tc]), ps2("psB_h1", [128, Tc])]
            ps_s, ps_ss = ps2("psB_s", [1, Tc]), ps2("psB_ss", [1, Tc])
            for ci in range(nch):
                t0 = ci * Tc
                dma(S, "sp", z2c[:], fm_dram(z2_d, t0, t0 + Tc), tag=f"B_ldz{ci}")
                for mi in range(NF):
                    wb = wB[mi % 2]
                    load_cast_w(S, d_fc1T, wb, D, 128, f"B_w{ci}_{mi}", colsl=(128 * mi, 128 * (mi + 1)))
                    ph_ = ps_h[mi % 2]
                    hs = hst[mi % 2]
                    for j in range(ND):
                        mm(S, ph_[:], wb[:, j, :], z2c[:, j, :], start=(j == 0), stop=False, tag=f"B_mm{ci}_{mi}_{j}")
                    mm(S, ph_[:], bias2[:, 128 * mi:128 * (mi + 1)], C.ones_r_bf[:, :Tc], start=False, stop=True, tag=f"B_bb{ci}_{mi}")
                    act(S, hs[:], ph_[:], AF.Gelu, tag=f"B_gelu{ci}_{mi}")
                    dma(S, "sp", h_d.ap()[128 * mi:128 * (mi + 1), t0:t0 + Tc], hs[:], tag=f"B_sth{ci}_{mi}")
                    act(S, hq[:], hs[:], AF.Square, tag=f"B_hq{ci}_{mi}")
                    mm(S, ps_s[:], C.ones_col_bf[:], hs[:], start=(mi == 0), stop=(mi == NF - 1), tag=f"B_s{ci}_{mi}")
                    mm(S, ps_ss[:], C.ones_col_bf[:], hq[:], start=(mi == 0), stop=(mi == NF - 1), tag=f"B_ss{ci}_{mi}")
                v_op(S, "tensor_copy", lnh_s[:, t0:t0 + Tc], ps_s[:], tag=f"B_cs{ci}")
                v_op(S, "tensor_copy", lnh_ss[:, t0:t0 + Tc], ps_ss[:], tag=f"B_css{ci}")

        with pscope(S) as pp:
            sb2 = lambda n, s, d=F32: pp.enter_context(nc.sbuf_tensor(f"{n}_{S.uid()}", list(s), d))
            ps2 = lambda n, s: pp.enter_context(nc.psum_tensor(f"{n}_{S.uid()}", list(s), F32))
            xC = sb2("xC", [128, ND, Tc])
            x0 = sb2("x0", [128, ND, Tc])
            hl = [sb2("hl0", [128, Tc], BF16), sb2("hl1", [128, Tc], BF16)]
            ht_all = sb2("ht_all", [128, NF, Tc], BF16)
            htf = sb2("htf", [128, Tc])
            mb_sb = sb2("mb_sbC", [128, Tc])
            rb_sb = sb2("rb_sbC", [128, Tc])
            wC = [sb2("wC0", [128, 512], BF16), sb2("wC1", [128, 512], BF16)]
            ps_big = ps2("psC_big", [128, 4, 512])
            ps_bc = ps2("psC_bc", [128, 512])
            rowsC = mk_ln_rows(S, Tc, "rC", alloc=sb2)
            for ci in range(nch):
                t0 = ci * Tc
                ln_rows(S, lnh_s[:, t0:t0 + Tc], lnh_ss[:, t0:t0 + Tc], rowsC, FF, EPS_LN, f"C_r{ci}")
                bcast_row(S, C, ps_bc[:], rowsC["m"][:], f"C_bm{ci}")
                v_op(S, "tensor_copy", mb_sb[:], ps_bc[:], tag=f"C_cm{ci}")
                bcast_row(S, C, ps_bc[:], rowsC["rs"][:], f"C_br{ci}")
                v_op(S, "tensor_copy", rb_sb[:], ps_bc[:], tag=f"C_cr{ci}")
                dma(S, "sp", xC[:], fm_dram(x_mid, t0, t0 + Tc), tag=f"C_ldx{ci}")
                dma(S, "sp", x0[:], fm_dram(x_in_h, t0, t0 + Tc), tag=f"C_ldx0{ci}")
                for mi in range(NF):
                    hli = hl[mi % 2]
                    dma(S, "sp", hli[:], h_d.ap()[128 * mi:128 * (mi + 1), t0:t0 + Tc], tag=f"C_ldh{ci}_{mi}")
                    v_op(S, "tensor_sub", htf[:], hli[:], mb_sb[:], tag=f"C_sub{ci}_{mi}")
                    v_op(S, "scalar_tensor_tensor", ht_all[:, mi, :], htf[:], lnhg[:, mi:mi + 1], rb_sb[:], ALU.mult, ALU.mult, tag=f"C_sc{ci}_{mi}")
                for half in range(2):
                    for mi in range(NF):
                        wc = wC[mi % 2]
                        load_cast_w(S, d_fc2T, wc, 128, 512, f"C_w{ci}_{half}_{mi}", rowsl=(128 * mi, 128 * (mi + 1)), colsl=(512 * half, 512 * (half + 1)))
                        for dq in range(4):
                            mm(S, ps_big[:, dq, :], wc[:, 128 * dq:128 * (dq + 1)], ht_all[:, mi, :], start=(mi == 0), stop=False, tag=f"C_mm{ci}_{half}_{mi}_{dq}")
                    for dq in range(4):
                        di = 4 * half + dq
                        mm(S, ps_big[:, dq, :], bias3[:, 128 * di:128 * (di + 1)], C.ones_r_bf[:, :Tc], start=False, stop=True, tag=f"C_bb{ci}_{half}_{dq}")
                        v_op(S, "scalar_tensor_tensor", xC[:, di, :], ps_big[:, dq, :], gam2[:, di:di + 1], xC[:, di, :], ALU.mult, ALU.add, tag=f"C_res{ci}_{half}_{dq}")
                        v_op(S, "tensor_add", xC[:, di, :], xC[:, di, :], x0[:, di, :], tag=f"C_addx{ci}_{half}_{dq}")
                dma(S, "sp", fm_dram(x_out_h, t0, t0 + Tc), xC[:], tag=f"C_stx{ci}")


def run_l1_attn_a(per_core, n_cores=NCORES, debug_spill=False, **kw):
    S = Prog("l1a")
    S.debug_spill = debug_spill
    x_in = S.dram("x_in", [D, T], F32, kind="ExternalInput")
    x_out = S.dram("x_out", [D, T], F32, kind="ExternalOutput")
    C = Common(S)
    emit_l1_attn_a(S, C, x_in, x_out)
    res = S.run(per_core, n_cores, **kw)
    return [r["x_out"] for r in res.results], res


def tr_op(S, out_ps, in_ap, ident, tag):
    return S.op("pe", lambda h: h.transpose(out_ps, in_ap, ident),
                reads=[in_ap, ident], writes=[out_ps], tag=tag)


def quantize_rows(S, sc, src_ap, out_bf_ap, Tn, tag):
    nb = Tn // 32
    bview = lambda ap: ap.rearrange("p (a b) -> p a b", b=32)
    S.op("dve", lambda h: h.tensor_reduce(sc["amax"][:], bview(src_ap), AX.X, ALU.max),
         reads=[src_ap], writes=[sc["amax"][:]], tag=f"{tag}_amax")
    S.op("dve", lambda h: h.tensor_reduce(sc["amin"][:], bview(src_ap), AX.X, ALU.min),
         reads=[src_ap], writes=[sc["amin"][:]], tag=f"{tag}_amin")
    v_op(S, "scalar_tensor_tensor", sc["aab"][:], sc["amin"][:], -1.0, sc["amax"][:], ALU.mult, ALU.max, tag=f"{tag}_aabs")
    v_op(S, "tensor_scalar", sc["s"][:], sc["aab"][:], 1.0 / 7.5, 1e-30, ALU.mult, ALU.max, tag=f"{tag}_s")
    v_op(S, "reciprocal", sc["si"][:], sc["s"][:], tag=f"{tag}_si")
    sx = bview(sc["sexp"][:])
    v_op(S, "tensor_copy", sx[:, :, 0:1], sc["si"][:].rearrange("p (a o) -> p a o", o=1), tag=f"{tag}_e0")
    w = 1
    while w < 32:
        v_op(S, "tensor_copy", sx[:, :, w:2 * w], sx[:, :, 0:w], tag=f"{tag}_e{w}")
        w *= 2
    v_op(S, "tensor_mul", sc["y"][:], src_ap, sc["sexp"][:], tag=f"{tag}_y")
    v_op(S, "tensor_mul", sc["ay"][:], sc["y"][:], sc["y"][:], tag=f"{tag}_ay")
    v_op(S, "tensor_single_scalar", sc["m2"][:], sc["ay"][:], 4.0, ALU.is_lt, tag=f"{tag}_m2")
    v_op(S, "tensor_single_scalar", sc["m4"][:], sc["ay"][:], 16.0, ALU.is_lt, tag=f"{tag}_m4")
    for nm, mul in (("r8", 8.0), ("r4", 4.0), ("r2", 2.0)):
        v_op(S, "tensor_scalar", sc[nm][:], sc["y"][:], mul, MAGIC, ALU.mult, ALU.add, tag=f"{tag}_{nm}a")
        v_op(S, "tensor_single_scalar", sc[nm][:], sc[nm][:], MAGIC, ALU.subtract, tag=f"{tag}_{nm}b")
    sx2 = bview(sc["sexp2"][:])
    v_op(S, "tensor_copy", sx2[:, :, 0:1], sc["s"][:].rearrange("p (a o) -> p a o", o=1), tag=f"{tag}_f0")
    w = 1
    while w < 32:
        v_op(S, "tensor_copy", sx2[:, :, w:2 * w], sx2[:, :, 0:w], tag=f"{tag}_f{w}")
        w *= 2
    for nm, inv in (("r8", 0.125), ("r4", 0.25), ("r2", 0.5)):
        v_op(S, "scalar_tensor_tensor", sc[nm][:], sc[nm][:], inv, sc["sexp2"][:], ALU.mult, ALU.mult, tag=f"{tag}_t{nm}")
    v_op(S, "select", sc["ay"][:], sc["m4"][:], sc["r4"][:], sc["r2"][:], tag=f"{tag}_sel1")
    S.op("dve", lambda h: h.select(out_bf_ap, sc["m2"][:], sc["r8"][:], sc["ay"][:]),
         reads=[sc["m2"][:], sc["r8"][:], sc["ay"][:]], writes=[out_bf_ap], tag=f"{tag}_sel2")


def mk_qscratch(alloc, Tn, name):
    d = {}
    for k in ("y", "ay", "r8", "r4", "r2", "sexp", "sexp2"):
        d[k] = alloc(f"{name}_{k}", [128, Tn])
    for k in ("m2", "m4"):
        d[k] = alloc(f"{name}_{k}", [128, Tn], I32)
    for k in ("amax", "amin", "aab", "s", "si"):
        d[k] = alloc(f"{name}_{k}", [128, Tn // 32])
    return d


def emit_l1_ffn(S, C, x_in_h, x_out_h, idx=None):
    nc = S.nc
    idx = 1 if hasattr(S, "d_qw") else 0
    if idx == 0:
        S.d_qw = S.dram("q_w", [D, D], F32, kind="ExternalInput")
        S.d_qb = S.dram("q_bias", [1, D], F32, kind="ExternalInput")
        S.qwT_d = S.dram(f"qwT_d", [D, D], BF16, kind="ExternalOutput" if getattr(S, "debug_spill", False) else "Internal")
    d_qw, d_qb, qwT_d = S.d_qw, S.d_qb, S.qwT_d

    with pscope(S) as ph:
        sbl = lambda n, s, d=F32: ph.enter_context(nc.sbuf_tensor(f"{n}_{S.uid()}", list(s), d))
        qbias = sbl("qbias", [1, D], BF16)
        qbf = sbl("qbf", [1, D])
        dma(S, "sp", qbf[:], d_qb.ap(), tag="ld_qb")
        v_op(S, "tensor_copy", qbias[:], qbf[:], tag="qb_bf")

        if idx == 0:
            with pscope(S) as pp:
                sb2 = lambda n, s, d=F32: pp.enter_context(nc.sbuf_tensor(f"{n}_{S.uid()}", list(s), d))
                ps2 = lambda n, s, d=F32: pp.enter_context(nc.psum_tensor(f"{n}_{S.uid()}", list(s), d))
                wq = sb2("wq", [128, ND, D])
                dma(S, "sp", wq[:], d_qw.ap().rearrange("(a p) n -> p a n", p=128), tag="ld_qw")
                wdf = sb2("wdf", [128, ND, D])
                sc = mk_qscratch(sb2, D, "qw")
                for j in range(ND):
                    quantize_rows(S, sc, wq[:, j, :], wdf[:, j, :], D, f"qw{j}")
                if getattr(S, "debug_spill", False):
                    wdf_dbg = S.dram(f"wdf_dbg{S.uid()}", [D, D], F32, kind="ExternalOutput")
                    dma(S, "sp", wdf_dbg.ap().rearrange("(a p) n -> p a n", p=128), wdf[:], tag="dbg_wdf")
                    y_dbg = S.dram(f"y_dbg{S.uid()}", [128, D], F32, kind="ExternalOutput")
                    s_dbg = S.dram(f"s_dbg{S.uid()}", [128, D // 32], F32, kind="ExternalOutput")
                    dma(S, "sp", y_dbg.ap(), sc["y"][:], tag="dbg_y")
                    dma(S, "sp", s_dbg.ap(), sc["s"][:], tag="dbg_s")
                ps_t = ps2("ps_tw", [128, 128])
                wrow = sb2("wrow", [128, D], BF16)
                for kj in range(ND):
                    for dj in range(ND):
                        tr_op(S, ps_t[:], wdf[:, dj, 128 * kj:128 * (kj + 1)], C.ident_f32[:], f"qwt{kj}_{dj}")
                        v_op(S, "tensor_copy", wrow[:, 128 * dj:128 * (dj + 1)], ps_t[:], tag=f"qwte{kj}_{dj}")
                    dma(S, "sp", qwT_d.ap()[128 * kj:128 * (kj + 1), :], wrow[:], tag=f"st_qwT{kj}")

        Tc = 512
        nch = T // Tc
        dbg = getattr(S, "debug_spill", False)
        if dbg:
            xn_dbg = S.dram(f"xn_dbg{S.uid()}", [D, T], F32, kind="ExternalOutput")
            xd_dbg = S.dram(f"xd_dbg{S.uid()}", [D, T], BF16, kind="ExternalOutput")
        with pscope(S) as pp:
            sb2 = lambda n, s, d=F32: pp.enter_context(nc.sbuf_tensor(f"{n}_{S.uid()}", list(s), d))
            ps2 = lambda n, s, d=F32: pp.enter_context(nc.psum_tensor(f"{n}_{S.uid()}", list(s), d))
            x_sb = sb2("xQ", [128, ND, Tc])
            xb = sb2("xQb", [128, ND, Tc], BF16)
            xq = sb2("xQq", [128, ND, Tc], BF16)
            xc = sb2("xQc", [128, ND, Tc])
            xn = sb2("xnQ", [128, ND, Tc])
            xtok = sb2("xtok", [128, D])
            xdt = sb2("xdt", [128, D], BF16)
            xd = sb2("xdQ", [128, ND, Tc], BF16)
            wT = sb2("wTQ", [128, ND, D], BF16)
            sc = mk_qscratch(sb2, D, "qx")
            rows = mk_ln_rows(S, Tc, "rQ", alloc=sb2)
            mb_sb = sb2("mbQ", [128, Tc])
            rb_sb = sb2("rbQ", [128, Tc])
            ps_s, ps_ss = ps2("psQ_s", [1, Tc]), ps2("psQ_ss", [1, Tc])
            ps_bc = ps2("psQ_bc", [128, Tc])
            ps_t = ps2("psQ_t", [128, 128])
            ps_tb = ps2("psQ_tb", [128, 128], BF16)
            ps_big = ps2("psQ_big", [128, 2, 512])
            dma(S, "sp", wT[:], qwT_d.ap().rearrange("(a p) n -> p a n", p=128), tag="ld_wT")
            for ci in range(nch):
                t0 = ci * Tc
                dma(S, "sp", x_sb[:], fm_dram(x_in_h, t0, t0 + Tc), tag=f"Q_ldx{ci}")
                for j in range(ND):
                    v_op(S, "tensor_copy", xb[:, j, :], x_sb[:, j, :], tag=f"Q_xb{ci}_{j}")
                    act(S, xq[:, j, :], x_sb[:, j, :], AF.Square, tag=f"Q_xq{ci}_{j}")
                ln_stats(S, C, xb, xq, ND, Tc, ps_s, ps_ss, f"Q_st{ci}")
                ln_rows(S, ps_s, ps_ss, rows, D, EPS_LN, f"Q_r{ci}")
                bcast_row(S, C, ps_bc[:], rows["m"][:], f"Q_bm{ci}")
                v_op(S, "tensor_copy", mb_sb[:], ps_bc[:], tag=f"Q_cm{ci}")
                bcast_row(S, C, ps_bc[:], rows["rs"][:], f"Q_br{ci}")
                v_op(S, "tensor_copy", rb_sb[:], ps_bc[:], tag=f"Q_cr{ci}")
                for j in range(ND):
                    v_op(S, "tensor_sub", xc[:, j, :], x_sb[:, j, :], mb_sb[:], tag=f"Q_sub{ci}_{j}")
                    v_op(S, "tensor_mul", xn[:, j, :], xc[:, j, :], rb_sb[:], tag=f"Q_mul{ci}_{j}")
                for st in range(Tc // 128):
                    s0 = st * 128
                    for j in range(ND):
                        tr_op(S, ps_t[:], xn[:, j, s0:s0 + 128], C.ident_f32[:], f"Q_t{ci}_{st}_{j}")
                        v_op(S, "tensor_copy", xtok[:, 128 * j:128 * (j + 1)], ps_t[:], tag=f"Q_te{ci}_{st}_{j}")
                    quantize_rows(S, sc, xtok[:], xdt[:], D, f"Q_q{ci}_{st}")
                    for j in range(ND):
                        tr_op(S, ps_tb[:], xdt[:, 128 * j:128 * (j + 1)], C.ident_bf[:], f"Q_tb{ci}_{st}_{j}")
                        v_op(S, "tensor_copy", xd[:, j, s0:s0 + 128], ps_tb[:], tag=f"Q_tbe{ci}_{st}_{j}")
                if dbg:
                    dma(S, "sp", fm_dram(xn_dbg, t0, t0 + Tc), xn[:], tag=f"dbg_xn{ci}")
                    dma(S, "sp", fm_dram(xd_dbg, t0, t0 + Tc), xd[:], tag=f"dbg_xd{ci}")
                for half in range(4):
                    for dq in range(2):
                        di = 2 * half + dq
                        for kj in range(ND):
                            mm(S, ps_big[:, dq, :], wT[:, kj, 128 * di:128 * (di + 1)], xd[:, kj, :], start=(kj == 0), stop=False, tag=f"Q_mm{ci}_{di}_{kj}")
                        mm(S, ps_big[:, dq, :], qbias[:, 128 * di:128 * (di + 1)], C.ones_r_bf[:, :Tc], start=False, stop=True, tag=f"Q_bb{ci}_{di}")
                        v_op(S, "tensor_add", x_sb[:, di, :], x_sb[:, di, :], ps_big[:, dq, :], tag=f"Q_res{ci}_{di}")
                dma(S, "sp", fm_dram(x_out_h, t0, t0 + Tc), x_sb[:], tag=f"Q_stx{ci}")


def run_l1_ffn(per_core, n_cores=NCORES, debug_spill=False, **kw):
    S = Prog("l1f")
    S.debug_spill = debug_spill
    x_in = S.dram("x_in", [D, T], F32, kind="ExternalInput")
    x_out = S.dram("x_out", [D, T], F32, kind="ExternalOutput")
    C = Common(S)
    emit_l1_ffn(S, C, x_in, x_out, idx=0)
    res = S.run(per_core, n_cores, **kw)
    return [r["x_out"] for r in res.results], res


def conv_w_to_bf16(S, src_h, dst_h, rows, cols, tmp_sb, tag, qload="pool", qstore="sp"):
    nt = rows // 128
    for i in range(nt):
        t = tmp_sb[i % 2]
        dma(S, qload, t[:, 0:cols], src_h.ap()[128 * i:128 * (i + 1), :], tag=f"{tag}_l{i}")
        dma(S, qstore, dst_h.ap()[128 * i:128 * (i + 1), :], t[:, 0:cols], tag=f"{tag}_s{i}")


WCONV = (("tb_wqT", [D, D]), ("tb_wkT", [D, D]), ("tb_wvT", [D, D]), ("tb_woT", [D, D]),
         ("tb_w1T", [D, FF]), ("tb_w2T", [FF, D]),
         ("nb_Wq", [D, D]), ("nb_Wk", [D, D]), ("nb_Wv", [D, D]))


def wconv_all(S, alloc):
    if hasattr(S, "wbf"):
        return
    S.wbf = {}
    tmp = getattr(S, "_wcv_tmp", None) or [alloc(f"wcv{i}_{S.uid()}", [128, FF], BF16) for i in range(2)]
    for nm, sh in WCONV:
        src = S.dram(nm, sh, F32, kind="ExternalInput")
        dst = S.dram(nm + "_bf", sh, BF16, kind="Internal")
        S.wbf[nm] = dst
        conv_w_to_bf16(S, src, dst, sh[0], sh[1], tmp, f"wc_{nm}", qload="pool", qstore="act")


def l0f_wprep(S, alloc=None):
    if hasattr(S, "f_d"):
        return
    S.f_d = {}
    for nm, sh in (("f_norm_w", [128, ND]), ("f_gateT", [D, NEXP]),
                   ("f_a1p", [2, D, 64]), ("f_a3p", [2, D, 64]),
                   ("f_b1p", [NEXP, 64, FF]), ("f_b3p", [NEXP, 64, FF]),
                   ("f_a2Tr", [FF, NEXP, RLORA]), ("f_b2T3", [16, NEXP * D]),
                   ("f_w1T", [D, FF]), ("f_w3T", [D, FF]), ("f_w2T", [FF, D])):
        S.f_d[nm] = S.dram(nm, sh, F32, kind="ExternalInput")
    S.f_scr = {"f_w1T": S.dram("f_w1T_bf", [D, FF], BF16, kind="Internal"),
               "f_w3T": S.dram("f_w3T_bf", [D, FF], BF16, kind="Internal"),
               "f_w2T": S.dram("f_w2T_bf", [FF, D], BF16, kind="Internal")}

    def _emit(al, qa, qb):
        tmp = [al(f"cvt{i}_{S.uid()}", [128, FF], BF16) for i in range(2)]
        S._wcv_tmp = tmp
        tmp2 = [al(f"cvu{i}_{S.uid()}", [128, D], BF16) for i in range(2)]
        conv_w_to_bf16(S, S.f_d["f_w1T"], S.f_scr["f_w1T"], D, FF, tmp, "cv_w1", qload=qa, qstore=qb)
        conv_w_to_bf16(S, S.f_d["f_w3T"], S.f_scr["f_w3T"], D, FF, tmp, "cv_w3", qload=qa, qstore=qb)
        conv_w_to_bf16(S, S.f_d["f_w2T"], S.f_scr["f_w2T"], FF, D, tmp2, "cv_w2", qload=qa, qstore=qb)

    if alloc is not None:
        _emit(alloc, "pool", "act")
    else:
        with pscope(S) as pp:
            al = lambda n, s, d=BF16: pp.enter_context(S.nc.sbuf_tensor(n, list(s), d))
            _emit(al, "pool", "sp")


def emit_l0_ffn(S, C, x_in_h, x_out_h, idx=None):
    nc = S.nc
    idx = 1 if getattr(S, "f_used", False) else 0
    S.f_used = True
    l0f_wprep(S)
    FD = S.f_d
    Tc = 512
    nch = T // Tc
    nst = Tc // 128
    u0 = S.uid()
    wrows_d = S.dram(f"wrows_d{u0}", [NEXP, Tc], F32, kind="Internal")

    with pscope(S) as ph:
        sbl = lambda n, s, d=F32: ph.enter_context(nc.sbuf_tensor(f"{n}_{S.uid()}", list(s), d))
        norm_w = sbl("fnw", [128, ND]); dma(S, "sp", norm_w[:], FD["f_norm_w"].ap(), tag="ld_fnw")
        gateT = sbl("gateT", [128, ND, NEXP]); dma(S, "sp", gateT[:], FD["f_gateT"].ap().rearrange("(a p) n -> p a n", p=128), tag="ld_gateT")
        for j in range(ND):
            v_op(S, "tensor_scalar", gateT[:, j, :], gateT[:, j, :], norm_w[:, j:j + 1], None, ALU.mult, tag=f"gw_fold{j}")
        a1p = sbl("a1p", [128, 2, ND, 64], BF16)
        a3p = sbl("a3p", [128, 2, ND, 64], BF16)
        for gi in range(2):
            dma(S, "pool", a1p[:, gi, :, :], FD["f_a1p"].ap()[gi].rearrange("(a p) n -> p a n", p=128), tag=f"ld_a1p{gi}")
            dma(S, "pool", a3p[:, gi, :, :], FD["f_a3p"].ap()[gi].rearrange("(a p) n -> p a n", p=128), tag=f"ld_a3p{gi}")
        a2T = sbl("a2T", [128, NF, NEXP, RLORA], BF16)
        dma(S, "pool", a2T[:], FD["f_a2Tr"].ap().rearrange("(a p) e r -> p a e r", p=128), tag="ld_a2T")
        b2T = sbl("b2T", [16, NEXP, D], BF16)
        dma(S, "pool", b2T[:], FD["f_b2T3"].ap().rearrange("p (e n) -> p e n", e=NEXP), tag="ld_b2T")

        x_sb = sbl("xF", [128, ND, Tc])
        xn = sbl("xnF", [128, ND, Tc], BF16)
        w_bc = sbl("w_bc", [128, NEXP, Tc], BF16)
        etr1 = sbl("etr1", [64, 2, Tc], BF16)
        etr3 = sbl("etr3", [64, 2, Tc], BF16)
        actmix = sbl("actmix", [128, NF, Tc], BF16)
        etr2w = [sbl(f"etr2w{e}", [16, Tc], BF16) for e in range(NEXP)]

        for ci in range(nch):
            t0 = ci * Tc
            with pscope(S) as pq:
                sb3 = lambda n, s, d=F32: pq.enter_context(nc.sbuf_tensor(f"{n}_{S.uid()}", list(s), d))
                ps3 = lambda n, s, d=F32: pq.enter_context(nc.psum_tensor(f"{n}_{S.uid()}", list(s), d))
                xb = sb3("xFb", [128, ND, Tc], BF16)
                xq = sb3("xFq", [128, ND, Tc], BF16)
                xc = sb3("xFc", [128, ND, Tc])
                rows = mk_ln_rows(S, Tc, "rF", eps2=True, alloc=sb3)
                glog = sb3("glog", [NEXP, Tc])
                wbe = sb3("wbe", [NEXP, Tc])
                wrows = sb3("wrows", [1, NEXP, Tc])
                u_tok = sb3("u_tok", [128, NEXP])
                l_tok = sb3("l_tok", [128, NEXP])
                msk1 = sb3("msk1", [128, NEXP])
                msk2 = sb3("msk2", [128, NEXP])
                l2m = sb3("l2m", [128, NEXP])
                wtok = sb3("wtok", [128, NEXP])
                rs2c = sb3("rs2c", [128, nst])
                lm = {k: sb3(f"lm_{k}", [128, 1]) for k in ("negm1", "m1", "m2", "e2", "den", "winv", "w2c")}
                ps_s, ps_ss = ps3("psF_s", [1, Tc]), ps3("psF_ss", [1, Tc])
                ps_bc = ps3("psF_bc", [128, Tc])
                ps_g = ps3("psF_g", [NEXP, Tc])
                ps_t = ps3("psF_t", [128, 128])
                ps_e = ps3("psF_e", [64, Tc])
                dma(S, "sp", x_sb[:], fm_dram(x_in_h, t0, t0 + Tc), tag=f"F_ldx{ci}")
                for j in range(ND):
                    v_op(S, "tensor_copy", xb[:, j, :], x_sb[:, j, :], tag=f"F_xb{ci}_{j}")
                    act(S, xq[:, j, :], x_sb[:, j, :], AF.Square, tag=f"F_xq{ci}_{j}")
                ln_stats(S, C, xb, xq, ND, Tc, ps_s, ps_ss, f"F_st{ci}")
                ln_rows(S, ps_s, ps_ss, rows, D, EPS_LN, f"F_r{ci}", eps2=1e-6)
                bcast_row(S, C, ps_bc[:], rows["m"][:], f"F_bm{ci}")
                for j in range(ND):
                    v_op(S, "tensor_sub", xc[:, j, :], x_sb[:, j, :], ps_bc[:], tag=f"F_sub{ci}_{j}")
                bcast_row(S, C, ps_bc[:], rows["rs2"][:], f"F_br{ci}")
                for j in range(ND):
                    v_op(S, "scalar_tensor_tensor", xn[:, j, :], xc[:, j, :], norm_w[:, j:j + 1], ps_bc[:], ALU.mult, ALU.mult, tag=f"F_xn{ci}_{j}")
                for j in range(ND):
                    mm(S, ps_g[:], gateT[:, j, :], xc[:, j, :], start=(j == 0), stop=(j == ND - 1), tag=f"F_g{ci}_{j}")
                v_op(S, "tensor_copy", glog[:], ps_g[:], tag=f"F_gc{ci}")
                for st in range(nst):
                    tr_op(S, ps_t[0:128, 0:1], rows["rs2"][:, 128 * st:128 * (st + 1)], C.ident_f32[0:1, 0:1], f"F_rt{ci}_{st}")
                    v_op(S, "tensor_copy", rs2c[:, st:st + 1], ps_t[0:128, 0:1], tag=f"F_rtc{ci}_{st}")
                for st in range(nst):
                    s0 = 128 * st
                    tr_op(S, ps_t[0:128, 0:NEXP], glog[:, s0:s0 + 128], C.ident_f32[0:NEXP, 0:NEXP], f"F_gt{ci}_{st}")
                    v_op(S, "tensor_copy", u_tok[:], ps_t[0:128, 0:NEXP], tag=f"F_gte{ci}_{st}")
                    v_op(S, "tensor_scalar", l_tok[:], u_tok[:], rs2c[:, st:st + 1], None, ALU.mult, tag=f"F_lt{ci}_{st}")
                    v_op(S, "reduce_max", lm["negm1"][:], l_tok[:], AX.X, negate=True, tag=f"F_m1{ci}_{st}")
                    v_op(S, "tensor_scalar", lm["m1"][:], lm["negm1"][:], -1.0, None, ALU.mult, tag=f"F_m1p{ci}_{st}")
                    v_op(S, "tensor_scalar", msk1[:], l_tok[:], lm["m1"][:], None, ALU.is_ge, tag=f"F_k1{ci}_{st}")
                    v_op(S, "scalar_tensor_tensor", l2m[:], msk1[:], -1e30, l_tok[:], ALU.mult, ALU.add, tag=f"F_l2{ci}_{st}")
                    v_op(S, "reduce_max", lm["m2"][:], l2m[:], AX.X, tag=f"F_m2{ci}_{st}")
                    v_op(S, "tensor_scalar", msk2[:], l2m[:], lm["m2"][:], None, ALU.is_ge, tag=f"F_k2{ci}_{st}")
                    act(S, lm["e2"][:], lm["m2"][:], AF.Exp, bias=lm["negm1"][:], tag=f"F_e2{ci}_{st}")
                    v_op(S, "tensor_scalar", lm["den"][:], lm["e2"][:], 1.0, None, ALU.add, tag=f"F_dn{ci}_{st}")
                    v_op(S, "reciprocal", lm["winv"][:], lm["den"][:], tag=f"F_wi{ci}_{st}")
                    v_op(S, "tensor_mul", lm["w2c"][:], lm["e2"][:], lm["winv"][:], tag=f"F_w2{ci}_{st}")
                    v_op(S, "tensor_scalar", wtok[:], msk1[:], lm["winv"][:], None, ALU.mult, tag=f"F_wt{ci}_{st}")
                    v_op(S, "scalar_tensor_tensor", wtok[:], msk2[:], lm["w2c"][:], wtok[:], ALU.mult, ALU.add, tag=f"F_wt2{ci}_{st}")
                    tr_op(S, ps_t[0:NEXP, 0:128], wtok[:], C.ident_f32[:], f"F_wtt{ci}_{st}")
                    v_op(S, "tensor_copy", wbe[:, s0:s0 + 128], ps_t[0:NEXP, 0:128], tag=f"F_wte{ci}_{st}")
                if getattr(S, "debug_spill", False):
                    if not hasattr(S, "wbe_dbg"):
                        S.wbe_dbg = S.dram(f"wbe_dbg{S.uid()}", [NEXP, T], F32, kind="ExternalOutput")
                    dma(S, "sp", S.wbe_dbg.ap()[:, t0:t0 + Tc], wbe[:], tag=f"dbg_wbe{ci}")
                dma(S, "sp", wrows_d.ap(), wbe[:], tag=f"F_wd1{ci}")
                dma(S, "sp", wrows[:], wrows_d.ap().rearrange("(o e) n -> o e n", o=1), tag=f"F_wd2{ci}")
                for e in range(NEXP):
                    mm(S, ps_bc[:], C.ones_c_f32[:], wrows[0:1, e, :], start=True, stop=True, tag=f"F_wb{ci}_{e}")
                    v_op(S, "tensor_copy", w_bc[:, e, :], ps_bc[:], tag=f"F_wbc{ci}_{e}")
                for gi in range(2):
                    for fam, ap_, dst in (("1", a1p, etr1), ("3", a3p, etr3)):
                        for j in range(ND):
                            mm(S, ps_e[:], ap_[:, gi, j, :], xn[:, j, :], start=(j == 0), stop=(j == ND - 1), tag=f"F_a{fam}_{ci}_{gi}_{j}")
                        v_op(S, "tensor_scalar", dst[:, gi, :], ps_e[:], 2.0, None, ALU.mult, tag=f"F_ae{fam}_{ci}_{gi}")

            with pscope(S) as pq:
                sb3 = lambda n, s, d=F32: pq.enter_context(nc.sbuf_tensor(f"{n}_{S.uid()}", list(s), d))
                ps3 = lambda n, s, d=F32: pq.enter_context(nc.psum_tensor(f"{n}_{S.uid()}", list(s), d))
                sigs = [sb3(f"sig{i}", [128, Tc], BF16) for i in range(2)]
                qqs = [sb3(f"qq{i}", [128, Tc], BF16) for i in range(2)]
                p_bfs = [sb3(f"p_bf{i}", [128, Tc], BF16) for i in range(2)]
                h1s = [sb3(f"h1s{i}", [128, Tc], BF16) for i in range(2)]
                h3s = [sb3(f"h3s{i}", [128, Tc], BF16) for i in range(2)]
                l1sb = [sb3(f"l1sb{i}", [128, Tc], BF16) for i in range(2)]
                l3sb = [sb3(f"l3sb{i}", [128, Tc], BF16) for i in range(2)]
                b1sb = [sb3(f"b1sb{i}", [128, Tc], BF16) for i in range(2)]
                b3sb = [sb3(f"b3sb{i}", [128, Tc], BF16) for i in range(2)]
                w1t = [sb3(f"w1t{i}", [128, ND, 512], BF16) for i in range(2)]
                w3t = [sb3(f"w3t{i}", [128, ND, 512], BF16) for i in range(2)]
                b1t = [sb3(f"b1t{i}", [64, NEXP, 512], BF16) for i in range(2)]
                b3t = [sb3(f"b3t{i}", [64, NEXP, 512], BF16) for i in range(2)]
                ps_bs = ps3("psF_bs", [128, Tc])
                ps_l1 = ps3("psF_l1", [128, 2, Tc])
                ps_l3 = ps3("psF_l3", [128, 2, Tc])
                ps_e2 = [ps3(f"psF_e2_{i}", [128, Tc]) for i in range(3)]

                def grp_load(fg, par):
                    c0 = 512 * fg
                    dma(S, "sp", w1t[par][:], S.f_scr["f_w1T"].ap()[:, c0:c0 + 512].rearrange("(a p) n -> p a n", p=128), tag=f"F_w1{ci}_{fg}")
                    dma(S, "act", w3t[par][:], S.f_scr["f_w3T"].ap()[:, c0:c0 + 512].rearrange("(a p) n -> p a n", p=128), tag=f"F_w3{ci}_{fg}")
                    dma(S, "pool", b1t[par][:], FD["f_b1p"].ap()[:, :, c0:c0 + 512].rearrange("e p n -> p e n"), tag=f"F_b1l{ci}_{fg}")
                    dma(S, "pool", b3t[par][:], FD["f_b3p"].ap()[:, :, c0:c0 + 512].rearrange("e p n -> p e n"), tag=f"F_b3l{ci}_{fg}")

                def base_gen(ft):
                    bpar = ft % 2
                    if ft % 4 == 1 and ft // 4 + 1 < NF // 4:
                        grp_load(ft // 4 + 1, (ft // 4 + 1) % 2)
                    w1 = w1t[(ft // 4) % 2]; w3 = w3t[(ft // 4) % 2]
                    fo = 128 * (ft % 4)
                    yield
                    for j in range(ND):
                        mm(S, ps_bs[:], w1[:, j, fo:fo + 128], xn[:, j, :], start=(j == 0), stop=(j == ND - 1), tag=f"F_b1{ci}_{ft}_{j}")
                        yield
                    yield
                    act(S, b1sb[bpar][:], ps_bs[:], AF.Identity, tag=f"F_be1{ci}_{ft}")
                    for j in range(ND):
                        mm(S, ps_bs[:], w3[:, j, fo:fo + 128], xn[:, j, :], start=(j == 0), stop=(j == ND - 1), tag=f"F_b3{ci}_{ft}_{j}")
                        yield
                    act(S, b3sb[bpar][:], ps_bs[:], AF.Identity, tag=f"F_be3{ci}_{ft}")
                    while True:
                        yield
                grp_load(0, 0)
                g = base_gen(0)
                for _ in range(19):
                    next(g)

                pend_a2 = []

                def emit_a2(ft, e):
                    bk, off = e // 3, 32 * (e % 3)
                    S.op("pe", lambda h, _o=ps_e2[bk][off:off + 16, :], _l=a2T[:, ft, e, :], _r=p_bfs[e % 2][:], _s=(ft == 0), _t=(ft == NF - 1), _p=(0, off):
                         h.matmul(_o, _l, _r, start=_s, stop=_t, tile_position=_p),
                         reads=[a2T[:, ft, e, :], p_bfs[e % 2][:]], writes=[ps_e2[bk][off:off + 16, :]], tag=f"F_e2_{ci}_{ft}_{e}")

                for ft in range(NF):
                    bpar = ft % 2
                    gpar = (ft // 4) % 2
                    fo = 128 * (ft % 4)
                    b1s = b1t[gpar]; b3s = b3t[gpar]
                    g = base_gen(ft + 1) if ft + 1 < NF else None
                    for e in range(NEXP):
                        gi = e // 4
                        ep = e % 2
                        sig = sigs[ep]; qq = qqs[ep]; p_bf = p_bfs[ep]
                        mm(S, ps_l1[:, ep, :], b1s[:, e, fo:fo + 128], etr1[:, gi, :], start=True, stop=True, tag=f"F_l1{ci}_{ft}_{e}")
                        act(S, l1sb[ep][:], ps_l1[:, ep, :], AF.Identity, tag=f"F_le1{ci}_{ft}_{e}")
                        mm(S, ps_l3[:, ep, :], b3s[:, e, fo:fo + 128], etr3[:, gi, :], start=True, stop=True, tag=f"F_l3{ci}_{ft}_{e}")
                        act(S, l3sb[ep][:], ps_l3[:, ep, :], AF.Identity, tag=f"F_le3{ci}_{ft}_{e}")
                        v_op(S, "tensor_add", h1s[ep][:], l1sb[ep][:], b1sb[bpar][:], tag=f"F_h1{ci}_{ft}_{e}")
                        act(S, sig[:], h1s[ep][:], AF.Silu, tag=f"F_sg{ci}_{ft}_{e}")
                        v_op(S, "tensor_add", h3s[ep][:], l3sb[ep][:], b3sb[bpar][:], tag=f"F_h3{ci}_{ft}_{e}")
                        v_op(S, "tensor_mul", qq[:], sig[:], h3s[ep][:], tag=f"F_q{ci}_{ft}_{e}")
                        v_op(S, "tensor_mul", p_bf[:], qq[:], w_bc[:, e, :], tag=f"F_p{ci}_{ft}_{e}")
                        if e == 0:
                            v_op(S, "tensor_copy", actmix[:, ft, :], p_bf[:], tag=f"F_ac{ci}_{ft}_{e}")
                        else:
                            v_op(S, "tensor_add", actmix[:, ft, :], actmix[:, ft, :], p_bf[:], tag=f"F_ac{ci}_{ft}_{e}")
                        if pend_a2:
                            emit_a2(*pend_a2.pop())
                        pend_a2.append((ft, e))
                        if g is not None:
                            next(g)
                            next(g)
                    if g is not None:
                        for _ in range(4):
                            next(g)
                if pend_a2:
                    emit_a2(*pend_a2.pop())
                if getattr(S, "debug_spill", False):
                    if not hasattr(S, "am_dbg"):
                        S.am_dbg = S.dram(f"am_dbg{S.uid()}", [FF, T], BF16, kind="ExternalOutput")
                    for ftd in range(NF):
                        dma(S, "sp", S.am_dbg.ap()[128 * ftd:128 * (ftd + 1), t0:t0 + Tc], actmix[:, ftd, :], tag=f"dbg_am{ci}_{ftd}")
                for e in range(NEXP):
                    bk, off = e // 3, 32 * (e % 3)
                    v_op(S, "tensor_scalar", etr2w[e][:], ps_e2[bk][off:off + 16, :], 2.0, None, ALU.mult, tag=f"F_e2w{ci}_{e}")
                if getattr(S, "debug_spill", False):
                    if not hasattr(S, "e2_dbg"):
                        S.e2_dbg = S.dram(f"e2_dbg{S.uid()}", [NEXP, 16, T], BF16, kind="ExternalOutput")
                    for e in range(NEXP):
                        dma(S, "sp", S.e2_dbg.ap()[e, :, t0:t0 + Tc], etr2w[e][:], tag=f"dbg_e2{ci}_{e}")

            with pscope(S) as pq:
                sb3 = lambda n, s, d=F32: pq.enter_context(nc.sbuf_tensor(f"{n}_{S.uid()}", list(s), d))
                ps3 = lambda n, s, d=F32: pq.enter_context(nc.psum_tensor(f"{n}_{S.uid()}", list(s), d))
                w2t = [sb3(f"w2t{i}", [128, D], BF16) for i in range(2)]
                ps_o = ps3("psF_o", [128, 8, 512])
                for ft in range(NF):
                    w2 = w2t[ft % 2]
                    dma(S, ("sp", "act")[ft % 2], w2[:], S.f_scr["f_w2T"].ap()[128 * ft:128 * (ft + 1), :], tag=f"F_w2{ci}_{ft}")
                    for di in range(8):
                        mm(S, ps_o[:, di, :], w2[:, 128 * di:128 * (di + 1)], actmix[:, ft, :], start=(ft == 0), stop=False, tag=f"F_o{ci}_{ft}_{di}")
                for di in range(8):
                    for e in range(NEXP):
                        mm(S, ps_o[:, di, :], b2T[0:16, e, 128 * di:128 * (di + 1)], etr2w[e][:], start=False, stop=(e == NEXP - 1), tag=f"F_ob{ci}_{di}_{e}")
                    v_op(S, "tensor_add", x_sb[:, di, :], x_sb[:, di, :], ps_o[:, di, :], tag=f"F_ox{ci}_{di}")
                dma(S, "sp", fm_dram(x_out_h, t0, t0 + Tc), x_sb[:], tag=f"F_stx{ci}")


def run_l0_ffn(per_core, n_cores=NCORES, debug_spill=False, **kw):
    S = Prog("l0f")
    S.debug_spill = debug_spill
    x_in = S.dram("x_in", [D, T], F32, kind="ExternalInput")
    x_out = S.dram("x_out", [D, T], F32, kind="ExternalOutput")
    C = Common(S)
    emit_l0_ffn(S, C, x_in, x_out, idx=0)
    res = S.run(per_core, n_cores, **kw)
    return [r["x_out"] for r in res.results], res


def emit_l1_attn_b(S, C, x_in_h, x_out_h):
    nc = S.nc
    dd = {}
    for nm, sh in (("tb_wqT", [D, D]), ("tb_wkT", [D, D]), ("tb_wvT", [D, D]), ("tb_woT", [D, D]),
                   ("tb_w1T", [D, FF]), ("tb_w2T", [FF, D]),
                   ("tb_bq", [1, D]), ("tb_bk", [1, D]), ("tb_bv", [1, D]), ("tb_bo", [1, D]),
                   ("tb_g1", [128, ND]), ("tb_b1c", [128, ND]), ("tb_g2", [128, ND]), ("tb_b2c", [128, ND]),
                   ("tb_bf1", [1, FF]), ("tb_bf2", [1, D]),
                   ("c_headmask", [ND, 128, NHEAD]), ("c_sel64", [NHEAD, NHEAD, 64]), ("c_ones_c_bf", [1, 128])):
        dd[nm] = S.dram(nm, sh, F32 if not nm.startswith("c_") else BF16, kind="ExternalInput")
    u0 = S.uid()
    y_d = S.dram(f"tb_y{u0}", [D, T], F32, kind="Internal")
    q_d = S.dram(f"tb_q{u0}", [D, T], BF16, kind="Internal")
    bias1_d = S.dram(f"tb_bias1{u0}", [1, FF], BF16, kind="Internal")
    MN = 1.0 - 1e-5
    ITEMP = float(np.float32(2.0 / math.sqrt(HD)))

    with pscope(S) as ph:
        sbl = lambda n, s, d=F32: ph.enter_context(nc.sbuf_tensor(f"{n}_{S.uid()}", list(s), d))
        g1 = sbl("g1", [128, ND]); dma(S, "sp", g1[:], dd["tb_g1"].ap(), tag="ld_g1")
        g2 = sbl("g2", [128, ND]); dma(S, "sp", g2[:], dd["tb_g2"].ap(), tag="ld_g2")
        hmask = sbl("hmask", [128, ND, NHEAD], BF16)
        dma(S, "sp", hmask[:], dd["c_headmask"].ap().rearrange("a p n -> p a n"), tag="ld_hm")
        sel64 = sbl("sel64", [NHEAD, NHEAD, 64], BF16)
        dma(S, "sp", sel64[:], dd["c_sel64"].ap().rearrange("a p n -> p a n"), tag="ld_sel")
        ones_cb = sbl("ones_cb", [1, 128], BF16); dma(S, "sp", ones_cb[:], dd["c_ones_c_bf"].ap(), tag="ld_ocb")
        bias_2 = sbl("bias_2", [1, D], BF16)

        Tc = 512
        nch = T // Tc
        pa = ph.enter_context(pscope(S))
        sba = lambda n, s, d=F32: pa.enter_context(nc.sbuf_tensor(f"{n}_{S.uid()}", list(s), d))
        bias_q = sba("bias_q", [1, D], BF16)
        bias_k = sba("bias_k", [1, D], BF16)
        bias_v = sba("bias_v", [1, D], BF16)
        bias_o = sba("bias_o", [1, D], BF16)
        kaug = sba("kaug", [128, NHEAD, T], BF16)
        vtok = sba("vtok", [128, T // 128, NHEAD * 65], BF16)
        fq_all = sba("fq_all", [16, T], BF16)
        fk_all = sba("fk_all", [16, T], BF16)
        with pscope(S) as pp:
            sb2 = lambda n, s, d=F32: pp.enter_context(nc.sbuf_tensor(f"{n}_{S.uid()}", list(s), d))
            ps2 = lambda n, s, d=F32: pp.enter_context(nc.psum_tensor(f"{n}_{S.uid()}", list(s), d))
            b1c = sb2("b1c", [128, ND], BF16); dma(S, "pool", b1c[:], dd["tb_b1c"].ap(), tag="ld_b1c")
            b2c = sb2("b2c", [128, ND], BF16); dma(S, "pool", b2c[:], dd["tb_b2c"].ap(), tag="ld_b2c")
            w512 = sb2("w512", [128, ND, 512], BF16)
            ps_b = ps2("ps_b", [1, 512])
            rowbuf = sb2("rowbuf", [1, FF])
            bias_1 = sb2("bias_1", [1, FF], BF16)
            for wn, bn, brow in (("tb_wqT", "tb_bq", bias_q), ("tb_wkT", "tb_bk", bias_k), ("tb_wvT", "tb_bv", bias_v)):
                dma(S, "sp", rowbuf[:, 0:D], dd[bn].ap(), tag=f"ld_{bn}")
                for n0 in range(0, D, 512):
                    load_cast_w(S, dd[wn], w512, D, 512, f"ldb_{wn}{n0}", colsl=(n0, n0 + 512))
                    bias_row_mm(S, C, ps_b, b1c, w512, ND, 0, 512, f"br_{wn}{n0}")
                    v_op(S, "tensor_tensor", brow[:, n0:n0 + 512], ps_b[:, :], rowbuf[:, n0:n0 + 512], ALU.add, tag=f"be_{wn}{n0}")
            dma(S, "sp", rowbuf[:, 0:D], dd["tb_bo"].ap(), tag="ld_bo")
            v_op(S, "tensor_copy", bias_o[:], rowbuf[:, 0:D], tag="bo_bf")
            dma(S, "sp", rowbuf[:, 0:D], dd["tb_bf2"].ap(), tag="ld_bf2")
            v_op(S, "tensor_copy", bias_2[:], rowbuf[:, 0:D], tag="bf2_bf")
            dma(S, "sp", rowbuf[:], dd["tb_bf1"].ap(), tag="ld_bf1")
            for n0 in range(0, FF, 512):
                load_cast_w(S, dd["tb_w1T"], w512, D, 512, f"ldb_w1{n0}", colsl=(n0, n0 + 512))
                bias_row_mm(S, C, ps_b, b2c, w512, ND, 0, 512, f"br_w1{n0}")
                v_op(S, "tensor_tensor", bias_1[:, n0:n0 + 512], ps_b[:, :], rowbuf[:, n0:n0 + 512], ALU.add, tag=f"be_w1{n0}")
            dma(S, "sp", bias1_d.ap(), bias_1[:], tag="st_b1d")

        with pscope(S) as pp:
            sb2 = lambda n, s, d=F32: pp.enter_context(nc.sbuf_tensor(f"{n}_{S.uid()}", list(s), d))
            ps2 = lambda n, s, d=F32: pp.enter_context(nc.psum_tensor(f"{n}_{S.uid()}", list(s), d))
            x_sb = sb2("xT", [128, ND, Tc])
            q_fm = sb2("q_fmP", [128, ND, Tc], BF16)
            xc = sb2("xTc", [128, ND, Tc])
            xn = sb2("xnT", [128, ND, Tc], BF16)
            k_fm = sb2("k_fmT", [128, ND, Tc], BF16)
            wsl = [sb2(f"wsl{i}", [128, ND, 128], BF16) for i in range(2)]
            wvl = [sb2(f"wvl{i}", [128, ND, 512], BF16) for i in range(1)]
            qsq = sb2("qsqT", [128, Tc], BF16)
            h2row = sb2("h2row", [16, Tc])
            k2c_c = sb2("k2c_c", [16, Tc], BF16)
            rows = mk_ln_rows(S, Tc, "rT", eps2=True, alloc=sb2)
            ps_s, ps_ss = ps2("psT_s", [1, Tc]), ps2("psT_ss", [1, Tc])
            ps_bc = ps2("psT_bc", [128, Tc])
            ps_o = [ps2("psT_o0", [128, Tc]), ps2("psT_o1", [128, Tc])]
            ps_h2 = ps2("psT_h2", [16, Tc])
            ps_f = ps2("psT_f", [64, Tc])
            for ci in range(nch):
                t0 = ci * Tc
                dma(S, "sp", x_sb[:], fm_dram(x_in_h, t0, t0 + Tc), tag=f"P_ldx{ci}")
                for j in range(ND):
                    v_op(S, "tensor_copy", q_fm[:, j, :], x_sb[:, j, :], tag=f"P_xb{ci}_{j}")
                    mm(S, ps_s[:], C.ones_col_bf[:], q_fm[:, j, :], start=(j == 0), stop=(j == ND - 1), tag=f"P_s{ci}_{j}")
                    act(S, qsq[:], x_sb[:, j, :], AF.Square, tag=f"P_xq{ci}_{j}")
                    mm(S, ps_ss[:], C.ones_col_bf[:], qsq[:], start=(j == 0), stop=(j == ND - 1), tag=f"P_ss{ci}_{j}")
                ln_rows(S, ps_s, ps_ss, rows, D, EPS_LN, f"P_r{ci}", eps2=EPS_LN)
                bcast_row(S, C, ps_bc[:], rows["m"][:], f"P_bm{ci}")
                for j in range(ND):
                    v_op(S, "tensor_sub", xc[:, j, :], x_sb[:, j, :], ps_bc[:], tag=f"P_sub{ci}_{j}")
                bcast_row(S, C, ps_bc[:], rows["rs"][:], f"P_bro{ci}")
                for j in range(ND):
                    v_op(S, "tensor_mul", x_sb[:, j, :], xc[:, j, :], ps_bc[:], tag=f"P_y{ci}_{j}")
                dma(S, "sp", fm_dram(y_d, t0, t0 + Tc), x_sb[:], tag=f"P_sty{ci}")
                bcast_row(S, C, ps_bc[:], rows["rs2"][:], f"P_br{ci}")
                for j in range(ND):
                    v_op(S, "scalar_tensor_tensor", xn[:, j, :], xc[:, j, :], g1[:, j:j + 1], ps_bc[:], ALU.mult, ALU.mult, tag=f"P_xn{ci}_{j}")
                for dst, wn, bias in ((q_fm, "tb_wqT", bias_q), (k_fm, "tb_wkT", bias_k)):
                    for nt in range(ND):
                        w = wsl[nt % 2]
                        load_cast_w(S, dd[wn], w, D, 128, f"P_{wn}{ci}_{nt}", colsl=(128 * nt, 128 * (nt + 1)))
                        po = ps_o[nt % 2]
                        for j in range(ND):
                            mm(S, po[:], w[:, j, :], xn[:, j, :], start=(j == 0), stop=False, tag=f"P_{wn}m{ci}_{nt}_{j}")
                        mm(S, po[:], bias[:, 128 * nt:128 * (nt + 1)], C.ones_r_bf[:, :Tc], start=False, stop=True, tag=f"P_{wn}b{ci}_{nt}")
                        act(S, dst[:, nt, :], po[:], AF.Identity, tag=f"P_{wn}e{ci}_{nt}")
                for tt in range(Tc // 128):
                    tile = (t0 + 128 * tt) // 128
                    v_op(S, "memset", vtok[:, tile, :].rearrange("p (h c) -> p h c", c=65)[:, :, 64:65], 1.0, tag=f"P_vone{ci}_{tt}")
                for half in range(2):
                    wv = wvl[0]
                    load_cast_w(S, dd["tb_wvT"], wv, D, 512, f"P_wv{ci}_{half}", colsl=(512 * half, 512 * (half + 1)))
                    for tt in range(Tc // 128):
                        tile = (t0 + 128 * tt) // 128
                        po = ps_o[tt % 2]
                        for j in range(ND):
                            mm(S, po[:], xn[:, j, 128 * tt:128 * (tt + 1)], wv[:, j, :], start=(j == 0), stop=False, tag=f"P_v{ci}_{half}_{tt}_{j}")
                        mm(S, po[:], ones_cb[:], bias_v[:, 512 * half:512 * (half + 1)], start=False, stop=True, tag=f"P_vb{ci}_{half}_{tt}")
                        for hh in range(8):
                            h = 8 * half + hh
                            v_op(S, "tensor_copy", vtok[:, tile, 65 * h:65 * h + 64], po[:, 64 * hh:64 * (hh + 1)], tag=f"P_ve{ci}_{half}_{tt}_{hh}")
                dma(S, "sp", fm_dram(q_d, t0, t0 + Tc), q_fm[:], tag=f"P_stq{ci}")
                for nm, src, frow in (("q", q_fm, fq_all), ("k", k_fm, fk_all)):
                    for j in range(ND):
                        act(S, qsq[:], src[:, j, :], AF.Square, tag=f"P_{nm}sq{ci}_{j}")
                        mm(S, ps_h2[:], hmask[:, j, :], qsq[:], start=(j == 0), stop=(j == ND - 1), tag=f"P_{nm}2{ci}_{j}")
                    v_op(S, "tensor_copy", h2row[:], ps_h2[:], tag=f"P_{nm}2c{ci}")
                    act(S, h2row[:], h2row[:], AF.Sqrt, tag=f"P_{nm}n{ci}")
                    v_op(S, "reciprocal", h2row[:], h2row[:], tag=f"P_{nm}ri{ci}")
                    v_op(S, "tensor_scalar", frow[:, t0:t0 + Tc], h2row[:], MN, 1.0, ALU.mult, ALU.min, tag=f"P_{nm}f{ci}")
                    if nm == "k":
                        v_op(S, "tensor_copy", k2c_c[:], ps_h2[:], tag=f"P_k2r{ci}")
                        v_op(S, "tensor_mul", h2row[:], frow[:, t0:t0 + Tc], frow[:, t0:t0 + Tc], tag=f"P_kf2{ci}")
                        v_op(S, "tensor_mul", k2c_c[:], k2c_c[:], h2row[:], tag=f"P_k2c{ci}")
                for h in range(NHEAD):
                    mm(S, ps_f[:], sel64[:, h, :], fk_all[:, t0:t0 + Tc], start=True, stop=True, tag=f"P_fkb{ci}_{h}")
                    v_op(S, "tensor_mul", kaug[0:64, h, t0:t0 + Tc], k_fm[64 * (h % 2):64 * (h % 2) + 64, h // 2, :], ps_f[:], tag=f"P_ka{ci}_{h}")
                    mm(S, ps_f[:], sel64[:, h, :], k2c_c[:], start=True, stop=True, tag=f"P_k2b{ci}_{h}")
                    v_op(S, "tensor_copy", kaug[64:65, h, t0:t0 + Tc], ps_f[0:1, :], tag=f"P_kr{ci}_{h}")

        with pscope(S) as pp:
            sb2 = lambda n, s, d=F32: pp.enter_context(nc.sbuf_tensor(f"{n}_{S.uid()}", list(s), d))
            ps2 = lambda n, s, d=F32: pp.enter_context(nc.psum_tensor(f"{n}_{S.uid()}", list(s), d))
            qaug = sb2("qaug", [128, NHEAD, Tc], BF16)
            q_fm = sb2("q_fmA", [128, ND, Tc], BF16)
            expS = sb2("expS", [128, T // 128, Tc], BF16)
            att_o = sb2("att_o", [128, ND, Tc], BF16)
            den = sb2("den", [1, Tc])
            otmp = sb2("otmp", [64, Tc], BF16)
            x1 = sb2("x1T", [128, ND, Tc])
            wsl = [sb2(f"wslA{i}", [128, ND, 128], BF16) for i in range(2)]
            rowsA = mk_ln_rows(S, Tc, "rTA", alloc=sb2)
            ps_S = [ps2("psA_S0", [128, Tc]), ps2("psA_S1", [128, Tc])]
            ps_av = ps2("psA_av", [128, Tc])
            ps_dn = ps2("psA_dn", [64, Tc])
            ps_f = ps2("psA_f", [64, Tc])
            ps_s, ps_ss = ps2("psA_s", [1, Tc]), ps2("psA_ss", [1, Tc])
            ps_bc = ps2("psA_bc", [128, Tc])
            for ci in range(nch):
                t0 = ci * Tc
                dma(S, "sp", q_fm[:], fm_dram(q_d, t0, t0 + Tc), tag=f"A_ldq{ci}")
                v_op(S, "memset", qaug[64:128, :, :], -1.0 / 8.0, tag=f"A_qm{ci}")
                for h in range(NHEAD):
                    mm(S, ps_f[:], sel64[:, h, :], fq_all[:, t0:t0 + Tc], start=True, stop=True, tag=f"A_fqb{ci}_{h}")
                    v_op(S, "scalar_tensor_tensor", qaug[0:64, h, :], ps_f[:], ITEMP, q_fm[64 * (h % 2):64 * (h % 2) + 64, h // 2, :], ALU.mult, ALU.mult, tag=f"A_qa{ci}_{h}")
                for h in range(NHEAD):
                    for kt in range(T // 128):
                        pS = ps_S[kt % 2]
                        mm(S, pS[:], kaug[0:65, h, 128 * kt:128 * (kt + 1)], qaug[0:65, h, :], start=True, stop=True, tag=f"A_S{ci}_{h}_{kt}")
                        act(S, expS[:, kt, :], pS[:], AF.Exp, tag=f"A_e{ci}_{h}_{kt}")
                    for kt in range(T // 128):
                        mm(S, ps_av[0:65, :], vtok[:, kt, 65 * h:65 * (h + 1)], expS[:, kt, :], start=(kt == 0), stop=(kt == T // 128 - 1), tag=f"A_av{ci}_{h}_{kt}")
                    v_op(S, "tensor_copy", den[:], ps_av[64:65, :], tag=f"A_dc{ci}_{h}")
                    v_op(S, "reciprocal", den[:], den[:], tag=f"A_dr{ci}_{h}")
                    act(S, otmp[:], ps_av[0:64, :], AF.Identity, tag=f"A_oc{ci}_{h}")
                    mm(S, ps_dn[:], C.ones_c_f32[0:1, 0:64], den[:], start=True, stop=True, tag=f"A_db{ci}_{h}")
                    v_op(S, "tensor_mul", att_o[64 * (h % 2):64 * (h % 2) + 64, h // 2, :], otmp[:], ps_dn[:], tag=f"A_om{ci}_{h}")
                dma(S, "sp", x1[:], fm_dram(y_d, t0, t0 + Tc), tag=f"A_ldy{ci}")
                for nt in range(ND):
                    w = wsl[nt % 2]
                    load_cast_w(S, dd["tb_woT"], w, D, 128, f"A_wo{ci}_{nt}", colsl=(128 * nt, 128 * (nt + 1)))
                    po = ps_S[nt % 2]
                    for j in range(ND):
                        mm(S, po[:], w[:, j, :], att_o[:, j, :], start=(j == 0), stop=False, tag=f"A_wom{ci}_{nt}_{j}")
                    mm(S, po[:], bias_o[:, 128 * nt:128 * (nt + 1)], C.ones_r_bf[:, :Tc], start=False, stop=True, tag=f"A_wob{ci}_{nt}")
                    v_op(S, "tensor_add", x1[:, nt, :], x1[:, nt, :], po[:], tag=f"A_res{ci}_{nt}")
                for j in range(ND):
                    act(S, qaug[:, 8 + j, :], x1[:, j, :], AF.Square, tag=f"A_sq{ci}_{j}")
                    mm(S, ps_s[:], C.ones_col_bf[:], qaug[:, 8 + j, :], start=(j == 0), stop=(j == ND - 1), tag=f"A_n2{ci}_{j}")
                act(S, rowsA["var"][:], ps_s[:], AF.Sqrt, tag=f"A_nn{ci}")
                v_op(S, "reciprocal", rowsA["rs"][:], rowsA["var"][:], tag=f"A_nr{ci}")
                v_op(S, "tensor_scalar", rowsA["m"][:], rowsA["rs"][:], MN, 1.0, ALU.mult, ALU.min, tag=f"A_nf{ci}")
                bcast_row(S, C, ps_bc[:], rowsA["m"][:], f"A_nb{ci}")
                for j in range(ND):
                    v_op(S, "tensor_mul", x1[:, j, :], x1[:, j, :], ps_bc[:], tag=f"A_px{ci}_{j}")
                dma(S, "sp", fm_dram(y_d, t0, t0 + Tc), x1[:], tag=f"A_stx1{ci}")

        pa.close()
        S.barrier()
        with pscope(S) as pp:
            sb2 = lambda n, s, d=F32: pp.enter_context(nc.sbuf_tensor(f"{n}_{S.uid()}", list(s), d))
            ps2 = lambda n, s, d=F32: pp.enter_context(nc.psum_tensor(f"{n}_{S.uid()}", list(s), d))
            x1 = sb2("x1B", [128, ND, Tc])
            xb = sb2("xBb", [128, ND, Tc], BF16)
            xq = sb2("xBq", [128, ND, Tc], BF16)
            xc = sb2("xBc", [128, ND, Tc])
            zn = sb2("znB", [128, ND, Tc], BF16)
            h_bf = sb2("h_bfB", [128, NF, Tc], BF16)
            x0 = sb2("x0B", [128, ND, Tc])
            wB = [sb2(f"wB{i}", [128, ND, 128], BF16) for i in range(2)]
            w2B = [sb2(f"w2B{i}", [128, 512], BF16) for i in range(2)]
            rowsB = mk_ln_rows(S, Tc, "rTB", alloc=sb2)
            bias_1 = sb2("bias_1B", [1, FF], BF16)
            dma(S, "sp", bias_1[:], bias1_d.ap(), tag="ld_b1d")
            ps_s, ps_ss = ps2("psB_s", [1, Tc]), ps2("psB_ss", [1, Tc])
            ps_bc = ps2("psB_bc", [128, Tc])
            ps_h = [ps2("psB_h0", [128, Tc]), ps2("psB_h1", [128, Tc])]
            ps_o = ps2("psB_o", [128, 2, Tc])
            for ci in range(nch):
                t0 = ci * Tc
                dma(S, "sp", x1[:], fm_dram(y_d, t0, t0 + Tc), tag=f"B_ldx{ci}")
                for j in range(ND):
                    v_op(S, "tensor_copy", xb[:, j, :], x1[:, j, :], tag=f"B_xb{ci}_{j}")
                    act(S, xq[:, j, :], x1[:, j, :], AF.Square, tag=f"B_xq{ci}_{j}")
                ln_stats(S, C, xb, xq, ND, Tc, ps_s, ps_ss, f"B_st{ci}")
                ln_rows(S, ps_s, ps_ss, rowsB, D, EPS_LN, f"B_r{ci}")
                bcast_row(S, C, ps_bc[:], rowsB["m"][:], f"B_bm{ci}")
                for j in range(ND):
                    v_op(S, "tensor_sub", xc[:, j, :], x1[:, j, :], ps_bc[:], tag=f"B_sub{ci}_{j}")
                bcast_row(S, C, ps_bc[:], rowsB["rs"][:], f"B_br{ci}")
                for j in range(ND):
                    v_op(S, "scalar_tensor_tensor", zn[:, j, :], xc[:, j, :], g2[:, j:j + 1], ps_bc[:], ALU.mult, ALU.mult, tag=f"B_zn{ci}_{j}")
                for mi in range(NF):
                    w = wB[mi % 2]
                    load_cast_w(S, dd["tb_w1T"], w, D, 128, f"B_w1{ci}_{mi}", colsl=(128 * mi, 128 * (mi + 1)))
                    ph_ = ps_h[mi % 2]
                    for j in range(ND):
                        mm(S, ph_[:], w[:, j, :], zn[:, j, :], start=(j == 0), stop=False, tag=f"B_mm{ci}_{mi}_{j}")
                    mm(S, ph_[:], bias_1[:, 128 * mi:128 * (mi + 1)], C.ones_r_bf[:, :Tc], start=False, stop=True, tag=f"B_bb{ci}_{mi}")
                    act(S, h_bf[:, mi, :], ph_[:], AF.Gelu, tag=f"B_g{ci}_{mi}")
                for half in range(4):
                    for mi in range(NF):
                        w2 = w2B[mi % 2]
                        load_cast_w(S, dd["tb_w2T"], w2, 128, 512, f"B_w2{ci}_{half}_{mi}", rowsl=(128 * mi, 128 * (mi + 1)), colsl=(256 * half, 256 * (half + 1)) if False else (512 * (half // 2), 512 * (half // 2 + 1)))
                        for dq in range(2):
                            di = 2 * half + dq
                            mm(S, ps_o[:, dq, :], w2[:, 128 * (di % 4):128 * (di % 4) + 128], h_bf[:, mi, :], start=(mi == 0), stop=False, tag=f"B_o{ci}_{half}_{mi}_{dq}")
                    for dq in range(2):
                        di = 2 * half + dq
                        mm(S, ps_o[:, dq, :], bias_2[:, 128 * di:128 * (di + 1)], C.ones_r_bf[:, :Tc], start=False, stop=True, tag=f"B_ob{ci}_{di}")
                        v_op(S, "tensor_add", x1[:, di, :], x1[:, di, :], ps_o[:, dq, :], tag=f"B_oa{ci}_{di}")
                for j in range(ND):
                    act(S, xq[:, j, :], x1[:, j, :], AF.Square, tag=f"B_sq2{ci}_{j}")
                    mm(S, ps_s[:], C.ones_col_bf[:], xq[:, j, :], start=(j == 0), stop=(j == ND - 1), tag=f"B_n2{ci}_{j}")
                act(S, rowsB["var"][:], ps_s[:], AF.Sqrt, tag=f"B_nn{ci}")
                v_op(S, "reciprocal", rowsB["rs"][:], rowsB["var"][:], tag=f"B_nr{ci}")
                v_op(S, "tensor_scalar", rowsB["m"][:], rowsB["rs"][:], MN, 1.0, ALU.mult, ALU.min, tag=f"B_nf{ci}")
                bcast_row(S, C, ps_bc[:], rowsB["m"][:], f"B_nb{ci}")
                dma(S, "sp", x0[:], fm_dram(x_in_h, t0, t0 + Tc), tag=f"B_ldx0{ci}")
                for j in range(ND):
                    v_op(S, "tensor_mul", x1[:, j, :], x1[:, j, :], ps_bc[:], tag=f"B_px{ci}_{j}")
                    v_op(S, "tensor_add", x1[:, j, :], x1[:, j, :], x0[:, j, :], tag=f"B_ax{ci}_{j}")
                dma(S, "sp", fm_dram(x_out_h, t0, t0 + Tc), x1[:], tag=f"B_stx{ci}")


def run_l1_attn_b(per_core, n_cores=NCORES, **kw):
    S = Prog("l1b")
    x_in = S.dram("x_in", [D, T], F32, kind="ExternalInput")
    x_out = S.dram("x_out", [D, T], F32, kind="ExternalOutput")
    C = Common(S)
    emit_l1_attn_b(S, C, x_in, x_out)
    res = S.run(per_core, n_cores, **kw)
    return [r["x_out"] for r in res.results], res


def emit_l0_attn_b(S, C, x_in_h, x_out_h):
    nc = S.nc
    dd = {}
    for nm, sh, dt in (("nb_Wq", [D, D], F32), ("nb_Wk", [D, D], F32), ("nb_Wv", [D, D], F32),
                       ("nb_bq", [1, D], F32), ("nb_bk", [1, D], F32), ("nb_bv", [1, D], F32),
                       ("nb_pat", [NHEAD, 3, 128, 128], F32)):
        dd[nm] = S.dram(nm, sh, dt, kind="ExternalInput")
    u0 = S.uid()
    att_d = S.dram(f"nb_att{u0}", [D, T], BF16, kind="Internal")
    cc_in = S.dram(f"nb_ccin{u0}", [1, 8], F32, kind="Internal")
    cc_out = S.dram(f"nb_ccout{u0}", [1, 8], F32, kind="Internal")
    SC = float(np.float32(HD ** -0.25))
    L64 = 64

    with pscope(S) as ph:
        sbl = lambda n, s, d=F32: ph.enter_context(nc.sbuf_tensor(f"{n}_{S.uid()}", list(s), d))
        q_fm = sbl("q_fmN", [128, ND, T], BF16)
        k_fm = sbl("k_fmN", [128, ND, T], BF16)
        vtok = sbl("vtokN", [128, T // 128, D], BF16)
        pat = sbl("patN", [128, NHEAD, 3, 128], BF16)
        dma(S, "pool", pat[:], dd["nb_pat"].ap().rearrange("h j p n -> p h j n"), tag="ld_pat")
        q_l = sbl("q_lN", [128, ND, L64], BF16)
        k_l = sbl("k_lN", [128, ND, L64], BF16)
        biasq = sbl("biasqN", [1, D], BF16)
        biask = sbl("biaskN", [1, D], BF16)
        biasv = sbl("biasvN", [1, D], BF16)
        rowf = sbl("rowfN", [1, D])
        for bn, dst, sc_ in (("nb_bq", biasq, SC), ("nb_bk", biask, SC), ("nb_bv", biasv, 1.0)):
            dma(S, "sp", rowf[:], dd[bn].ap(), tag=f"ld_{bn}")
            v_op(S, "tensor_scalar", dst[:], rowf[:], sc_, None, ALU.mult, tag=f"sc_{bn}")

        Tc = 512
        nch = T // Tc
        with pscope(S) as pp:
            sb2 = lambda n, s, d=F32: pp.enter_context(nc.sbuf_tensor(f"{n}_{S.uid()}", list(s), d))
            ps2 = lambda n, s, d=F32: pp.enter_context(nc.psum_tensor(f"{n}_{S.uid()}", list(s), d))
            x_sb = sb2("xN", [128, ND, Tc])
            xb = sb2("xNb", [128, ND, Tc], BF16)
            xc = sb2("xNc", [128, ND, Tc])
            xn = sb2("xnN", [128, ND, Tc], BF16)
            xnq = sb2("xnqN", [128, ND, Tc], BF16)
            sq = sb2("sqN", [128, Tc], BF16)
            poolf = sb2("poolfN", [128, ND, 16])
            wsl = [sb2(f"wslN{i}", [128, ND, 128], BF16) for i in range(2)]
            wvl = [sb2(f"wvlN{i}", [128, ND, 512], BF16) for i in range(1)]
            rows = mk_ln_rows(S, Tc, "rN", alloc=sb2)
            ps_s, ps_ss = ps2("psN_s", [1, Tc]), ps2("psN_ss", [1, Tc])
            ps_bc = ps2("psN_bc", [128, Tc])
            ps_o = [ps2("psN_o0", [128, Tc]), ps2("psN_o1", [128, Tc])]
            for ci in range(nch):
                t0 = ci * Tc
                dma(S, "sp", x_sb[:], fm_dram(x_in_h, t0, t0 + Tc), tag=f"N_ldx{ci}")
                for j in range(ND):
                    v_op(S, "tensor_copy", xb[:, j, :], x_sb[:, j, :], tag=f"N_xb{ci}_{j}")
                    mm(S, ps_s[:], C.ones_col_bf[:], xb[:, j, :], start=(j == 0), stop=(j == ND - 1), tag=f"N_s{ci}_{j}")
                    act(S, sq[:], x_sb[:, j, :], AF.Square, tag=f"N_xq{ci}_{j}")
                    mm(S, ps_ss[:], C.ones_col_bf[:], sq[:], start=(j == 0), stop=(j == ND - 1), tag=f"N_ss{ci}_{j}")
                ln_rows(S, ps_s, ps_ss, rows, D, EPS_LN, f"N_r{ci}")
                bcast_row(S, C, ps_bc[:], rows["m"][:], f"N_bm{ci}")
                for j in range(ND):
                    v_op(S, "tensor_sub", xc[:, j, :], x_sb[:, j, :], ps_bc[:], tag=f"N_sub{ci}_{j}")
                bcast_row(S, C, ps_bc[:], rows["rs"][:], f"N_br{ci}")
                for j in range(ND):
                    v_op(S, "tensor_mul", xn[:, j, :], xc[:, j, :], ps_bc[:], tag=f"N_xn{ci}_{j}")
                    v_op(S, "tensor_scalar", xnq[:, j, :], xn[:, j, :], SC, None, ALU.mult, tag=f"N_xnq{ci}_{j}")
                for dst, wn, bias in ((q_fm, "nb_Wq", biasq), (k_fm, "nb_Wk", biask)):
                    for nt in range(ND):
                        w = wsl[nt % 2]
                        load_cast_w(S, dd[wn], w, D, 128, f"N_{wn}{ci}_{nt}", colsl=(128 * nt, 128 * (nt + 1)))
                        po = ps_o[nt % 2]
                        for j in range(ND):
                            mm(S, po[:], w[:, j, :], xnq[:, j, :], start=(j == 0), stop=False, tag=f"N_{wn}m{ci}_{nt}_{j}")
                        mm(S, po[:], bias[:, 128 * nt:128 * (nt + 1)], C.ones_r_bf[:, :Tc], start=False, stop=True, tag=f"N_{wn}b{ci}_{nt}")
                        act(S, dst[:, nt, t0:t0 + Tc], po[:], AF.Identity, tag=f"N_{wn}e{ci}_{nt}")
                for half in range(2):
                    wv = wvl[0]
                    load_cast_w(S, dd["nb_Wv"], wv, D, 512, f"N_wv{ci}_{half}", colsl=(512 * half, 512 * (half + 1)))
                    for tt in range(Tc // 128):
                        tile = (t0 + 128 * tt) // 128
                        po = ps_o[tt % 2]
                        for j in range(ND):
                            mm(S, po[:], xn[:, j, 128 * tt:128 * (tt + 1)], wv[:, j, :], start=(j == 0), stop=False, tag=f"N_v{ci}_{half}_{tt}_{j}")
                        mm(S, po[:], C.ones_r_bf[0:1, 0:128] if False else C.ones_col_bf[:, 0:1].rearrange("p o -> o p") if False else C.ones_r_bf[0:1, 0:128], biasv[:, 512 * half:512 * (half + 1)], start=False, stop=True, tag=f"N_vb{ci}_{half}_{tt}")
                        act(S, vtok[:, tile, 512 * half:512 * (half + 1)], po[:], AF.Identity, tag=f"N_ve{ci}_{half}_{tt}")
                for src, dst in ((q_fm, q_l), (k_fm, k_l)):
                    for j in range(ND):
                        S.op("dve", lambda h, _o=poolf[:, j, :], _i=src[:, j, t0:t0 + Tc].rearrange("p (a b) -> p a b", b=32):
                             h.tensor_reduce(_o, _i, AX.X, ALU.add),
                             reads=[src[:, j, t0:t0 + Tc]], writes=[poolf[:, j, :]], tag=f"N_pool{ci}_{j}")
                        v_op(S, "tensor_scalar", dst[:, j, 16 * ci:16 * (ci + 1)], poolf[:, j, :], 1.0 / 32.0, None, ALU.mult, tag=f"N_poolsc{ci}_{j}")

        inv_bf = sbl("inv_bf", [64, NHEAD, L64], BF16)
        with pscope(S) as pp:
            sb2 = lambda n, s, d=F32: pp.enter_context(nc.sbuf_tensor(f"{n}_{S.uid()}", list(s), d))
            ps2 = lambda n, s, d=F32: pp.enter_context(nc.psum_tensor(f"{n}_{S.uid()}", list(s), d))
            K2 = sb2("K2", [64, NHEAD, L64])
            K2t = sb2("K2t", [64, NHEAD, L64])
            Vt_ = sb2("Vt_", [64, NHEAD, L64])
            V_ = sb2("V_", [64, NHEAD, L64])
            colr = sb2("colr", [1, NHEAD * L64])
            lmax = sb2("lmax", [1, 8])
            i7 = sb2("i7", [64, L64])
            i15 = sb2("i15", [64, L64])
            i13 = sb2("i13", [64, L64])
            tA = sb2("tA", [64, L64])
            tAt = sb2("tAt", [64, L64])
            tB = sb2("tB", [64, L64])
            tBt = sb2("tBt", [64, L64])
            nmax = sb2("nmax", [64, 1])
            nsum = sb2("nsum", [64, 1])
            ps_k = [ps2(f"psK{i}", [64, 512]) for i in range(2)]
            ps_c = ps2("psKc", [1, 512])
            v_op(S, "tensor_scalar", i7[:], C.ident_f32[0:64, 0:64], 7.0, None, ALU.mult, tag="mk_i7")
            v_op(S, "tensor_scalar", i15[:], C.ident_f32[0:64, 0:64], 15.0, None, ALU.mult, tag="mk_i15")
            v_op(S, "tensor_scalar", i13[:], C.ident_f32[0:64, 0:64], 13.0, None, ALU.mult, tag="mk_i13")
            for h in range(NHEAD):
                hp, hj = 64 * (h % 2), h // 2
                mm(S, ps_k[0][:, 0:L64], q_l[hp:hp + 64, hj, :], k_l[hp:hp + 64, hj, :], start=True, stop=True, tag=f"K2_{h}")
                v_op(S, "reduce_max", nmax[:], ps_k[0][:, 0:L64], AX.X, negate=True, tag=f"K2m_{h}")
                act(S, K2[:, h, :], ps_k[0][:, 0:L64], AF.Exp, bias=nmax[:], accum_out=nsum[:], tag=f"K2e_{h}")
                v_op(S, "reciprocal", nsum[:], nsum[:], tag=f"K2r_{h}")
                v_op(S, "tensor_scalar", K2[:, h, :], K2[:, h, :], nsum[:], None, ALU.mult, tag=f"K2n_{h}")
                mm(S, ps_c[:, 0:L64], C.ones_col_f32[0:64, :], K2[:, h, :], start=True, stop=True, tag=f"K2c_{h}")
                v_op(S, "tensor_copy", colr[:, L64 * h:L64 * (h + 1)], ps_c[:, 0:L64], tag=f"K2cc_{h}")
                tr_op(S, ps_k[1][:, 0:L64], K2[:, h, :], C.ident_f32[0:64, 0:64], f"K2t_{h}")
                v_op(S, "tensor_copy", K2t[:, h, :], ps_k[1][:, 0:L64], tag=f"K2te_{h}")
            v_op(S, "reduce_max", lmax[0:1, 0:1], colr[:], AX.X, tag="gmax_l")
            v_op(S, "tensor_copy", lmax[0:1, 1:8], lmax[0:1, 0:1].to_broadcast((1, 7)) if False else lmax[0:1, 0:1], tag="gmax_pad") if False else None
            v_op(S, "memset", lmax[0:1, 1:8], 0.0, tag="gmax_z")
            dma(S, "sp", cc_in.ap(), lmax[:], tag="cc_st")
            S.op("pool", lambda h: h.collective_compute(
                "AllReduce", ALU.max, replica_groups=[list(range(NCORES))],
                ins=[cc_in[:]], outs=[cc_out[:]]),
                reads=[cc_in.ap()], writes=[cc_out.ap()], tag="cc_allreduce")
            dma(S, "sp", lmax[:], cc_out.ap(), tag="cc_ld")
            v_op(S, "reciprocal", lmax[0:1, 0:1], lmax[0:1, 0:1], tag="gmax_inv")
            mm(S, ps_c[:, 0:L64] if False else ps_k[0][:, 0:1], C.ones_c_f32[0:1, 0:64], lmax[0:1, 0:1], start=True, stop=True, tag="ginv_b")
            ginv = sb2("ginv", [64, 1])
            v_op(S, "tensor_copy", ginv[:], ps_k[0][0:64, 0:1], tag="ginv_c")
            for h in range(NHEAD):
                v_op(S, "tensor_scalar", V_[:, h, :], K2t[:, h, :], ginv[:], None, ALU.mult, tag=f"V0_{h}")
                v_op(S, "tensor_scalar", Vt_[:, h, :], K2[:, h, :], ginv[:], None, ALU.mult, tag=f"V0t_{h}")
                for it in range(6):
                    mm(S, ps_k[0][:, 0:L64], K2t[:, h, :], V_[:, h, :], start=True, stop=True, tag=f"NwM_{h}_{it}")
                    mm(S, ps_k[1][:, 0:L64], V_[:, h, :], K2t[:, h, :], start=True, stop=True, tag=f"NwMt_{h}_{it}")
                    v_op(S, "tensor_copy", tA[:], ps_k[0][:, 0:L64], tag=f"NwMc_{h}_{it}")
                    v_op(S, "tensor_copy", tAt[:], ps_k[1][:, 0:L64], tag=f"NwMtc_{h}_{it}")
                    v_op(S, "scalar_tensor_tensor", tB[:], tA[:], -1.0, i7[:], ALU.mult, ALU.add, tag=f"NwB_{h}_{it}")
                    v_op(S, "scalar_tensor_tensor", tBt[:], tAt[:], -1.0, i7[:], ALU.mult, ALU.add, tag=f"NwBt_{h}_{it}")
                    mm(S, ps_k[0][:, 0:L64], tAt[:], tB[:], start=True, stop=True, tag=f"NwMB_{h}_{it}")
                    mm(S, ps_k[1][:, 0:L64], tB[:], tAt[:], start=True, stop=True, tag=f"NwMBt_{h}_{it}")
                    v_op(S, "scalar_tensor_tensor", tB[:], ps_k[0][:, 0:L64], -1.0, i15[:], ALU.mult, ALU.add, tag=f"NwC_{h}_{it}")
                    v_op(S, "scalar_tensor_tensor", tBt[:], ps_k[1][:, 0:L64], -1.0, i15[:], ALU.mult, ALU.add, tag=f"NwCt_{h}_{it}")
                    mm(S, ps_k[0][:, 0:L64], tAt[:], tB[:], start=True, stop=True, tag=f"NwMC_{h}_{it}")
                    mm(S, ps_k[1][:, 0:L64], tBt[:] if False else tB[:], tAt[:], start=True, stop=True, tag=f"NwMCt_{h}_{it}")
                    v_op(S, "scalar_tensor_tensor", tB[:], ps_k[0][:, 0:L64], -1.0, i13[:], ALU.mult, ALU.add, tag=f"NwD_{h}_{it}")
                    v_op(S, "scalar_tensor_tensor", tBt[:], ps_k[1][:, 0:L64], -1.0, i13[:], ALU.mult, ALU.add, tag=f"NwDt_{h}_{it}")
                    mm(S, ps_k[0][:, 0:L64], Vt_[:, h, :], tB[:], start=True, stop=True, tag=f"NwV_{h}_{it}")
                    mm(S, ps_k[1][:, 0:L64], tB[:], Vt_[:, h, :], start=True, stop=True, tag=f"NwVt_{h}_{it}")
                    v_op(S, "tensor_scalar", V_[:, h, :], ps_k[0][:, 0:L64], 0.25, None, ALU.mult, tag=f"NwVc_{h}_{it}")
                    v_op(S, "tensor_scalar", Vt_[:, h, :], ps_k[1][:, 0:L64], 0.25, None, ALU.mult, tag=f"NwVtc_{h}_{it}")
                v_op(S, "tensor_copy", inv_bf[:, h, :], V_[:, h, :], tag=f"NwF_{h}")

        with pscope(S) as pp:
            sb2 = lambda n, s, d=F32: pp.enter_context(nc.sbuf_tensor(f"{n}_{S.uid()}", list(s), d))
            ps2 = lambda n, s, d=F32: pp.enter_context(nc.psum_tensor(f"{n}_{S.uid()}", list(s), d))
            K1n = [sb2(f"K1n{i}", [128, L64], BF16) for i in range(2)]
            K1t = [sb2(f"K1t{i}", [64, T], BF16) for i in range(2)]
            K3raw = [sb2(f"K3raw{i}", [64, T]) for i in range(2)]
            K3bf = [sb2(f"K3bf{i}", [64, T], BF16) for i in range(2)]
            K3t = [sb2(f"K3t{i}", [128, T // 128, L64], BF16) for i in range(2)]
            B3 = [sb2(f"B3{i}", [64, L64], BF16) for i in range(2)]
            at_t = [sb2(f"at_t{i}", [64, T], BF16) for i in range(2)]
            ctx_bf = [sb2(f"ctx_bf{i}", [64, Tc], BF16) for i in range(2)]
            nmax = [sb2(f"nmaxH{i}", [128, 1]) for i in range(2)]
            nsum = [sb2(f"nsumH{i}", [128, 1]) for i in range(2)]
            rinv3 = [sb2(f"rinv3{i}", [64, 1]) for i in range(2)]
            ps_a = [ps2(f"psH{i}", [128, 512]) for i in range(4)]
            ps_b3 = [ps2(f"psHb3{i}", [64, L64]) for i in range(2)]
            ps_t = [ps2(f"psHt{i}", [128, 128], BF16) for i in range(2)]

            def head_stages(h):
                par = h % 2
                K1n_, K1t_, K3raw_, K3bf_, K3t_, B3_, at_t_, ctx_bf_ = (
                    K1n[par], K1t[par], K3raw[par], K3bf[par], K3t[par], B3[par], at_t[par], ctx_bf[par])
                nmax_, nsum_, rinv3_ = nmax[par], nsum[par], rinv3[par]
                psa = [ps_a[2 * par], ps_a[2 * par + 1]]
                psb3 = ps_b3[par]
                pst = ps_t[par]
                hp, hj = 64 * (h % 2), h // 2
                for tt in range(T // 128):
                    pk = psa[tt % 2]
                    mm(S, pk[:, 0:L64], q_fm[hp:hp + 64, hj, 128 * tt:128 * (tt + 1)], k_l[hp:hp + 64, hj, :], start=True, stop=True, tag=f"H_K1_{h}_{tt}")
                    v_op(S, "reduce_max", nmax_[:], pk[:, 0:L64], AX.X, negate=True, tag=f"H_K1m_{h}_{tt}")
                    act(S, K1n_[:], pk[:, 0:L64], AF.Exp, bias=nmax_[:], accum_out=nsum_[:], tag=f"H_K1e_{h}_{tt}")
                    v_op(S, "reciprocal", nsum_[:], nsum_[:], tag=f"H_K1r_{h}_{tt}")
                    v_op(S, "tensor_scalar", K1n_[:], K1n_[:], nsum_[:], None, ALU.mult, tag=f"H_K1n_{h}_{tt}")
                    tr_op(S, pst[0:64, :], K1n_[:], C.ident_bf[:], f"H_K1t_{h}_{tt}")
                    v_op(S, "tensor_copy", K1t_[:, 128 * tt:128 * (tt + 1)], pst[0:64, :], tag=f"H_K1te_{h}_{tt}")
                    yield
                for cj in range(nch):
                    mm(S, psa[cj % 2][0:64, :], q_l[hp:hp + 64, hj, :], k_fm[hp:hp + 64, hj, 512 * cj:512 * (cj + 1)], start=True, stop=True, tag=f"H_K3_{h}_{cj}")
                    v_op(S, "tensor_copy", K3raw_[:, 512 * cj:512 * (cj + 1)], psa[cj % 2][0:64, :], tag=f"H_K3c_{h}_{cj}")
                    yield
                v_op(S, "reduce_max", rinv3_[:], K3raw_[:], AX.X, negate=True, tag=f"H_K3m_{h}")
                act(S, K3bf_[:], K3raw_[:], AF.Exp, bias=rinv3_[:], accum_out=rinv3_[:], tag=f"H_K3e_{h}")
                v_op(S, "reciprocal", rinv3_[:], rinv3_[:], tag=f"H_K3r_{h}")
                yield
                for tt in range(T // 128):
                    tr_op(S, pst[:, 0:64], K3bf_[:, 128 * tt:128 * (tt + 1)], C.ident_bf[0:64, 0:64], f"H_K3t_{h}_{tt}")
                    v_op(S, "tensor_copy", K3t_[:, tt, :], pst[:, 0:64], tag=f"H_K3te_{h}_{tt}")
                    if tt % 4 == 3:
                        yield
                for tt in range(T // 128):
                    mm(S, psb3[:], K3t_[:, tt, :], vtok[:, tt, 64 * h:64 * (h + 1)], start=(tt == 0), stop=(tt == T // 128 - 1), tag=f"H_B3_{h}_{tt}")
                v_op(S, "tensor_scalar", B3_[:], psb3[:], rinv3_[:], None, ALU.mult, tag=f"H_B3e_{h}")
                yield
                for cj in range(nch):
                    mm(S, psa[cj % 2][0:64, :], inv_bf[:, h, :], K1t_[:, 512 * cj:512 * (cj + 1)], start=True, stop=True, tag=f"H_at_{h}_{cj}")
                    v_op(S, "tensor_copy", at_t_[:, 512 * cj:512 * (cj + 1)], psa[cj % 2][0:64, :], tag=f"H_ate_{h}_{cj}")
                    yield
                for cj in range(nch):
                    pc = psa[cj % 2]
                    mm(S, pc[0:64, :], B3_[:], at_t_[:, 512 * cj:512 * (cj + 1)], start=True, stop=True, tag=f"H_ctx_{h}_{cj}")
                    for sub in range(4):
                        st0 = 512 * cj + 128 * sub
                        for rel in (-1, 0, 1):
                            tl = st0 // 128 + rel
                            if 0 <= tl < T // 128:
                                S.op("pe", lambda hh, _o=pc[0:64, 128 * sub:128 * (sub + 1)], _l=vtok[:, tl, 64 * h:64 * (h + 1)], _r=pat[:, h, rel + 1, :]:
                                     hh.matmul(_o, _l, _r, start=False, stop=True, skip_group_check=True),
                                     reads=[vtok[:, tl, 64 * h:64 * (h + 1)], pat[:, h, rel + 1, :]], writes=[pc[0:64, 128 * sub:128 * (sub + 1)]], tag=f"H_cv_{h}_{cj}_{sub}_{rel}")
                    v_op(S, "tensor_copy", ctx_bf_[:], pc[0:64, :], tag=f"H_ce_{h}_{cj}")
                    dma(S, "sp", att_d.ap()[64 * h:64 * (h + 1), 512 * cj:512 * (cj + 1)], ctx_bf_[:], tag=f"H_cst_{h}_{cj}")
                    yield

            for hp0 in range(0, NHEAD, 2):
                ga, gb = head_stages(hp0), head_stages(hp0 + 1)
                act_a = act_b = True
                while act_a or act_b:
                    if act_a:
                        try:
                            next(ga)
                        except StopIteration:
                            act_a = False
                    if act_b:
                        try:
                            next(gb)
                        except StopIteration:
                            act_b = False

        with pscope(S) as pp:
            sb2 = lambda n, s, d=F32: pp.enter_context(nc.sbuf_tensor(f"{n}_{S.uid()}", list(s), d))
            x_sb = sb2("xR", [128, ND, Tc])
            a_sb = sb2("aR", [128, ND, Tc], BF16)
            for ci in range(nch):
                t0 = ci * Tc
                dma(S, "sp", x_sb[:], fm_dram(x_in_h, t0, t0 + Tc), tag=f"R_ldx{ci}")
                dma(S, "sp", a_sb[:], fm_dram(att_d, t0, t0 + Tc), tag=f"R_lda{ci}")
                for j in range(ND):
                    v_op(S, "tensor_add", x_sb[:, j, :], x_sb[:, j, :], a_sb[:, j, :], tag=f"R_add{ci}_{j}")
                dma(S, "sp", fm_dram(x_out_h, t0, t0 + Tc), x_sb[:], tag=f"R_stx{ci}")


def run_l0_attn_b(per_core, n_cores=NCORES, **kw):
    S = Prog("l0b")
    x_in = S.dram("x_in", [D, T], F32, kind="ExternalInput")
    x_out = S.dram("x_out", [D, T], F32, kind="ExternalOutput")
    C = Common(S)
    emit_l0_attn_b(S, C, x_in, x_out)
    res = S.run(per_core, n_cores, **kw)
    return [r["x_out"] for r in res.results], res


def emit_l0_attn_a(S, C, x_in_h, x_out_h):
    nc = S.nc
    dd = {}
    for nm, sh in (("m_w_in", [D, 2 * DI]), ("m_norm_scale", [128, ND]), ("m_conv_w", [128, NDI, 4]),
                   ("m_conv_b", [128, NDI]), ("m_A_log", [128, NDI, NSTATE]), ("m_D", [128, NDI]),
                   ("m_wx_delta", [DI, 64]), ("m_wx_B", [DI, NSTATE]), ("m_wx_C", [DI, NSTATE]),
                   ("m_w_dt", [64, DI]), ("m_b_dt", [128, NDI]), ("m_w_out", [DI, D])):
        dd[nm] = S.dram(nm, sh, F32, kind="ExternalInput")
    u0 = S.uid()
    y_d = S.dram(f"m_y{u0}", [D, T], F32, kind="Internal")
    delta_d = S.dram(f"m_delta{u0}", [DI, T], F32, kind="Internal")
    xsc_d = S.dram(f"m_xsc{u0}", [DI, T], F32, kind="Internal")
    res_d = S.dram(f"m_res{u0}", [DI, T], F32, kind="Internal")
    oact_d = S.dram(f"m_oact{u0}", [DI, T], BF16, kind="Internal")
    Tc = 512
    nch = T // Tc

    with pscope(S) as ph:
        sbl = lambda n, s, d=F32: ph.enter_context(nc.sbuf_tensor(f"{n}_{S.uid()}", list(s), d))
        norm_scale = sbl("m_ns", [128, ND]); dma(S, "sp", norm_scale[:], dd["m_norm_scale"].ap(), tag="ld_mns")
        conv_w = sbl("m_cw", [128, NDI, 4]); dma(S, "sp", conv_w[:], dd["m_conv_w"].ap(), tag="ld_mcw")
        conv_b = sbl("m_cb", [128, NDI]); dma(S, "sp", conv_b[:], dd["m_conv_b"].ap(), tag="ld_mcb")
        D_sb = sbl("m_Dv", [128, NDI]); dma(S, "sp", D_sb[:], dd["m_D"].ap(), tag="ld_mD")
        b_dt = sbl("m_bdt", [128, NDI]); dma(S, "sp", b_dt[:], dd["m_b_dt"].ap(), tag="ld_mbdt")
        A_sb = sbl("m_A", [128, NDI, NSTATE])
        dma(S, "sp", A_sb[:], dd["m_A_log"].ap(), tag="ld_mA")
        act(S, A_sb[:], A_sb[:], AF.Exp, tag="mA_exp")
        v_op(S, "tensor_scalar", A_sb[:], A_sb[:], -1.0, None, ALU.mult, tag="mA_neg")
        wxd = sbl("m_wxd", [128, NDI, 64], BF16)
        dma(S, "pool", wxd[:], dd["m_wx_delta"].ap().rearrange("(a p) n -> p a n", p=128), tag="ld_wxd")
        wxB = sbl("m_wxB", [128, NDI, NSTATE], BF16)
        dma(S, "pool", wxB[:], dd["m_wx_B"].ap().rearrange("(a p) n -> p a n", p=128), tag="ld_wxB")
        wxC = sbl("m_wxC", [128, NDI, NSTATE], BF16)
        dma(S, "pool", wxC[:], dd["m_wx_C"].ap().rearrange("(a p) n -> p a n", p=128), tag="ld_wxC")
        w_dt = sbl("m_wdt", [64, DI], BF16)
        dma(S, "pool", w_dt[:], dd["m_w_dt"].ap(), tag="ld_wdt")
        B_all = sbl("m_Ball", [NSTATE, T], BF16)
        C_all = sbl("m_Call", [NSTATE, T], BF16)
        d_sel = S.dram("c_sel128", [NHEAD, NHEAD, 128], BF16, kind="ExternalInput")
        sel128 = sbl("m_sel", [NHEAD, NHEAD, 128], BF16)
        dma(S, "sp", sel128[:], d_sel.ap().rearrange("a p n -> p a n"), tag="ld_msel")

        with pscope(S) as pp:
            sb2 = lambda n, s, d=F32: pp.enter_context(nc.sbuf_tensor(f"{n}_{S.uid()}", list(s), d))
            ps2 = lambda n, s, d=F32: pp.enter_context(nc.psum_tensor(f"{n}_{S.uid()}", list(s), d))
            x_sb = sb2("xM", [128, ND, Tc])
            xh = sb2("xhM", [128, ND, Tc], BF16)
            xbq = sb2("xbqM", [128, Tc], BF16)
            rows = mk_ln_rows(S, Tc, "rM", eps2=True, alloc=sb2)
            xs_sb = sb2("xsM", [128, NDI, Tc])
            xs_ext = sb2("xseM", [128, NDI, Tc + 4])
            prev3 = sb2("p3M", [128, NDI, 4])
            xsb_bf = sb2("xsbM", [128, NDI, Tc], BF16)
            resb = [sb2(f"resM{i}", [128, Tc]) for i in range(2)]
            din_bf = sb2("dinM", [64, Tc], BF16)
            dbuf = [sb2(f"dbM{i}", [128, Tc]) for i in range(2)]
            obuf = [sb2(f"obM{i}", [128, Tc]) for i in range(2)]
            wsl = [sb2(f"wslM{i}", [128, ND, 128], BF16) for i in range(2)]
            ps_s, ps_ss = ps2("psM_s", [1, Tc]), ps2("psM_ss", [1, Tc])
            ps_bc = ps2("psM_bc", [128, Tc])
            ps_o = [ps2("psM_o0", [128, Tc]), ps2("psM_o1", [128, Tc])]
            ps_p = ps2("psM_p", [64, Tc])
            ps_n = ps2("psM_n", [NSTATE, 2 * Tc])
            v_op(S, "memset", prev3[:], 0.0, tag="m_p3z")
            for ci in range(nch):
                t0 = ci * Tc
                dma(S, "sp", x_sb[:], fm_dram(x_in_h, t0, t0 + Tc), tag=f"M_ldx{ci}")
                for j in range(ND):
                    v_op(S, "tensor_copy", xh[:, j, :], x_sb[:, j, :], tag=f"M_xb{ci}_{j}")
                    mm(S, ps_s[:], C.ones_col_bf[:], xh[:, j, :], start=(j == 0), stop=(j == ND - 1), tag=f"M_s{ci}_{j}")
                    act(S, xbq[:], x_sb[:, j, :], AF.Square, tag=f"M_xq{ci}_{j}")
                    mm(S, ps_ss[:], C.ones_col_bf[:], xbq[:], start=(j == 0), stop=(j == ND - 1), tag=f"M_ss{ci}_{j}")
                ln_rows(S, ps_s, ps_ss, rows, D, EPS_LN, f"M_r{ci}", eps2=EPS_LN)
                bcast_row(S, C, ps_bc[:], rows["m"][:], f"M_bm{ci}")
                for j in range(ND):
                    v_op(S, "tensor_sub", x_sb[:, j, :], x_sb[:, j, :], ps_bc[:], tag=f"M_sub{ci}_{j}")
                bcast_row(S, C, ps_bc[:], rows["rs2"][:], f"M_br{ci}")
                for j in range(ND):
                    v_op(S, "scalar_tensor_tensor", xh[:, j, :], x_sb[:, j, :], norm_scale[:, j:j + 1], ps_bc[:], ALU.mult, ALU.mult, tag=f"M_xh{ci}_{j}")
                bcast_row(S, C, ps_bc[:], rows["rs"][:], f"M_bro{ci}")
                for j in range(ND):
                    v_op(S, "tensor_mul", x_sb[:, j, :], x_sb[:, j, :], ps_bc[:], tag=f"M_y{ci}_{j}")
                dma(S, "sp", fm_dram(y_d, t0, t0 + Tc), x_sb[:], tag=f"M_sty{ci}")
                for mt in range(2 * NDI):
                    w = wsl[mt % 2]
                    load_cast_w(S, dd["m_w_in"], w, D, 128, f"M_wi{ci}_{mt}", colsl=(128 * mt, 128 * (mt + 1)))
                    po = ps_o[mt % 2]
                    for j in range(ND):
                        mm(S, po[:], w[:, j, :], xh[:, j, :], start=(j == 0), stop=(j == ND - 1), tag=f"M_h4{ci}_{mt}_{j}")
                    if mt < NDI:
                        act(S, xs_sb[:, mt, :], po[:], AF.Identity, tag=f"M_xse{ci}_{mt}")
                    else:
                        rb = resb[mt % 2]
                        act(S, rb[:], po[:], AF.Silu, tag=f"M_rese{ci}_{mt}")
                        dma(S, "sp", res_d.ap()[128 * (mt - NDI):128 * (mt - NDI + 1), t0:t0 + Tc], rb[:], tag=f"M_resst{ci}_{mt}")
                for c in range(NDI):
                    v_op(S, "tensor_copy", xs_ext[:, c, 0:4], prev3[:, c, :], tag=f"M_cp{ci}_{c}")
                    v_op(S, "tensor_copy", xs_ext[:, c, 4:4 + Tc], xs_sb[:, c, :], tag=f"M_ce{ci}_{c}")
                    v_op(S, "tensor_copy", prev3[:, c, :], xs_ext[:, c, Tc:Tc + 4], tag=f"M_cu{ci}_{c}")
                    v_op(S, "tensor_scalar", xs_sb[:, c, :], xs_ext[:, c, 1:1 + Tc], conv_w[:, c, 0:1], None, ALU.mult, tag=f"M_cv0{ci}_{c}")
                    for tap in range(1, 4):
                        v_op(S, "scalar_tensor_tensor", xs_sb[:, c, :], xs_ext[:, c, 1 + tap:1 + tap + Tc], conv_w[:, c, tap:tap + 1], xs_sb[:, c, :], ALU.mult, ALU.add, tag=f"M_cv{tap}{ci}_{c}")
                    act(S, xs_sb[:, c, :], xs_sb[:, c, :], AF.Silu, bias=conv_b[:, c:c + 1], tag=f"M_cs{ci}_{c}")
                    v_op(S, "tensor_copy", xsb_bf[:, c, :], xs_sb[:, c, :], tag=f"M_cb{ci}_{c}")
                dma(S, "sp", xsc_d.ap()[:, t0:t0 + Tc].rearrange("(a p) t -> p a t", p=128), xs_sb[:], tag=f"M_xsst{ci}")
                for j in range(NDI):
                    mm(S, ps_p[:], wxd[:, j, :], xsb_bf[:, j, :], start=(j == 0), stop=(j == NDI - 1), tag=f"M_pd{ci}_{j}")
                for j in range(NDI):
                    mm(S, ps_n[:, 0:Tc], wxB[:, j, :], xsb_bf[:, j, :], start=(j == 0), stop=(j == NDI - 1), tag=f"M_pB{ci}_{j}")
                for j in range(NDI):
                    mm(S, ps_n[:, Tc:2 * Tc], wxC[:, j, :], xsb_bf[:, j, :], start=(j == 0), stop=(j == NDI - 1), tag=f"M_pC{ci}_{j}")
                v_op(S, "tensor_copy", din_bf[:], ps_p[:], tag=f"M_dine{ci}")
                v_op(S, "tensor_copy", B_all[:, t0:t0 + Tc], ps_n[:, 0:Tc], tag=f"M_Be{ci}")
                v_op(S, "tensor_copy", C_all[:, t0:t0 + Tc], ps_n[:, Tc:2 * Tc], tag=f"M_Ce{ci}")
                for c in range(NDI):
                    po = ps_o[c % 2]
                    mm(S, po[:], w_dt[:, 128 * c:128 * (c + 1)], din_bf[:], start=True, stop=True, tag=f"M_dt{ci}_{c}")
                    db = dbuf[c % 2]
                    act(S, db[:], po[:], AF.Exp, bias=b_dt[:, c:c + 1], tag=f"M_de{ci}_{c}")
                    act(S, db[:], db[:], AF.Ln, bias=1.0, tag=f"M_dl{ci}_{c}")
                    dma(S, "sp", delta_d.ap()[128 * c:128 * (c + 1), t0:t0 + Tc], db[:], tag=f"M_dst{ci}_{c}")

        if getattr(S, "l0a_mid_hook", None) is not None:
            S.l0a_mid_hook(lambda n, s, d=BF16: ph.enter_context(nc.sbuf_tensor(n, list(s), d)))
        wo_sb = sbl("wo_sb", [128, NDI, D], BF16)
        for kt in range(NDI):
            dma(S, "pool", wo_sb[:, kt, :], dd["m_w_out"].ap()[128 * kt:128 * (kt + 1), :], tag=f"ld_wo{kt}")

        TH = T // 2
        with pscope(S) as pp:
            sb2 = lambda n, s, d=F32: pp.enter_context(nc.sbuf_tensor(f"{n}_{S.uid()}", list(s), d))
            ps2 = lambda n, s, d=F32: pp.enter_context(nc.psum_tensor(f"{n}_{S.uid()}", list(s), d))
            B_bc = sb2("B_bc", [128, NSTATE, TH], BF16)
            C_bc = sb2("C_bc", [128, NSTATE, TH], BF16)
            carries = sb2("carr", [128, NDI, NSTATE])
            ps_w = ps2("psB_w", [128, 2048])
            delta_c = [sb2(f"delta_c{i}", [128, TH]) for i in range(2)]
            xs_c = [sb2(f"xs_cB{i}", [128, TH]) for i in range(2)]
            res_c = [sb2(f"res_cB{i}", [128, TH]) for i in range(2)]
            u_c = [sb2(f"u_cB{i}", [128, TH], BF16) for i in range(2)]
            dA = [sb2(f"dA_B{i}", [128, TH]) for i in range(2)]
            dBx = [sb2(f"dBx_B{i}", [128, TH], BF16) for i in range(2)]
            s_f = sb2("s_fB", [128, TH], BF16)
            y_c = sb2("y_cB", [128, TH])
            p_f = [sb2(f"p_fB{i}", [128, TH], BF16) for i in range(2)]
            ob = sb2("obB", [128, TH], BF16)
            ps_y = pp.enter_context(nc.psum_tensor(f"psB_y_{S.uid()}", [128, TH], F32))
            v_op(S, "memset", carries[:], 0.0, tag="MB_cz")
            for half in range(2):
                h0 = half * TH
                for n in range(NSTATE):
                    for cj in range(TH // 512):
                        mm(S, ps_w[:, 512 * cj:512 * (cj + 1)], sel128[:, n, :], B_all[:, h0 + 512 * cj:h0 + 512 * (cj + 1)], start=True, stop=True, tag=f"MB_b{half}_{n}_{cj}")
                    v_op(S, "tensor_copy", B_bc[:, n, :], ps_w[:, 0:TH], tag=f"MB_be{half}_{n}")
                    for cj in range(TH // 512):
                        mm(S, ps_w[:, 512 * cj:512 * (cj + 1)], sel128[:, n, :], C_all[:, h0 + 512 * cj:h0 + 512 * (cj + 1)], start=True, stop=True, tag=f"MC_b{half}_{n}_{cj}")
                    v_op(S, "tensor_copy", C_bc[:, n, :], ps_w[:, 0:TH], tag=f"MC_be{half}_{n}")
                for c in range(NDI):
                    cb = c % 2
                    dma(S, "sp", delta_c[cb][:], delta_d.ap()[128 * c:128 * (c + 1), h0:h0 + TH], tag=f"MB_ldd{half}_{c}")
                    dma(S, "sp", xs_c[cb][:], xsc_d.ap()[128 * c:128 * (c + 1), h0:h0 + TH], tag=f"MB_ldxs{half}_{c}")
                    dma(S, "sp", res_c[cb][:], res_d.ap()[128 * c:128 * (c + 1), h0:h0 + TH], tag=f"MB_ldr{half}_{c}")
                    v_op(S, "tensor_mul", u_c[cb][:], delta_c[cb][:], xs_c[cb][:], tag=f"MB_u{half}_{c}")
                    for n in range(NSTATE):
                        da = dA[n % 2]
                        act(S, da[:], delta_c[cb][:], AF.Exp, scale=A_sb[:, c, n:n + 1], tag=f"MB_dA{half}_{c}_{n}")
                        db = dBx[n % 2]
                        v_op(S, "tensor_mul", db[:], u_c[cb][:], B_bc[:, n, :], tag=f"MB_dBx{half}_{c}_{n}")
                        S.op("dve", lambda h, _s=s_f[:], _a=da[:], _b=db[:], _i=(0.0 if half == 0 else carries[:, c, n:n + 1]):
                             h.tensor_tensor_scan(_s, _a, _b, _i, ALU.mult, ALU.add),
                             reads=[da[:], db[:]] + ([carries[:, c, n:n + 1]] if half else []),
                             writes=[s_f[:]], tag=f"MB_scan{half}_{c}_{n}")
                        if half == 0:
                            v_op(S, "tensor_copy", carries[:, c, n:n + 1], s_f[:, TH - 1:TH], tag=f"MB_cs{c}_{n}")
                        pf = p_f[n % 2]
                        v_op(S, "tensor_mul", pf[:], s_f[:], C_bc[:, n, :], tag=f"MB_p{half}_{c}_{n}")
                        for cj in range(TH // 512):
                            mm(S, ps_y[:, 512 * cj:512 * (cj + 1)], C.ident_bf[:], pf[:, 512 * cj:512 * (cj + 1)], start=(n == 0), stop=(n == NSTATE - 1), tag=f"MB_ya{half}_{c}_{n}_{cj}")
                    v_op(S, "scalar_tensor_tensor", y_c[:], xs_c[cb][:], D_sb[:, c:c + 1], ps_y[:], ALU.mult, ALU.add, tag=f"MB_yd{half}_{c}")
                    v_op(S, "tensor_mul", ob[:], y_c[:], res_c[cb][:], tag=f"MB_or{half}_{c}")
                    dma(S, "sp", oact_d.ap()[128 * c:128 * (c + 1), h0:h0 + TH], ob[:], tag=f"MB_ost{half}_{c}")

        with pscope(S) as pp:
            sb2 = lambda n, s, d=F32: pp.enter_context(nc.sbuf_tensor(f"{n}_{S.uid()}", list(s), d))
            ps2 = lambda n, s, d=F32: pp.enter_context(nc.psum_tensor(f"{n}_{S.uid()}", list(s), d))
            x_sb = sb2("xMC", [128, ND, Tc])
            y_sb = sb2("yMC", [128, ND, Tc])
            oa = [sb2(f"oaC{i}", [128, Tc], BF16) for i in range(2)]
            ps_o = ps2("psC_o", [128, 8, 512])
            for ci in range(nch):
                t0 = ci * Tc
                dma(S, "sp", x_sb[:], fm_dram(x_in_h, t0, t0 + Tc), tag=f"C_ldx{ci}")
                dma(S, "sp", y_sb[:], fm_dram(y_d, t0, t0 + Tc), tag=f"C_ldy{ci}")
                for j in range(ND):
                    v_op(S, "tensor_add", x_sb[:, j, :], x_sb[:, j, :], y_sb[:, j, :], tag=f"C_xy{ci}_{j}")
                for kt in range(NDI):
                    o = oa[kt % 2]
                    dma(S, "sp", o[:], oact_d.ap()[128 * kt:128 * (kt + 1), t0:t0 + Tc], tag=f"C_lo{ci}_{kt}")
                    for di in range(ND):
                        mm(S, ps_o[:, di, :], wo_sb[:, kt, 128 * di:128 * (di + 1)], o[:], start=(kt == 0), stop=(kt == NDI - 1), tag=f"C_mm{ci}_{kt}_{di}")
                for di in range(ND):
                    v_op(S, "tensor_add", x_sb[:, di, :], x_sb[:, di, :], ps_o[:, di, :], tag=f"C_res{ci}_{di}")
                dma(S, "sp", fm_dram(x_out_h, t0, t0 + Tc), x_sb[:], tag=f"C_stx{ci}")


def run_l0_attn_a(per_core, n_cores=NCORES, **kw):
    S = Prog("l0a")
    x_in = S.dram("x_in", [D, T], F32, kind="ExternalInput")
    x_out = S.dram("x_out", [D, T], F32, kind="ExternalOutput")
    C = Common(S)
    emit_l0_attn_a(S, C, x_in, x_out)
    res = S.run(per_core, n_cores, **kw)
    return [r["x_out"] for r in res.results], res


SUBLAYER_SEQ = (
    ("l0a", emit_l0_attn_a, {}),
    ("l0f1", emit_l0_ffn, {"idx": 0}),
    ("l0b", emit_l0_attn_b, {}),
    ("l0f2", emit_l0_ffn, {"idx": 0}),
    ("l1a", emit_l1_attn_a, {}),
    ("l1f1", emit_l1_ffn, {"idx": 0}),
    ("l1b", emit_l1_attn_b, {}),
    ("l1f2", emit_l1_ffn, {"idx": 0}),
)


def kernel_unfused(**inputs):
    per_core = prep_host(inputs)
    for name, emitfn, kw in SUBLAYER_SEQ:
        S = Prog(name)
        x_in = S.dram("x_in", [D, T], F32, kind="ExternalInput")
        x_out = S.dram("x_out", [D, T], F32, kind="ExternalOutput")
        C = Common(S)
        emitfn(S, C, x_in, x_out, **kw)
        res = S.run(per_core, NCORES)
        for c in range(NCORES):
            per_core[c]["x_in"] = np.ascontiguousarray(np.asarray(res.results[c]["x_out"]))
    out = np.stack([per_core[c]["x_in"].T for c in range(NCORES)]).astype(np.float32)
    return out


def build_mega():
    S = Prog("mega")
    def _hook(alloc):
        l0f_wprep(S, alloc)
        wconv_all(S, alloc)
    S.l0a_mid_hook = _hook
    x_ext_in = S.dram("x_in", [D, T], F32, kind="ExternalInput")
    x_ext_out = S.dram("x_out", [D, T], F32, kind="ExternalOutput")
    xs = [x_ext_in] + [S.dram(f"x_mid_{k}", [D, T], F32, kind="Internal") for k in range(len(SUBLAYER_SEQ) - 1)] + [x_ext_out]
    C = Common(S)
    S.phase_marks = []
    for k, (name, emitfn, kw) in enumerate(SUBLAYER_SEQ):
        with S.nc.named_scope(name):
            emitfn(S, C, xs[k], xs[k + 1], **kw)
        S.phase_marks.append((name, len(S.ops)))
    return S


def kernel(**inputs):
    per_core = prep_host(inputs)
    S = build_mega()
    res = S.run(per_core, NCORES)
    out = np.stack([np.asarray(res.results[c]["x_out"]).T for c in range(NCORES)]).astype(np.float32)
    return out


# version 19 (final)
# speedup vs baseline: 1.1850x; 1.0250x over previous
import math
from contextlib import ExitStack

import numpy as np
import ml_dtypes
import sys

sys.path.insert(0, "/opt/trn_rl_repo")
import concourse.bass as bass
import concourse.mybir as mybir
from concourse.bass_utils import run_bass_kernel_spmd

F32 = mybir.dt.float32
BF16 = mybir.dt.bfloat16
AF = mybir.ActivationFunctionType
ALU = mybir.AluOpType
AX = mybir.AxisListType
BF = ml_dtypes.bfloat16
I32 = mybir.dt.int32

NCORES = 8
T = 2048
D = 1024
ND = D // 128
DI = 2048
NDI = DI // 128
FF = 4096
NF = FF // 128
NEXP = 8
RLORA = 16
NHEAD = 16
HD = 64
NSTATE = 16
EPS_LN = 1e-5

MAGIC = float(np.float32(1.5 * 2 ** 23))


ENGS = ("pe", "dve", "act", "pool", "sp")


def _key(x):
    if isinstance(x, tuple):
        return x
    if isinstance(x, str):
        return (x, "*")
    return (x.name, "*")


class Op:
    __slots__ = ("i", "eng", "fn", "deps", "dma", "tag", "sem", "val", "waits", "dkey")

    def __init__(self, i, eng, fn, deps, dma, tag):
        self.i, self.eng, self.fn, self.deps, self.dma, self.tag = i, eng, fn, deps, dma, tag
        self.sem = None
        self.val = None


class Prog:
    def __init__(self, name="prog"):
        self.name = name
        self.nc = bass.Bass("TRN2", target_bir_lowering=False)
        self.stack = ExitStack()
        self.ops = []
        self.writers = {}
        self.readers = {}
        self.last_dma = {}
        self.bar = []
        self.ext_inputs = []
        self.ext_outputs = []
        self._uid = 0
        self._consts = {}

    def const_col(self, val):
        val = float(val)
        t = self._consts.get(val)
        if t is None:
            t = self.stack.enter_context(self.nc.sbuf_tensor(f"constc_{len(self._consts)}", [128, 1], F32, side="right"))
            self.op("dve", lambda h: h.memset(t[:], val), writes=[t[:]], tag=f"const_{val}")
            self._consts[val] = t
        return t

    def dram(self, name, shape, dtype, kind="Internal"):
        if not hasattr(self, "_dram_cache"):
            self._dram_cache = {}
        if name in self._dram_cache:
            return self._dram_cache[name]
        h = self.nc.dram_tensor(name, list(shape), dtype, kind=kind)
        if kind == "ExternalInput":
            self.ext_inputs.append(name)
        elif kind == "ExternalOutput":
            self.ext_outputs.append(name)
        self._dram_cache[name] = h
        return h

    def sb(self, name, shape, dtype):
        return self.stack.enter_context(self.nc.sbuf_tensor(name, list(shape), dtype))

    def ps(self, name, shape, dtype=F32):
        return self.stack.enter_context(self.nc.psum_tensor(name, list(shape), dtype))

    def uid(self):
        self._uid += 1
        return self._uid

    def op(self, eng, fn, reads=(), writes=(), dma=False, tag=""):
        deps = set(self.bar)
        for r in reads:
            n, t = _key(r)
            for wt, wi in self.writers.get(n, {}).items():
                if t == "*" or wt == "*" or wt == t:
                    deps.add(wi)
        wkeys = [_key(w) for w in writes]
        for n, t in wkeys:
            for wt, wi in self.writers.get(n, {}).items():
                if t == "*" or wt == "*" or wt == t:
                    deps.add(wi)
            for rt, ris in self.readers.get(n, {}).items():
                if t == "*" or rt == "*" or rt == t:
                    deps.update(ris)
        i = len(self.ops)
        o = Op(i, eng, fn, deps, dma, tag)
        o.dkey = None
        self.ops.append(o)
        for r in reads:
            n, t = _key(r)
            self.readers.setdefault(n, {}).setdefault(t, []).append(i)
        for n, t in wkeys:
            w = self.writers.setdefault(n, {})
            if t == "*":
                w.clear()
            w[t] = i
            rd = self.readers.get(n)
            if rd is not None:
                if t == "*":
                    rd.clear()
                else:
                    rd.pop(t, None)
        return i

    def barrier(self):
        last = {}
        for o in self.ops:
            if not o.dma:
                last[o.eng] = o.i
        keep = {}
        for i in self.bar:
            if self.ops[i].dma:
                keep[self.ops[i].dkey] = i
        keep.update(self.last_dma)
        self.bar = list(last.values()) + list(keep.values())
        self.last_dma = {}

    def finalize(self):
        ops = self.ops
        for o in ops:
            if o.eng == "pe":
                o.deps = {d for d in o.deps if ops[d].eng != "pe"}
        needed = set()
        for o in ops:
            needed.update(o.deps)
        for o in ops:
            if o.dma:
                needed.add(o.i)
        dnames = []
        for o in ops:
            if o.dma:
                if o.dkey is None:
                    o.dkey = f"anon{o.i}"
                if o.dkey not in dnames:
                    dnames.append(o.dkey)
        ngroups = min(len(dnames), 90) if dnames else 0
        dmap = {nm: f"dma{idx % ngroups}" for idx, nm in enumerate(dnames)}
        counters = {}
        for o in ops:
            if o.i in needed:
                kind = dmap[o.dkey] if o.dma else o.eng
                step = 16 if o.dma else 1
                counters[kind] = counters.get(kind, 0) + step
                o.sem, o.val = kind, counters[kind]
        self.final_counts = dict(counters)
        for o in ops:
            w = {}
            for d in o.deps:
                od = ops[d]
                if od.sem is None:
                    continue
                if w.get(od.sem, 0) < od.val:
                    w[od.sem] = od.val
            o.waits = sorted(w.items())
        qs = {e: [o for o in ops if o.eng == e] for e in ENGS}
        pos = {e: 0 for e in ENGS}
        sem = {k: 0 for k in counters}
        total = len(ops)
        done = 0
        progress = True
        while done < total and progress:
            progress = False
            for e in ENGS:
                while pos[e] < len(qs[e]):
                    o = qs[e][pos[e]]
                    if all(sem.get(k, 0) >= v for k, v in o.waits):
                        if o.sem:
                            sem[o.sem] = o.val
                        pos[e] += 1
                        done += 1
                        progress = True
                    else:
                        break
        if done < total:
            stuck = [(e, qs[e][pos[e]].tag, qs[e][pos[e]].waits) for e in ENGS if pos[e] < len(qs[e])]
            raise RuntimeError(f"deadlock in schedule: {stuck[:5]} sem={sem}")

    def emit(self):
        self.finalize()
        nc = self.nc
        sems = {}
        for k in sorted(set(list(self.final_counts.keys()))):
            sems[k] = self.stack.enter_context(nc.semaphore("sem_" + k))
        block = self.stack.enter_context(nc.Block())
        handles = {"pe": None, "dve": None, "act": None, "pool": None, "sp": None}

        def emit_engine(eng_name, h):
            seen = {}
            for o in self.ops:
                if o.eng != eng_name:
                    continue
                for k, v in o.waits:
                    if seen.get(k, 0) < v:
                        h.wait_ge(sems[k], v)
                        seen[k] = v
                inst = o.fn(h)
                if o.sem is not None:
                    inst.then_inc(sems[o.sem], 16 if o.dma else 1)

        engs_with_ops = {o.eng for o in self.ops}

        if "pe" in engs_with_ops:
            @block.tensor
            def _(h):
                emit_engine("pe", h)
        if "dve" in engs_with_ops:
            @block.vector
            def _(h):
                emit_engine("dve", h)
        if "act" in engs_with_ops:
            @block.scalar
            def _(h):
                emit_engine("act", h)
        if "pool" in engs_with_ops:
            @block.gpsimd
            def _(h):
                emit_engine("pool", h)

        @block.sync
        def _(h):
            emit_engine("sp", h)
            for k, v in sorted(self.final_counts.items()):
                h.wait_ge(sems[k], v)

    def run(self, host_arrays, n_cores=NCORES, **kw):
        self.emit()
        in_maps = []
        for c in range(n_cores):
            in_maps.append({n: host_arrays[c][n] for n in self.ext_inputs})
        res = run_bass_kernel_spmd(self.nc, in_maps, core_ids=list(range(n_cores)), **kw)
        self.stack.close()
        return res



def dma(S, q, out_ap, in_ap, reads=(), writes=(), tag="dma"):
    i = S.op(q, lambda h: h.dma_start(out=out_ap, in_=in_ap),
             reads=list(reads) + [in_ap], writes=list(writes) + [out_ap], dma=True, tag=tag)
    S.ops[i].dkey = out_ap.name
    S.last_dma[out_ap.name] = i
    return i


def mm(S, out, lhsT, rhs, start, stop, reads=(), writes=(), tag="mm"):
    return S.op("pe", lambda h: h.matmul(out, lhsT, rhs, start=start, stop=stop),
                reads=list(reads) + [lhsT, rhs], writes=list(writes) + [out], tag=tag)


def act(S, out, in_, func, bias=0.0, scale=1.0, accum_out=None, reads=(), writes=(), tag="act"):
    np_ = in_.shape[0]
    if isinstance(bias, (int, float)) and float(bias) != 0.0:
        bias = S.const_col(bias)[0:np_, 0:1]
    if isinstance(scale, (int, float)) and float(scale) != 1.0:
        scale = S.const_col(scale)[0:np_, 0:1]
    ex_r = [x for x in (bias, scale) if isinstance(x, bass.AP)]
    ex_w = [accum_out] if accum_out is not None else []
    return S.op("act", lambda h: h.activation(out, in_, func, bias=bias, scale=scale, accum_out=accum_out),
                reads=list(reads) + [in_] + ex_r, writes=list(writes) + [out] + ex_w, tag=tag)


def v_op(S, fname, out, *ins, reads=(), writes=(), tag=None, **kw):
    aps = [x for x in ins if isinstance(x, bass.AP)]
    return S.op("dve", lambda h: getattr(h, fname)(out, *ins, **kw),
                reads=list(reads) + aps, writes=list(writes) + [out], tag=tag or fname)



def col_layout(v):
    n = v.shape[0] // 128
    return np.ascontiguousarray(v.reshape(n, 128).T).astype(np.float32)


def prep_host(inputs):
    g = {}
    x = np.asarray(inputs["L0_attn_a_x"])

    g["c_ident_bf"] = np.eye(128, dtype=BF)
    g["c_ident_f32"] = np.eye(128, dtype=np.float32)
    g["c_ones_r_bf"] = np.ones((1, 512), dtype=BF)
    g["c_ones_c_f32"] = np.ones((1, 128), dtype=np.float32)
    g["c_ones_col_bf"] = np.ones((128, 1), dtype=BF)
    g["c_ones_col_f32"] = np.ones((128, 1), dtype=np.float32)

    g["m_w_in"] = np.ascontiguousarray(inputs["L0_attn_a_w_in"])
    g["m_norm_scale"] = col_layout(inputs["L0_attn_a_norm_scale"])
    g["m_conv_w"] = np.ascontiguousarray(
        inputs["L0_attn_a_conv_w"][:, 0, :].T.reshape(NDI, 128, 4).transpose(1, 0, 2)).astype(np.float32)
    g["m_conv_b"] = col_layout(inputs["L0_attn_a_conv_b"])
    g["m_A_log"] = np.ascontiguousarray(
        inputs["L0_attn_a_A_log"].reshape(NDI, 128, NSTATE).transpose(1, 0, 2)).astype(np.float32)
    g["m_D"] = col_layout(inputs["L0_attn_a_D"])
    wx = np.asarray(inputs["L0_attn_a_w_xproj"])
    g["m_wx_delta"] = np.ascontiguousarray(wx[:, :64])
    g["m_wx_B"] = np.ascontiguousarray(wx[:, 64:80])
    g["m_wx_C"] = np.ascontiguousarray(wx[:, 80:96])
    g["m_w_dt"] = np.ascontiguousarray(inputs["L0_attn_a_w_dt"])
    g["m_b_dt"] = col_layout(inputs["L0_attn_a_b_dt"])
    g["m_w_out"] = np.ascontiguousarray(inputs["L0_attn_a_w_out"])

    g["f_norm_w"] = col_layout(inputs["L0_ffn_norm_w"])
    g["f_w1T"] = np.ascontiguousarray(inputs["L0_ffn_w1"].T)
    g["f_w3T"] = np.ascontiguousarray(inputs["L0_ffn_w3"].T)
    g["f_w2T"] = np.ascontiguousarray(inputs["L0_ffn_w2"].T)
    g["f_gateT"] = np.ascontiguousarray(inputs["L0_ffn_gate_w"].T)
    a1 = np.asarray(inputs["L0_ffn_a1"])
    a3 = np.asarray(inputs["L0_ffn_a3"])
    g["f_a1p"] = np.stack([np.concatenate([a1[4 * gi + j].T for j in range(4)], axis=1) for gi in range(2)])
    g["f_a3p"] = np.stack([np.concatenate([a3[4 * gi + j].T for j in range(4)], axis=1) for gi in range(2)])
    b1 = np.asarray(inputs["L0_ffn_b1"])
    b3 = np.asarray(inputs["L0_ffn_b3"])

    def padrows(bm, e):
        z = np.zeros((64, FF), dtype=np.float32)
        z[16 * (e % 4):16 * (e % 4) + 16, :] = bm[e].T
        return z
    g["f_b1p"] = np.ascontiguousarray(np.stack([padrows(b1, e) for e in range(NEXP)]))
    g["f_b3p"] = np.ascontiguousarray(np.stack([padrows(b3, e) for e in range(NEXP)]))
    g["f_b2T3"] = np.ascontiguousarray(np.asarray(inputs["L0_ffn_b2"]).transpose(2, 0, 1).reshape(16, NEXP * D))
    g["f_a2Tr"] = np.ascontiguousarray(np.asarray(inputs["L0_ffn_a2"]).transpose(2, 0, 1))
    g["f_b2T"] = np.ascontiguousarray(np.asarray(inputs["L0_ffn_b2"]).transpose(0, 2, 1))

    g["nb_Wq"] = np.ascontiguousarray(inputs["L0_attn_b_Wq"])
    g["nb_Wk"] = np.ascontiguousarray(inputs["L0_attn_b_Wk"])
    g["nb_Wv"] = np.ascontiguousarray(inputs["L0_attn_b_Wv"])
    g["nb_bq"] = np.asarray(inputs["L0_attn_b_bq"]).reshape(1, D).astype(np.float32)
    g["nb_bk"] = np.asarray(inputs["L0_attn_b_bk"]).reshape(1, D).astype(np.float32)
    g["nb_bv"] = np.asarray(inputs["L0_attn_b_bv"]).reshape(1, D).astype(np.float32)
    cw = np.asarray(inputs["L0_attn_b_conv_w"])[:, 0, :, 0]
    pat = np.zeros((NHEAD, 3, 128, 128), dtype=np.float32)
    for h in range(NHEAD):
        for j in range(3):
            rel = (j - 1) * 128
            for r in range(128):
                for c in range(128):
                    idx = rel + r - c + 32
                    if 0 <= idx <= 64:
                        pat[h, j, r, c] = cw[h, idx]
    g["nb_pat"] = pat

    g["ta_projT"] = np.ascontiguousarray(inputs["L1_attn_a_proj_w"].T)
    g["ta_fc1T"] = np.ascontiguousarray(inputs["L1_attn_a_fc1_w"].T)
    g["ta_fc2T"] = np.ascontiguousarray(inputs["L1_attn_a_fc2_w"].T)
    g["ta_g1"] = col_layout(inputs["L1_attn_a_ln1_g"])
    g["ta_b1c"] = col_layout(inputs["L1_attn_a_ln1_b"])
    g["ta_projb"] = np.asarray(inputs["L1_attn_a_proj_b"]).reshape(1, D).astype(np.float32)
    g["ta_g2"] = col_layout(inputs["L1_attn_a_ln2_g"])
    g["ta_b2c"] = col_layout(inputs["L1_attn_a_ln2_b"])
    g["ta_fc1b"] = np.asarray(inputs["L1_attn_a_fc1_b"]).reshape(1, FF).astype(np.float32)
    g["ta_lnhg"] = col_layout(inputs["L1_attn_a_lnh_g"])
    g["ta_lnhbc"] = col_layout(inputs["L1_attn_a_lnh_b"])
    g["ta_fc2b"] = np.asarray(inputs["L1_attn_a_fc2_b"]).reshape(1, D).astype(np.float32)
    g["ta_gam1"] = col_layout(inputs["L1_attn_a_gamma1"])
    g["ta_gam2"] = col_layout(inputs["L1_attn_a_gamma2"])

    for wn, k in (("tb_wqT", "L1_attn_b_wq"), ("tb_wkT", "L1_attn_b_wk"),
                  ("tb_wvT", "L1_attn_b_wv"), ("tb_woT", "L1_attn_b_wo")):
        g[wn] = np.ascontiguousarray(inputs[k].T)
    g["tb_w1T"] = np.ascontiguousarray(inputs["L1_attn_b_w1"].T)
    g["tb_w2T"] = np.ascontiguousarray(inputs["L1_attn_b_w2"].T)
    g["tb_bq"] = np.asarray(inputs["L1_attn_b_bq"]).reshape(1, D).astype(np.float32)
    g["tb_bk"] = np.asarray(inputs["L1_attn_b_bk"]).reshape(1, D).astype(np.float32)
    g["tb_bv"] = np.asarray(inputs["L1_attn_b_bv"]).reshape(1, D).astype(np.float32)
    g["tb_bo"] = np.asarray(inputs["L1_attn_b_bo"]).reshape(1, D).astype(np.float32)
    g["tb_g1"] = col_layout(inputs["L1_attn_b_g1"])
    g["tb_b1c"] = col_layout(inputs["L1_attn_b_b1"])
    g["tb_g2"] = col_layout(inputs["L1_attn_b_g2"])
    g["tb_b2c"] = col_layout(inputs["L1_attn_b_b2"])
    g["tb_bf1"] = np.asarray(inputs["L1_attn_b_bf1"]).reshape(1, FF).astype(np.float32)
    g["tb_bf2"] = np.asarray(inputs["L1_attn_b_bf2"]).reshape(1, D).astype(np.float32)
    hm = np.zeros((ND, 128, NHEAD), dtype=BF)
    for j in range(ND):
        for p in range(128):
            hm[j, p, 2 * j + p // 64] = 1
    g["c_headmask"] = hm
    sel = np.zeros((NHEAD, NHEAD, 64), dtype=BF)
    for h in range(NHEAD):
        sel[h, h, :] = 1
    g["c_sel64"] = sel
    sel2 = np.zeros((NHEAD, NHEAD, 128), dtype=BF)
    for h in range(NHEAD):
        sel2[h, h, :] = 1
    g["c_sel128"] = sel2
    g["c_ones_c_bf"] = np.ones((1, 128), dtype=BF)

    g["q_w"] = np.ascontiguousarray(inputs["L1_ffn_weight"])
    g["q_bias"] = np.asarray(inputs["L1_ffn_bias"]).reshape(1, D).astype(np.float32)

    per_core = []
    for c in range(NCORES):
        d = dict(g)
        d["x_in"] = np.ascontiguousarray(x[c].T).astype(np.float32)
        per_core.append(d)
    return per_core



class Common:

    def __init__(self, S):
        self.ident_bf = S.sb("c_ident", [128, 128], BF16)
        self.ident_f32 = S.sb("c_identf", [128, 128], F32)
        self.ones_r_bf = S.sb("c_ones_r", [1, 512], BF16)
        self.ones_c_f32 = S.sb("c_ones_cf", [1, 128], F32)
        self.ones_col_bf = S.sb("c_ones_col", [128, 1], BF16)
        self.ones_col_f32 = S.sb("c_ones_colf", [128, 1], F32)
        d_id = S.dram("c_ident_bf", [128, 128], BF16, kind="ExternalInput")
        d_idf = S.dram("c_ident_f32", [128, 128], F32, kind="ExternalInput")
        d_or = S.dram("c_ones_r_bf", [1, 512], BF16, kind="ExternalInput")
        d_oc = S.dram("c_ones_c_f32", [1, 128], F32, kind="ExternalInput")
        d_ocol = S.dram("c_ones_col_bf", [128, 1], BF16, kind="ExternalInput")
        d_ocolf = S.dram("c_ones_col_f32", [128, 1], F32, kind="ExternalInput")
        dma(S, "sp", self.ident_bf[:], d_id.ap(), tag="ld_ident")
        dma(S, "sp", self.ident_f32[:], d_idf.ap(), tag="ld_identf")
        dma(S, "sp", self.ones_r_bf[:], d_or.ap(), tag="ld_ones_r")
        dma(S, "sp", self.ones_c_f32[:], d_oc.ap(), tag="ld_ones_c")
        dma(S, "sp", self.ones_col_bf[:], d_ocol.ap(), tag="ld_ones_col")
        dma(S, "sp", self.ones_col_f32[:], d_ocolf.ap(), tag="ld_ones_colf")


def fm_dram(h, t0, t1, nd=ND):
    return h.ap()[:, t0:t1].rearrange("(a p) t -> p a t", p=128)


def ln_stats(S, C, xb_bf, xsq_bf, nfeat_tiles, Tc, ps_s, ps_ss, name):
    for j in range(nfeat_tiles):
        mm(S, ps_s[:], C.ones_col_bf[:], xb_bf[:, j, :], start=(j == 0), stop=(j == nfeat_tiles - 1), tag=f"{name}_s{j}")
        mm(S, ps_ss[:], C.ones_col_bf[:], xsq_bf[:, j, :], start=(j == 0), stop=(j == nfeat_tiles - 1), tag=f"{name}_ss{j}")


def ln_rows(S, ps_s, ps_ss, rows, nfeat, eps, name, eps2=None):
    inv = 1.0 / nfeat
    v_op(S, "tensor_scalar", rows["m"][:], ps_s[:], inv, None, ALU.mult, tag=f"{name}_m")
    v_op(S, "tensor_mul", rows["t1"][:], rows["m"][:], rows["m"][:], tag=f"{name}_msq")
    v_op(S, "scalar_tensor_tensor", rows["var"][:], ps_ss[:], inv, rows["t1"][:], ALU.mult, ALU.subtract, tag=f"{name}_var")
    act(S, rows["t1"][:], rows["var"][:], AF.Ln, bias=float(eps), tag=f"{name}_lnv")
    act(S, rows["rs"][:], rows["t1"][:], AF.Exp, scale=-0.5, tag=f"{name}_rs")
    if eps2 is not None:
        v_op(S, "tensor_mul", rows["t1"][:], rows["rs"][:], rows["rs"][:], tag=f"{name}_u")
        v_op(S, "tensor_mul", rows["t2"][:], rows["t1"][:], rows["var"][:], tag=f"{name}_u2")
        act(S, rows["t1"][:], rows["t2"][:], AF.Ln, bias=float(eps2), tag=f"{name}_lnv2")
        act(S, rows["t2"][:], rows["t1"][:], AF.Exp, scale=-0.5, tag=f"{name}_r2")
        v_op(S, "tensor_mul", rows["rs2"][:], rows["rs"][:], rows["t2"][:], tag=f"{name}_rs2")


def bcast_row(S, C, ps_out, row_ap, tag):
    mm(S, ps_out, C.ones_c_f32[:], row_ap, start=True, stop=True, tag=tag)


def mk_ln_rows(S, Tc, name, eps2=False, alloc=None):
    keys = ["m", "var", "rs", "t1", "t2"] + (["rs2"] if eps2 else [])
    al = alloc if alloc is not None else (lambda n, s, d=F32: S.sb(n, s, d))
    return {k: al(f"{name}_{k}_{S.uid()}", [1, Tc]) for k in keys}


def ln_normalize(S, C, x_sb, nd, Tc, mb_ps, sb_ps, out_bf, gain=None, xc_scratch=None, tag="ln"):
    for j in range(nd):
        xc = xc_scratch[:, j, :] if xc_scratch is not None else None
        v_op(S, "tensor_sub", xc, x_sb[:, j, :], mb_ps, tag=f"{tag}_sub{j}")
        if gain is not None:
            v_op(S, "scalar_tensor_tensor", out_bf[:, j, :], xc, gain[:, j:j + 1], sb_ps, ALU.mult, ALU.mult, tag=f"{tag}_sc{j}")
        else:
            v_op(S, "tensor_mul", out_bf[:, j, :], xc, sb_ps, tag=f"{tag}_mul{j}")


def load_cast_w(S, dram_h, sb_t, rows, cols, tag, rowsl=None, colsl=None):
    q = "pool"
    bf = getattr(S, "wbf", {}).get(dram_h.name)
    if bf is not None:
        dram_h = bf
    src = dram_h.ap()
    if rowsl is not None:
        src = src[rowsl[0]:rowsl[1], :]
        rows = rowsl[1] - rowsl[0]
    if colsl is not None:
        src = src[:, colsl[0]:colsl[1]]
        cols = colsl[1] - colsl[0]
    if rows > 128:
        src = src.rearrange("(a p) n -> p a n", p=128)
    dma(S, q, sb_t[:], src, tag=tag)


def bias_row_mm(S, C, ps, bias_col_bf, w_sb, nk, n0, n1, tag):
    for j in range(nk):
        mm(S, ps[:, 0:n1 - n0], bias_col_bf[:, j:j + 1], w_sb[:, j, n0:n1], start=(j == 0), stop=(j == nk - 1), tag=f"{tag}{j}")



from contextlib import contextmanager


@contextmanager
def pscope(S):
    st = ExitStack()
    try:
        yield st
    finally:
        S.barrier()
        st.close()


def emit_l1_attn_a(S, C, x_in_h, x_out_h):
    nc = S.nc
    d_projT = S.dram("ta_projT", [D, D], F32, kind="ExternalInput")
    d_fc1T = S.dram("ta_fc1T", [D, FF], F32, kind="ExternalInput")
    d_fc2T = S.dram("ta_fc2T", [FF, D], F32, kind="ExternalInput")
    d_g1 = S.dram("ta_g1", [128, ND], F32, kind="ExternalInput")
    d_b1c = S.dram("ta_b1c", [128, ND], F32, kind="ExternalInput")
    d_projb = S.dram("ta_projb", [1, D], F32, kind="ExternalInput")
    d_g2 = S.dram("ta_g2", [128, ND], F32, kind="ExternalInput")
    d_b2c = S.dram("ta_b2c", [128, ND], F32, kind="ExternalInput")
    d_fc1b = S.dram("ta_fc1b", [1, FF], F32, kind="ExternalInput")
    d_lnhg = S.dram("ta_lnhg", [128, NF], F32, kind="ExternalInput")
    d_lnhbc = S.dram("ta_lnhbc", [128, NF], F32, kind="ExternalInput")
    d_fc2b = S.dram("ta_fc2b", [1, D], F32, kind="ExternalInput")
    d_gam1 = S.dram("ta_gam1", [128, ND], F32, kind="ExternalInput")
    d_gam2 = S.dram("ta_gam2", [128, ND], F32, kind="ExternalInput")
    u0 = S.uid()
    kind_dbg = "ExternalOutput" if getattr(S, "debug_spill", False) else "Internal"
    z2_d = S.dram(f"z2_d{u0}", [D, T], BF16, kind=kind_dbg)
    h_d = S.dram(f"h_d{u0}", [FF, T], BF16, kind=kind_dbg)
    x_mid = S.dram(f"xmid_d{u0}", [D, T], F32, kind=kind_dbg)

    with pscope(S) as ph:
        sbl = lambda n, s, d=F32: ph.enter_context(nc.sbuf_tensor(f"{n}_{S.uid()}", list(s), d))
        g1 = sbl("g1", [128, ND]); dma(S, "sp", g1[:], d_g1.ap(), tag="ld_g1")
        g2 = sbl("g2", [128, ND]); dma(S, "sp", g2[:], d_g2.ap(), tag="ld_g2")
        lnhg = sbl("lnhg", [128, NF]); dma(S, "sp", lnhg[:], d_lnhg.ap(), tag="ld_lnhg")
        gam1 = sbl("gam1", [128, ND]); dma(S, "sp", gam1[:], d_gam1.ap(), tag="ld_gam1")
        gam2 = sbl("gam2", [128, ND]); dma(S, "sp", gam2[:], d_gam2.ap(), tag="ld_gam2")
        bias1 = sbl("bias1", [1, D], BF16)
        bias2 = sbl("bias2", [1, FF], BF16)
        bias3 = sbl("bias3", [1, D], BF16)
        lnh_s = sbl("lnh_s", [1, T])
        lnh_ss = sbl("lnh_ss", [1, T])

        with pscope(S) as pp:
            sb2 = lambda n, s, d=F32: pp.enter_context(nc.sbuf_tensor(f"{n}_{S.uid()}", list(s), d))
            ps2 = lambda n, s: pp.enter_context(nc.psum_tensor(f"{n}_{S.uid()}", list(s), F32))
            projb = sb2("projb", [1, D]); dma(S, "sp", projb[:], d_projb.ap(), tag="ld_projb")
            fc1b = sb2("fc1b", [1, FF]); dma(S, "sp", fc1b[:], d_fc1b.ap(), tag="ld_fc1b")
            fc2b = sb2("fc2b", [1, D]); dma(S, "sp", fc2b[:], d_fc2b.ap(), tag="ld_fc2b")
            b1c = sb2("b1c", [128, ND], BF16); dma(S, "pool", b1c[:], d_b1c.ap(), tag="ld_b1c")
            b2c = sb2("b2c", [128, ND], BF16); dma(S, "pool", b2c[:], d_b2c.ap(), tag="ld_b2c")
            lnhbc = sb2("lnhbc", [128, NF], BF16); dma(S, "pool", lnhbc[:], d_lnhbc.ap(), tag="ld_lnhbc")
            ps_b = ps2("ps_b", [1, 512])
            w512 = sb2("w512", [128, ND, 512], BF16)
            for n0 in range(0, D, 512):
                load_cast_w(S, d_projT, w512, D, 512, f"ld_pjs{n0}", colsl=(n0, n0 + 512))
                bias_row_mm(S, C, ps_b, b1c, w512, ND, 0, 512, f"b1r{n0}")
                v_op(S, "tensor_tensor", bias1[:, n0:n0 + 512], ps_b[:, :], projb[:, n0:n0 + 512], ALU.add, tag=f"b1e{n0}")
            for n0 in range(0, FF, 512):
                load_cast_w(S, d_fc1T, w512, D, 512, f"ld_fc1s{n0}", colsl=(n0, n0 + 512))
                bias_row_mm(S, C, ps_b, b2c, w512, ND, 0, 512, f"b2r{n0}")
                v_op(S, "tensor_tensor", bias2[:, n0:n0 + 512], ps_b[:, :], fc1b[:, n0:n0 + 512], ALU.add, tag=f"b2e{n0}")
            wf512 = sb2("wf512", [128, NF, 512], BF16)
            for n0 in range(0, D, 512):
                load_cast_w(S, d_fc2T, wf512, FF, 512, f"ld_fc2s{n0}", colsl=(n0, n0 + 512))
                bias_row_mm(S, C, ps_b, lnhbc, wf512, NF, 0, 512, f"b3r{n0}")
                v_op(S, "tensor_tensor", bias3[:, n0:n0 + 512], ps_b[:, :], fc2b[:, n0:n0 + 512], ALU.add, tag=f"b3e{n0}")

        Tc = 512
        nch = T // Tc
        with pscope(S) as pp:
            sb2 = lambda n, s, d=F32: pp.enter_context(nc.sbuf_tensor(f"{n}_{S.uid()}", list(s), d))
            ps2 = lambda n, s: pp.enter_context(nc.psum_tensor(f"{n}_{S.uid()}", list(s), F32))
            projT = sb2("projT", [128, ND, D], BF16)
            load_cast_w(S, d_projT, projT, D, D, "ld_projT")
            x_sb = sb2("xA", [128, ND, Tc])
            xb = sb2("xAb", [128, ND, Tc], BF16)
            xq = sb2("xAq", [128, ND, Tc], BF16)
            xc = sb2("xAc", [128, ND, Tc])
            y_bf = sb2("yA", [128, ND, Tc], BF16)
            z2buf = sb2("z2buf", [128, ND, Tc], BF16)
            ps_s, ps_ss = ps2("psA_s", [1, Tc]), ps2("psA_ss", [1, Tc])
            ps_bc = ps2("psA_bc", [128, 2 * Tc])
            ps_m, ps_r = ps_bc[:, 0:Tc], ps_bc[:, Tc:2 * Tc]
            ps_o = [ps2("psA_o0", [128, Tc]), ps2("psA_o1", [128, Tc])]
            rowsA = mk_ln_rows(S, Tc, "rA", eps2=True, alloc=sb2)
            rows2 = mk_ln_rows(S, Tc, "r2A", alloc=sb2)
            for ci in range(nch):
                t0 = ci * Tc
                dma(S, "sp", x_sb[:], fm_dram(x_in_h, t0, t0 + Tc), tag=f"A_ldx{ci}")
                for j in range(ND):
                    v_op(S, "tensor_copy", xb[:, j, :], x_sb[:, j, :], tag=f"A_xb{ci}_{j}")
                    act(S, xq[:, j, :], x_sb[:, j, :], AF.Square, tag=f"A_xq{ci}_{j}")
                ln_stats(S, C, xb, xq, ND, Tc, ps_s, ps_ss, f"A_st{ci}")
                ln_rows(S, ps_s, ps_ss, rowsA, D, EPS_LN, f"A_r{ci}", eps2=EPS_LN)
                bcast_row(S, C, ps_m, rowsA["m"][:], f"A_bm{ci}")
                bcast_row(S, C, ps_r, rowsA["rs2"][:], f"A_br{ci}")
                ln_normalize(S, C, x_sb, ND, Tc, ps_m, ps_r, y_bf, gain=g1, xc_scratch=xc, tag=f"A_ln{ci}")
                bcast_row(S, C, ps_m, rowsA["rs"][:], f"A_bro{ci}")
                for j in range(ND):
                    v_op(S, "tensor_mul", x_sb[:, j, :], xc[:, j, :], ps_m, tag=f"A_z{ci}_{j}")
                for mi in range(ND):
                    po = ps_o[mi % 2]
                    for j in range(ND):
                        mm(S, po[:], projT[:, j, 128 * mi:128 * (mi + 1)], y_bf[:, j, :], start=(j == 0), stop=False, tag=f"A_pj{ci}_{mi}_{j}")
                    mm(S, po[:], bias1[:, 128 * mi:128 * (mi + 1)], C.ones_r_bf[:, :Tc], start=False, stop=True, tag=f"A_pb{ci}_{mi}")
                    v_op(S, "scalar_tensor_tensor", x_sb[:, mi, :], po[:], gam1[:, mi:mi + 1], x_sb[:, mi, :], ALU.mult, ALU.add, tag=f"A_res{ci}_{mi}")
                dma(S, "sp", fm_dram(x_mid, t0, t0 + Tc), x_sb[:], tag=f"A_stx{ci}")
                for j in range(ND):
                    v_op(S, "tensor_copy", xb[:, j, :], x_sb[:, j, :], tag=f"A_x1b{ci}_{j}")
                    act(S, xq[:, j, :], x_sb[:, j, :], AF.Square, tag=f"A_x1q{ci}_{j}")
                ln_stats(S, C, xb, xq, ND, Tc, ps_s, ps_ss, f"A2_st{ci}")
                ln_rows(S, ps_s, ps_ss, rows2, D, EPS_LN, f"A2_r{ci}")
                bcast_row(S, C, ps_m, rows2["m"][:], f"A2_bm{ci}")
                bcast_row(S, C, ps_r, rows2["rs"][:], f"A2_br{ci}")
                for j in range(ND):
                    v_op(S, "tensor_sub", xc[:, j, :], x_sb[:, j, :], ps_m, tag=f"A2_sub{ci}_{j}")
                    v_op(S, "scalar_tensor_tensor", z2buf[:, j, :], xc[:, j, :], g2[:, j:j + 1], ps_r, ALU.mult, ALU.mult, tag=f"A2_sc{ci}_{j}")
                dma(S, "sp", fm_dram(z2_d, t0, t0 + Tc), z2buf[:], tag=f"A_stz{ci}")

        with pscope(S) as pp:
            sb2 = lambda n, s, d=F32: pp.enter_context(nc.sbuf_tensor(f"{n}_{S.uid()}", list(s), d))
            ps2 = lambda n, s: pp.enter_context(nc.psum_tensor(f"{n}_{S.uid()}", list(s), F32))
            z2c = sb2("z2c", [128, ND, Tc], BF16)
            hq = sb2("hq", [128, Tc], BF16)
            hst = [sb2("hst0", [128, Tc], BF16), sb2("hst1", [128, Tc], BF16)]
            wB = [sb2("wB0", [128, ND, 128], BF16), sb2("wB1", [128, ND, 128], BF16)]
            ps_h = [ps2("psB_h0", [128, Tc]), ps2("psB_h1", [128, Tc])]
            ps_s, ps_ss = ps2("psB_s", [1, Tc]), ps2("psB_ss", [1, Tc])
            for ci in range(nch):
                t0 = ci * Tc
                dma(S, "sp", z2c[:], fm_dram(z2_d, t0, t0 + Tc), tag=f"B_ldz{ci}")
                for mi in range(NF):
                    wb = wB[mi % 2]
                    load_cast_w(S, d_fc1T, wb, D, 128, f"B_w{ci}_{mi}", colsl=(128 * mi, 128 * (mi + 1)))
                    ph_ = ps_h[mi % 2]
                    hs = hst[mi % 2]
                    for j in range(ND):
                        mm(S, ph_[:], wb[:, j, :], z2c[:, j, :], start=(j == 0), stop=False, tag=f"B_mm{ci}_{mi}_{j}")
                    mm(S, ph_[:], bias2[:, 128 * mi:128 * (mi + 1)], C.ones_r_bf[:, :Tc], start=False, stop=True, tag=f"B_bb{ci}_{mi}")
                    act(S, hs[:], ph_[:], AF.Gelu, tag=f"B_gelu{ci}_{mi}")
                    dma(S, "sp", h_d.ap()[128 * mi:128 * (mi + 1), t0:t0 + Tc], hs[:], tag=f"B_sth{ci}_{mi}")
                    act(S, hq[:], hs[:], AF.Square, tag=f"B_hq{ci}_{mi}")
                    mm(S, ps_s[:], C.ones_col_bf[:], hs[:], start=(mi == 0), stop=(mi == NF - 1), tag=f"B_s{ci}_{mi}")
                    mm(S, ps_ss[:], C.ones_col_bf[:], hq[:], start=(mi == 0), stop=(mi == NF - 1), tag=f"B_ss{ci}_{mi}")
                v_op(S, "tensor_copy", lnh_s[:, t0:t0 + Tc], ps_s[:], tag=f"B_cs{ci}")
                v_op(S, "tensor_copy", lnh_ss[:, t0:t0 + Tc], ps_ss[:], tag=f"B_css{ci}")

        with pscope(S) as pp:
            sb2 = lambda n, s, d=F32: pp.enter_context(nc.sbuf_tensor(f"{n}_{S.uid()}", list(s), d))
            ps2 = lambda n, s: pp.enter_context(nc.psum_tensor(f"{n}_{S.uid()}", list(s), F32))
            xC = sb2("xC", [128, ND, Tc])
            x0 = sb2("x0", [128, ND, Tc])
            hl = [sb2("hl0", [128, Tc], BF16), sb2("hl1", [128, Tc], BF16)]
            ht_all = sb2("ht_all", [128, NF, Tc], BF16)
            htf = sb2("htf", [128, Tc])
            mb_sb = sb2("mb_sbC", [128, Tc])
            rb_sb = sb2("rb_sbC", [128, Tc])
            wC = [sb2("wC0", [128, 512], BF16), sb2("wC1", [128, 512], BF16)]
            ps_big = ps2("psC_big", [128, 4, 512])
            ps_bc = ps2("psC_bc", [128, 512])
            rowsC = mk_ln_rows(S, Tc, "rC", alloc=sb2)
            for ci in range(nch):
                t0 = ci * Tc
                ln_rows(S, lnh_s[:, t0:t0 + Tc], lnh_ss[:, t0:t0 + Tc], rowsC, FF, EPS_LN, f"C_r{ci}")
                bcast_row(S, C, ps_bc[:], rowsC["m"][:], f"C_bm{ci}")
                v_op(S, "tensor_copy", mb_sb[:], ps_bc[:], tag=f"C_cm{ci}")
                bcast_row(S, C, ps_bc[:], rowsC["rs"][:], f"C_br{ci}")
                v_op(S, "tensor_copy", rb_sb[:], ps_bc[:], tag=f"C_cr{ci}")
                dma(S, "sp", xC[:], fm_dram(x_mid, t0, t0 + Tc), tag=f"C_ldx{ci}")
                dma(S, "sp", x0[:], fm_dram(x_in_h, t0, t0 + Tc), tag=f"C_ldx0{ci}")
                for mi in range(NF):
                    hli = hl[mi % 2]
                    dma(S, "sp", hli[:], h_d.ap()[128 * mi:128 * (mi + 1), t0:t0 + Tc], tag=f"C_ldh{ci}_{mi}")
                    v_op(S, "tensor_sub", htf[:], hli[:], mb_sb[:], tag=f"C_sub{ci}_{mi}")
                    v_op(S, "scalar_tensor_tensor", ht_all[:, mi, :], htf[:], lnhg[:, mi:mi + 1], rb_sb[:], ALU.mult, ALU.mult, tag=f"C_sc{ci}_{mi}")
                for half in range(2):
                    for mi in range(NF):
                        wc = wC[mi % 2]
                        load_cast_w(S, d_fc2T, wc, 128, 512, f"C_w{ci}_{half}_{mi}", rowsl=(128 * mi, 128 * (mi + 1)), colsl=(512 * half, 512 * (half + 1)))
                        for dq in range(4):
                            mm(S, ps_big[:, dq, :], wc[:, 128 * dq:128 * (dq + 1)], ht_all[:, mi, :], start=(mi == 0), stop=False, tag=f"C_mm{ci}_{half}_{mi}_{dq}")
                    for dq in range(4):
                        di = 4 * half + dq
                        mm(S, ps_big[:, dq, :], bias3[:, 128 * di:128 * (di + 1)], C.ones_r_bf[:, :Tc], start=False, stop=True, tag=f"C_bb{ci}_{half}_{dq}")
                        v_op(S, "scalar_tensor_tensor", xC[:, di, :], ps_big[:, dq, :], gam2[:, di:di + 1], xC[:, di, :], ALU.mult, ALU.add, tag=f"C_res{ci}_{half}_{dq}")
                        v_op(S, "tensor_add", xC[:, di, :], xC[:, di, :], x0[:, di, :], tag=f"C_addx{ci}_{half}_{dq}")
                dma(S, "sp", fm_dram(x_out_h, t0, t0 + Tc), xC[:], tag=f"C_stx{ci}")


def run_l1_attn_a(per_core, n_cores=NCORES, debug_spill=False, **kw):
    S = Prog("l1a")
    S.debug_spill = debug_spill
    x_in = S.dram("x_in", [D, T], F32, kind="ExternalInput")
    x_out = S.dram("x_out", [D, T], F32, kind="ExternalOutput")
    C = Common(S)
    emit_l1_attn_a(S, C, x_in, x_out)
    res = S.run(per_core, n_cores, **kw)
    return [r["x_out"] for r in res.results], res


def tr_op(S, out_ps, in_ap, ident, tag):
    return S.op("pe", lambda h: h.transpose(out_ps, in_ap, ident),
                reads=[in_ap, ident], writes=[out_ps], tag=tag)


def quantize_rows(S, sc, src_ap, out_bf_ap, Tn, tag):
    nb = Tn // 32
    bview = lambda ap: ap.rearrange("p (a b) -> p a b", b=32)
    S.op("dve", lambda h: h.tensor_reduce(sc["amax"][:], bview(src_ap), AX.X, ALU.max),
         reads=[src_ap], writes=[sc["amax"][:]], tag=f"{tag}_amax")
    S.op("dve", lambda h: h.tensor_reduce(sc["amin"][:], bview(src_ap), AX.X, ALU.min),
         reads=[src_ap], writes=[sc["amin"][:]], tag=f"{tag}_amin")
    v_op(S, "scalar_tensor_tensor", sc["aab"][:], sc["amin"][:], -1.0, sc["amax"][:], ALU.mult, ALU.max, tag=f"{tag}_aabs")
    v_op(S, "tensor_scalar", sc["s"][:], sc["aab"][:], 1.0 / 7.5, 1e-30, ALU.mult, ALU.max, tag=f"{tag}_s")
    v_op(S, "reciprocal", sc["si"][:], sc["s"][:], tag=f"{tag}_si")
    sx = bview(sc["sexp"][:])
    v_op(S, "tensor_copy", sx[:, :, 0:1], sc["si"][:].rearrange("p (a o) -> p a o", o=1), tag=f"{tag}_e0")
    w = 1
    while w < 32:
        v_op(S, "tensor_copy", sx[:, :, w:2 * w], sx[:, :, 0:w], tag=f"{tag}_e{w}")
        w *= 2
    v_op(S, "tensor_mul", sc["y"][:], src_ap, sc["sexp"][:], tag=f"{tag}_y")
    v_op(S, "tensor_mul", sc["ay"][:], sc["y"][:], sc["y"][:], tag=f"{tag}_ay")
    v_op(S, "tensor_single_scalar", sc["m2"][:], sc["ay"][:], 4.0, ALU.is_lt, tag=f"{tag}_m2")
    v_op(S, "tensor_single_scalar", sc["m4"][:], sc["ay"][:], 16.0, ALU.is_lt, tag=f"{tag}_m4")
    for nm, mul in (("r8", 8.0), ("r4", 4.0), ("r2", 2.0)):
        v_op(S, "tensor_scalar", sc[nm][:], sc["y"][:], mul, MAGIC, ALU.mult, ALU.add, tag=f"{tag}_{nm}a")
        v_op(S, "tensor_single_scalar", sc[nm][:], sc[nm][:], MAGIC, ALU.subtract, tag=f"{tag}_{nm}b")
    sx2 = bview(sc["sexp2"][:])
    v_op(S, "tensor_copy", sx2[:, :, 0:1], sc["s"][:].rearrange("p (a o) -> p a o", o=1), tag=f"{tag}_f0")
    w = 1
    while w < 32:
        v_op(S, "tensor_copy", sx2[:, :, w:2 * w], sx2[:, :, 0:w], tag=f"{tag}_f{w}")
        w *= 2
    for nm, inv in (("r8", 0.125), ("r4", 0.25), ("r2", 0.5)):
        v_op(S, "scalar_tensor_tensor", sc[nm][:], sc[nm][:], inv, sc["sexp2"][:], ALU.mult, ALU.mult, tag=f"{tag}_t{nm}")
    v_op(S, "select", sc["ay"][:], sc["m4"][:], sc["r4"][:], sc["r2"][:], tag=f"{tag}_sel1")
    S.op("dve", lambda h: h.select(out_bf_ap, sc["m2"][:], sc["r8"][:], sc["ay"][:]),
         reads=[sc["m2"][:], sc["r8"][:], sc["ay"][:]], writes=[out_bf_ap], tag=f"{tag}_sel2")


def mk_qscratch(alloc, Tn, name):
    d = {}
    for k in ("y", "ay", "r8", "r4", "r2", "sexp", "sexp2"):
        d[k] = alloc(f"{name}_{k}", [128, Tn])
    for k in ("m2", "m4"):
        d[k] = alloc(f"{name}_{k}", [128, Tn], I32)
    for k in ("amax", "amin", "aab", "s", "si"):
        d[k] = alloc(f"{name}_{k}", [128, Tn // 32])
    return d


def emit_l1_ffn(S, C, x_in_h, x_out_h, idx=None):
    nc = S.nc
    idx = 1 if hasattr(S, "d_qw") else 0
    if idx == 0:
        S.d_qw = S.dram("q_w", [D, D], F32, kind="ExternalInput")
        S.d_qb = S.dram("q_bias", [1, D], F32, kind="ExternalInput")
        S.qwT_d = S.dram(f"qwT_d", [D, D], BF16, kind="ExternalOutput" if getattr(S, "debug_spill", False) else "Internal")
    d_qw, d_qb, qwT_d = S.d_qw, S.d_qb, S.qwT_d

    with pscope(S) as ph:
        sbl = lambda n, s, d=F32: ph.enter_context(nc.sbuf_tensor(f"{n}_{S.uid()}", list(s), d))
        qbias = sbl("qbias", [1, D], BF16)
        qbf = sbl("qbf", [1, D])
        dma(S, "sp", qbf[:], d_qb.ap(), tag="ld_qb")
        v_op(S, "tensor_copy", qbias[:], qbf[:], tag="qb_bf")

        if idx == 0:
            with pscope(S) as pp:
                sb2 = lambda n, s, d=F32: pp.enter_context(nc.sbuf_tensor(f"{n}_{S.uid()}", list(s), d))
                ps2 = lambda n, s, d=F32: pp.enter_context(nc.psum_tensor(f"{n}_{S.uid()}", list(s), d))
                wq = sb2("wq", [128, ND, D])
                dma(S, "sp", wq[:], d_qw.ap().rearrange("(a p) n -> p a n", p=128), tag="ld_qw")
                wdf = sb2("wdf", [128, ND, D])
                sc = mk_qscratch(sb2, D, "qw")
                for j in range(ND):
                    quantize_rows(S, sc, wq[:, j, :], wdf[:, j, :], D, f"qw{j}")
                if getattr(S, "debug_spill", False):
                    wdf_dbg = S.dram(f"wdf_dbg{S.uid()}", [D, D], F32, kind="ExternalOutput")
                    dma(S, "sp", wdf_dbg.ap().rearrange("(a p) n -> p a n", p=128), wdf[:], tag="dbg_wdf")
                    y_dbg = S.dram(f"y_dbg{S.uid()}", [128, D], F32, kind="ExternalOutput")
                    s_dbg = S.dram(f"s_dbg{S.uid()}", [128, D // 32], F32, kind="ExternalOutput")
                    dma(S, "sp", y_dbg.ap(), sc["y"][:], tag="dbg_y")
                    dma(S, "sp", s_dbg.ap(), sc["s"][:], tag="dbg_s")
                ps_t = ps2("ps_tw", [128, 128])
                wrow = sb2("wrow", [128, D], BF16)
                for kj in range(ND):
                    for dj in range(ND):
                        tr_op(S, ps_t[:], wdf[:, dj, 128 * kj:128 * (kj + 1)], C.ident_f32[:], f"qwt{kj}_{dj}")
                        v_op(S, "tensor_copy", wrow[:, 128 * dj:128 * (dj + 1)], ps_t[:], tag=f"qwte{kj}_{dj}")
                    dma(S, "sp", qwT_d.ap()[128 * kj:128 * (kj + 1), :], wrow[:], tag=f"st_qwT{kj}")

        Tc = 512
        nch = T // Tc
        dbg = getattr(S, "debug_spill", False)
        if dbg:
            xn_dbg = S.dram(f"xn_dbg{S.uid()}", [D, T], F32, kind="ExternalOutput")
            xd_dbg = S.dram(f"xd_dbg{S.uid()}", [D, T], BF16, kind="ExternalOutput")
        with pscope(S) as pp:
            sb2 = lambda n, s, d=F32: pp.enter_context(nc.sbuf_tensor(f"{n}_{S.uid()}", list(s), d))
            ps2 = lambda n, s, d=F32: pp.enter_context(nc.psum_tensor(f"{n}_{S.uid()}", list(s), d))
            x_sb = sb2("xQ", [128, ND, Tc])
            xb = sb2("xQb", [128, ND, Tc], BF16)
            xq = sb2("xQq", [128, ND, Tc], BF16)
            xc = sb2("xQc", [128, ND, Tc])
            xn = sb2("xnQ", [128, ND, Tc])
            xtok = sb2("xtok", [128, D])
            xdt = sb2("xdt", [128, D], BF16)
            xd = sb2("xdQ", [128, ND, Tc], BF16)
            wT = sb2("wTQ", [128, ND, D], BF16)
            sc = mk_qscratch(sb2, D, "qx")
            rows = mk_ln_rows(S, Tc, "rQ", alloc=sb2)
            mb_sb = sb2("mbQ", [128, Tc])
            rb_sb = sb2("rbQ", [128, Tc])
            ps_s, ps_ss = ps2("psQ_s", [1, Tc]), ps2("psQ_ss", [1, Tc])
            ps_bc = ps2("psQ_bc", [128, Tc])
            ps_t = ps2("psQ_t", [128, 128])
            ps_tb = ps2("psQ_tb", [128, 128], BF16)
            ps_big = ps2("psQ_big", [128, 2, 512])
            dma(S, "sp", wT[:], qwT_d.ap().rearrange("(a p) n -> p a n", p=128), tag="ld_wT")
            for ci in range(nch):
                t0 = ci * Tc
                dma(S, "sp", x_sb[:], fm_dram(x_in_h, t0, t0 + Tc), tag=f"Q_ldx{ci}")
                for j in range(ND):
                    v_op(S, "tensor_copy", xb[:, j, :], x_sb[:, j, :], tag=f"Q_xb{ci}_{j}")
                    act(S, xq[:, j, :], x_sb[:, j, :], AF.Square, tag=f"Q_xq{ci}_{j}")
                ln_stats(S, C, xb, xq, ND, Tc, ps_s, ps_ss, f"Q_st{ci}")
                ln_rows(S, ps_s, ps_ss, rows, D, EPS_LN, f"Q_r{ci}")
                bcast_row(S, C, ps_bc[:], rows["m"][:], f"Q_bm{ci}")
                v_op(S, "tensor_copy", mb_sb[:], ps_bc[:], tag=f"Q_cm{ci}")
                bcast_row(S, C, ps_bc[:], rows["rs"][:], f"Q_br{ci}")
                v_op(S, "tensor_copy", rb_sb[:], ps_bc[:], tag=f"Q_cr{ci}")
                for j in range(ND):
                    v_op(S, "tensor_sub", xc[:, j, :], x_sb[:, j, :], mb_sb[:], tag=f"Q_sub{ci}_{j}")
                    v_op(S, "tensor_mul", xn[:, j, :], xc[:, j, :], rb_sb[:], tag=f"Q_mul{ci}_{j}")
                for st in range(Tc // 128):
                    s0 = st * 128
                    for j in range(ND):
                        tr_op(S, ps_t[:], xn[:, j, s0:s0 + 128], C.ident_f32[:], f"Q_t{ci}_{st}_{j}")
                        v_op(S, "tensor_copy", xtok[:, 128 * j:128 * (j + 1)], ps_t[:], tag=f"Q_te{ci}_{st}_{j}")
                    quantize_rows(S, sc, xtok[:], xdt[:], D, f"Q_q{ci}_{st}")
                    for j in range(ND):
                        tr_op(S, ps_tb[:], xdt[:, 128 * j:128 * (j + 1)], C.ident_bf[:], f"Q_tb{ci}_{st}_{j}")
                        v_op(S, "tensor_copy", xd[:, j, s0:s0 + 128], ps_tb[:], tag=f"Q_tbe{ci}_{st}_{j}")
                if dbg:
                    dma(S, "sp", fm_dram(xn_dbg, t0, t0 + Tc), xn[:], tag=f"dbg_xn{ci}")
                    dma(S, "sp", fm_dram(xd_dbg, t0, t0 + Tc), xd[:], tag=f"dbg_xd{ci}")
                for half in range(4):
                    for dq in range(2):
                        di = 2 * half + dq
                        for kj in range(ND):
                            mm(S, ps_big[:, dq, :], wT[:, kj, 128 * di:128 * (di + 1)], xd[:, kj, :], start=(kj == 0), stop=False, tag=f"Q_mm{ci}_{di}_{kj}")
                        mm(S, ps_big[:, dq, :], qbias[:, 128 * di:128 * (di + 1)], C.ones_r_bf[:, :Tc], start=False, stop=True, tag=f"Q_bb{ci}_{di}")
                        v_op(S, "tensor_add", x_sb[:, di, :], x_sb[:, di, :], ps_big[:, dq, :], tag=f"Q_res{ci}_{di}")
                dma(S, "sp", fm_dram(x_out_h, t0, t0 + Tc), x_sb[:], tag=f"Q_stx{ci}")


def run_l1_ffn(per_core, n_cores=NCORES, debug_spill=False, **kw):
    S = Prog("l1f")
    S.debug_spill = debug_spill
    x_in = S.dram("x_in", [D, T], F32, kind="ExternalInput")
    x_out = S.dram("x_out", [D, T], F32, kind="ExternalOutput")
    C = Common(S)
    emit_l1_ffn(S, C, x_in, x_out, idx=0)
    res = S.run(per_core, n_cores, **kw)
    return [r["x_out"] for r in res.results], res


def conv_w_to_bf16(S, src_h, dst_h, rows, cols, tmp_sb, tag, qload="pool", qstore="sp"):
    nt = rows // 128
    for i in range(nt):
        t = tmp_sb[i % 2]
        dma(S, qload, t[:, 0:cols], src_h.ap()[128 * i:128 * (i + 1), :], tag=f"{tag}_l{i}")
        dma(S, qstore, dst_h.ap()[128 * i:128 * (i + 1), :], t[:, 0:cols], tag=f"{tag}_s{i}")


WCONV = (("tb_wqT", [D, D]), ("tb_wkT", [D, D]), ("tb_wvT", [D, D]), ("tb_woT", [D, D]),
         ("tb_w1T", [D, FF]), ("tb_w2T", [FF, D]),
         ("nb_Wq", [D, D]), ("nb_Wk", [D, D]), ("nb_Wv", [D, D]))


def wconv_all(S, alloc):
    if hasattr(S, "wbf"):
        return
    S.wbf = {}
    tmp = getattr(S, "_wcv_tmp", None) or [alloc(f"wcv{i}_{S.uid()}", [128, FF], BF16) for i in range(2)]
    for nm, sh in WCONV:
        src = S.dram(nm, sh, F32, kind="ExternalInput")
        dst = S.dram(nm + "_bf", sh, BF16, kind="Internal")
        S.wbf[nm] = dst
        conv_w_to_bf16(S, src, dst, sh[0], sh[1], tmp, f"wc_{nm}", qload="pool", qstore="act")


def l0f_wprep(S, alloc=None):
    if hasattr(S, "f_d"):
        return
    S.f_d = {}
    for nm, sh in (("f_norm_w", [128, ND]), ("f_gateT", [D, NEXP]),
                   ("f_a1p", [2, D, 64]), ("f_a3p", [2, D, 64]),
                   ("f_b1p", [NEXP, 64, FF]), ("f_b3p", [NEXP, 64, FF]),
                   ("f_a2Tr", [FF, NEXP, RLORA]), ("f_b2T3", [16, NEXP * D]),
                   ("f_w1T", [D, FF]), ("f_w3T", [D, FF]), ("f_w2T", [FF, D])):
        S.f_d[nm] = S.dram(nm, sh, F32, kind="ExternalInput")
    S.f_scr = {"f_w1T": S.dram("f_w1T_bf", [D, FF], BF16, kind="Internal"),
               "f_w3T": S.dram("f_w3T_bf", [D, FF], BF16, kind="Internal"),
               "f_w2T": S.dram("f_w2T_bf", [FF, D], BF16, kind="Internal")}

    def _emit(al, qa, qb):
        tmp = [al(f"cvt{i}_{S.uid()}", [128, FF], BF16) for i in range(2)]
        S._wcv_tmp = tmp
        tmp2 = [al(f"cvu{i}_{S.uid()}", [128, D], BF16) for i in range(2)]
        conv_w_to_bf16(S, S.f_d["f_w1T"], S.f_scr["f_w1T"], D, FF, tmp, "cv_w1", qload=qa, qstore=qb)
        conv_w_to_bf16(S, S.f_d["f_w3T"], S.f_scr["f_w3T"], D, FF, tmp, "cv_w3", qload=qa, qstore=qb)
        conv_w_to_bf16(S, S.f_d["f_w2T"], S.f_scr["f_w2T"], FF, D, tmp2, "cv_w2", qload=qa, qstore=qb)

    if alloc is not None:
        _emit(alloc, "pool", "act")
    else:
        with pscope(S) as pp:
            al = lambda n, s, d=BF16: pp.enter_context(S.nc.sbuf_tensor(n, list(s), d))
            _emit(al, "pool", "sp")


def emit_l0_ffn(S, C, x_in_h, x_out_h, idx=None):
    nc = S.nc
    idx = 1 if getattr(S, "f_used", False) else 0
    S.f_used = True
    l0f_wprep(S)
    FD = S.f_d
    Tc = 512
    nch = T // Tc
    nst = Tc // 128
    u0 = S.uid()
    wrows_d = S.dram(f"wrows_d{u0}", [NEXP, Tc], F32, kind="Internal")

    with pscope(S) as ph:
        sbl = lambda n, s, d=F32: ph.enter_context(nc.sbuf_tensor(f"{n}_{S.uid()}", list(s), d))
        norm_w = sbl("fnw", [128, ND]); dma(S, "sp", norm_w[:], FD["f_norm_w"].ap(), tag="ld_fnw")
        gateT = sbl("gateT", [128, ND, NEXP]); dma(S, "sp", gateT[:], FD["f_gateT"].ap().rearrange("(a p) n -> p a n", p=128), tag="ld_gateT")
        for j in range(ND):
            v_op(S, "tensor_scalar", gateT[:, j, :], gateT[:, j, :], norm_w[:, j:j + 1], None, ALU.mult, tag=f"gw_fold{j}")
        a1p = sbl("a1p", [128, 2, ND, 64], BF16)
        a3p = sbl("a3p", [128, 2, ND, 64], BF16)
        for gi in range(2):
            dma(S, "pool", a1p[:, gi, :, :], FD["f_a1p"].ap()[gi].rearrange("(a p) n -> p a n", p=128), tag=f"ld_a1p{gi}")
            dma(S, "pool", a3p[:, gi, :, :], FD["f_a3p"].ap()[gi].rearrange("(a p) n -> p a n", p=128), tag=f"ld_a3p{gi}")
        a2T = sbl("a2T", [128, NF, NEXP, RLORA], BF16)
        dma(S, "pool", a2T[:], FD["f_a2Tr"].ap().rearrange("(a p) e r -> p a e r", p=128), tag="ld_a2T")
        b2T = sbl("b2T", [16, NEXP, D], BF16)
        dma(S, "pool", b2T[:], FD["f_b2T3"].ap().rearrange("p (e n) -> p e n", e=NEXP), tag="ld_b2T")

        x_sb = sbl("xF", [128, ND, Tc])
        xn = sbl("xnF", [128, ND, Tc], BF16)
        w_bc = sbl("w_bc", [128, NEXP, Tc], BF16)
        etr1 = sbl("etr1", [64, 2, Tc], BF16)
        etr3 = sbl("etr3", [64, 2, Tc], BF16)
        actmix = sbl("actmix", [128, NF, Tc], BF16)
        etr2w = [sbl(f"etr2w{e}", [16, Tc], BF16) for e in range(NEXP)]

        for ci in range(nch):
            t0 = ci * Tc
            with pscope(S) as pq:
                sb3 = lambda n, s, d=F32: pq.enter_context(nc.sbuf_tensor(f"{n}_{S.uid()}", list(s), d))
                ps3 = lambda n, s, d=F32: pq.enter_context(nc.psum_tensor(f"{n}_{S.uid()}", list(s), d))
                xb = sb3("xFb", [128, ND, Tc], BF16)
                xq = sb3("xFq", [128, ND, Tc], BF16)
                xc = sb3("xFc", [128, ND, Tc])
                rows = mk_ln_rows(S, Tc, "rF", eps2=True, alloc=sb3)
                glog = sb3("glog", [NEXP, Tc])
                wbe = sb3("wbe", [NEXP, Tc])
                wrows = sb3("wrows", [1, NEXP, Tc])
                u_tok = sb3("u_tok", [128, NEXP])
                l_tok = sb3("l_tok", [128, NEXP])
                msk1 = sb3("msk1", [128, NEXP])
                msk2 = sb3("msk2", [128, NEXP])
                l2m = sb3("l2m", [128, NEXP])
                wtok = sb3("wtok", [128, NEXP])
                rs2c = sb3("rs2c", [128, nst])
                lm = {k: sb3(f"lm_{k}", [128, 1]) for k in ("negm1", "m1", "m2", "e2", "den", "winv", "w2c")}
                ps_s, ps_ss = ps3("psF_s", [1, Tc]), ps3("psF_ss", [1, Tc])
                ps_bc = ps3("psF_bc", [128, Tc])
                ps_g = ps3("psF_g", [NEXP, Tc])
                ps_t = ps3("psF_t", [128, 128])
                ps_e = ps3("psF_e", [64, Tc])
                dma(S, "sp", x_sb[:], fm_dram(x_in_h, t0, t0 + Tc), tag=f"F_ldx{ci}")
                for j in range(ND):
                    v_op(S, "tensor_copy", xb[:, j, :], x_sb[:, j, :], tag=f"F_xb{ci}_{j}")
                    act(S, xq[:, j, :], x_sb[:, j, :], AF.Square, tag=f"F_xq{ci}_{j}")
                ln_stats(S, C, xb, xq, ND, Tc, ps_s, ps_ss, f"F_st{ci}")
                ln_rows(S, ps_s, ps_ss, rows, D, EPS_LN, f"F_r{ci}", eps2=1e-6)
                bcast_row(S, C, ps_bc[:], rows["m"][:], f"F_bm{ci}")
                for j in range(ND):
                    v_op(S, "tensor_sub", xc[:, j, :], x_sb[:, j, :], ps_bc[:], tag=f"F_sub{ci}_{j}")
                bcast_row(S, C, ps_bc[:], rows["rs2"][:], f"F_br{ci}")
                for j in range(ND):
                    v_op(S, "scalar_tensor_tensor", xn[:, j, :], xc[:, j, :], norm_w[:, j:j + 1], ps_bc[:], ALU.mult, ALU.mult, tag=f"F_xn{ci}_{j}")
                for j in range(ND):
                    mm(S, ps_g[:], gateT[:, j, :], xc[:, j, :], start=(j == 0), stop=(j == ND - 1), tag=f"F_g{ci}_{j}")
                v_op(S, "tensor_copy", glog[:], ps_g[:], tag=f"F_gc{ci}")
                for st in range(nst):
                    tr_op(S, ps_t[0:128, 0:1], rows["rs2"][:, 128 * st:128 * (st + 1)], C.ident_f32[0:1, 0:1], f"F_rt{ci}_{st}")
                    v_op(S, "tensor_copy", rs2c[:, st:st + 1], ps_t[0:128, 0:1], tag=f"F_rtc{ci}_{st}")
                for st in range(nst):
                    s0 = 128 * st
                    tr_op(S, ps_t[0:128, 0:NEXP], glog[:, s0:s0 + 128], C.ident_f32[0:NEXP, 0:NEXP], f"F_gt{ci}_{st}")
                    v_op(S, "tensor_copy", u_tok[:], ps_t[0:128, 0:NEXP], tag=f"F_gte{ci}_{st}")
                    v_op(S, "tensor_scalar", l_tok[:], u_tok[:], rs2c[:, st:st + 1], None, ALU.mult, tag=f"F_lt{ci}_{st}")
                    v_op(S, "reduce_max", lm["negm1"][:], l_tok[:], AX.X, negate=True, tag=f"F_m1{ci}_{st}")
                    v_op(S, "tensor_scalar", lm["m1"][:], lm["negm1"][:], -1.0, None, ALU.mult, tag=f"F_m1p{ci}_{st}")
                    v_op(S, "tensor_scalar", msk1[:], l_tok[:], lm["m1"][:], None, ALU.is_ge, tag=f"F_k1{ci}_{st}")
                    v_op(S, "scalar_tensor_tensor", l2m[:], msk1[:], -1e30, l_tok[:], ALU.mult, ALU.add, tag=f"F_l2{ci}_{st}")
                    v_op(S, "reduce_max", lm["m2"][:], l2m[:], AX.X, tag=f"F_m2{ci}_{st}")
                    v_op(S, "tensor_scalar", msk2[:], l2m[:], lm["m2"][:], None, ALU.is_ge, tag=f"F_k2{ci}_{st}")
                    act(S, lm["e2"][:], lm["m2"][:], AF.Exp, bias=lm["negm1"][:], tag=f"F_e2{ci}_{st}")
                    v_op(S, "tensor_scalar", lm["den"][:], lm["e2"][:], 1.0, None, ALU.add, tag=f"F_dn{ci}_{st}")
                    v_op(S, "reciprocal", lm["winv"][:], lm["den"][:], tag=f"F_wi{ci}_{st}")
                    v_op(S, "tensor_mul", lm["w2c"][:], lm["e2"][:], lm["winv"][:], tag=f"F_w2{ci}_{st}")
                    v_op(S, "tensor_scalar", wtok[:], msk1[:], lm["winv"][:], None, ALU.mult, tag=f"F_wt{ci}_{st}")
                    v_op(S, "scalar_tensor_tensor", wtok[:], msk2[:], lm["w2c"][:], wtok[:], ALU.mult, ALU.add, tag=f"F_wt2{ci}_{st}")
                    tr_op(S, ps_t[0:NEXP, 0:128], wtok[:], C.ident_f32[:], f"F_wtt{ci}_{st}")
                    v_op(S, "tensor_copy", wbe[:, s0:s0 + 128], ps_t[0:NEXP, 0:128], tag=f"F_wte{ci}_{st}")
                if getattr(S, "debug_spill", False):
                    if not hasattr(S, "wbe_dbg"):
                        S.wbe_dbg = S.dram(f"wbe_dbg{S.uid()}", [NEXP, T], F32, kind="ExternalOutput")
                    dma(S, "sp", S.wbe_dbg.ap()[:, t0:t0 + Tc], wbe[:], tag=f"dbg_wbe{ci}")
                dma(S, "sp", wrows_d.ap(), wbe[:], tag=f"F_wd1{ci}")
                dma(S, "sp", wrows[:], wrows_d.ap().rearrange("(o e) n -> o e n", o=1), tag=f"F_wd2{ci}")
                for e in range(NEXP):
                    mm(S, ps_bc[:], C.ones_c_f32[:], wrows[0:1, e, :], start=True, stop=True, tag=f"F_wb{ci}_{e}")
                    v_op(S, "tensor_copy", w_bc[:, e, :], ps_bc[:], tag=f"F_wbc{ci}_{e}")
                for gi in range(2):
                    for fam, ap_, dst in (("1", a1p, etr1), ("3", a3p, etr3)):
                        for j in range(ND):
                            mm(S, ps_e[:], ap_[:, gi, j, :], xn[:, j, :], start=(j == 0), stop=(j == ND - 1), tag=f"F_a{fam}_{ci}_{gi}_{j}")
                        v_op(S, "tensor_scalar", dst[:, gi, :], ps_e[:], 2.0, None, ALU.mult, tag=f"F_ae{fam}_{ci}_{gi}")

            with pscope(S) as pq:
                sb3 = lambda n, s, d=F32: pq.enter_context(nc.sbuf_tensor(f"{n}_{S.uid()}", list(s), d))
                ps3 = lambda n, s, d=F32: pq.enter_context(nc.psum_tensor(f"{n}_{S.uid()}", list(s), d))
                sigs = [sb3(f"sig{i}", [128, Tc], BF16) for i in range(2)]
                qqs = [sb3(f"qq{i}", [128, Tc], BF16) for i in range(2)]
                p_bfs = [sb3(f"p_bf{i}", [128, Tc], BF16) for i in range(2)]
                h1s = [sb3(f"h1s{i}", [128, Tc], BF16) for i in range(2)]
                h3s = [sb3(f"h3s{i}", [128, Tc], BF16) for i in range(2)]
                l1sb = [sb3(f"l1sb{i}", [128, Tc], BF16) for i in range(2)]
                l3sb = [sb3(f"l3sb{i}", [128, Tc], BF16) for i in range(2)]
                b1sb = [sb3(f"b1sb{i}", [128, Tc], BF16) for i in range(2)]
                b3sb = [sb3(f"b3sb{i}", [128, Tc], BF16) for i in range(2)]
                w1t = [sb3(f"w1t{i}", [128, ND, 512], BF16) for i in range(2)]
                w3t = [sb3(f"w3t{i}", [128, ND, 512], BF16) for i in range(2)]
                b1t = [sb3(f"b1t{i}", [64, NEXP, 512], BF16) for i in range(2)]
                b3t = [sb3(f"b3t{i}", [64, NEXP, 512], BF16) for i in range(2)]
                ps_bs = ps3("psF_bs", [128, Tc])
                ps_l1 = ps3("psF_l1", [128, 2, Tc])
                ps_l3 = ps3("psF_l3", [128, 2, Tc])
                ps_e2 = [ps3(f"psF_e2_{i}", [128, Tc]) for i in range(3)]

                def grp_load(fg, par):
                    c0 = 512 * fg
                    dma(S, "sp", w1t[par][:], S.f_scr["f_w1T"].ap()[:, c0:c0 + 512].rearrange("(a p) n -> p a n", p=128), tag=f"F_w1{ci}_{fg}")
                    dma(S, "act", w3t[par][:], S.f_scr["f_w3T"].ap()[:, c0:c0 + 512].rearrange("(a p) n -> p a n", p=128), tag=f"F_w3{ci}_{fg}")
                    dma(S, "pool", b1t[par][:], FD["f_b1p"].ap()[:, :, c0:c0 + 512].rearrange("e p n -> p e n"), tag=f"F_b1l{ci}_{fg}")
                    dma(S, "pool", b3t[par][:], FD["f_b3p"].ap()[:, :, c0:c0 + 512].rearrange("e p n -> p e n"), tag=f"F_b3l{ci}_{fg}")

                def base_gen(ft):
                    bpar = ft % 2
                    if ft % 4 == 1 and ft // 4 + 1 < NF // 4:
                        grp_load(ft // 4 + 1, (ft // 4 + 1) % 2)
                    w1 = w1t[(ft // 4) % 2]; w3 = w3t[(ft // 4) % 2]
                    fo = 128 * (ft % 4)
                    yield
                    for j in range(ND):
                        mm(S, ps_bs[:], w1[:, j, fo:fo + 128], xn[:, j, :], start=(j == 0), stop=(j == ND - 1), tag=f"F_b1{ci}_{ft}_{j}")
                        yield
                    yield
                    act(S, b1sb[bpar][:], ps_bs[:], AF.Identity, tag=f"F_be1{ci}_{ft}")
                    for j in range(ND):
                        mm(S, ps_bs[:], w3[:, j, fo:fo + 128], xn[:, j, :], start=(j == 0), stop=(j == ND - 1), tag=f"F_b3{ci}_{ft}_{j}")
                        yield
                    act(S, b3sb[bpar][:], ps_bs[:], AF.Identity, tag=f"F_be3{ci}_{ft}")
                    while True:
                        yield
                grp_load(0, 0)
                g = base_gen(0)
                for _ in range(19):
                    next(g)

                pend_a2 = []

                def emit_a2(ft, e):
                    bk, off = e // 3, 32 * (e % 3)
                    S.op("pe", lambda h, _o=ps_e2[bk][off:off + 16, :], _l=a2T[:, ft, e, :], _r=p_bfs[e % 2][:], _s=(ft == 0), _t=(ft == NF - 1), _p=(0, off):
                         h.matmul(_o, _l, _r, start=_s, stop=_t, tile_position=_p),
                         reads=[a2T[:, ft, e, :], p_bfs[e % 2][:]], writes=[ps_e2[bk][off:off + 16, :]], tag=f"F_e2_{ci}_{ft}_{e}")

                pend_tail = []

                def emit_tail(ft, e):
                    ep = e % 2
                    sig = sigs[ep]; qq = qqs[ep]; p_bf = p_bfs[ep]
                    act(S, sig[:], h1s[ep][:], AF.Silu, tag=f"F_sg{ci}_{ft}_{e}")
                    v_op(S, "tensor_mul", qq[:], sig[:], h3s[ep][:], tag=f"F_q{ci}_{ft}_{e}")
                    v_op(S, "tensor_mul", p_bf[:], qq[:], w_bc[:, e, :], tag=f"F_p{ci}_{ft}_{e}")
                    if e == 0:
                        v_op(S, "tensor_copy", actmix[:, ft, :], p_bf[:], tag=f"F_ac{ci}_{ft}_{e}")
                    else:
                        v_op(S, "tensor_add", actmix[:, ft, :], actmix[:, ft, :], p_bf[:], tag=f"F_ac{ci}_{ft}_{e}")
                    if pend_a2:
                        emit_a2(*pend_a2.pop())
                    pend_a2.append((ft, e))

                for ft in range(NF):
                    bpar = ft % 2
                    gpar = (ft // 4) % 2
                    fo = 128 * (ft % 4)
                    b1s = b1t[gpar]; b3s = b3t[gpar]
                    g = base_gen(ft + 1) if ft + 1 < NF else None
                    for e in range(NEXP):
                        gi = e // 4
                        ep = e % 2
                        mm(S, ps_l1[:, ep, :], b1s[:, e, fo:fo + 128], etr1[:, gi, :], start=True, stop=True, tag=f"F_l1{ci}_{ft}_{e}")
                        act(S, l1sb[ep][:], ps_l1[:, ep, :], AF.Identity, tag=f"F_le1{ci}_{ft}_{e}")
                        mm(S, ps_l3[:, ep, :], b3s[:, e, fo:fo + 128], etr3[:, gi, :], start=True, stop=True, tag=f"F_l3{ci}_{ft}_{e}")
                        act(S, l3sb[ep][:], ps_l3[:, ep, :], AF.Identity, tag=f"F_le3{ci}_{ft}_{e}")
                        v_op(S, "tensor_add", h1s[ep][:], l1sb[ep][:], b1sb[bpar][:], tag=f"F_h1{ci}_{ft}_{e}")
                        v_op(S, "tensor_add", h3s[ep][:], l3sb[ep][:], b3sb[bpar][:], tag=f"F_h3{ci}_{ft}_{e}")
                        if pend_tail:
                            emit_tail(*pend_tail.pop())
                        pend_tail.append((ft, e))
                        if g is not None:
                            next(g)
                            next(g)
                    if g is not None:
                        for _ in range(4):
                            next(g)
                if pend_tail:
                    emit_tail(*pend_tail.pop())
                if pend_a2:
                    emit_a2(*pend_a2.pop())
                if getattr(S, "debug_spill", False):
                    if not hasattr(S, "am_dbg"):
                        S.am_dbg = S.dram(f"am_dbg{S.uid()}", [FF, T], BF16, kind="ExternalOutput")
                    for ftd in range(NF):
                        dma(S, "sp", S.am_dbg.ap()[128 * ftd:128 * (ftd + 1), t0:t0 + Tc], actmix[:, ftd, :], tag=f"dbg_am{ci}_{ftd}")
                for e in range(NEXP):
                    bk, off = e // 3, 32 * (e % 3)
                    v_op(S, "tensor_scalar", etr2w[e][:], ps_e2[bk][off:off + 16, :], 2.0, None, ALU.mult, tag=f"F_e2w{ci}_{e}")
                if getattr(S, "debug_spill", False):
                    if not hasattr(S, "e2_dbg"):
                        S.e2_dbg = S.dram(f"e2_dbg{S.uid()}", [NEXP, 16, T], BF16, kind="ExternalOutput")
                    for e in range(NEXP):
                        dma(S, "sp", S.e2_dbg.ap()[e, :, t0:t0 + Tc], etr2w[e][:], tag=f"dbg_e2{ci}_{e}")

            with pscope(S) as pq:
                sb3 = lambda n, s, d=F32: pq.enter_context(nc.sbuf_tensor(f"{n}_{S.uid()}", list(s), d))
                ps3 = lambda n, s, d=F32: pq.enter_context(nc.psum_tensor(f"{n}_{S.uid()}", list(s), d))
                w2t = [sb3(f"w2t{i}", [128, D], BF16) for i in range(2)]
                ps_o = ps3("psF_o", [128, 8, 512])
                for ft in range(NF):
                    w2 = w2t[ft % 2]
                    dma(S, ("sp", "act")[ft % 2], w2[:], S.f_scr["f_w2T"].ap()[128 * ft:128 * (ft + 1), :], tag=f"F_w2{ci}_{ft}")
                    for di in range(8):
                        mm(S, ps_o[:, di, :], w2[:, 128 * di:128 * (di + 1)], actmix[:, ft, :], start=(ft == 0), stop=False, tag=f"F_o{ci}_{ft}_{di}")
                for di in range(8):
                    for e in range(NEXP):
                        mm(S, ps_o[:, di, :], b2T[0:16, e, 128 * di:128 * (di + 1)], etr2w[e][:], start=False, stop=(e == NEXP - 1), tag=f"F_ob{ci}_{di}_{e}")
                    v_op(S, "tensor_add", x_sb[:, di, :], x_sb[:, di, :], ps_o[:, di, :], tag=f"F_ox{ci}_{di}")
                dma(S, "sp", fm_dram(x_out_h, t0, t0 + Tc), x_sb[:], tag=f"F_stx{ci}")


def run_l0_ffn(per_core, n_cores=NCORES, debug_spill=False, **kw):
    S = Prog("l0f")
    S.debug_spill = debug_spill
    x_in = S.dram("x_in", [D, T], F32, kind="ExternalInput")
    x_out = S.dram("x_out", [D, T], F32, kind="ExternalOutput")
    C = Common(S)
    emit_l0_ffn(S, C, x_in, x_out, idx=0)
    res = S.run(per_core, n_cores, **kw)
    return [r["x_out"] for r in res.results], res


def emit_l1_attn_b(S, C, x_in_h, x_out_h):
    nc = S.nc
    dd = {}
    for nm, sh in (("tb_wqT", [D, D]), ("tb_wkT", [D, D]), ("tb_wvT", [D, D]), ("tb_woT", [D, D]),
                   ("tb_w1T", [D, FF]), ("tb_w2T", [FF, D]),
                   ("tb_bq", [1, D]), ("tb_bk", [1, D]), ("tb_bv", [1, D]), ("tb_bo", [1, D]),
                   ("tb_g1", [128, ND]), ("tb_b1c", [128, ND]), ("tb_g2", [128, ND]), ("tb_b2c", [128, ND]),
                   ("tb_bf1", [1, FF]), ("tb_bf2", [1, D]),
                   ("c_headmask", [ND, 128, NHEAD]), ("c_sel64", [NHEAD, NHEAD, 64]), ("c_ones_c_bf", [1, 128])):
        dd[nm] = S.dram(nm, sh, F32 if not nm.startswith("c_") else BF16, kind="ExternalInput")
    u0 = S.uid()
    y_d = S.dram(f"tb_y{u0}", [D, T], F32, kind="Internal")
    q_d = S.dram(f"tb_q{u0}", [D, T], BF16, kind="Internal")
    bias1_d = S.dram(f"tb_bias1{u0}", [1, FF], BF16, kind="Internal")
    MN = 1.0 - 1e-5
    ITEMP = float(np.float32(2.0 / math.sqrt(HD)))

    with pscope(S) as ph:
        sbl = lambda n, s, d=F32: ph.enter_context(nc.sbuf_tensor(f"{n}_{S.uid()}", list(s), d))
        g1 = sbl("g1", [128, ND]); dma(S, "sp", g1[:], dd["tb_g1"].ap(), tag="ld_g1")
        g2 = sbl("g2", [128, ND]); dma(S, "sp", g2[:], dd["tb_g2"].ap(), tag="ld_g2")
        hmask = sbl("hmask", [128, ND, NHEAD], BF16)
        dma(S, "sp", hmask[:], dd["c_headmask"].ap().rearrange("a p n -> p a n"), tag="ld_hm")
        sel64 = sbl("sel64", [NHEAD, NHEAD, 64], BF16)
        dma(S, "sp", sel64[:], dd["c_sel64"].ap().rearrange("a p n -> p a n"), tag="ld_sel")
        ones_cb = sbl("ones_cb", [1, 128], BF16); dma(S, "sp", ones_cb[:], dd["c_ones_c_bf"].ap(), tag="ld_ocb")
        bias_2 = sbl("bias_2", [1, D], BF16)

        Tc = 512
        nch = T // Tc
        pa = ph.enter_context(pscope(S))
        sba = lambda n, s, d=F32: pa.enter_context(nc.sbuf_tensor(f"{n}_{S.uid()}", list(s), d))
        bias_q = sba("bias_q", [1, D], BF16)
        bias_k = sba("bias_k", [1, D], BF16)
        bias_v = sba("bias_v", [1, D], BF16)
        bias_o = sba("bias_o", [1, D], BF16)
        kaug = sba("kaug", [128, NHEAD, T], BF16)
        vtok = sba("vtok", [128, T // 128, NHEAD * 65], BF16)
        fq_all = sba("fq_all", [16, T], BF16)
        fk_all = sba("fk_all", [16, T], BF16)
        with pscope(S) as pp:
            sb2 = lambda n, s, d=F32: pp.enter_context(nc.sbuf_tensor(f"{n}_{S.uid()}", list(s), d))
            ps2 = lambda n, s, d=F32: pp.enter_context(nc.psum_tensor(f"{n}_{S.uid()}", list(s), d))
            b1c = sb2("b1c", [128, ND], BF16); dma(S, "pool", b1c[:], dd["tb_b1c"].ap(), tag="ld_b1c")
            b2c = sb2("b2c", [128, ND], BF16); dma(S, "pool", b2c[:], dd["tb_b2c"].ap(), tag="ld_b2c")
            w512 = sb2("w512", [128, ND, 512], BF16)
            ps_b = ps2("ps_b", [1, 512])
            rowbuf = sb2("rowbuf", [1, FF])
            bias_1 = sb2("bias_1", [1, FF], BF16)
            for wn, bn, brow in (("tb_wqT", "tb_bq", bias_q), ("tb_wkT", "tb_bk", bias_k), ("tb_wvT", "tb_bv", bias_v)):
                dma(S, "sp", rowbuf[:, 0:D], dd[bn].ap(), tag=f"ld_{bn}")
                for n0 in range(0, D, 512):
                    load_cast_w(S, dd[wn], w512, D, 512, f"ldb_{wn}{n0}", colsl=(n0, n0 + 512))
                    bias_row_mm(S, C, ps_b, b1c, w512, ND, 0, 512, f"br_{wn}{n0}")
                    v_op(S, "tensor_tensor", brow[:, n0:n0 + 512], ps_b[:, :], rowbuf[:, n0:n0 + 512], ALU.add, tag=f"be_{wn}{n0}")
            dma(S, "sp", rowbuf[:, 0:D], dd["tb_bo"].ap(), tag="ld_bo")
            v_op(S, "tensor_copy", bias_o[:], rowbuf[:, 0:D], tag="bo_bf")
            dma(S, "sp", rowbuf[:, 0:D], dd["tb_bf2"].ap(), tag="ld_bf2")
            v_op(S, "tensor_copy", bias_2[:], rowbuf[:, 0:D], tag="bf2_bf")
            dma(S, "sp", rowbuf[:], dd["tb_bf1"].ap(), tag="ld_bf1")
            for n0 in range(0, FF, 512):
                load_cast_w(S, dd["tb_w1T"], w512, D, 512, f"ldb_w1{n0}", colsl=(n0, n0 + 512))
                bias_row_mm(S, C, ps_b, b2c, w512, ND, 0, 512, f"br_w1{n0}")
                v_op(S, "tensor_tensor", bias_1[:, n0:n0 + 512], ps_b[:, :], rowbuf[:, n0:n0 + 512], ALU.add, tag=f"be_w1{n0}")
            dma(S, "sp", bias1_d.ap(), bias_1[:], tag="st_b1d")

        with pscope(S) as pp:
            sb2 = lambda n, s, d=F32: pp.enter_context(nc.sbuf_tensor(f"{n}_{S.uid()}", list(s), d))
            ps2 = lambda n, s, d=F32: pp.enter_context(nc.psum_tensor(f"{n}_{S.uid()}", list(s), d))
            x_sb = sb2("xT", [128, ND, Tc])
            q_fm = sb2("q_fmP", [128, ND, Tc], BF16)
            xc = sb2("xTc", [128, ND, Tc])
            xn = sb2("xnT", [128, ND, Tc], BF16)
            k_fm = sb2("k_fmT", [128, ND, Tc], BF16)
            wsl = [sb2(f"wsl{i}", [128, ND, 128], BF16) for i in range(2)]
            wvl = [sb2(f"wvl{i}", [128, ND, 512], BF16) for i in range(1)]
            qsq = sb2("qsqT", [128, Tc], BF16)
            h2row = sb2("h2row", [16, Tc])
            k2c_c = sb2("k2c_c", [16, Tc], BF16)
            rows = mk_ln_rows(S, Tc, "rT", eps2=True, alloc=sb2)
            ps_s, ps_ss = ps2("psT_s", [1, Tc]), ps2("psT_ss", [1, Tc])
            ps_bc = ps2("psT_bc", [128, Tc])
            ps_o = [ps2("psT_o0", [128, Tc]), ps2("psT_o1", [128, Tc])]
            ps_h2 = ps2("psT_h2", [16, Tc])
            ps_f = ps2("psT_f", [64, Tc])
            for ci in range(nch):
                t0 = ci * Tc
                dma(S, "sp", x_sb[:], fm_dram(x_in_h, t0, t0 + Tc), tag=f"P_ldx{ci}")
                for j in range(ND):
                    v_op(S, "tensor_copy", q_fm[:, j, :], x_sb[:, j, :], tag=f"P_xb{ci}_{j}")
                    mm(S, ps_s[:], C.ones_col_bf[:], q_fm[:, j, :], start=(j == 0), stop=(j == ND - 1), tag=f"P_s{ci}_{j}")
                    act(S, qsq[:], x_sb[:, j, :], AF.Square, tag=f"P_xq{ci}_{j}")
                    mm(S, ps_ss[:], C.ones_col_bf[:], qsq[:], start=(j == 0), stop=(j == ND - 1), tag=f"P_ss{ci}_{j}")
                ln_rows(S, ps_s, ps_ss, rows, D, EPS_LN, f"P_r{ci}", eps2=EPS_LN)
                bcast_row(S, C, ps_bc[:], rows["m"][:], f"P_bm{ci}")
                for j in range(ND):
                    v_op(S, "tensor_sub", xc[:, j, :], x_sb[:, j, :], ps_bc[:], tag=f"P_sub{ci}_{j}")
                bcast_row(S, C, ps_bc[:], rows["rs"][:], f"P_bro{ci}")
                for j in range(ND):
                    v_op(S, "tensor_mul", x_sb[:, j, :], xc[:, j, :], ps_bc[:], tag=f"P_y{ci}_{j}")
                dma(S, "sp", fm_dram(y_d, t0, t0 + Tc), x_sb[:], tag=f"P_sty{ci}")
                bcast_row(S, C, ps_bc[:], rows["rs2"][:], f"P_br{ci}")
                for j in range(ND):
                    v_op(S, "scalar_tensor_tensor", xn[:, j, :], xc[:, j, :], g1[:, j:j + 1], ps_bc[:], ALU.mult, ALU.mult, tag=f"P_xn{ci}_{j}")
                for dst, wn, bias in ((q_fm, "tb_wqT", bias_q), (k_fm, "tb_wkT", bias_k)):
                    for nt in range(ND):
                        w = wsl[nt % 2]
                        load_cast_w(S, dd[wn], w, D, 128, f"P_{wn}{ci}_{nt}", colsl=(128 * nt, 128 * (nt + 1)))
                        po = ps_o[nt % 2]
                        for j in range(ND):
                            mm(S, po[:], w[:, j, :], xn[:, j, :], start=(j == 0), stop=False, tag=f"P_{wn}m{ci}_{nt}_{j}")
                        mm(S, po[:], bias[:, 128 * nt:128 * (nt + 1)], C.ones_r_bf[:, :Tc], start=False, stop=True, tag=f"P_{wn}b{ci}_{nt}")
                        act(S, dst[:, nt, :], po[:], AF.Identity, tag=f"P_{wn}e{ci}_{nt}")
                for tt in range(Tc // 128):
                    tile = (t0 + 128 * tt) // 128
                    v_op(S, "memset", vtok[:, tile, :].rearrange("p (h c) -> p h c", c=65)[:, :, 64:65], 1.0, tag=f"P_vone{ci}_{tt}")
                for half in range(2):
                    wv = wvl[0]
                    load_cast_w(S, dd["tb_wvT"], wv, D, 512, f"P_wv{ci}_{half}", colsl=(512 * half, 512 * (half + 1)))
                    for tt in range(Tc // 128):
                        tile = (t0 + 128 * tt) // 128
                        po = ps_o[tt % 2]
                        for j in range(ND):
                            mm(S, po[:], xn[:, j, 128 * tt:128 * (tt + 1)], wv[:, j, :], start=(j == 0), stop=False, tag=f"P_v{ci}_{half}_{tt}_{j}")
                        mm(S, po[:], ones_cb[:], bias_v[:, 512 * half:512 * (half + 1)], start=False, stop=True, tag=f"P_vb{ci}_{half}_{tt}")
                        for hh in range(8):
                            h = 8 * half + hh
                            v_op(S, "tensor_copy", vtok[:, tile, 65 * h:65 * h + 64], po[:, 64 * hh:64 * (hh + 1)], tag=f"P_ve{ci}_{half}_{tt}_{hh}")
                dma(S, "sp", fm_dram(q_d, t0, t0 + Tc), q_fm[:], tag=f"P_stq{ci}")
                for nm, src, frow in (("q", q_fm, fq_all), ("k", k_fm, fk_all)):
                    for j in range(ND):
                        act(S, qsq[:], src[:, j, :], AF.Square, tag=f"P_{nm}sq{ci}_{j}")
                        mm(S, ps_h2[:], hmask[:, j, :], qsq[:], start=(j == 0), stop=(j == ND - 1), tag=f"P_{nm}2{ci}_{j}")
                    v_op(S, "tensor_copy", h2row[:], ps_h2[:], tag=f"P_{nm}2c{ci}")
                    act(S, h2row[:], h2row[:], AF.Sqrt, tag=f"P_{nm}n{ci}")
                    v_op(S, "reciprocal", h2row[:], h2row[:], tag=f"P_{nm}ri{ci}")
                    v_op(S, "tensor_scalar", frow[:, t0:t0 + Tc], h2row[:], MN, 1.0, ALU.mult, ALU.min, tag=f"P_{nm}f{ci}")
                    if nm == "k":
                        v_op(S, "tensor_copy", k2c_c[:], ps_h2[:], tag=f"P_k2r{ci}")
                        v_op(S, "tensor_mul", h2row[:], frow[:, t0:t0 + Tc], frow[:, t0:t0 + Tc], tag=f"P_kf2{ci}")
                        v_op(S, "tensor_mul", k2c_c[:], k2c_c[:], h2row[:], tag=f"P_k2c{ci}")
                for h in range(NHEAD):
                    mm(S, ps_f[:], sel64[:, h, :], fk_all[:, t0:t0 + Tc], start=True, stop=True, tag=f"P_fkb{ci}_{h}")
                    v_op(S, "tensor_mul", kaug[0:64, h, t0:t0 + Tc], k_fm[64 * (h % 2):64 * (h % 2) + 64, h // 2, :], ps_f[:], tag=f"P_ka{ci}_{h}")
                    mm(S, ps_f[:], sel64[:, h, :], k2c_c[:], start=True, stop=True, tag=f"P_k2b{ci}_{h}")
                    v_op(S, "tensor_copy", kaug[64:65, h, t0:t0 + Tc], ps_f[0:1, :], tag=f"P_kr{ci}_{h}")

        with pscope(S) as pp:
            sb2 = lambda n, s, d=F32: pp.enter_context(nc.sbuf_tensor(f"{n}_{S.uid()}", list(s), d))
            ps2 = lambda n, s, d=F32: pp.enter_context(nc.psum_tensor(f"{n}_{S.uid()}", list(s), d))
            qaug = sb2("qaug", [128, NHEAD, Tc], BF16)
            q_fm = sb2("q_fmA", [128, ND, Tc], BF16)
            expS = sb2("expS", [128, T // 128, Tc], BF16)
            att_o = sb2("att_o", [128, ND, Tc], BF16)
            den = sb2("den", [1, Tc])
            otmp = sb2("otmp", [64, Tc], BF16)
            x1 = sb2("x1T", [128, ND, Tc])
            wsl = [sb2(f"wslA{i}", [128, ND, 128], BF16) for i in range(2)]
            rowsA = mk_ln_rows(S, Tc, "rTA", alloc=sb2)
            ps_S = [ps2("psA_S0", [128, Tc]), ps2("psA_S1", [128, Tc])]
            ps_av = ps2("psA_av", [128, Tc])
            ps_dn = ps2("psA_dn", [64, Tc])
            ps_f = ps2("psA_f", [64, Tc])
            ps_s, ps_ss = ps2("psA_s", [1, Tc]), ps2("psA_ss", [1, Tc])
            ps_bc = ps2("psA_bc", [128, Tc])
            for ci in range(nch):
                t0 = ci * Tc
                dma(S, "sp", q_fm[:], fm_dram(q_d, t0, t0 + Tc), tag=f"A_ldq{ci}")
                v_op(S, "memset", qaug[64:128, :, :], -1.0 / 8.0, tag=f"A_qm{ci}")
                for h in range(NHEAD):
                    mm(S, ps_f[:], sel64[:, h, :], fq_all[:, t0:t0 + Tc], start=True, stop=True, tag=f"A_fqb{ci}_{h}")
                    v_op(S, "scalar_tensor_tensor", qaug[0:64, h, :], ps_f[:], ITEMP, q_fm[64 * (h % 2):64 * (h % 2) + 64, h // 2, :], ALU.mult, ALU.mult, tag=f"A_qa{ci}_{h}")
                for h in range(NHEAD):
                    for kt in range(T // 128):
                        pS = ps_S[kt % 2]
                        mm(S, pS[:], kaug[0:65, h, 128 * kt:128 * (kt + 1)], qaug[0:65, h, :], start=True, stop=True, tag=f"A_S{ci}_{h}_{kt}")
                        act(S, expS[:, kt, :], pS[:], AF.Exp, tag=f"A_e{ci}_{h}_{kt}")
                    for kt in range(T // 128):
                        mm(S, ps_av[0:65, :], vtok[:, kt, 65 * h:65 * (h + 1)], expS[:, kt, :], start=(kt == 0), stop=(kt == T // 128 - 1), tag=f"A_av{ci}_{h}_{kt}")
                    v_op(S, "tensor_copy", den[:], ps_av[64:65, :], tag=f"A_dc{ci}_{h}")
                    v_op(S, "reciprocal", den[:], den[:], tag=f"A_dr{ci}_{h}")
                    act(S, otmp[:], ps_av[0:64, :], AF.Identity, tag=f"A_oc{ci}_{h}")
                    mm(S, ps_dn[:], C.ones_c_f32[0:1, 0:64], den[:], start=True, stop=True, tag=f"A_db{ci}_{h}")
                    v_op(S, "tensor_mul", att_o[64 * (h % 2):64 * (h % 2) + 64, h // 2, :], otmp[:], ps_dn[:], tag=f"A_om{ci}_{h}")
                dma(S, "sp", x1[:], fm_dram(y_d, t0, t0 + Tc), tag=f"A_ldy{ci}")
                for nt in range(ND):
                    w = wsl[nt % 2]
                    load_cast_w(S, dd["tb_woT"], w, D, 128, f"A_wo{ci}_{nt}", colsl=(128 * nt, 128 * (nt + 1)))
                    po = ps_S[nt % 2]
                    for j in range(ND):
                        mm(S, po[:], w[:, j, :], att_o[:, j, :], start=(j == 0), stop=False, tag=f"A_wom{ci}_{nt}_{j}")
                    mm(S, po[:], bias_o[:, 128 * nt:128 * (nt + 1)], C.ones_r_bf[:, :Tc], start=False, stop=True, tag=f"A_wob{ci}_{nt}")
                    v_op(S, "tensor_add", x1[:, nt, :], x1[:, nt, :], po[:], tag=f"A_res{ci}_{nt}")
                for j in range(ND):
                    act(S, qaug[:, 8 + j, :], x1[:, j, :], AF.Square, tag=f"A_sq{ci}_{j}")
                    mm(S, ps_s[:], C.ones_col_bf[:], qaug[:, 8 + j, :], start=(j == 0), stop=(j == ND - 1), tag=f"A_n2{ci}_{j}")
                act(S, rowsA["var"][:], ps_s[:], AF.Sqrt, tag=f"A_nn{ci}")
                v_op(S, "reciprocal", rowsA["rs"][:], rowsA["var"][:], tag=f"A_nr{ci}")
                v_op(S, "tensor_scalar", rowsA["m"][:], rowsA["rs"][:], MN, 1.0, ALU.mult, ALU.min, tag=f"A_nf{ci}")
                bcast_row(S, C, ps_bc[:], rowsA["m"][:], f"A_nb{ci}")
                for j in range(ND):
                    v_op(S, "tensor_mul", x1[:, j, :], x1[:, j, :], ps_bc[:], tag=f"A_px{ci}_{j}")
                dma(S, "sp", fm_dram(y_d, t0, t0 + Tc), x1[:], tag=f"A_stx1{ci}")

        pa.close()
        S.barrier()
        with pscope(S) as pp:
            sb2 = lambda n, s, d=F32: pp.enter_context(nc.sbuf_tensor(f"{n}_{S.uid()}", list(s), d))
            ps2 = lambda n, s, d=F32: pp.enter_context(nc.psum_tensor(f"{n}_{S.uid()}", list(s), d))
            x1 = sb2("x1B", [128, ND, Tc])
            xb = sb2("xBb", [128, ND, Tc], BF16)
            xq = sb2("xBq", [128, ND, Tc], BF16)
            xc = sb2("xBc", [128, ND, Tc])
            zn = sb2("znB", [128, ND, Tc], BF16)
            h_bf = sb2("h_bfB", [128, NF, Tc], BF16)
            x0 = sb2("x0B", [128, ND, Tc])
            wB = [sb2(f"wB{i}", [128, ND, 128], BF16) for i in range(2)]
            w2B = [sb2(f"w2B{i}", [128, 512], BF16) for i in range(2)]
            rowsB = mk_ln_rows(S, Tc, "rTB", alloc=sb2)
            bias_1 = sb2("bias_1B", [1, FF], BF16)
            dma(S, "sp", bias_1[:], bias1_d.ap(), tag="ld_b1d")
            ps_s, ps_ss = ps2("psB_s", [1, Tc]), ps2("psB_ss", [1, Tc])
            ps_bc = ps2("psB_bc", [128, Tc])
            ps_h = [ps2("psB_h0", [128, Tc]), ps2("psB_h1", [128, Tc])]
            ps_o = ps2("psB_o", [128, 2, Tc])
            for ci in range(nch):
                t0 = ci * Tc
                dma(S, "sp", x1[:], fm_dram(y_d, t0, t0 + Tc), tag=f"B_ldx{ci}")
                for j in range(ND):
                    v_op(S, "tensor_copy", xb[:, j, :], x1[:, j, :], tag=f"B_xb{ci}_{j}")
                    act(S, xq[:, j, :], x1[:, j, :], AF.Square, tag=f"B_xq{ci}_{j}")
                ln_stats(S, C, xb, xq, ND, Tc, ps_s, ps_ss, f"B_st{ci}")
                ln_rows(S, ps_s, ps_ss, rowsB, D, EPS_LN, f"B_r{ci}")
                bcast_row(S, C, ps_bc[:], rowsB["m"][:], f"B_bm{ci}")
                for j in range(ND):
                    v_op(S, "tensor_sub", xc[:, j, :], x1[:, j, :], ps_bc[:], tag=f"B_sub{ci}_{j}")
                bcast_row(S, C, ps_bc[:], rowsB["rs"][:], f"B_br{ci}")
                for j in range(ND):
                    v_op(S, "scalar_tensor_tensor", zn[:, j, :], xc[:, j, :], g2[:, j:j + 1], ps_bc[:], ALU.mult, ALU.mult, tag=f"B_zn{ci}_{j}")
                for mi in range(NF):
                    w = wB[mi % 2]
                    load_cast_w(S, dd["tb_w1T"], w, D, 128, f"B_w1{ci}_{mi}", colsl=(128 * mi, 128 * (mi + 1)))
                    ph_ = ps_h[mi % 2]
                    for j in range(ND):
                        mm(S, ph_[:], w[:, j, :], zn[:, j, :], start=(j == 0), stop=False, tag=f"B_mm{ci}_{mi}_{j}")
                    mm(S, ph_[:], bias_1[:, 128 * mi:128 * (mi + 1)], C.ones_r_bf[:, :Tc], start=False, stop=True, tag=f"B_bb{ci}_{mi}")
                    act(S, h_bf[:, mi, :], ph_[:], AF.Gelu, tag=f"B_g{ci}_{mi}")
                for half in range(4):
                    for mi in range(NF):
                        w2 = w2B[mi % 2]
                        load_cast_w(S, dd["tb_w2T"], w2, 128, 512, f"B_w2{ci}_{half}_{mi}", rowsl=(128 * mi, 128 * (mi + 1)), colsl=(256 * half, 256 * (half + 1)) if False else (512 * (half // 2), 512 * (half // 2 + 1)))
                        for dq in range(2):
                            di = 2 * half + dq
                            mm(S, ps_o[:, dq, :], w2[:, 128 * (di % 4):128 * (di % 4) + 128], h_bf[:, mi, :], start=(mi == 0), stop=False, tag=f"B_o{ci}_{half}_{mi}_{dq}")
                    for dq in range(2):
                        di = 2 * half + dq
                        mm(S, ps_o[:, dq, :], bias_2[:, 128 * di:128 * (di + 1)], C.ones_r_bf[:, :Tc], start=False, stop=True, tag=f"B_ob{ci}_{di}")
                        v_op(S, "tensor_add", x1[:, di, :], x1[:, di, :], ps_o[:, dq, :], tag=f"B_oa{ci}_{di}")
                for j in range(ND):
                    act(S, xq[:, j, :], x1[:, j, :], AF.Square, tag=f"B_sq2{ci}_{j}")
                    mm(S, ps_s[:], C.ones_col_bf[:], xq[:, j, :], start=(j == 0), stop=(j == ND - 1), tag=f"B_n2{ci}_{j}")
                act(S, rowsB["var"][:], ps_s[:], AF.Sqrt, tag=f"B_nn{ci}")
                v_op(S, "reciprocal", rowsB["rs"][:], rowsB["var"][:], tag=f"B_nr{ci}")
                v_op(S, "tensor_scalar", rowsB["m"][:], rowsB["rs"][:], MN, 1.0, ALU.mult, ALU.min, tag=f"B_nf{ci}")
                bcast_row(S, C, ps_bc[:], rowsB["m"][:], f"B_nb{ci}")
                dma(S, "sp", x0[:], fm_dram(x_in_h, t0, t0 + Tc), tag=f"B_ldx0{ci}")
                for j in range(ND):
                    v_op(S, "tensor_mul", x1[:, j, :], x1[:, j, :], ps_bc[:], tag=f"B_px{ci}_{j}")
                    v_op(S, "tensor_add", x1[:, j, :], x1[:, j, :], x0[:, j, :], tag=f"B_ax{ci}_{j}")
                dma(S, "sp", fm_dram(x_out_h, t0, t0 + Tc), x1[:], tag=f"B_stx{ci}")


def run_l1_attn_b(per_core, n_cores=NCORES, **kw):
    S = Prog("l1b")
    x_in = S.dram("x_in", [D, T], F32, kind="ExternalInput")
    x_out = S.dram("x_out", [D, T], F32, kind="ExternalOutput")
    C = Common(S)
    emit_l1_attn_b(S, C, x_in, x_out)
    res = S.run(per_core, n_cores, **kw)
    return [r["x_out"] for r in res.results], res


def emit_l0_attn_b(S, C, x_in_h, x_out_h):
    nc = S.nc
    dd = {}
    for nm, sh, dt in (("nb_Wq", [D, D], F32), ("nb_Wk", [D, D], F32), ("nb_Wv", [D, D], F32),
                       ("nb_bq", [1, D], F32), ("nb_bk", [1, D], F32), ("nb_bv", [1, D], F32),
                       ("nb_pat", [NHEAD, 3, 128, 128], F32)):
        dd[nm] = S.dram(nm, sh, dt, kind="ExternalInput")
    u0 = S.uid()
    att_d = S.dram(f"nb_att{u0}", [D, T], BF16, kind="Internal")
    cc_in = S.dram(f"nb_ccin{u0}", [1, 8], F32, kind="Internal")
    cc_out = S.dram(f"nb_ccout{u0}", [1, 8], F32, kind="Internal")
    SC = float(np.float32(HD ** -0.25))
    L64 = 64

    with pscope(S) as ph:
        sbl = lambda n, s, d=F32: ph.enter_context(nc.sbuf_tensor(f"{n}_{S.uid()}", list(s), d))
        q_fm = sbl("q_fmN", [128, ND, T], BF16)
        k_fm = sbl("k_fmN", [128, ND, T], BF16)
        vtok = sbl("vtokN", [128, T // 128, D], BF16)
        pat = sbl("patN", [128, NHEAD, 3, 128], BF16)
        dma(S, "pool", pat[:], dd["nb_pat"].ap().rearrange("h j p n -> p h j n"), tag="ld_pat")
        q_l = sbl("q_lN", [128, ND, L64], BF16)
        k_l = sbl("k_lN", [128, ND, L64], BF16)
        biasq = sbl("biasqN", [1, D], BF16)
        biask = sbl("biaskN", [1, D], BF16)
        biasv = sbl("biasvN", [1, D], BF16)
        rowf = sbl("rowfN", [1, D])
        for bn, dst, sc_ in (("nb_bq", biasq, SC), ("nb_bk", biask, SC), ("nb_bv", biasv, 1.0)):
            dma(S, "sp", rowf[:], dd[bn].ap(), tag=f"ld_{bn}")
            v_op(S, "tensor_scalar", dst[:], rowf[:], sc_, None, ALU.mult, tag=f"sc_{bn}")

        Tc = 512
        nch = T // Tc
        with pscope(S) as pp:
            sb2 = lambda n, s, d=F32: pp.enter_context(nc.sbuf_tensor(f"{n}_{S.uid()}", list(s), d))
            ps2 = lambda n, s, d=F32: pp.enter_context(nc.psum_tensor(f"{n}_{S.uid()}", list(s), d))
            x_sb = sb2("xN", [128, ND, Tc])
            xb = sb2("xNb", [128, ND, Tc], BF16)
            xc = sb2("xNc", [128, ND, Tc])
            xn = sb2("xnN", [128, ND, Tc], BF16)
            xnq = sb2("xnqN", [128, ND, Tc], BF16)
            sq = sb2("sqN", [128, Tc], BF16)
            poolf = sb2("poolfN", [128, ND, 16])
            wsl = [sb2(f"wslN{i}", [128, ND, 128], BF16) for i in range(2)]
            wvl = [sb2(f"wvlN{i}", [128, ND, 512], BF16) for i in range(1)]
            rows = mk_ln_rows(S, Tc, "rN", alloc=sb2)
            ps_s, ps_ss = ps2("psN_s", [1, Tc]), ps2("psN_ss", [1, Tc])
            ps_bc = ps2("psN_bc", [128, Tc])
            ps_o = [ps2("psN_o0", [128, Tc]), ps2("psN_o1", [128, Tc])]
            for ci in range(nch):
                t0 = ci * Tc
                dma(S, "sp", x_sb[:], fm_dram(x_in_h, t0, t0 + Tc), tag=f"N_ldx{ci}")
                for j in range(ND):
                    v_op(S, "tensor_copy", xb[:, j, :], x_sb[:, j, :], tag=f"N_xb{ci}_{j}")
                    mm(S, ps_s[:], C.ones_col_bf[:], xb[:, j, :], start=(j == 0), stop=(j == ND - 1), tag=f"N_s{ci}_{j}")
                    act(S, sq[:], x_sb[:, j, :], AF.Square, tag=f"N_xq{ci}_{j}")
                    mm(S, ps_ss[:], C.ones_col_bf[:], sq[:], start=(j == 0), stop=(j == ND - 1), tag=f"N_ss{ci}_{j}")
                ln_rows(S, ps_s, ps_ss, rows, D, EPS_LN, f"N_r{ci}")
                bcast_row(S, C, ps_bc[:], rows["m"][:], f"N_bm{ci}")
                for j in range(ND):
                    v_op(S, "tensor_sub", xc[:, j, :], x_sb[:, j, :], ps_bc[:], tag=f"N_sub{ci}_{j}")
                bcast_row(S, C, ps_bc[:], rows["rs"][:], f"N_br{ci}")
                for j in range(ND):
                    v_op(S, "tensor_mul", xn[:, j, :], xc[:, j, :], ps_bc[:], tag=f"N_xn{ci}_{j}")
                    v_op(S, "tensor_scalar", xnq[:, j, :], xn[:, j, :], SC, None, ALU.mult, tag=f"N_xnq{ci}_{j}")
                for dst, wn, bias in ((q_fm, "nb_Wq", biasq), (k_fm, "nb_Wk", biask)):
                    for nt in range(ND):
                        w = wsl[nt % 2]
                        load_cast_w(S, dd[wn], w, D, 128, f"N_{wn}{ci}_{nt}", colsl=(128 * nt, 128 * (nt + 1)))
                        po = ps_o[nt % 2]
                        for j in range(ND):
                            mm(S, po[:], w[:, j, :], xnq[:, j, :], start=(j == 0), stop=False, tag=f"N_{wn}m{ci}_{nt}_{j}")
                        mm(S, po[:], bias[:, 128 * nt:128 * (nt + 1)], C.ones_r_bf[:, :Tc], start=False, stop=True, tag=f"N_{wn}b{ci}_{nt}")
                        act(S, dst[:, nt, t0:t0 + Tc], po[:], AF.Identity, tag=f"N_{wn}e{ci}_{nt}")
                for half in range(2):
                    wv = wvl[0]
                    load_cast_w(S, dd["nb_Wv"], wv, D, 512, f"N_wv{ci}_{half}", colsl=(512 * half, 512 * (half + 1)))
                    for tt in range(Tc // 128):
                        tile = (t0 + 128 * tt) // 128
                        po = ps_o[tt % 2]
                        for j in range(ND):
                            mm(S, po[:], xn[:, j, 128 * tt:128 * (tt + 1)], wv[:, j, :], start=(j == 0), stop=False, tag=f"N_v{ci}_{half}_{tt}_{j}")
                        mm(S, po[:], C.ones_r_bf[0:1, 0:128] if False else C.ones_col_bf[:, 0:1].rearrange("p o -> o p") if False else C.ones_r_bf[0:1, 0:128], biasv[:, 512 * half:512 * (half + 1)], start=False, stop=True, tag=f"N_vb{ci}_{half}_{tt}")
                        act(S, vtok[:, tile, 512 * half:512 * (half + 1)], po[:], AF.Identity, tag=f"N_ve{ci}_{half}_{tt}")
                for src, dst in ((q_fm, q_l), (k_fm, k_l)):
                    for j in range(ND):
                        S.op("dve", lambda h, _o=poolf[:, j, :], _i=src[:, j, t0:t0 + Tc].rearrange("p (a b) -> p a b", b=32):
                             h.tensor_reduce(_o, _i, AX.X, ALU.add),
                             reads=[src[:, j, t0:t0 + Tc]], writes=[poolf[:, j, :]], tag=f"N_pool{ci}_{j}")
                        v_op(S, "tensor_scalar", dst[:, j, 16 * ci:16 * (ci + 1)], poolf[:, j, :], 1.0 / 32.0, None, ALU.mult, tag=f"N_poolsc{ci}_{j}")

        inv_bf = sbl("inv_bf", [64, NHEAD, L64], BF16)
        with pscope(S) as pp:
            sb2 = lambda n, s, d=F32: pp.enter_context(nc.sbuf_tensor(f"{n}_{S.uid()}", list(s), d))
            ps2 = lambda n, s, d=F32: pp.enter_context(nc.psum_tensor(f"{n}_{S.uid()}", list(s), d))
            K2 = sb2("K2", [64, NHEAD, L64])
            K2t = sb2("K2t", [64, NHEAD, L64])
            Vt_ = sb2("Vt_", [64, NHEAD, L64])
            V_ = sb2("V_", [64, NHEAD, L64])
            colr = sb2("colr", [1, NHEAD * L64])
            lmax = sb2("lmax", [1, 8])
            i7 = sb2("i7", [64, L64])
            i15 = sb2("i15", [64, L64])
            i13 = sb2("i13", [64, L64])
            tA = sb2("tA", [64, L64])
            tAt = sb2("tAt", [64, L64])
            tB = sb2("tB", [64, L64])
            tBt = sb2("tBt", [64, L64])
            nmax = sb2("nmax", [64, 1])
            nsum = sb2("nsum", [64, 1])
            ps_k = [ps2(f"psK{i}", [64, 512]) for i in range(2)]
            ps_c = ps2("psKc", [1, 512])
            v_op(S, "tensor_scalar", i7[:], C.ident_f32[0:64, 0:64], 7.0, None, ALU.mult, tag="mk_i7")
            v_op(S, "tensor_scalar", i15[:], C.ident_f32[0:64, 0:64], 15.0, None, ALU.mult, tag="mk_i15")
            v_op(S, "tensor_scalar", i13[:], C.ident_f32[0:64, 0:64], 13.0, None, ALU.mult, tag="mk_i13")
            for h in range(NHEAD):
                hp, hj = 64 * (h % 2), h // 2
                mm(S, ps_k[0][:, 0:L64], q_l[hp:hp + 64, hj, :], k_l[hp:hp + 64, hj, :], start=True, stop=True, tag=f"K2_{h}")
                v_op(S, "reduce_max", nmax[:], ps_k[0][:, 0:L64], AX.X, negate=True, tag=f"K2m_{h}")
                act(S, K2[:, h, :], ps_k[0][:, 0:L64], AF.Exp, bias=nmax[:], accum_out=nsum[:], tag=f"K2e_{h}")
                v_op(S, "reciprocal", nsum[:], nsum[:], tag=f"K2r_{h}")
                v_op(S, "tensor_scalar", K2[:, h, :], K2[:, h, :], nsum[:], None, ALU.mult, tag=f"K2n_{h}")
                mm(S, ps_c[:, 0:L64], C.ones_col_f32[0:64, :], K2[:, h, :], start=True, stop=True, tag=f"K2c_{h}")
                v_op(S, "tensor_copy", colr[:, L64 * h:L64 * (h + 1)], ps_c[:, 0:L64], tag=f"K2cc_{h}")
                tr_op(S, ps_k[1][:, 0:L64], K2[:, h, :], C.ident_f32[0:64, 0:64], f"K2t_{h}")
                v_op(S, "tensor_copy", K2t[:, h, :], ps_k[1][:, 0:L64], tag=f"K2te_{h}")
            v_op(S, "reduce_max", lmax[0:1, 0:1], colr[:], AX.X, tag="gmax_l")
            v_op(S, "tensor_copy", lmax[0:1, 1:8], lmax[0:1, 0:1].to_broadcast((1, 7)) if False else lmax[0:1, 0:1], tag="gmax_pad") if False else None
            v_op(S, "memset", lmax[0:1, 1:8], 0.0, tag="gmax_z")
            dma(S, "sp", cc_in.ap(), lmax[:], tag="cc_st")
            S.op("pool", lambda h: h.collective_compute(
                "AllReduce", ALU.max, replica_groups=[list(range(NCORES))],
                ins=[cc_in[:]], outs=[cc_out[:]]),
                reads=[cc_in.ap()], writes=[cc_out.ap()], tag="cc_allreduce")
            dma(S, "sp", lmax[:], cc_out.ap(), tag="cc_ld")
            v_op(S, "reciprocal", lmax[0:1, 0:1], lmax[0:1, 0:1], tag="gmax_inv")
            mm(S, ps_c[:, 0:L64] if False else ps_k[0][:, 0:1], C.ones_c_f32[0:1, 0:64], lmax[0:1, 0:1], start=True, stop=True, tag="ginv_b")
            ginv = sb2("ginv", [64, 1])
            v_op(S, "tensor_copy", ginv[:], ps_k[0][0:64, 0:1], tag="ginv_c")
            for h in range(NHEAD):
                v_op(S, "tensor_scalar", V_[:, h, :], K2t[:, h, :], ginv[:], None, ALU.mult, tag=f"V0_{h}")
                v_op(S, "tensor_scalar", Vt_[:, h, :], K2[:, h, :], ginv[:], None, ALU.mult, tag=f"V0t_{h}")
                for it in range(6):
                    mm(S, ps_k[0][:, 0:L64], K2t[:, h, :], V_[:, h, :], start=True, stop=True, tag=f"NwM_{h}_{it}")
                    mm(S, ps_k[1][:, 0:L64], V_[:, h, :], K2t[:, h, :], start=True, stop=True, tag=f"NwMt_{h}_{it}")
                    v_op(S, "tensor_copy", tA[:], ps_k[0][:, 0:L64], tag=f"NwMc_{h}_{it}")
                    v_op(S, "tensor_copy", tAt[:], ps_k[1][:, 0:L64], tag=f"NwMtc_{h}_{it}")
                    v_op(S, "scalar_tensor_tensor", tB[:], tA[:], -1.0, i7[:], ALU.mult, ALU.add, tag=f"NwB_{h}_{it}")
                    v_op(S, "scalar_tensor_tensor", tBt[:], tAt[:], -1.0, i7[:], ALU.mult, ALU.add, tag=f"NwBt_{h}_{it}")
                    mm(S, ps_k[0][:, 0:L64], tAt[:], tB[:], start=True, stop=True, tag=f"NwMB_{h}_{it}")
                    mm(S, ps_k[1][:, 0:L64], tB[:], tAt[:], start=True, stop=True, tag=f"NwMBt_{h}_{it}")
                    v_op(S, "scalar_tensor_tensor", tB[:], ps_k[0][:, 0:L64], -1.0, i15[:], ALU.mult, ALU.add, tag=f"NwC_{h}_{it}")
                    v_op(S, "scalar_tensor_tensor", tBt[:], ps_k[1][:, 0:L64], -1.0, i15[:], ALU.mult, ALU.add, tag=f"NwCt_{h}_{it}")
                    mm(S, ps_k[0][:, 0:L64], tAt[:], tB[:], start=True, stop=True, tag=f"NwMC_{h}_{it}")
                    mm(S, ps_k[1][:, 0:L64], tBt[:] if False else tB[:], tAt[:], start=True, stop=True, tag=f"NwMCt_{h}_{it}")
                    v_op(S, "scalar_tensor_tensor", tB[:], ps_k[0][:, 0:L64], -1.0, i13[:], ALU.mult, ALU.add, tag=f"NwD_{h}_{it}")
                    v_op(S, "scalar_tensor_tensor", tBt[:], ps_k[1][:, 0:L64], -1.0, i13[:], ALU.mult, ALU.add, tag=f"NwDt_{h}_{it}")
                    mm(S, ps_k[0][:, 0:L64], Vt_[:, h, :], tB[:], start=True, stop=True, tag=f"NwV_{h}_{it}")
                    mm(S, ps_k[1][:, 0:L64], tB[:], Vt_[:, h, :], start=True, stop=True, tag=f"NwVt_{h}_{it}")
                    v_op(S, "tensor_scalar", V_[:, h, :], ps_k[0][:, 0:L64], 0.25, None, ALU.mult, tag=f"NwVc_{h}_{it}")
                    v_op(S, "tensor_scalar", Vt_[:, h, :], ps_k[1][:, 0:L64], 0.25, None, ALU.mult, tag=f"NwVtc_{h}_{it}")
                v_op(S, "tensor_copy", inv_bf[:, h, :], V_[:, h, :], tag=f"NwF_{h}")

        with pscope(S) as pp:
            sb2 = lambda n, s, d=F32: pp.enter_context(nc.sbuf_tensor(f"{n}_{S.uid()}", list(s), d))
            ps2 = lambda n, s, d=F32: pp.enter_context(nc.psum_tensor(f"{n}_{S.uid()}", list(s), d))
            K1n = [sb2(f"K1n{i}", [128, L64], BF16) for i in range(2)]
            K1t = [sb2(f"K1t{i}", [64, T], BF16) for i in range(2)]
            K3raw = [sb2(f"K3raw{i}", [64, T]) for i in range(2)]
            K3bf = [sb2(f"K3bf{i}", [64, T], BF16) for i in range(2)]
            K3t = [sb2(f"K3t{i}", [128, T // 128, L64], BF16) for i in range(2)]
            B3 = [sb2(f"B3{i}", [64, L64], BF16) for i in range(2)]
            at_t = [sb2(f"at_t{i}", [64, T], BF16) for i in range(2)]
            ctx_bf = [sb2(f"ctx_bf{i}", [64, Tc], BF16) for i in range(2)]
            nmax = [sb2(f"nmaxH{i}", [128, 1]) for i in range(2)]
            nsum = [sb2(f"nsumH{i}", [128, 1]) for i in range(2)]
            rinv3 = [sb2(f"rinv3{i}", [64, 1]) for i in range(2)]
            ps_a = [ps2(f"psH{i}", [128, 512]) for i in range(4)]
            ps_b3 = [ps2(f"psHb3{i}", [64, L64]) for i in range(2)]
            ps_t = [ps2(f"psHt{i}", [128, 128], BF16) for i in range(2)]

            def head_stages(h):
                par = h % 2
                K1n_, K1t_, K3raw_, K3bf_, K3t_, B3_, at_t_, ctx_bf_ = (
                    K1n[par], K1t[par], K3raw[par], K3bf[par], K3t[par], B3[par], at_t[par], ctx_bf[par])
                nmax_, nsum_, rinv3_ = nmax[par], nsum[par], rinv3[par]
                psa = [ps_a[2 * par], ps_a[2 * par + 1]]
                psb3 = ps_b3[par]
                pst = ps_t[par]
                hp, hj = 64 * (h % 2), h // 2
                for tt in range(T // 128):
                    pk = psa[tt % 2]
                    mm(S, pk[:, 0:L64], q_fm[hp:hp + 64, hj, 128 * tt:128 * (tt + 1)], k_l[hp:hp + 64, hj, :], start=True, stop=True, tag=f"H_K1_{h}_{tt}")
                    v_op(S, "reduce_max", nmax_[:], pk[:, 0:L64], AX.X, negate=True, tag=f"H_K1m_{h}_{tt}")
                    act(S, K1n_[:], pk[:, 0:L64], AF.Exp, bias=nmax_[:], accum_out=nsum_[:], tag=f"H_K1e_{h}_{tt}")
                    v_op(S, "reciprocal", nsum_[:], nsum_[:], tag=f"H_K1r_{h}_{tt}")
                    v_op(S, "tensor_scalar", K1n_[:], K1n_[:], nsum_[:], None, ALU.mult, tag=f"H_K1n_{h}_{tt}")
                    tr_op(S, pst[0:64, :], K1n_[:], C.ident_bf[:], f"H_K1t_{h}_{tt}")
                    v_op(S, "tensor_copy", K1t_[:, 128 * tt:128 * (tt + 1)], pst[0:64, :], tag=f"H_K1te_{h}_{tt}")
                    yield
                for cj in range(nch):
                    mm(S, psa[cj % 2][0:64, :], q_l[hp:hp + 64, hj, :], k_fm[hp:hp + 64, hj, 512 * cj:512 * (cj + 1)], start=True, stop=True, tag=f"H_K3_{h}_{cj}")
                    v_op(S, "tensor_copy", K3raw_[:, 512 * cj:512 * (cj + 1)], psa[cj % 2][0:64, :], tag=f"H_K3c_{h}_{cj}")
                    yield
                v_op(S, "reduce_max", rinv3_[:], K3raw_[:], AX.X, negate=True, tag=f"H_K3m_{h}")
                act(S, K3bf_[:], K3raw_[:], AF.Exp, bias=rinv3_[:], accum_out=rinv3_[:], tag=f"H_K3e_{h}")
                v_op(S, "reciprocal", rinv3_[:], rinv3_[:], tag=f"H_K3r_{h}")
                yield
                for tt in range(T // 128):
                    tr_op(S, pst[:, 0:64], K3bf_[:, 128 * tt:128 * (tt + 1)], C.ident_bf[0:64, 0:64], f"H_K3t_{h}_{tt}")
                    v_op(S, "tensor_copy", K3t_[:, tt, :], pst[:, 0:64], tag=f"H_K3te_{h}_{tt}")
                    if tt % 4 == 3:
                        yield
                for tt in range(T // 128):
                    mm(S, psb3[:], K3t_[:, tt, :], vtok[:, tt, 64 * h:64 * (h + 1)], start=(tt == 0), stop=(tt == T // 128 - 1), tag=f"H_B3_{h}_{tt}")
                v_op(S, "tensor_scalar", B3_[:], psb3[:], rinv3_[:], None, ALU.mult, tag=f"H_B3e_{h}")
                yield
                for cj in range(nch):
                    mm(S, psa[cj % 2][0:64, :], inv_bf[:, h, :], K1t_[:, 512 * cj:512 * (cj + 1)], start=True, stop=True, tag=f"H_at_{h}_{cj}")
                    v_op(S, "tensor_copy", at_t_[:, 512 * cj:512 * (cj + 1)], psa[cj % 2][0:64, :], tag=f"H_ate_{h}_{cj}")
                    yield
                for cj in range(nch):
                    pc = psa[cj % 2]
                    mm(S, pc[0:64, :], B3_[:], at_t_[:, 512 * cj:512 * (cj + 1)], start=True, stop=True, tag=f"H_ctx_{h}_{cj}")
                    for sub in range(4):
                        st0 = 512 * cj + 128 * sub
                        for rel in (-1, 0, 1):
                            tl = st0 // 128 + rel
                            if 0 <= tl < T // 128:
                                S.op("pe", lambda hh, _o=pc[0:64, 128 * sub:128 * (sub + 1)], _l=vtok[:, tl, 64 * h:64 * (h + 1)], _r=pat[:, h, rel + 1, :]:
                                     hh.matmul(_o, _l, _r, start=False, stop=True, skip_group_check=True),
                                     reads=[vtok[:, tl, 64 * h:64 * (h + 1)], pat[:, h, rel + 1, :]], writes=[pc[0:64, 128 * sub:128 * (sub + 1)]], tag=f"H_cv_{h}_{cj}_{sub}_{rel}")
                    v_op(S, "tensor_copy", ctx_bf_[:], pc[0:64, :], tag=f"H_ce_{h}_{cj}")
                    dma(S, "sp", att_d.ap()[64 * h:64 * (h + 1), 512 * cj:512 * (cj + 1)], ctx_bf_[:], tag=f"H_cst_{h}_{cj}")
                    yield

            for hp0 in range(0, NHEAD, 2):
                ga, gb = head_stages(hp0), head_stages(hp0 + 1)
                act_a = act_b = True
                while act_a or act_b:
                    if act_a:
                        try:
                            next(ga)
                        except StopIteration:
                            act_a = False
                    if act_b:
                        try:
                            next(gb)
                        except StopIteration:
                            act_b = False

        with pscope(S) as pp:
            sb2 = lambda n, s, d=F32: pp.enter_context(nc.sbuf_tensor(f"{n}_{S.uid()}", list(s), d))
            x_sb = sb2("xR", [128, ND, Tc])
            a_sb = sb2("aR", [128, ND, Tc], BF16)
            for ci in range(nch):
                t0 = ci * Tc
                dma(S, "sp", x_sb[:], fm_dram(x_in_h, t0, t0 + Tc), tag=f"R_ldx{ci}")
                dma(S, "sp", a_sb[:], fm_dram(att_d, t0, t0 + Tc), tag=f"R_lda{ci}")
                for j in range(ND):
                    v_op(S, "tensor_add", x_sb[:, j, :], x_sb[:, j, :], a_sb[:, j, :], tag=f"R_add{ci}_{j}")
                dma(S, "sp", fm_dram(x_out_h, t0, t0 + Tc), x_sb[:], tag=f"R_stx{ci}")


def run_l0_attn_b(per_core, n_cores=NCORES, **kw):
    S = Prog("l0b")
    x_in = S.dram("x_in", [D, T], F32, kind="ExternalInput")
    x_out = S.dram("x_out", [D, T], F32, kind="ExternalOutput")
    C = Common(S)
    emit_l0_attn_b(S, C, x_in, x_out)
    res = S.run(per_core, n_cores, **kw)
    return [r["x_out"] for r in res.results], res


def emit_l0_attn_a(S, C, x_in_h, x_out_h):
    nc = S.nc
    dd = {}
    for nm, sh in (("m_w_in", [D, 2 * DI]), ("m_norm_scale", [128, ND]), ("m_conv_w", [128, NDI, 4]),
                   ("m_conv_b", [128, NDI]), ("m_A_log", [128, NDI, NSTATE]), ("m_D", [128, NDI]),
                   ("m_wx_delta", [DI, 64]), ("m_wx_B", [DI, NSTATE]), ("m_wx_C", [DI, NSTATE]),
                   ("m_w_dt", [64, DI]), ("m_b_dt", [128, NDI]), ("m_w_out", [DI, D])):
        dd[nm] = S.dram(nm, sh, F32, kind="ExternalInput")
    u0 = S.uid()
    y_d = S.dram(f"m_y{u0}", [D, T], F32, kind="Internal")
    delta_d = S.dram(f"m_delta{u0}", [DI, T], F32, kind="Internal")
    xsc_d = S.dram(f"m_xsc{u0}", [DI, T], F32, kind="Internal")
    res_d = S.dram(f"m_res{u0}", [DI, T], F32, kind="Internal")
    oact_d = S.dram(f"m_oact{u0}", [DI, T], BF16, kind="Internal")
    Tc = 512
    nch = T // Tc

    with pscope(S) as ph:
        sbl = lambda n, s, d=F32: ph.enter_context(nc.sbuf_tensor(f"{n}_{S.uid()}", list(s), d))
        norm_scale = sbl("m_ns", [128, ND]); dma(S, "sp", norm_scale[:], dd["m_norm_scale"].ap(), tag="ld_mns")
        conv_w = sbl("m_cw", [128, NDI, 4]); dma(S, "sp", conv_w[:], dd["m_conv_w"].ap(), tag="ld_mcw")
        conv_b = sbl("m_cb", [128, NDI]); dma(S, "sp", conv_b[:], dd["m_conv_b"].ap(), tag="ld_mcb")
        D_sb = sbl("m_Dv", [128, NDI]); dma(S, "sp", D_sb[:], dd["m_D"].ap(), tag="ld_mD")
        b_dt = sbl("m_bdt", [128, NDI]); dma(S, "sp", b_dt[:], dd["m_b_dt"].ap(), tag="ld_mbdt")
        A_sb = sbl("m_A", [128, NDI, NSTATE])
        dma(S, "sp", A_sb[:], dd["m_A_log"].ap(), tag="ld_mA")
        act(S, A_sb[:], A_sb[:], AF.Exp, tag="mA_exp")
        v_op(S, "tensor_scalar", A_sb[:], A_sb[:], -1.0, None, ALU.mult, tag="mA_neg")
        wxd = sbl("m_wxd", [128, NDI, 64], BF16)
        dma(S, "pool", wxd[:], dd["m_wx_delta"].ap().rearrange("(a p) n -> p a n", p=128), tag="ld_wxd")
        wxB = sbl("m_wxB", [128, NDI, NSTATE], BF16)
        dma(S, "pool", wxB[:], dd["m_wx_B"].ap().rearrange("(a p) n -> p a n", p=128), tag="ld_wxB")
        wxC = sbl("m_wxC", [128, NDI, NSTATE], BF16)
        dma(S, "pool", wxC[:], dd["m_wx_C"].ap().rearrange("(a p) n -> p a n", p=128), tag="ld_wxC")
        w_dt = sbl("m_wdt", [64, DI], BF16)
        dma(S, "pool", w_dt[:], dd["m_w_dt"].ap(), tag="ld_wdt")
        B_all = sbl("m_Ball", [NSTATE, T], BF16)
        C_all = sbl("m_Call", [NSTATE, T], BF16)
        d_sel = S.dram("c_sel128", [NHEAD, NHEAD, 128], BF16, kind="ExternalInput")
        sel128 = sbl("m_sel", [NHEAD, NHEAD, 128], BF16)
        dma(S, "sp", sel128[:], d_sel.ap().rearrange("a p n -> p a n"), tag="ld_msel")

        with pscope(S) as pp:
            sb2 = lambda n, s, d=F32: pp.enter_context(nc.sbuf_tensor(f"{n}_{S.uid()}", list(s), d))
            ps2 = lambda n, s, d=F32: pp.enter_context(nc.psum_tensor(f"{n}_{S.uid()}", list(s), d))
            x_sb = sb2("xM", [128, ND, Tc])
            xh = sb2("xhM", [128, ND, Tc], BF16)
            xbq = sb2("xbqM", [128, Tc], BF16)
            rows = mk_ln_rows(S, Tc, "rM", eps2=True, alloc=sb2)
            xs_sb = sb2("xsM", [128, NDI, Tc])
            xs_ext = sb2("xseM", [128, NDI, Tc + 4])
            prev3 = sb2("p3M", [128, NDI, 4])
            xsb_bf = sb2("xsbM", [128, NDI, Tc], BF16)
            resb = [sb2(f"resM{i}", [128, Tc]) for i in range(2)]
            din_bf = sb2("dinM", [64, Tc], BF16)
            dbuf = [sb2(f"dbM{i}", [128, Tc]) for i in range(2)]
            obuf = [sb2(f"obM{i}", [128, Tc]) for i in range(2)]
            wsl = [sb2(f"wslM{i}", [128, ND, 128], BF16) for i in range(2)]
            ps_s, ps_ss = ps2("psM_s", [1, Tc]), ps2("psM_ss", [1, Tc])
            ps_bc = ps2("psM_bc", [128, Tc])
            ps_o = [ps2("psM_o0", [128, Tc]), ps2("psM_o1", [128, Tc])]
            ps_p = ps2("psM_p", [64, Tc])
            ps_n = ps2("psM_n", [NSTATE, 2 * Tc])
            v_op(S, "memset", prev3[:], 0.0, tag="m_p3z")
            for ci in range(nch):
                t0 = ci * Tc
                dma(S, "sp", x_sb[:], fm_dram(x_in_h, t0, t0 + Tc), tag=f"M_ldx{ci}")
                for j in range(ND):
                    v_op(S, "tensor_copy", xh[:, j, :], x_sb[:, j, :], tag=f"M_xb{ci}_{j}")
                    mm(S, ps_s[:], C.ones_col_bf[:], xh[:, j, :], start=(j == 0), stop=(j == ND - 1), tag=f"M_s{ci}_{j}")
                    act(S, xbq[:], x_sb[:, j, :], AF.Square, tag=f"M_xq{ci}_{j}")
                    mm(S, ps_ss[:], C.ones_col_bf[:], xbq[:], start=(j == 0), stop=(j == ND - 1), tag=f"M_ss{ci}_{j}")
                ln_rows(S, ps_s, ps_ss, rows, D, EPS_LN, f"M_r{ci}", eps2=EPS_LN)
                bcast_row(S, C, ps_bc[:], rows["m"][:], f"M_bm{ci}")
                for j in range(ND):
                    v_op(S, "tensor_sub", x_sb[:, j, :], x_sb[:, j, :], ps_bc[:], tag=f"M_sub{ci}_{j}")
                bcast_row(S, C, ps_bc[:], rows["rs2"][:], f"M_br{ci}")
                for j in range(ND):
                    v_op(S, "scalar_tensor_tensor", xh[:, j, :], x_sb[:, j, :], norm_scale[:, j:j + 1], ps_bc[:], ALU.mult, ALU.mult, tag=f"M_xh{ci}_{j}")
                bcast_row(S, C, ps_bc[:], rows["rs"][:], f"M_bro{ci}")
                for j in range(ND):
                    v_op(S, "tensor_mul", x_sb[:, j, :], x_sb[:, j, :], ps_bc[:], tag=f"M_y{ci}_{j}")
                dma(S, "sp", fm_dram(y_d, t0, t0 + Tc), x_sb[:], tag=f"M_sty{ci}")
                for mt in range(2 * NDI):
                    w = wsl[mt % 2]
                    load_cast_w(S, dd["m_w_in"], w, D, 128, f"M_wi{ci}_{mt}", colsl=(128 * mt, 128 * (mt + 1)))
                    po = ps_o[mt % 2]
                    for j in range(ND):
                        mm(S, po[:], w[:, j, :], xh[:, j, :], start=(j == 0), stop=(j == ND - 1), tag=f"M_h4{ci}_{mt}_{j}")
                    if mt < NDI:
                        act(S, xs_sb[:, mt, :], po[:], AF.Identity, tag=f"M_xse{ci}_{mt}")
                    else:
                        rb = resb[mt % 2]
                        act(S, rb[:], po[:], AF.Silu, tag=f"M_rese{ci}_{mt}")
                        dma(S, "sp", res_d.ap()[128 * (mt - NDI):128 * (mt - NDI + 1), t0:t0 + Tc], rb[:], tag=f"M_resst{ci}_{mt}")
                for c in range(NDI):
                    v_op(S, "tensor_copy", xs_ext[:, c, 0:4], prev3[:, c, :], tag=f"M_cp{ci}_{c}")
                    v_op(S, "tensor_copy", xs_ext[:, c, 4:4 + Tc], xs_sb[:, c, :], tag=f"M_ce{ci}_{c}")
                    v_op(S, "tensor_copy", prev3[:, c, :], xs_ext[:, c, Tc:Tc + 4], tag=f"M_cu{ci}_{c}")
                    v_op(S, "tensor_scalar", xs_sb[:, c, :], xs_ext[:, c, 1:1 + Tc], conv_w[:, c, 0:1], None, ALU.mult, tag=f"M_cv0{ci}_{c}")
                    for tap in range(1, 4):
                        v_op(S, "scalar_tensor_tensor", xs_sb[:, c, :], xs_ext[:, c, 1 + tap:1 + tap + Tc], conv_w[:, c, tap:tap + 1], xs_sb[:, c, :], ALU.mult, ALU.add, tag=f"M_cv{tap}{ci}_{c}")
                    act(S, xs_sb[:, c, :], xs_sb[:, c, :], AF.Silu, bias=conv_b[:, c:c + 1], tag=f"M_cs{ci}_{c}")
                    v_op(S, "tensor_copy", xsb_bf[:, c, :], xs_sb[:, c, :], tag=f"M_cb{ci}_{c}")
                dma(S, "sp", xsc_d.ap()[:, t0:t0 + Tc].rearrange("(a p) t -> p a t", p=128), xs_sb[:], tag=f"M_xsst{ci}")
                for j in range(NDI):
                    mm(S, ps_p[:], wxd[:, j, :], xsb_bf[:, j, :], start=(j == 0), stop=(j == NDI - 1), tag=f"M_pd{ci}_{j}")
                for j in range(NDI):
                    mm(S, ps_n[:, 0:Tc], wxB[:, j, :], xsb_bf[:, j, :], start=(j == 0), stop=(j == NDI - 1), tag=f"M_pB{ci}_{j}")
                for j in range(NDI):
                    mm(S, ps_n[:, Tc:2 * Tc], wxC[:, j, :], xsb_bf[:, j, :], start=(j == 0), stop=(j == NDI - 1), tag=f"M_pC{ci}_{j}")
                v_op(S, "tensor_copy", din_bf[:], ps_p[:], tag=f"M_dine{ci}")
                v_op(S, "tensor_copy", B_all[:, t0:t0 + Tc], ps_n[:, 0:Tc], tag=f"M_Be{ci}")
                v_op(S, "tensor_copy", C_all[:, t0:t0 + Tc], ps_n[:, Tc:2 * Tc], tag=f"M_Ce{ci}")
                for c in range(NDI):
                    po = ps_o[c % 2]
                    mm(S, po[:], w_dt[:, 128 * c:128 * (c + 1)], din_bf[:], start=True, stop=True, tag=f"M_dt{ci}_{c}")
                    db = dbuf[c % 2]
                    act(S, db[:], po[:], AF.Exp, bias=b_dt[:, c:c + 1], tag=f"M_de{ci}_{c}")
                    act(S, db[:], db[:], AF.Ln, bias=1.0, tag=f"M_dl{ci}_{c}")
                    dma(S, "sp", delta_d.ap()[128 * c:128 * (c + 1), t0:t0 + Tc], db[:], tag=f"M_dst{ci}_{c}")

        if getattr(S, "l0a_mid_hook", None) is not None:
            S.l0a_mid_hook(lambda n, s, d=BF16: ph.enter_context(nc.sbuf_tensor(n, list(s), d)))
        wo_sb = sbl("wo_sb", [128, NDI, D], BF16)
        for kt in range(NDI):
            dma(S, "pool", wo_sb[:, kt, :], dd["m_w_out"].ap()[128 * kt:128 * (kt + 1), :], tag=f"ld_wo{kt}")

        TH = T // 2
        with pscope(S) as pp:
            sb2 = lambda n, s, d=F32: pp.enter_context(nc.sbuf_tensor(f"{n}_{S.uid()}", list(s), d))
            ps2 = lambda n, s, d=F32: pp.enter_context(nc.psum_tensor(f"{n}_{S.uid()}", list(s), d))
            B_bc = sb2("B_bc", [128, NSTATE, TH], BF16)
            C_bc = sb2("C_bc", [128, NSTATE, TH], BF16)
            carries = sb2("carr", [128, NDI, NSTATE])
            ps_w = ps2("psB_w", [128, 2048])
            delta_c = [sb2(f"delta_c{i}", [128, TH]) for i in range(2)]
            xs_c = [sb2(f"xs_cB{i}", [128, TH]) for i in range(2)]
            res_c = [sb2(f"res_cB{i}", [128, TH]) for i in range(2)]
            u_c = [sb2(f"u_cB{i}", [128, TH], BF16) for i in range(2)]
            dA = [sb2(f"dA_B{i}", [128, TH]) for i in range(2)]
            dBx = [sb2(f"dBx_B{i}", [128, TH], BF16) for i in range(2)]
            s_f = sb2("s_fB", [128, TH], BF16)
            y_c = sb2("y_cB", [128, TH])
            p_f = [sb2(f"p_fB{i}", [128, TH], BF16) for i in range(2)]
            ob = sb2("obB", [128, TH], BF16)
            ps_y = pp.enter_context(nc.psum_tensor(f"psB_y_{S.uid()}", [128, TH], F32))
            v_op(S, "memset", carries[:], 0.0, tag="MB_cz")
            for half in range(2):
                h0 = half * TH
                for n in range(NSTATE):
                    for cj in range(TH // 512):
                        mm(S, ps_w[:, 512 * cj:512 * (cj + 1)], sel128[:, n, :], B_all[:, h0 + 512 * cj:h0 + 512 * (cj + 1)], start=True, stop=True, tag=f"MB_b{half}_{n}_{cj}")
                    v_op(S, "tensor_copy", B_bc[:, n, :], ps_w[:, 0:TH], tag=f"MB_be{half}_{n}")
                    for cj in range(TH // 512):
                        mm(S, ps_w[:, 512 * cj:512 * (cj + 1)], sel128[:, n, :], C_all[:, h0 + 512 * cj:h0 + 512 * (cj + 1)], start=True, stop=True, tag=f"MC_b{half}_{n}_{cj}")
                    v_op(S, "tensor_copy", C_bc[:, n, :], ps_w[:, 0:TH], tag=f"MC_be{half}_{n}")
                for c in range(NDI):
                    cb = c % 2
                    dma(S, "sp", delta_c[cb][:], delta_d.ap()[128 * c:128 * (c + 1), h0:h0 + TH], tag=f"MB_ldd{half}_{c}")
                    dma(S, "sp", xs_c[cb][:], xsc_d.ap()[128 * c:128 * (c + 1), h0:h0 + TH], tag=f"MB_ldxs{half}_{c}")
                    dma(S, "sp", res_c[cb][:], res_d.ap()[128 * c:128 * (c + 1), h0:h0 + TH], tag=f"MB_ldr{half}_{c}")
                    v_op(S, "tensor_mul", u_c[cb][:], delta_c[cb][:], xs_c[cb][:], tag=f"MB_u{half}_{c}")
                    for n in range(NSTATE):
                        da = dA[n % 2]
                        act(S, da[:], delta_c[cb][:], AF.Exp, scale=A_sb[:, c, n:n + 1], tag=f"MB_dA{half}_{c}_{n}")
                        db = dBx[n % 2]
                        v_op(S, "tensor_mul", db[:], u_c[cb][:], B_bc[:, n, :], tag=f"MB_dBx{half}_{c}_{n}")
                        S.op("dve", lambda h, _s=s_f[:], _a=da[:], _b=db[:], _i=(0.0 if half == 0 else carries[:, c, n:n + 1]):
                             h.tensor_tensor_scan(_s, _a, _b, _i, ALU.mult, ALU.add),
                             reads=[da[:], db[:]] + ([carries[:, c, n:n + 1]] if half else []),
                             writes=[s_f[:]], tag=f"MB_scan{half}_{c}_{n}")
                        if half == 0:
                            v_op(S, "tensor_copy", carries[:, c, n:n + 1], s_f[:, TH - 1:TH], tag=f"MB_cs{c}_{n}")
                        pf = p_f[n % 2]
                        v_op(S, "tensor_mul", pf[:], s_f[:], C_bc[:, n, :], tag=f"MB_p{half}_{c}_{n}")
                        for cj in range(TH // 512):
                            mm(S, ps_y[:, 512 * cj:512 * (cj + 1)], C.ident_bf[:], pf[:, 512 * cj:512 * (cj + 1)], start=(n == 0), stop=(n == NSTATE - 1), tag=f"MB_ya{half}_{c}_{n}_{cj}")
                    v_op(S, "scalar_tensor_tensor", y_c[:], xs_c[cb][:], D_sb[:, c:c + 1], ps_y[:], ALU.mult, ALU.add, tag=f"MB_yd{half}_{c}")
                    v_op(S, "tensor_mul", ob[:], y_c[:], res_c[cb][:], tag=f"MB_or{half}_{c}")
                    dma(S, "sp", oact_d.ap()[128 * c:128 * (c + 1), h0:h0 + TH], ob[:], tag=f"MB_ost{half}_{c}")

        with pscope(S) as pp:
            sb2 = lambda n, s, d=F32: pp.enter_context(nc.sbuf_tensor(f"{n}_{S.uid()}", list(s), d))
            ps2 = lambda n, s, d=F32: pp.enter_context(nc.psum_tensor(f"{n}_{S.uid()}", list(s), d))
            x_sb = sb2("xMC", [128, ND, Tc])
            y_sb = sb2("yMC", [128, ND, Tc])
            oa = [sb2(f"oaC{i}", [128, Tc], BF16) for i in range(2)]
            ps_o = ps2("psC_o", [128, 8, 512])
            for ci in range(nch):
                t0 = ci * Tc
                dma(S, "sp", x_sb[:], fm_dram(x_in_h, t0, t0 + Tc), tag=f"C_ldx{ci}")
                dma(S, "sp", y_sb[:], fm_dram(y_d, t0, t0 + Tc), tag=f"C_ldy{ci}")
                for j in range(ND):
                    v_op(S, "tensor_add", x_sb[:, j, :], x_sb[:, j, :], y_sb[:, j, :], tag=f"C_xy{ci}_{j}")
                for kt in range(NDI):
                    o = oa[kt % 2]
                    dma(S, "sp", o[:], oact_d.ap()[128 * kt:128 * (kt + 1), t0:t0 + Tc], tag=f"C_lo{ci}_{kt}")
                    for di in range(ND):
                        mm(S, ps_o[:, di, :], wo_sb[:, kt, 128 * di:128 * (di + 1)], o[:], start=(kt == 0), stop=(kt == NDI - 1), tag=f"C_mm{ci}_{kt}_{di}")
                for di in range(ND):
                    v_op(S, "tensor_add", x_sb[:, di, :], x_sb[:, di, :], ps_o[:, di, :], tag=f"C_res{ci}_{di}")
                dma(S, "sp", fm_dram(x_out_h, t0, t0 + Tc), x_sb[:], tag=f"C_stx{ci}")


def run_l0_attn_a(per_core, n_cores=NCORES, **kw):
    S = Prog("l0a")
    x_in = S.dram("x_in", [D, T], F32, kind="ExternalInput")
    x_out = S.dram("x_out", [D, T], F32, kind="ExternalOutput")
    C = Common(S)
    emit_l0_attn_a(S, C, x_in, x_out)
    res = S.run(per_core, n_cores, **kw)
    return [r["x_out"] for r in res.results], res


SUBLAYER_SEQ = (
    ("l0a", emit_l0_attn_a, {}),
    ("l0f1", emit_l0_ffn, {"idx": 0}),
    ("l0b", emit_l0_attn_b, {}),
    ("l0f2", emit_l0_ffn, {"idx": 0}),
    ("l1a", emit_l1_attn_a, {}),
    ("l1f1", emit_l1_ffn, {"idx": 0}),
    ("l1b", emit_l1_attn_b, {}),
    ("l1f2", emit_l1_ffn, {"idx": 0}),
)


def kernel_unfused(**inputs):
    per_core = prep_host(inputs)
    for name, emitfn, kw in SUBLAYER_SEQ:
        S = Prog(name)
        x_in = S.dram("x_in", [D, T], F32, kind="ExternalInput")
        x_out = S.dram("x_out", [D, T], F32, kind="ExternalOutput")
        C = Common(S)
        emitfn(S, C, x_in, x_out, **kw)
        res = S.run(per_core, NCORES)
        for c in range(NCORES):
            per_core[c]["x_in"] = np.ascontiguousarray(np.asarray(res.results[c]["x_out"]))
    out = np.stack([per_core[c]["x_in"].T for c in range(NCORES)]).astype(np.float32)
    return out


def build_mega():
    S = Prog("mega")
    def _hook(alloc):
        l0f_wprep(S, alloc)
        wconv_all(S, alloc)
    S.l0a_mid_hook = _hook
    x_ext_in = S.dram("x_in", [D, T], F32, kind="ExternalInput")
    x_ext_out = S.dram("x_out", [D, T], F32, kind="ExternalOutput")
    xs = [x_ext_in] + [S.dram(f"x_mid_{k}", [D, T], F32, kind="Internal") for k in range(len(SUBLAYER_SEQ) - 1)] + [x_ext_out]
    C = Common(S)
    S.phase_marks = []
    for k, (name, emitfn, kw) in enumerate(SUBLAYER_SEQ):
        with S.nc.named_scope(name):
            emitfn(S, C, xs[k], xs[k + 1], **kw)
        S.phase_marks.append((name, len(S.ops)))
    return S


def kernel(**inputs):
    per_core = prep_host(inputs)
    S = build_mega()
    res = S.run(per_core, NCORES)
    out = np.stack([np.asarray(res.results[c]["x_out"]).T for c in range(NCORES)]).astype(np.float32)
    return out
